# version 04
# baseline (speedup 1.0000x reference)
import numpy as np
import ml_dtypes
import concourse.bass as bass
import concourse.mybir as mybir
from concourse.bass_utils import run_bass_kernel_spmd

F32 = mybir.dt.float32
BF16 = mybir.dt.bfloat16
I32 = mybir.dt.int32
AF = mybir.ActivationFunctionType
ALU = mybir.AluOpType
AX = mybir.AxisListType

NCORES = 8
SEQ = 8192
D = 2048
T = SEQ // NCORES
NKT = SEQ // 128
EPS = 1e-5


def build_nc():
    nc = bass.Bass("TRN2", target_bir_lowering=False)

    H = {}
    H["xf"] = nc.dram_tensor("x_full", [SEQ, D], F32, kind="ExternalInput").ap()
    H["xo"] = nc.dram_tensor("x_own", [T, D], F32, kind="ExternalInput").ap()
    H["wq"] = nc.dram_tensor("wq_t", [16, 128, 2048], F32, kind="ExternalInput").ap()
    H["bq"] = nc.dram_tensor("bq_t", [128, 16], F32, kind="ExternalInput").ap()
    H["wk"] = nc.dram_tensor("wk_t", [128, 8192], F32, kind="ExternalInput").ap()
    H["bk"] = nc.dram_tensor("bk_t", [128, 4], F32, kind="ExternalInput").ap()
    H["wv"] = nc.dram_tensor("wv_t", [128, 8192], F32, kind="ExternalInput").ap()
    H["bv"] = nc.dram_tensor("bv_bc", [128, 512], F32, kind="ExternalInput").ap()
    H["wo"] = nc.dram_tensor("wo_t", [16, 128, 2048], F32, kind="ExternalInput").ap()
    H["bo"] = nc.dram_tensor("bo_bc", [128, 2048], F32, kind="ExternalInput").ap()
    H["wg"] = nc.dram_tensor("wg_t", [64, 128, 2048], I32, kind="ExternalInput").ap()
    H["wu"] = nc.dram_tensor("wu_t", [64, 128, 2048], I32, kind="ExternalInput").ap()
    H["wd"] = nc.dram_tensor("wd_t", [16, 64, 128, 128], I32, kind="ExternalInput").ap()
    H["gs"] = nc.dram_tensor("gs_t", [128, 64], F32, kind="ExternalInput").ap()
    H["us"] = nc.dram_tensor("us_t", [128, 64], F32, kind="ExternalInput").ap()
    H["ds"] = nc.dram_tensor("ds_t", [128, 16], F32, kind="ExternalInput").ap()
    H["idb"] = nc.dram_tensor("identb", [128, 128], BF16, kind="ExternalInput").ap()
    H["idf"] = nc.dram_tensor("identf", [128, 128], F32, kind="ExternalInput").ap()
    H["o64"] = nc.dram_tensor("ones64b", [128, 64], BF16, kind="ExternalInput").ap()
    H["of"] = nc.dram_tensor("onesf", [128, 64], F32, kind="ExternalInput").ap()
    H["y"] = nc.dram_tensor("y", [T, D], F32, kind="ExternalOutput").ap()
    import os
    H["_dbg"] = bool(os.environ.get("KDEBUG"))
    if H["_dbg"]:
        H["dkt"] = nc.dram_tensor("dbg_kt", [128, 32768], BF16, kind="ExternalOutput").ap()
        H["dv"] = nc.dram_tensor("dbg_v", [128, 32768], BF16, kind="ExternalOutput").ap()
        H["dqt"] = nc.dram_tensor("dbg_qt", [128, 16384], BF16, kind="ExternalOutput").ap()
        H["dao"] = nc.dram_tensor("dbg_ao", [128, 16384], BF16, kind="ExternalOutput").ap()
        H["dwo"] = nc.dram_tensor("dbg_wo", [128, 32768], BF16, kind="ExternalOutput").ap()
        H["ds_"] = nc.dram_tensor("dbg_s", [T, D], F32, kind="ExternalOutput").ap()
    import os as _os
    _scrkind = "ExternalOutput" if _os.environ.get("KDEBUG") else "Internal"
    H["x2s"] = nc.dram_tensor("x2_scr", [T, D], F32, kind=_scrkind).ap()
    H["ffs"] = nc.dram_tensor("ffn_scr", [T, D], F32, kind=_scrkind).ap()

    SB_BASE = 16512

    def at(name, shape, dtype, off):
        return nc.alloc_sbuf_tensor_at(name, shape, dtype, offset=SB_BASE + off)

    B = {}
    B["kt"] = at("kt_b", [128, 32768], BF16, 0)
    B["v"] = at("v_b", [128, 32768], BF16, 65536)
    B["wo"] = at("wo_b", [128, 32768], BF16, 0)
    B["hf"] = at("hfull", [128, 65536], BF16, 0)
    B["bo"] = at("bo_s", [128, 2048], F32, 65536)
    B["xr"] = [at(f"xr{i}", [128, 2048], F32, 73728 + 8192 * i) for i in range(2)]
    B["x2t"] = [at(f"x2t{i}", [128, 2048], F32, 90112 + 8192 * i) for i in range(2)]
    B["x2bf"] = at("x2bf", [128, 2048], BF16, 106496)
    B["sqs"] = at("sqs", [128, 2048], F32, 110592)
    B["xbo"] = at("xbo", [128, 2048], F32, 118784)
    B["qt"] = at("qt_b", [128, 16384], BF16, 131072)
    B["x2f"] = at("x2f", [128, 16384], BF16, 131072)
    B["xfblk"] = at("xfblk", [128, 4096], BF16, 131072)
    B["wkb"] = at("wkb", [128, 8192], BF16, 139264)
    B["ffin"] = [at(f"ffin{i}", [128, 2048], F32, 131072 + 8192 * i) for i in range(2)]
    B["x2in"] = [at(f"x2in{i}", [128, 2048], F32, 147456 + 8192 * i) for i in range(2)]
    B["ao"] = at("ao_b", [128, 16384], BF16, 163840)
    B["xfo"] = at("xfown", [128, 16384], BF16, 163840)
    B["wvb"] = at("wvb", [128, 8192], BF16, 163840)
    B["bv"] = at("bv_s", [128, 512], F32, 180224)
    B["wgst"] = at("wgstage", [128, 2048], I32, 163840)
    B["wust"] = at("wustage", [128, 2048], I32, 172032)
    B["wgb"] = [at(f"wgb{i}", [128, 2048], BF16, 180224 + 4096 * i) for i in range(2)]
    B["wub"] = [at(f"wub{i}", [128, 2048], BF16, 188416 + 4096 * i) for i in range(2)]
    B["wdst"] = [at(f"wdst{i}", [128, 1024], I32, 163840 + 4096 * i) for i in range(2)]
    B["wdb"] = [at(f"wdb{i}", [128, 1024], BF16, 172032 + 2048 * i) for i in range(2)]
    B["fm"] = at("fm_s", [128, 1024], F32, 176128)
    B["tok"] = [at(f"tokst{i}", [128, 512], F32, 180224 + 2048 * i) for i in range(2)]
    B["outt"] = [at(f"outt{i}", [128, 2048], F32, 163840 + 8192 * i) for i in range(2)]
    B["sq2"] = at("sq2", [128, 2048], F32, 180224)
    B["idb"] = at("idb_s", [128, 128], BF16, 196608)
    B["idf"] = at("idf_s", [128, 128], F32, 196864)
    B["o64"] = at("o64_s", [128, 64], BF16, 197376)
    B["of"] = at("of_s", [128, 64], F32, 197504)
    B["bq"] = at("bq_s", [128, 16], F32, 197760)
    B["bk"] = at("bk_s", [128, 4], F32, 197824)
    B["gs"] = at("gs_s", [128, 64], F32, 197888)
    B["us"] = at("us_s", [128, 64], F32, 198144)
    B["ds"] = at("ds_s", [128, 16], F32, 198400)
    B["ln"] = at("lnst", [128, 16], F32, 210784)
    B["xst"] = [at(f"xstag{i}", [128, 2048], BF16, 198496 + 4096 * i) for i in range(2)]
    B["pt"] = [at(f"pt{i}", [128, 1024], BF16, 198496 + 2048 * i) for i in range(4)]
    B["rd"] = at("rd_t", [128, 512], F32, 206688)
    B["bcs"] = at("bcs", [128, 512], F32, 208736)
    B["sg"] = at("sg_s", [128, 512], F32, 206688)
    B["usv"] = at("usv_s", [128, 512], F32, 208736)

    B["ps"] = nc.alloc_psum_tensor("ps32", [128, 3072], F32)
    B["psbf"] = nc.alloc_psum_tensor("psbf", [128, 2048], BF16)

    from contextlib import ExitStack
    with ExitStack() as es:
        ec = es.enter_context
        sems = {k: ec(nc.semaphore(f"sem{k}")) for k in
                ("P", "A", "V", "G0", "G1", "W", "S0", "S1", "Y0", "Y1")}
        block = ec(nc.Block())

        class Sched:
            def __init__(self, only, eng):
                self.only = only
                self.eng = eng
                self.cnt = {k: 0 for k in
                            ("P", "A", "V", "G0", "G1", "W", "S0", "S1", "Y0", "Y1")}

            def wait(self, engname, sem, val):
                if val is None or val <= 0:
                    return
                if self.only == engname:
                    self.eng.wait_ge(sems[sem], val)

            def run(self, engname, fn, inc=None, amt=1, selfwait=False):
                if inc is not None:
                    self.cnt[inc] += amt
                if self.only == engname:
                    inst = fn(self.eng)
                    if inc is not None:
                        inst.then_inc(sems[inc], amt)
                        if selfwait:
                            self.eng.wait_ge(sems[inc], self.cnt[inc])
                return self.cnt[inc] if inc is not None else None

        @block.gpsimd
        def _(g):
            emit_program(Sched("g", g), H, B)

        @block.tensor
        def _(t):
            emit_program(Sched("t", t), H, B)

        @block.scalar
        def _(s):
            emit_program(Sched("s", s), H, B)

        @block.vector
        def _(v):
            emit_program(Sched("v", v), H, B)

        @block.sync
        def _(sy):
            emit_program(Sched("y", sy), H, B)

    return nc

def emit_program(S_, H, B):
    w = S_.wait
    r = S_.run
    ps = B["ps"]
    psbf = B["psbf"]

    def bk32(i, n=512):
        return ps[:, 512 * i:512 * i + n]

    for k in ("idb", "idf", "o64", "of", "bq", "bk", "gs", "us", "ds", "bv"):
        r("g", lambda e, k=k: e.dma_start(out=B[k][:], in_=H[k]), inc="W", amt=16)
    r("g", lambda e: e.dma_start(out=B["wkb"][:], in_=H["wk"]), inc="W", amt=16)
    wconsts = r("g", lambda e: e.dma_start(out=B["wvb"][:], in_=H["wv"]), inc="W", amt=16)

    ev = {}

    def transpose_tile(src_buf, dst3):
        g0 = transpose_tile.seq
        for grp in range(4):
            half = grp % 2
            w("t", "A", ev.get(g0 + grp - 2, 0))
            for cc in range(4):
                c = 4 * grp + cc
                last = cc == 3
                tid = r("t", lambda e, c=c, half=half, cc=cc, src_buf=src_buf: e.transpose(
                    psbf[:, 1024 * half + 128 * cc:1024 * half + 128 * cc + 128],
                    src_buf[:, 128 * c:128 * c + 128], B["idb"][:]),
                    inc=("P" if last else None))
            w("s", "P", tid)
            ev[g0 + grp] = r("s", lambda e, half=half, grp=grp, dst3=dst3: e.copy(
                dst3(grp),
                psbf[:, 1024 * half:1024 * half + 512].rearrange("p (g q) -> p g q", g=4)),
                inc="A")
        transpose_tile.seq += 4
        return tid
    transpose_tile.seq = 0

    kev = {}
    vev = {}
    last_tr = {}
    for ti in range(NKT):
        buf = ti % 2
        slot_x = 2048 * buf
        w("g", "P", last_tr.get(ti - 2, 0))
        gsem = "G%d" % buf
        r("g", lambda e, ti=ti, buf=buf: e.dma_start(
            out=B["xst"][buf][:], in_=H["xf"][128 * ti:128 * ti + 128, :]),
          inc=gsem, amt=16)
        gneed = S_.cnt[gsem]
        if ti < 2:
            w("t", "W", wconsts)
            w("v", "W", wconsts)
        w("t", gsem, gneed)

        def dview(grp, slot_x=slot_x):
            return B["xfblk"][:, slot_x + 512 * grp:slot_x + 512 * grp + 512].rearrange(
                "p (g q) -> p g q", g=4)
        last_tr[ti] = transpose_tile(B["xst"][buf], dview)
        w("t", "A", ev[transpose_tile.seq - 1])
        for cc in range(4):
            kseq = 4 * ti + cc
            pb = cc % 2
            w("t", "V", kev.get(kseq - 2, 0))
            for c in range(16):
                last = c == 15
                kmm = r("t", lambda e, cc=cc, c=c, pb=pb, slot_x=slot_x: e.matmul(
                    bk32(pb, 128),
                    B["wkb"][:, 512 * c + 128 * cc:512 * c + 128 * cc + 128],
                    B["xfblk"][:, slot_x + 128 * c:slot_x + 128 * c + 128],
                    start=(c == 0), stop=(c == 15)), inc=("P" if last else None))
            w("v", "P", kmm)
            kev[kseq] = r("v", lambda e, cc=cc, ti=ti, pb=pb: e.tensor_scalar(
                B["kt"][:, 8192 * cc + 128 * ti:8192 * cc + 128 * ti + 128],
                bk32(pb, 128), B["bk"][:, cc:cc + 1], None, ALU.add),
                inc="V")
        vb = 2 + ti % 2
        w("t", "V", vev.get(ti - 2, 0))
        for c in range(16):
            last = c == 15
            vmm = r("t", lambda e, c=c, vb=vb, slot_x=slot_x: e.matmul(
                bk32(vb), B["xfblk"][:, slot_x + 128 * c:slot_x + 128 * c + 128],
                B["wvb"][:, 512 * c:512 * c + 512],
                start=(c == 0), stop=(c == 15)), inc=("P" if last else None))
        w("v", "P", vmm)
        vev[ti] = r("v", lambda e, ti=ti, vb=vb: e.tensor_tensor(
            B["v"][:, 512 * ti:512 * ti + 512], bk32(vb), B["bv"][:], ALU.add),
            inc="V")

    w("s", "V", vev[NKT - 1])
    for s8 in range(8):
        buf = s8 % 2
        w("g", "P", last_tr.get(NKT + s8 - 2, 0))
        gsem = "G%d" % buf
        r("g", lambda e, s8=s8, buf=buf: e.dma_start(
            out=B["xst"][buf][:], in_=H["xo"][128 * s8:128 * s8 + 128, :]),
          inc=gsem, amt=16)
        gneed = S_.cnt[gsem]
        w("t", gsem, gneed)

        def oview(grp, s8=s8):
            vv = B["xfo"][:].rearrange("p (c q) -> p c q", c=16)
            return vv[:, 4 * grp:4 * grp + 4, 128 * s8:128 * s8 + 128]
        last_tr[NKT + s8] = transpose_tile(B["xst"][buf], oview)

    qmm = {}
    qev = {}
    for j in range(16):
        buf = j % 2
        w("g", "P", qmm.get(j - 2, last_tr[NKT + 7]))
        gsem = "G%d" % buf
        r("g", lambda e, j=j, buf=buf: e.dma_start(
            out=B["xst"][buf][:], in_=H["wq"][j]), inc=gsem, amt=16)
        gneed = S_.cnt[gsem]
        w("t", gsem, gneed)
        qb_ = j % 2
        w("t", "V", qev.get(j - 2, max(kev[4 * (NKT - 1) + 3], vev[NKT - 1])))
        for c in range(16):
            for hh in range(2):
                last = (c == 15 and hh == 1)
                idx = r("t", lambda e, c=c, hh=hh, qb_=qb_, buf=buf: e.matmul(
                    ps[:, 1024 * qb_ + 512 * hh:1024 * qb_ + 512 * hh + 512],
                    B["xst"][buf][:, 128 * c:128 * c + 128],
                    B["xfo"][:, 1024 * c + 512 * hh:1024 * c + 512 * hh + 512],
                    start=(c == 0), stop=(c == 15)), inc=("P" if last else None))
                if last:
                    qmm[j] = idx
        w("v", "P", qmm[j])
        qev[j] = r("v", lambda e, j=j, qb_=qb_: e.tensor_scalar(
            B["qt"][:, 1024 * j:1024 * j + 1024], ps[:, 1024 * qb_:1024 * qb_ + 1024],
            B["bq"][:, j:j + 1], None, ALU.add), inc="V")

    if H["_dbg"]:
        w("y", "V", max(qev[15], vev[NKT - 1], kev[4 * (NKT - 1) + 3]))
        r("y", lambda e: e.dma_start(out=H["dkt"], in_=B["kt"][:]), inc="Y0", amt=16)
        r("y", lambda e: e.dma_start(out=H["dv"], in_=B["v"][:]), inc="Y0", amt=16)
        dbg0 = r("y", lambda e: e.dma_start(out=H["dqt"], in_=B["qt"][:]), inc="Y0", amt=16)
    psA = ps[:, 2048:2560]
    psB = ps[:, 2560:3072]
    pair_exp_base = qev[15] * 0 + S_.cnt["A"]
    prev_tb = vev[NKT - 1]
    prev_av = {}
    sc_last = {}
    wo_dma = {}
    qt_ready = qev[15]
    attn_first = True
    for j in range(16):
        for qb in range(2):
            cc = j // 4
            kvA = j // 4
            kvB = 4 + j // 4
            exp_id = {}
            av_id = {}
            sc_id = {}

            def emit_av(kt, kvA=kvA, kvB=kvB, exp_id=exp_id, av_id=av_id,
                        first_gate=None):
                pti = kt % 4
                w("t", "A", exp_id[kt])
                if kt == 0:
                    w("t", "V", first_gate)
                st = kt == 0
                sp = kt == 63
                r("t", lambda e, kt=kt, pti=pti, kvA=kvA, st=st, sp=sp: e.matmul(
                    psA[0:64, :], B["v"][:, 512 * kt + 64 * kvA:512 * kt + 64 * kvA + 64],
                    B["pt"][pti][:, 0:512], start=st, stop=sp, tile_position=(0, 0),
                    skip_group_check=True))
                r("t", lambda e, pti=pti, st=st, sp=sp: e.matmul(
                    psA[64:128, :], B["o64"][:], B["pt"][pti][:, 512:1024],
                    start=st, stop=sp, tile_position=(0, 64), skip_group_check=True))
                r("t", lambda e, kt=kt, pti=pti, kvB=kvB, st=st, sp=sp: e.matmul(
                    psB[64:128, :], B["v"][:, 512 * kt + 64 * kvB:512 * kt + 64 * kvB + 64],
                    B["pt"][pti][:, 512:1024], start=st, stop=sp, tile_position=(0, 64),
                    skip_group_check=True))
                av_id[kt] = r("t", lambda e, pti=pti, st=st, sp=sp: e.matmul(
                    psB[0:64, :], B["o64"][:], B["pt"][pti][:, 0:512],
                    start=st, stop=sp, tile_position=(0, 0), skip_group_check=True), inc="P")

            gate0 = prev_tb
            for kt in range(64):
                scb = kt % 2
                w("t", "A", exp_id.get(kt - 2, pair_exp_base))
                if attn_first and kt == 0:
                    w("t", "V", qt_ready)
                r("t", lambda e, kt=kt, scb=scb, cc=cc, j=j, qb=qb: e.matmul(
                    bk32(2 * scb),
                    B["kt"][0:64, 8192 * cc + 128 * kt:8192 * cc + 128 * kt + 128],
                    B["qt"][0:64, 1024 * j + 512 * qb:1024 * j + 512 * qb + 512],
                    start=True, stop=True, tile_position=(0, 0)))
                sc_id[kt] = r("t", lambda e, kt=kt, scb=scb, cc=cc, j=j, qb=qb: e.matmul(
                    bk32(2 * scb + 1),
                    B["kt"][64:128, 8192 * cc + 128 * kt:8192 * cc + 128 * kt + 128],
                    B["qt"][64:128, 1024 * j + 512 * qb:1024 * j + 512 * qb + 512],
                    start=True, stop=True, tile_position=(64, 0)), inc="P")
                if kt >= 1:
                    emit_av(kt - 1, first_gate=gate0)
                ptw = av_id.get(kt - 4, prev_av.get(60 + kt, 0))
                w("s", "P", max(sc_id[kt], ptw))
                exp_id[kt] = r("s", lambda e, kt=kt, scb=scb: e.activation(
                    B["pt"][kt % 4][:], ps[:, 1024 * scb:1024 * scb + 1024],
                    AF.Exp, bias=0.0, scale=0.125), inc="A")
            emit_av(63, first_gate=gate0)
            attn_first = False

            acols = slice(1024 * j + 512 * qb, 1024 * j + 512 * qb + 512)
            w("v", "P", av_id[63])
            r("v", lambda e: e.reciprocal(B["rd"][0:1, :], psB[0:1, :]),
              inc="V", selfwait=True)
            rid = r("v", lambda e: e.reciprocal(B["rd"][64:65, :], psA[64:65, :]), inc="V")
            w("t", "V", rid)
            bA = r("t", lambda e: e.matmul(
                psB[0:64, :], B["of"][0:1, 0:64], B["rd"][0:1, :],
                start=True, stop=True), inc="P")
            w("v", "P", bA)
            r("v", lambda e: e.tensor_copy(B["bcs"][0:64, :], psB[0:64, :]),
              inc="V", selfwait=True)
            tA = r("v", lambda e, acols=acols: e.tensor_tensor(
                B["ao"][0:64, acols], psA[0:64, :], B["bcs"][0:64, :], ALU.mult),
                inc="V")
            w("t", "V", tA)
            bB = r("t", lambda e: e.matmul(
                psA[64:128, :], B["of"][64:65, 0:64], B["rd"][64:65, :],
                start=True, stop=True), inc="P")
            w("v", "P", bB)
            r("v", lambda e: e.tensor_copy(B["bcs"][64:128, :], psA[64:128, :]),
              inc="V", selfwait=True)
            tB = r("v", lambda e, acols=acols: e.tensor_tensor(
                B["ao"][64:128, acols], psB[64:128, :], B["bcs"][64:128, :], ALU.mult),
                inc="V")
            prev_tb = tB
            prev_av = {k: v for k, v in av_id.items() if k >= 60}
            pair_exp_base = exp_id[63]
            sc_last[(j, qb)] = sc_id[63]
        if j % 4 == 3:
            grp4 = j // 4
            for c4 in range(4):
                c = 4 * grp4 + c4
                w("g", "P", sc_last[(j, 1)])
                if H["_dbg"]:
                    w("g", "Y0", dbg0)
                wo_dma[c] = r("g", lambda e, c=c: e.dma_start(
                    out=B["wo"][:, 2048 * c:2048 * c + 2048], in_=H["wo"][c]),
                    inc="W", amt=16)

    attn_done_v = prev_tb
    attn_done_p = S_.cnt["P"]
    if H["_dbg"]:
        w("y", "V", attn_done_v)
        dbg1 = r("y", lambda e: e.dma_start(out=H["dao"], in_=B["ao"][:]), inc="Y0", amt=16)
    attn_last_exp = pair_exp_base

    w("g", "P", attn_done_p)
    bo_id = r("g", lambda e: e.dma_start(out=B["bo"][:], in_=H["bo"]), inc="W", amt=16)

    if H["_dbg"]:
        w("y", "W", wo_dma[15])
        dbg2 = r("y", lambda e: e.dma_start(out=H["dwo"], in_=B["wo"][:]), inc="Y0", amt=16)
    fbid = {}
    fbev = {}
    xboid = {}
    ln1 = {}
    castid = {}
    trxid = {}
    x2fev = {}
    x2store = {}
    for tt in range(8):
        par = tt % 2
        w("y", "V", xboid.get(tt - 2, 0))
        if tt < 2:
            w("y", "P", attn_done_p)
        ssem = "S%d" % par
        xrid = r("y", lambda e, tt=tt, par=par: e.dma_start(
            out=B["xr"][par][:], in_=H["xo"][128 * tt:128 * tt + 128, :]),
            inc=ssem, amt=16)
        w("v", ssem, xrid)
        w("v", "W", bo_id)
        xboid[tt] = r("v", lambda e, par=par: e.tensor_tensor(
            B["xbo"][:], B["xr"][par][:], B["bo"][:], ALU.add), inc="V", selfwait=True)
        w("v", "A", castid.get(tt - 2, 0))
        w("v", "Y%d" % par, x2store.get(tt - 2, 0))
        for fb in range(4):
            n = 4 * tt + fb
            slot = n % 4
            w("t", "V", fbev.get(n - 4, 0))
            if n == 0:
                w("t", "V", attn_done_v)
                w("t", "A", attn_last_exp)
                w("t", "W", wo_dma[15])
            for c in range(16):
                last = c == 15
                idx = r("t", lambda e, c=c, fb=fb, tt=tt, slot=slot: e.matmul(
                    bk32(slot), B["ao"][:, 1024 * c + 128 * tt:1024 * c + 128 * tt + 128],
                    B["wo"][:, 2048 * c + 512 * fb:2048 * c + 512 * fb + 512],
                    start=(c == 0), stop=(c == 15)), inc=("P" if last else None))
            fbid[(tt, fb)] = idx
            w("v", "P", idx)
            fbev[n] = r("v", lambda e, fb=fb, par=par, slot=slot: e.tensor_tensor(
                B["x2t"][par][:, 512 * fb:512 * fb + 512], bk32(slot),
                B["xbo"][:, 512 * fb:512 * fb + 512], ALU.add), inc="V",
                selfwait=(fb == 3))
        if H["_dbg"]:
            w("y", "V", fbev[4 * tt + 3])
            sdump = r("y", lambda e, tt=tt, par=par: e.dma_start(
                out=H["ds_"][128 * tt:128 * tt + 128, :], in_=B["x2t"][par][:]),
                inc="Y0", amt=16)
            w("v", "Y0", sdump)
        r("v", lambda e, par=par: e.reduce_sum(B["ln"][:, 0:1], B["x2t"][par][:], axis=AX.X),
          inc="V", selfwait=True)
        r("v", lambda e: e.tensor_scalar(B["ln"][:, 1:2], B["ln"][:, 0:1], 1.0 / D, None, ALU.mult),
          inc="V", selfwait=True)
        cen = r("v", lambda e, par=par: e.tensor_scalar(
            B["sqs"][:], B["x2t"][par][:], B["ln"][:, 1:2], None, ALU.subtract),
            inc="V", selfwait=True)
        w("s", "V", cen)
        sqid = r("s", lambda e, par=par: e.activation(
            B["x2t"][par][:], B["sqs"][:], AF.Square), inc="A")
        w("v", "A", sqid)
        r("v", lambda e, par=par: e.reduce_sum(B["ln"][:, 2:3], B["x2t"][par][:], axis=AX.X),
          inc="V", selfwait=True)
        vv_ = r("v", lambda e: e.tensor_scalar(
            B["ln"][:, 3:4], B["ln"][:, 2:3], 1.0 / D, EPS, ALU.mult, ALU.add),
            inc="V", selfwait=True)
        w("s", "V", vv_)
        sqr = r("s", lambda e: e.activation(B["ln"][:, 4:5], B["ln"][:, 3:4], AF.Sqrt), inc="A")
        w("v", "A", sqr)
        r("v", lambda e: e.reciprocal(B["ln"][:, 5:6], B["ln"][:, 4:5]),
          inc="V", selfwait=True)
        ln1[tt] = r("v", lambda e, par=par: e.tensor_scalar(
            B["x2t"][par][:], B["sqs"][:], B["ln"][:, 5:6], None, ALU.mult), inc="V")
        w("s", "V", ln1[tt])
        w("s", "P", trxid.get(tt - 1, 0))
        castid[tt] = r("s", lambda e, par=par: e.copy(B["x2bf"][:], B["x2t"][par][:]), inc="A")
        w("t", "A", castid[tt])

        def xview(grp, tt=tt):
            vv2 = B["x2f"][:].rearrange("p (c q) -> p c q", c=16)
            return vv2[:, 4 * grp:4 * grp + 4, 128 * tt:128 * tt + 128]
        trxid[tt] = transpose_tile(B["x2bf"], xview)
        x2fev[tt] = ev[transpose_tile.seq - 1]
        w("y", "V", ln1[tt])
        x2store[tt] = r("y", lambda e, tt=tt, par=par: e.dma_start(
            out=H["x2s"][128 * tt:128 * tt + 128, :], in_=B["x2t"][par][:]),
            inc="Y%d" % par, amt=16)

    oproj_done_p = S_.cnt["P"]
    oproj_last_fbev = fbev[31]

    conv_id = {}
    gmm = {}
    umm = {}
    hTT = {}
    silu_id = {}
    for ib in range(64):
        par = ib % 2
        w("y", "V", conv_id.get(ib - 1, 0))
        if ib == 0:
            w("y", "P", oproj_done_p)
            if H["_dbg"]:
                w("y", "Y0", dbg1)
        r("y", lambda e, ib=ib: e.dma_start(out=B["wgst"][:], in_=H["wg"][ib]),
          inc="S0", amt=16)
        wld = r("y", lambda e, ib=ib: e.dma_start(out=B["wust"][:], in_=H["wu"][ib]),
                inc="S0", amt=16)
        w("v", "S0", wld)
        w("v", "P", umm.get(ib - 2, 0))
        r("v", lambda e, par=par: e.tensor_scalar(
            B["wgb"][par][:], B["wgst"][:], -128.0, None, ALU.add))
        conv_id[ib] = r("v", lambda e, par=par: e.tensor_scalar(
            B["wub"][par][:], B["wust"][:], -128.0, None, ALU.add), inc="V")
        for th in range(2):
            n = 2 * ib + th
            s3 = n % 3
            w("t", "V", conv_id[ib])
            w("t", "V", hTT.get(n - 3, 0))
            if n == 0:
                w("t", "A", x2fev[7])
                w("t", "V", oproj_last_fbev)
            for c in range(16):
                last = c == 15
                idx = r("t", lambda e, c=c, th=th, par=par, s3=s3: e.matmul(
                    ps[:, 1024 * s3:1024 * s3 + 512],
                    B["wgb"][par][:, 128 * c:128 * c + 128],
                    B["x2f"][:, 1024 * c + 512 * th:1024 * c + 512 * th + 512],
                    start=(c == 0), stop=(c == 15)), inc=("P" if last else None))
            gmm[n] = idx
            for c in range(16):
                last = c == 15
                idx = r("t", lambda e, c=c, th=th, par=par, s3=s3: e.matmul(
                    ps[:, 1024 * s3 + 512:1024 * s3 + 1024],
                    B["wub"][par][:, 128 * c:128 * c + 128],
                    B["x2f"][:, 1024 * c + 512 * th:1024 * c + 512 * th + 512],
                    start=(c == 0), stop=(c == 15)), inc=("P" if last else None))
            umm[n] = idx
            w("s", "P", gmm[n])
            w("s", "V", hTT.get(n - 1, 0))
            silu_id[n] = r("s", lambda e, ib=ib, s3=s3: e.activation(
                B["sg"][:], ps[:, 1024 * s3:1024 * s3 + 512], AF.Silu,
                bias=0.0, scale=B["gs"][:, ib:ib + 1]), inc="A")
            w("v", "P", umm[n])
            r("v", lambda e, ib=ib, s3=s3: e.tensor_scalar(
                B["usv"][:], ps[:, 1024 * s3 + 512:1024 * s3 + 1024],
                B["us"][:, ib:ib + 1], None, ALU.mult), inc="V", selfwait=True)
            w("v", "A", silu_id[n])
            if n == 0 and H["_dbg"]:
                w("v", "Y0", dbg2)
            if ib == 32 and th == 0:
                w("v", "Y1", x2store[7])
                w("v", "A", castid[7])
            hTT[n] = r("v", lambda e, ib=ib, th=th: e.tensor_tensor(
                B["hf"][:, 1024 * ib + 512 * th:1024 * ib + 512 * th + 512],
                B["sg"][:], B["usv"][:], ALU.mult), inc="V")
    hlast = hTT[127]
    ffa_last_silu = silu_id[127]

    dmm = {}
    fmid = {}
    trf = {}
    tev = {}
    tokstore = {}
    wdc = {}
    for hc in range(16):
        par = hc % 2
        for ibg in range(8):
            gi = 8 * hc + ibg
            sp_ = gi % 2
            w("y", "V", wdc.get(gi - 2, 0))
            if gi == 0:
                w("y", "V", hlast)
            ssem = "S%d" % sp_
            ldid = r("y", lambda e, hc=hc, ibg=ibg, sp_=sp_: e.dma_start(
                out=B["wdst"][sp_][:].rearrange("p (g h) -> p g h", g=8),
                in_=H["wd"][hc, 8 * ibg:8 * ibg + 8].rearrange("g p h -> p g h")),
                inc=ssem, amt=16)
            w("v", ssem, ldid)
            w("v", "P", dmm.get(gi - 2, 0))
            wdc[gi] = r("v", lambda e, sp_=sp_: e.tensor_scalar(
                B["wdb"][sp_][:], B["wdst"][sp_][:], -128.0, None, ALU.add), inc="V")
            w("t", "V", wdc[gi])
            if ibg == 0:
                w("t", "V", fmid.get(hc - 2, 0))
                if hc == 0:
                    w("t", "V", hlast)
                    w("t", "A", ffa_last_silu)
            for i8 in range(8):
                ib = 8 * ibg + i8
                for th in range(2):
                    last = (i8 == 7 and th == 1)
                    idx = r("t", lambda e, i8=i8, th=th, ib=ib, par=par, sp_=sp_: e.matmul(
                        ps[:, 1024 * par + 512 * th:1024 * par + 512 * th + 512],
                        B["wdb"][sp_][:, 128 * i8:128 * i8 + 128],
                        B["hf"][:, 1024 * ib + 512 * th:1024 * ib + 512 * th + 512],
                        start=(ib == 0), stop=(ib == 63)), inc=("P" if last else None))
            dmm[gi] = idx
        w("v", "P", dmm[8 * hc + 7])
        w("v", "P", trf.get(hc - 1, 0))
        fmid[hc] = r("v", lambda e, hc=hc, par=par: e.tensor_scalar(
            B["fm"][:], ps[:, 1024 * par:1024 * par + 1024],
            B["ds"][:, hc:hc + 1], None, ALU.mult), inc="V")
        w("t", "V", fmid[hc])
        w("t", "A", tev.get(hc - 1, 0))
        for wv_ in range(8):
            off = 2048 + 128 * wv_
            last = wv_ == 7
            idx = r("t", lambda e, wv_=wv_, off=off: e.transpose(
                ps[:, off:off + 128], B["fm"][:, 128 * wv_:128 * wv_ + 128],
                B["idf"][:]), inc=("P" if last else None))
        trf[hc] = idx
        w("s", "P", trf[hc])
        w("s", "Y0", tokstore.get(hc - 1, 0))
        r("s", lambda e: e.copy(B["tok"][0][:], bk32(4)))
        tev[hc] = r("s", lambda e: e.copy(B["tok"][1][:], bk32(5)), inc="A")
        w("y", "A", tev[hc])
        for gi2 in range(2):
            tokstore[hc] = r("y", lambda e, gi2=gi2, hc=hc: e.dma_start(
                out=H["ffs"].rearrange("(w p) h -> p w h", p=128)[
                    :, 4 * gi2:4 * gi2 + 4, 128 * hc:128 * hc + 128],
                in_=B["tok"][gi2][:].rearrange("p (w h) -> p w h", w=4)),
                inc="Y0", amt=16)

    ffb_done_p = dmm[127]

    ln2 = {}
    ystore = {}
    for tt in range(8):
        par = tt % 2
        w("y", "Y0", tokstore[15])
        w("y", "V", ln2.get(tt - 2, 0))
        ssem = "S%d" % par
        r("y", lambda e, tt=tt, par=par: e.dma_start(
            out=B["ffin"][par][:], in_=H["ffs"][128 * tt:128 * tt + 128, :]),
          inc=ssem, amt=16)
        ffld = r("y", lambda e, tt=tt, par=par: e.dma_start(
            out=B["x2in"][par][:], in_=H["x2s"][128 * tt:128 * tt + 128, :]),
            inc=ssem, amt=16)
        w("v", ssem, ffld)
        if tt == 0:
            w("v", "P", ffb_done_p)
            w("v", "Y0", tokstore[15])
        w("v", "Y%d" % par, ystore.get(tt - 2, 0))
        r("v", lambda e, par=par: e.tensor_tensor(
            B["outt"][par][:], B["ffin"][par][:], B["x2in"][par][:], ALU.add),
          inc="V", selfwait=True)
        r("v", lambda e, par=par: e.reduce_sum(B["ln"][:, 8:9], B["outt"][par][:], axis=AX.X),
          inc="V", selfwait=True)
        r("v", lambda e: e.tensor_scalar(B["ln"][:, 9:10], B["ln"][:, 8:9], 1.0 / D, None, ALU.mult),
          inc="V", selfwait=True)
        cen = r("v", lambda e, par=par: e.tensor_scalar(
            B["sq2"][:], B["outt"][par][:], B["ln"][:, 9:10], None, ALU.subtract),
            inc="V", selfwait=True)
        w("s", "V", cen)
        sqid = r("s", lambda e, par=par: e.activation(
            B["outt"][par][:], B["sq2"][:], AF.Square), inc="A")
        w("v", "A", sqid)
        r("v", lambda e, par=par: e.reduce_sum(B["ln"][:, 10:11], B["outt"][par][:], axis=AX.X),
          inc="V", selfwait=True)
        vv_ = r("v", lambda e: e.tensor_scalar(
            B["ln"][:, 11:12], B["ln"][:, 10:11], 1.0 / D, EPS, ALU.mult, ALU.add),
            inc="V", selfwait=True)
        w("s", "V", vv_)
        sqr = r("s", lambda e: e.activation(B["ln"][:, 12:13], B["ln"][:, 11:12], AF.Sqrt), inc="A")
        w("v", "A", sqr)
        r("v", lambda e: e.reciprocal(B["ln"][:, 13:14], B["ln"][:, 12:13]),
          inc="V", selfwait=True)
        ln2[tt] = r("v", lambda e, par=par: e.tensor_scalar(
            B["outt"][par][:], B["sq2"][:], B["ln"][:, 13:14], None, ALU.mult), inc="V")
        w("y", "V", ln2[tt])
        ystore[tt] = r("y", lambda e, tt=tt, par=par: e.dma_start(
            out=H["y"][128 * tt:128 * tt + 128, :], in_=B["outt"][par][:]),
            inc="Y%d" % par, amt=16)

    w("y", "Y0", S_.cnt["Y0"])
    w("y", "Y1", S_.cnt["Y1"])


def _pack(inputs):
    f32 = np.float32
    Wq = np.asarray(inputs["L0_attn_Wq"], f32)
    bq = np.asarray(inputs["L0_attn_bq"], f32)
    Wk = np.asarray(inputs["L0_attn_Wk"], f32)
    bk = np.asarray(inputs["L0_attn_bk"], f32)
    Wv = np.asarray(inputs["L0_attn_Wv"], f32)
    bv = np.asarray(inputs["L0_attn_bv"], f32)
    Wo = np.asarray(inputs["L0_attn_Wo"], f32)
    bo = np.asarray(inputs["L0_attn_bo"], f32)
    gw = np.ascontiguousarray(np.asarray(inputs["L0_ffn_gate_w"], np.int32))
    uw = np.ascontiguousarray(np.asarray(inputs["L0_ffn_up_w"], np.int32))
    dw = np.ascontiguousarray(np.asarray(inputs["L0_ffn_down_w"], np.int32))
    gsc = np.asarray(inputs["L0_ffn_gate_scale"], f32)
    usc = np.asarray(inputs["L0_ffn_up_scale"], f32)
    dsc = np.asarray(inputs["L0_ffn_down_scale"], f32)

    shared = {}
    wq_t = np.empty((16, 128, 2048), f32)
    bq_t = np.empty((128, 16), f32)
    for j in range(16):
        cols = np.r_[64 * j:64 * j + 64, 64 * (j + 16):64 * (j + 16) + 64]
        tmp = Wq[:, cols].reshape(16, 128, 128)
        wq_t[j] = tmp.transpose(1, 0, 2).reshape(128, 2048)
        bq_t[:, j] = bq[cols]
    shared["wq_t"] = np.ascontiguousarray(wq_t)
    shared["bq_t"] = np.ascontiguousarray(bq_t)
    col_order = np.empty(512, np.int64)
    for cc in range(4):
        for dd in range(128):
            kv = cc + (4 if dd >= 64 else 0)
            col_order[128 * cc + dd] = 64 * kv + dd % 64
    wk_re = Wk[:, col_order]
    shared["wk_t"] = np.ascontiguousarray(
        wk_re.reshape(16, 128, 512).transpose(1, 0, 2).reshape(128, 8192))
    shared["bk_t"] = np.ascontiguousarray(bk[col_order].reshape(4, 128).T)
    shared["wv_t"] = np.ascontiguousarray(
        Wv.reshape(16, 128, 512).transpose(1, 0, 2).reshape(128, 8192))
    shared["bv_bc"] = np.ascontiguousarray(np.broadcast_to(bv, (128, 512)))
    wo_t = np.empty((16, 128, 2048), f32)
    for c in range(16):
        rows = np.r_[64 * c:64 * c + 64, 64 * (c + 16):64 * (c + 16) + 64]
        wo_t[c] = Wo[rows, :]
    shared["wo_t"] = np.ascontiguousarray(wo_t)
    shared["bo_bc"] = np.ascontiguousarray(np.broadcast_to(bo, (128, 2048)))
    shared["wg_t"] = np.ascontiguousarray(
        gw.reshape(64, 128, 16, 128).transpose(0, 3, 2, 1).reshape(64, 128, 2048))
    shared["wu_t"] = np.ascontiguousarray(
        uw.reshape(64, 128, 16, 128).transpose(0, 3, 2, 1).reshape(64, 128, 2048))
    shared["wd_t"] = np.ascontiguousarray(
        dw.reshape(16, 128, 64, 128).transpose(0, 2, 3, 1))
    shared["gs_t"] = np.ascontiguousarray(gsc.reshape(64, 128).T)
    shared["us_t"] = np.ascontiguousarray(usc.reshape(64, 128).T)
    shared["ds_t"] = np.ascontiguousarray(dsc.reshape(16, 128).T)
    shared["identb"] = np.eye(128, dtype=np.float32).astype(ml_dtypes.bfloat16)
    shared["identf"] = np.eye(128, dtype=np.float32)
    shared["ones64b"] = np.ones((128, 64), np.float32).astype(ml_dtypes.bfloat16)
    shared["onesf"] = np.ones((128, 64), np.float32)
    return shared


def kernel(**inputs):
    x = np.ascontiguousarray(np.asarray(inputs["L0_attn_x"], np.float32).reshape(SEQ, D))
    shared = _pack(inputs)
    nc = build_nc()
    in_maps = []
    for c in range(NCORES):
        m = dict(shared)
        m["x_full"] = x
        m["x_own"] = np.ascontiguousarray(x[T * c:T * (c + 1)])
        in_maps.append(m)
    res = run_bass_kernel_spmd(nc, in_maps, core_ids=list(range(NCORES)))
    y = np.concatenate([np.asarray(res.results[c]["y"]) for c in range(NCORES)], axis=0)
    return y.reshape(1, SEQ, D).astype(np.float32)


if __name__ == "__main__":
    import reference as _ref
    inp = _ref.setup_inputs()
    out = kernel(**{k: np.asarray(v) for k, v in inp.items()})
    want = np.asarray(_ref.reference(**inp))
    err = np.abs(out - want)
    denom = np.mean(want.astype(np.float64) ** 2)
    print("rvar:", float(np.mean((out.astype(np.float64) - want.astype(np.float64)) ** 2) / denom))
    print("maxabs:", float(err.max()))


# version 06
# speedup vs baseline: 1.1885x; 1.1885x over previous
import numpy as np
import ml_dtypes
import concourse.bass as bass
import concourse.mybir as mybir
from concourse.bass_utils import run_bass_kernel_spmd

F32 = mybir.dt.float32
BF16 = mybir.dt.bfloat16
I32 = mybir.dt.int32
AF = mybir.ActivationFunctionType
ALU = mybir.AluOpType
AX = mybir.AxisListType

NCORES = 8
SEQ = 8192
D = 2048
T = SEQ // NCORES
NKT = SEQ // 128
EPS = 1e-5


def build_nc():
    nc = bass.Bass("TRN2", target_bir_lowering=False)

    H = {}
    H["xf"] = nc.dram_tensor("x_full", [SEQ, D], F32, kind="ExternalInput").ap()
    H["xo"] = nc.dram_tensor("x_own", [T, D], F32, kind="ExternalInput").ap()
    H["wq"] = nc.dram_tensor("wq_t", [16, 128, 2048], F32, kind="ExternalInput").ap()
    H["bq"] = nc.dram_tensor("bq_t", [128, 16], F32, kind="ExternalInput").ap()
    H["wk"] = nc.dram_tensor("wk_t", [128, 8192], F32, kind="ExternalInput").ap()
    H["bk"] = nc.dram_tensor("bk_t", [128, 4], F32, kind="ExternalInput").ap()
    H["wv"] = nc.dram_tensor("wv_t", [128, 8192], F32, kind="ExternalInput").ap()
    H["bv"] = nc.dram_tensor("bv_bc", [128, 512], F32, kind="ExternalInput").ap()
    H["wo"] = nc.dram_tensor("wo_t", [16, 128, 2048], F32, kind="ExternalInput").ap()
    H["bo"] = nc.dram_tensor("bo_bc", [128, 2048], F32, kind="ExternalInput").ap()
    H["wg"] = nc.dram_tensor("wg_t", [64, 128, 2048], I32, kind="ExternalInput").ap()
    H["wu"] = nc.dram_tensor("wu_t", [64, 128, 2048], I32, kind="ExternalInput").ap()
    H["wd"] = nc.dram_tensor("wd_t", [16, 64, 128, 128], I32, kind="ExternalInput").ap()
    H["gs"] = nc.dram_tensor("gs_t", [128, 64], F32, kind="ExternalInput").ap()
    H["us"] = nc.dram_tensor("us_t", [128, 64], F32, kind="ExternalInput").ap()
    H["ds"] = nc.dram_tensor("ds_t", [128, 16], F32, kind="ExternalInput").ap()
    H["idb"] = nc.dram_tensor("identb", [128, 128], BF16, kind="ExternalInput").ap()
    H["idf"] = nc.dram_tensor("identf", [128, 128], F32, kind="ExternalInput").ap()
    H["o64"] = nc.dram_tensor("ones64b", [128, 64], BF16, kind="ExternalInput").ap()
    H["of"] = nc.dram_tensor("onesf", [128, 64], F32, kind="ExternalInput").ap()
    H["y"] = nc.dram_tensor("y", [T, D], F32, kind="ExternalOutput").ap()
    import os
    H["_dbg"] = bool(os.environ.get("KDEBUG"))
    if H["_dbg"]:
        H["dkt"] = nc.dram_tensor("dbg_kt", [128, 32768], BF16, kind="ExternalOutput").ap()
        H["dv"] = nc.dram_tensor("dbg_v", [128, 32768], BF16, kind="ExternalOutput").ap()
        H["dqt"] = nc.dram_tensor("dbg_qt", [128, 16384], BF16, kind="ExternalOutput").ap()
        H["dao"] = nc.dram_tensor("dbg_ao", [128, 16384], BF16, kind="ExternalOutput").ap()
        H["dwo"] = nc.dram_tensor("dbg_wo", [128, 32768], BF16, kind="ExternalOutput").ap()
        H["ds_"] = nc.dram_tensor("dbg_s", [T, D], F32, kind="ExternalOutput").ap()
    import os as _os
    _scrkind = "ExternalOutput" if _os.environ.get("KDEBUG") else "Internal"
    H["x2s"] = nc.dram_tensor("x2_scr", [T, D], F32, kind=_scrkind).ap()
    H["ffs"] = nc.dram_tensor("ffn_scr", [T, D], F32, kind=_scrkind).ap()

    SB_BASE = 16512

    def at(name, shape, dtype, off):
        return nc.alloc_sbuf_tensor_at(name, shape, dtype, offset=SB_BASE + off)

    B = {}
    B["kt"] = at("kt_b", [128, 32768], BF16, 0)
    B["v"] = at("v_b", [128, 32768], BF16, 65536)
    B["wo"] = at("wo_b", [128, 32768], BF16, 0)
    B["hf"] = at("hfull", [128, 65536], BF16, 0)
    B["bo"] = at("bo_s", [128, 2048], F32, 65536)
    B["xr"] = [at(f"xr{i}", [128, 2048], F32, 73728 + 8192 * i) for i in range(2)]
    B["x2t"] = [at(f"x2t{i}", [128, 2048], F32, 90112 + 8192 * i) for i in range(2)]
    B["x2bf"] = at("x2bf", [128, 2048], BF16, 106496)
    B["sqs"] = at("sqs", [128, 2048], F32, 110592)
    B["xbo"] = at("xbo", [128, 2048], F32, 118784)
    B["qt"] = at("qt_b", [128, 16384], BF16, 131072)
    B["x2f"] = at("x2f", [128, 16384], BF16, 131072)
    B["xfblk"] = at("xfblk", [128, 16384], BF16, 131072)
    B["wkb"] = at("wkb", [128, 8192], BF16, 163840)
    B["ffin"] = [at(f"ffin{i}", [128, 2048], F32, 131072 + 8192 * i) for i in range(2)]
    B["x2in"] = [at(f"x2in{i}", [128, 2048], F32, 147456 + 8192 * i) for i in range(2)]
    B["ao"] = at("ao_b", [128, 16384], BF16, 163840)
    B["xfo"] = at("xfown", [128, 16384], BF16, 163840)
    B["wvb"] = at("wvb", [128, 8192], BF16, 180224)
    B["bv"] = at("bv_s", [128, 512], F32, 206688)
    B["wgst"] = at("wgstage", [128, 2048], I32, 163840)
    B["wust"] = at("wustage", [128, 2048], I32, 172032)
    B["wgb"] = [at(f"wgb{i}", [128, 2048], BF16, 180224 + 4096 * i) for i in range(2)]
    B["wub"] = [at(f"wub{i}", [128, 2048], BF16, 188416 + 4096 * i) for i in range(2)]
    B["wdst"] = [at(f"wdst{i}", [128, 1024], I32, 163840 + 4096 * i) for i in range(2)]
    B["wdb"] = [at(f"wdb{i}", [128, 1024], BF16, 172032 + 2048 * i) for i in range(2)]
    B["fm"] = at("fm_s", [128, 1024], F32, 176128)
    B["tok"] = [at(f"tokst{i}", [128, 512], F32, 180224 + 2048 * i) for i in range(2)]
    B["outt"] = [at(f"outt{i}", [128, 2048], F32, 163840 + 8192 * i) for i in range(2)]
    B["sq2"] = at("sq2", [128, 2048], F32, 180224)
    B["idb"] = at("idb_s", [128, 128], BF16, 196608)
    B["idf"] = at("idf_s", [128, 128], F32, 196864)
    B["o64"] = at("o64_s", [128, 64], BF16, 197376)
    B["of"] = at("of_s", [128, 64], F32, 197504)
    B["bq"] = at("bq_s", [128, 16], F32, 197760)
    B["bk"] = at("bk_s", [128, 4], F32, 197824)
    B["gs"] = at("gs_s", [128, 64], F32, 197888)
    B["us"] = at("us_s", [128, 64], F32, 198144)
    B["ds"] = at("ds_s", [128, 16], F32, 198400)
    B["ln"] = at("lnst", [128, 16], F32, 210784)
    B["xst"] = [at(f"xstag{i}", [128, 2048], BF16, 198496 + 4096 * i) for i in range(2)]
    B["pt"] = [at(f"pt{i}", [128, 1024], BF16, 198496 + 2048 * i) for i in range(4)]
    B["rd"] = at("rd_t", [128, 512], F32, 206688)
    B["bcs"] = at("bcs", [128, 512], F32, 208736)
    B["sg"] = at("sg_s", [128, 512], F32, 206688)
    B["rdb"] = at("rdb", [128, 512], BF16, 210848)
    B["usv"] = at("usv_s", [128, 512], F32, 208736)

    B["nc"] = nc
    B["ps"] = nc.alloc_psum_tensor("ps32", [128, 3072], F32)
    B["psbf"] = nc.alloc_psum_tensor("psbf", [128, 2048], BF16)

    from contextlib import ExitStack
    with ExitStack() as es:
        ec = es.enter_context
        sems = {k: ec(nc.semaphore(f"sem{k}")) for k in
                ("P", "A", "V", "G0", "G1", "W", "S0", "S1", "Y0", "Y1")}
        block = ec(nc.Block())

        class Sched:
            def __init__(self, only, eng):
                self.only = only
                self.eng = eng
                self.cnt = {k: 0 for k in
                            ("P", "A", "V", "G0", "G1", "W", "S0", "S1", "Y0", "Y1")}

            def wait(self, engname, sem, val):
                if val is None or val <= 0:
                    return
                if self.only == engname:
                    self.eng.wait_ge(sems[sem], val)

            def run(self, engname, fn, inc=None, amt=1, selfwait=False):
                if inc is not None:
                    self.cnt[inc] += amt
                if self.only == engname:
                    inst = fn(self.eng)
                    if inc is not None:
                        inst.then_inc(sems[inc], amt)
                        if selfwait:
                            self.eng.wait_ge(sems[inc], self.cnt[inc])
                return self.cnt[inc] if inc is not None else None

        @block.gpsimd
        def _(g):
            emit_program(Sched("g", g), H, B)

        @block.tensor
        def _(t):
            emit_program(Sched("t", t), H, B)

        @block.scalar
        def _(s):
            emit_program(Sched("s", s), H, B)

        @block.vector
        def _(v):
            emit_program(Sched("v", v), H, B)

        @block.sync
        def _(sy):
            emit_program(Sched("y", sy), H, B)

    return nc

def emit_program(S_, H, B):
    w = S_.wait
    r = S_.run
    ps = B["ps"]
    psbf = B["psbf"]

    def bk32(i, n=512):
        return ps[:, 512 * i:512 * i + n]

    for k in ("idb", "idf", "o64", "of", "bq", "bk", "gs", "us", "ds", "bv"):
        r("g", lambda e, k=k: e.dma_start(out=B[k][:], in_=H[k]), inc="W", amt=16)
    r("g", lambda e: e.dma_start(out=B["wkb"][:], in_=H["wk"]), inc="W", amt=16)
    wconsts = r("g", lambda e: e.dma_start(out=B["wvb"][:], in_=H["wv"]), inc="W", amt=16)

    ev = {}

    def transpose_tile(src_buf, dst3):
        g0 = transpose_tile.seq
        for grp in range(4):
            half = grp % 2
            w("t", "A", ev.get(g0 + grp - 2, 0))
            for cc in range(4):
                c = 4 * grp + cc
                last = cc == 3
                tid = r("t", lambda e, c=c, half=half, cc=cc, src_buf=src_buf: e.transpose(
                    psbf[:, 1024 * half + 128 * cc:1024 * half + 128 * cc + 128],
                    src_buf[:, 128 * c:128 * c + 128], B["idb"][:]),
                    inc=("P" if last else None))
            w("s", "P", tid)
            ev[g0 + grp] = r("s", lambda e, half=half, grp=grp, dst3=dst3: e.copy(
                dst3(grp),
                psbf[:, 1024 * half:1024 * half + 512].rearrange("p (g q) -> p g q", g=4)),
                inc="A")
        transpose_tile.seq += 4
        return tid
    transpose_tile.seq = 0

    kev = {}
    vev = {}
    last_tr = {}
    NBLK = SEQ // 512
    for blk in range(NBLK):
        slot_x = 8192 * (blk % 2)
        for s4 in range(4):
            ti = 4 * blk + s4
            buf = ti % 2
            w("g", "P", last_tr.get(ti - 2, 0))
            gsem = "G%d" % buf
            r("g", lambda e, ti=ti, buf=buf: e.dma_start(
                out=B["xst"][buf][:], in_=H["xf"][128 * ti:128 * ti + 128, :]),
              inc=gsem, amt=16)
            gneed = S_.cnt[gsem]
            if ti < 2:
                w("t", "W", wconsts)
                w("v", "W", wconsts)
            w("t", gsem, gneed)

            def dview(grp, slot_x=slot_x, s4=s4):
                vv = B["xfblk"][:].rearrange("p (c q) -> p c q", q=512)
                c0 = slot_x // 512 + 4 * grp
                return vv[:, c0:c0 + 4, 128 * s4:128 * s4 + 128]
            last_tr[ti] = transpose_tile(B["xst"][buf], dview)
        w("t", "A", ev[transpose_tile.seq - 1])
        for cc in range(4):
            kseq = 4 * blk + cc
            pb = cc % 2
            w("t", "V", kev.get(kseq - 2, 0))
            for c in range(16):
                last = c == 15
                kmm = r("t", lambda e, cc=cc, c=c, pb=pb, slot_x=slot_x: e.matmul(
                    bk32(pb),
                    B["wkb"][:, 512 * c + 128 * cc:512 * c + 128 * cc + 128],
                    B["xfblk"][:, slot_x + 512 * c:slot_x + 512 * c + 512],
                    start=(c == 0), stop=(c == 15)), inc=("P" if last else None))
            w("v", "P", kmm)
            kev[kseq] = r("v", lambda e, cc=cc, blk=blk, pb=pb: e.tensor_scalar(
                B["kt"][:, 8192 * cc + 512 * blk:8192 * cc + 512 * blk + 512],
                bk32(pb), B["bk"][:, cc:cc + 1], None, ALU.add),
                inc="V")
        for s4 in range(4):
            ti = 4 * blk + s4
            vb = 2 + ti % 2
            w("t", "V", vev.get(ti - 2, 0))
            for c in range(16):
                last = c == 15
                vmm = r("t", lambda e, c=c, vb=vb, slot_x=slot_x, s4=s4: e.matmul(
                    bk32(vb), B["xfblk"][:, slot_x + 512 * c + 128 * s4:slot_x + 512 * c + 128 * s4 + 128],
                    B["wvb"][:, 512 * c:512 * c + 512],
                    start=(c == 0), stop=(c == 15)), inc=("P" if last else None))
            w("v", "P", vmm)
            vev[ti] = r("v", lambda e, ti=ti, vb=vb: e.tensor_tensor(
                B["v"][:, 512 * ti:512 * ti + 512], bk32(vb), B["bv"][:], ALU.add),
                inc="V")

    w("s", "V", vev[NKT - 1])
    for s8 in range(8):
        buf = s8 % 2
        w("g", "P", last_tr.get(NKT + s8 - 2, 0))
        gsem = "G%d" % buf
        r("g", lambda e, s8=s8, buf=buf: e.dma_start(
            out=B["xst"][buf][:], in_=H["xo"][128 * s8:128 * s8 + 128, :]),
          inc=gsem, amt=16)
        gneed = S_.cnt[gsem]
        w("t", gsem, gneed)

        def oview(grp, s8=s8):
            vv = B["xfo"][:].rearrange("p (c q) -> p c q", c=16)
            return vv[:, 4 * grp:4 * grp + 4, 128 * s8:128 * s8 + 128]
        last_tr[NKT + s8] = transpose_tile(B["xst"][buf], oview)

    qmm = {}
    qev = {}
    for j in range(16):
        buf = j % 2
        w("g", "P", qmm.get(j - 2, last_tr[NKT + 7]))
        gsem = "G%d" % buf
        r("g", lambda e, j=j, buf=buf: e.dma_start(
            out=B["xst"][buf][:], in_=H["wq"][j]), inc=gsem, amt=16)
        gneed = S_.cnt[gsem]
        w("t", gsem, gneed)
        qb_ = j % 2
        w("t", "V", qev.get(j - 2, max(kev[4 * (SEQ // 512 - 1) + 3], vev[NKT - 1])))
        for c in range(16):
            for hh in range(2):
                last = (c == 15 and hh == 1)
                idx = r("t", lambda e, c=c, hh=hh, qb_=qb_, buf=buf: e.matmul(
                    ps[:, 1024 * qb_ + 512 * hh:1024 * qb_ + 512 * hh + 512],
                    B["xst"][buf][:, 128 * c:128 * c + 128],
                    B["xfo"][:, 1024 * c + 512 * hh:1024 * c + 512 * hh + 512],
                    start=(c == 0), stop=(c == 15)), inc=("P" if last else None))
                if last:
                    qmm[j] = idx
        w("v", "P", qmm[j])
        qev[j] = r("v", lambda e, j=j, qb_=qb_: e.tensor_scalar(
            B["qt"][:, 1024 * j:1024 * j + 1024], ps[:, 1024 * qb_:1024 * qb_ + 1024],
            B["bq"][:, j:j + 1], None, ALU.add), inc="V")

    if H["_dbg"]:
        w("y", "V", max(qev[15], vev[NKT - 1], kev[4 * (SEQ // 512 - 1) + 3]))
        r("y", lambda e: e.dma_start(out=H["dkt"], in_=B["kt"][:]), inc="Y0", amt=16)
        r("y", lambda e: e.dma_start(out=H["dv"], in_=B["v"][:]), inc="Y0", amt=16)
        dbg0 = r("y", lambda e: e.dma_start(out=H["dqt"], in_=B["qt"][:]), inc="Y0", amt=16)
    psA = ps[:, 2048:2560]
    psB = ps[:, 2560:3072]
    pair_exp_base = qev[15] * 0 + S_.cnt["A"]
    prev_tb = vev[NKT - 1]
    prev_av = {}
    sc_last = {}
    wo_dma = {}
    qt_ready = qev[15]
    attn_first = True
    for j in range(16):
        for qb in range(2):
            cc = j // 4
            kvA = j // 4
            kvB = 4 + j // 4
            exp_id = {}
            av_id = {}
            sc_id = {}

            def emit_av(kt, kvA=kvA, kvB=kvB, exp_id=exp_id, av_id=av_id,
                        first_gate=None):
                pti = kt % 4
                w("t", "A", exp_id[kt])
                if kt == 0:
                    w("t", "V", first_gate)
                st = kt == 0
                sp = kt == 63
                r("t", lambda e, kt=kt, pti=pti, kvA=kvA, st=st, sp=sp: e.matmul(
                    psA[0:64, :], B["v"][:, 512 * kt + 64 * kvA:512 * kt + 64 * kvA + 64],
                    B["pt"][pti][:, 0:512], start=st, stop=sp, tile_position=(0, 0),
                    skip_group_check=True))
                r("t", lambda e, pti=pti, st=st, sp=sp: e.matmul(
                    psA[64:128, :], B["o64"][:], B["pt"][pti][:, 512:1024],
                    start=st, stop=sp, tile_position=(0, 64), skip_group_check=True))
                r("t", lambda e, kt=kt, pti=pti, kvB=kvB, st=st, sp=sp: e.matmul(
                    psB[64:128, :], B["v"][:, 512 * kt + 64 * kvB:512 * kt + 64 * kvB + 64],
                    B["pt"][pti][:, 512:1024], start=st, stop=sp, tile_position=(0, 64),
                    skip_group_check=True))
                av_id[kt] = r("t", lambda e, pti=pti, st=st, sp=sp: e.matmul(
                    psB[0:64, :], B["o64"][:], B["pt"][pti][:, 0:512],
                    start=st, stop=sp, tile_position=(0, 0), skip_group_check=True), inc="P")

            gate0 = prev_tb
            for kt in range(64):
                scb = kt % 2
                w("t", "A", exp_id.get(kt - 2, pair_exp_base))
                if attn_first and kt == 0:
                    w("t", "V", qt_ready)
                r("t", lambda e, kt=kt, scb=scb, cc=cc, j=j, qb=qb: e.matmul(
                    bk32(2 * scb),
                    B["kt"][0:64, 8192 * cc + 128 * kt:8192 * cc + 128 * kt + 128],
                    B["qt"][0:64, 1024 * j + 512 * qb:1024 * j + 512 * qb + 512],
                    start=True, stop=True, tile_position=(0, 0)))
                sc_id[kt] = r("t", lambda e, kt=kt, scb=scb, cc=cc, j=j, qb=qb: e.matmul(
                    bk32(2 * scb + 1),
                    B["kt"][64:128, 8192 * cc + 128 * kt:8192 * cc + 128 * kt + 128],
                    B["qt"][64:128, 1024 * j + 512 * qb:1024 * j + 512 * qb + 512],
                    start=True, stop=True, tile_position=(64, 0)), inc="P")
                if kt >= 1:
                    emit_av(kt - 1, first_gate=gate0)
                ptw = av_id.get(kt - 4, prev_av.get(60 + kt, 0))
                w("s", "P", max(sc_id[kt], ptw))
                exp_id[kt] = r("s", lambda e, kt=kt, scb=scb: e.activation(
                    B["pt"][kt % 4][:], ps[:, 1024 * scb:1024 * scb + 1024],
                    AF.Exp, bias=0.0, scale=0.125), inc="A")
            emit_av(63, first_gate=gate0)
            attn_first = False

            acols = slice(1024 * j + 512 * qb, 1024 * j + 512 * qb + 512)
            w("v", "P", av_id[63])
            def _recip(e, dst, src_):
                with B["nc"].allow_low_precision(reason="softmax 1/D broadcast via bf16 matmul"):
                    return e.reciprocal(dst, src_)
            r("v", lambda e: _recip(e, B["rdb"][0:1, :], psB[0:1, :]),
              inc="V", selfwait=True)
            rid = r("v", lambda e: _recip(e, B["rdb"][64:65, :], psA[64:65, :]), inc="V")
            w("t", "V", rid)
            bA = r("t", lambda e: e.matmul(
                psB[0:64, :], B["o64"][0:1, 0:64], B["rdb"][0:1, :],
                start=True, stop=True), inc="P")
            w("v", "P", bA)
            r("v", lambda e: e.tensor_copy(B["bcs"][0:64, :], psB[0:64, :]),
              inc="V", selfwait=True)
            tA = r("v", lambda e, acols=acols: e.tensor_tensor(
                B["ao"][0:64, acols], psA[0:64, :], B["bcs"][0:64, :], ALU.mult),
                inc="V")
            w("t", "V", tA)
            bB = r("t", lambda e: e.matmul(
                psA[64:128, :], B["o64"][64:65, 0:64], B["rdb"][64:65, :],
                start=True, stop=True), inc="P")
            w("v", "P", bB)
            r("v", lambda e: e.tensor_copy(B["bcs"][64:128, :], psA[64:128, :]),
              inc="V", selfwait=True)
            tB = r("v", lambda e, acols=acols: e.tensor_tensor(
                B["ao"][64:128, acols], psB[64:128, :], B["bcs"][64:128, :], ALU.mult),
                inc="V")
            prev_tb = tB
            prev_av = {k: v for k, v in av_id.items() if k >= 60}
            pair_exp_base = exp_id[63]
            sc_last[(j, qb)] = sc_id[63]
        if j % 4 == 3:
            grp4 = j // 4
            for c4 in range(4):
                c = 4 * grp4 + c4
                w("g", "P", sc_last[(j, 1)])
                if H["_dbg"]:
                    w("g", "Y0", dbg0)
                wo_dma[c] = r("g", lambda e, c=c: e.dma_start(
                    out=B["wo"][:, 2048 * c:2048 * c + 2048], in_=H["wo"][c]),
                    inc="W", amt=16)

    attn_done_v = prev_tb
    attn_done_p = S_.cnt["P"]
    if H["_dbg"]:
        w("y", "V", attn_done_v)
        dbg1 = r("y", lambda e: e.dma_start(out=H["dao"], in_=B["ao"][:]), inc="Y0", amt=16)
    attn_last_exp = pair_exp_base

    w("g", "P", attn_done_p)
    bo_id = r("g", lambda e: e.dma_start(out=B["bo"][:], in_=H["bo"]), inc="W", amt=16)

    if H["_dbg"]:
        w("y", "W", wo_dma[15])
        dbg2 = r("y", lambda e: e.dma_start(out=H["dwo"], in_=B["wo"][:]), inc="Y0", amt=16)
    fbid = {}
    fbev = {}
    xboid = {}
    ln1 = {}
    castid = {}
    trxid = {}
    x2fev = {}
    x2store = {}

    def op_part1(tt):
        par = tt % 2
        w("y", "V", xboid.get(tt - 2, 0))
        if tt < 2:
            w("y", "P", attn_done_p)
        ssem = "S%d" % par
        xrid = r("y", lambda e, tt=tt, par=par: e.dma_start(
            out=B["xr"][par][:], in_=H["xo"][128 * tt:128 * tt + 128, :]),
            inc=ssem, amt=16)
        w("v", ssem, xrid)
        w("v", "W", bo_id)
        xboid[tt] = r("v", lambda e, par=par: e.tensor_tensor(
            B["xbo"][:], B["xr"][par][:], B["bo"][:], ALU.add), inc="V", selfwait=True)
        w("v", "A", castid.get(tt - 2, 0))
        w("v", "Y%d" % par, x2store.get(tt - 2, 0))
        for fb in range(4):
            n = 4 * tt + fb
            slot = n % 4
            w("t", "V", fbev.get(n - 4, 0))
            if n == 0:
                w("t", "V", attn_done_v)
                w("t", "A", attn_last_exp)
                w("t", "W", wo_dma[15])
            for c in range(16):
                last = c == 15
                idx = r("t", lambda e, c=c, fb=fb, tt=tt, slot=slot: e.matmul(
                    bk32(slot), B["ao"][:, 1024 * c + 128 * tt:1024 * c + 128 * tt + 128],
                    B["wo"][:, 2048 * c + 512 * fb:2048 * c + 512 * fb + 512],
                    start=(c == 0), stop=(c == 15)), inc=("P" if last else None))
            fbid[(tt, fb)] = idx
            w("v", "P", idx)
            fbev[n] = r("v", lambda e, fb=fb, par=par, slot=slot: e.tensor_tensor(
                B["x2t"][par][:, 512 * fb:512 * fb + 512], bk32(slot),
                B["xbo"][:, 512 * fb:512 * fb + 512], ALU.add), inc="V",
                selfwait=(fb == 3))

    def op_part2(tt):
        par = tt % 2
        r("v", lambda e, par=par: e.reduce_sum(B["ln"][:, 0:1], B["x2t"][par][:], axis=AX.X),
          inc="V", selfwait=True)
        r("v", lambda e: e.tensor_scalar(B["ln"][:, 1:2], B["ln"][:, 0:1], 1.0 / D, None, ALU.mult),
          inc="V", selfwait=True)
        cen = r("v", lambda e, par=par: e.tensor_scalar(
            B["sqs"][:], B["x2t"][par][:], B["ln"][:, 1:2], None, ALU.subtract),
            inc="V", selfwait=True)
        w("s", "V", cen)
        sqid = r("s", lambda e, par=par: e.activation(
            B["x2t"][par][:], B["sqs"][:], AF.Square), inc="A")
        w("v", "A", sqid)
        r("v", lambda e, par=par: e.reduce_sum(B["ln"][:, 2:3], B["x2t"][par][:], axis=AX.X),
          inc="V", selfwait=True)
        vv_ = r("v", lambda e: e.tensor_scalar(
            B["ln"][:, 3:4], B["ln"][:, 2:3], 1.0 / D, EPS, ALU.mult, ALU.add),
            inc="V", selfwait=True)
        w("s", "V", vv_)
        sqr = r("s", lambda e: e.activation(B["ln"][:, 4:5], B["ln"][:, 3:4], AF.Sqrt), inc="A")
        w("v", "A", sqr)
        r("v", lambda e: e.reciprocal(B["ln"][:, 5:6], B["ln"][:, 4:5]),
          inc="V", selfwait=True)
        ln1[tt] = r("v", lambda e, par=par: e.tensor_scalar(
            B["x2t"][par][:], B["sqs"][:], B["ln"][:, 5:6], None, ALU.mult), inc="V")
        w("s", "V", ln1[tt])
        w("s", "P", trxid.get(tt - 1, 0))
        castid[tt] = r("s", lambda e, par=par: e.copy(B["x2bf"][:], B["x2t"][par][:]), inc="A")
        w("t", "A", castid[tt])

        def xview(grp, tt=tt):
            vv2 = B["x2f"][:].rearrange("p (c q) -> p c q", c=16)
            return vv2[:, 4 * grp:4 * grp + 4, 128 * tt:128 * tt + 128]
        trxid[tt] = transpose_tile(B["x2bf"], xview)
        x2fev[tt] = ev[transpose_tile.seq - 1]
        w("y", "V", ln1[tt])
        x2store[tt] = r("y", lambda e, tt=tt, par=par: e.dma_start(
            out=H["x2s"][128 * tt:128 * tt + 128, :], in_=B["x2t"][par][:]),
            inc="Y%d" % par, amt=16)

    for tt in range(8):
        op_part1(tt)
        if tt >= 1:
            op_part2(tt - 1)
    op_part2(7)

    oproj_done_p = S_.cnt["P"]
    oproj_last_fbev = fbev[31]

    conv_id = {}
    gmm = {}
    umm = {}
    hTT = {}
    silu_id = {}
    for ib in range(64):
        par = ib % 2
        w("y", "V", conv_id.get(ib - 1, 0))
        if ib == 0:
            w("y", "P", oproj_done_p)
            if H["_dbg"]:
                w("y", "Y0", dbg1)
        r("y", lambda e, ib=ib: e.dma_start(out=B["wgst"][:], in_=H["wg"][ib]),
          inc="S0", amt=16)
        wld = r("y", lambda e, ib=ib: e.dma_start(out=B["wust"][:], in_=H["wu"][ib]),
                inc="S0", amt=16)
        w("v", "S0", wld)
        w("v", "P", umm.get(ib - 2, 0))
        r("v", lambda e, par=par: e.tensor_scalar(
            B["wgb"][par][:], B["wgst"][:], -128.0, None, ALU.add))
        conv_id[ib] = r("v", lambda e, par=par: e.tensor_scalar(
            B["wub"][par][:], B["wust"][:], -128.0, None, ALU.add), inc="V")
        for th in range(2):
            n = 2 * ib + th
            s3 = n % 3
            w("t", "V", conv_id[ib])
            w("t", "V", hTT.get(n - 3, 0))
            if n == 0:
                w("t", "A", x2fev[7])
                w("t", "V", oproj_last_fbev)
            for c in range(16):
                last = c == 15
                idx = r("t", lambda e, c=c, th=th, par=par, s3=s3: e.matmul(
                    ps[:, 1024 * s3:1024 * s3 + 512],
                    B["wgb"][par][:, 128 * c:128 * c + 128],
                    B["x2f"][:, 1024 * c + 512 * th:1024 * c + 512 * th + 512],
                    start=(c == 0), stop=(c == 15)), inc=("P" if last else None))
            gmm[n] = idx
            for c in range(16):
                last = c == 15
                idx = r("t", lambda e, c=c, th=th, par=par, s3=s3: e.matmul(
                    ps[:, 1024 * s3 + 512:1024 * s3 + 1024],
                    B["wub"][par][:, 128 * c:128 * c + 128],
                    B["x2f"][:, 1024 * c + 512 * th:1024 * c + 512 * th + 512],
                    start=(c == 0), stop=(c == 15)), inc=("P" if last else None))
            umm[n] = idx
            w("s", "P", gmm[n])
            w("s", "V", hTT.get(n - 1, 0))
            silu_id[n] = r("s", lambda e, ib=ib, s3=s3: e.activation(
                B["sg"][:], ps[:, 1024 * s3:1024 * s3 + 512], AF.Silu,
                bias=0.0, scale=B["gs"][:, ib:ib + 1]), inc="A")
            w("v", "P", umm[n])
            r("v", lambda e, ib=ib, s3=s3: e.tensor_scalar(
                B["usv"][:], ps[:, 1024 * s3 + 512:1024 * s3 + 1024],
                B["us"][:, ib:ib + 1], None, ALU.mult), inc="V", selfwait=True)
            w("v", "A", silu_id[n])
            if n == 0 and H["_dbg"]:
                w("v", "Y0", dbg2)
            if ib == 32 and th == 0:
                w("v", "Y1", x2store[7])
                w("v", "A", castid[7])
            hTT[n] = r("v", lambda e, ib=ib, th=th: e.tensor_tensor(
                B["hf"][:, 1024 * ib + 512 * th:1024 * ib + 512 * th + 512],
                B["sg"][:], B["usv"][:], ALU.mult), inc="V")
    hlast = hTT[127]
    ffa_last_silu = silu_id[127]

    dmm = {}
    fmid = {}
    trf = {}
    tev = {}
    tokstore = {}
    wdc = {}

    def ffb_evac(hc):
        par = hc % 2
        w("v", "P", dmm[8 * hc + 7])
        w("v", "P", trf.get(hc - 1, 0))
        fmid[hc] = r("v", lambda e, hc=hc, par=par: e.tensor_scalar(
            B["fm"][:], ps[:, 1024 * par:1024 * par + 1024],
            B["ds"][:, hc:hc + 1], None, ALU.mult), inc="V", selfwait=True)
        w("t", "V", fmid[hc])
        w("t", "A", tev.get(hc - 1, 0))
        for wv_ in range(8):
            off = 2048 + 128 * wv_
            last = wv_ == 7
            idx = r("t", lambda e, wv_=wv_, off=off: e.transpose(
                ps[:, off:off + 128], B["fm"][:, 128 * wv_:128 * wv_ + 128],
                B["idf"][:]), inc=("P" if last else None))
        trf[hc] = idx
        w("s", "P", trf[hc])
        w("s", "Y0", tokstore.get(hc - 1, 0))
        r("s", lambda e: e.copy(B["tok"][0][:], bk32(4)))
        tev[hc] = r("s", lambda e: e.copy(B["tok"][1][:], bk32(5)), inc="A")
        w("y", "A", tev[hc])
        for gi2 in range(2):
            tokstore[hc] = r("y", lambda e, gi2=gi2, hc=hc: e.dma_start(
                out=H["ffs"].rearrange("(w p) h -> p w h", p=128)[
                    :, 4 * gi2:4 * gi2 + 4, 128 * hc:128 * hc + 128],
                in_=B["tok"][gi2][:].rearrange("p (w h) -> p w h", w=4)),
                inc="Y0", amt=16)

    for hc in range(16):
        par = hc % 2
        for ibg in range(8):
            gi = 8 * hc + ibg
            sp_ = gi % 2
            w("y", "V", wdc.get(gi - 2, 0))
            if gi == 0:
                w("y", "V", hlast)
            ssem = "S%d" % sp_
            ldid = r("y", lambda e, hc=hc, ibg=ibg, sp_=sp_: e.dma_start(
                out=B["wdst"][sp_][:].rearrange("p (g h) -> p g h", g=8),
                in_=H["wd"][hc, 8 * ibg:8 * ibg + 8].rearrange("g p h -> p g h")),
                inc=ssem, amt=16)
            w("v", ssem, ldid)
            w("v", "P", dmm.get(gi - 2, 0))
            wdc[gi] = r("v", lambda e, sp_=sp_: e.tensor_scalar(
                B["wdb"][sp_][:], B["wdst"][sp_][:], -128.0, None, ALU.add), inc="V")
            w("t", "V", wdc[gi])
            if ibg == 0:
                w("t", "V", fmid.get(hc - 2, 0))
                if hc == 0:
                    w("t", "V", hlast)
                    w("t", "A", ffa_last_silu)
            for i8 in range(8):
                ib = 8 * ibg + i8
                for th in range(2):
                    last = (i8 == 7 and th == 1)
                    idx = r("t", lambda e, i8=i8, th=th, ib=ib, par=par, sp_=sp_: e.matmul(
                        ps[:, 1024 * par + 512 * th:1024 * par + 512 * th + 512],
                        B["wdb"][sp_][:, 128 * i8:128 * i8 + 128],
                        B["hf"][:, 1024 * ib + 512 * th:1024 * ib + 512 * th + 512],
                        start=(ib == 0), stop=(ib == 63)), inc=("P" if last else None))
            dmm[gi] = idx
            if ibg == 0 and hc >= 1:
                ffb_evac(hc - 1)
    ffb_evac(15)

    ffb_done_p = dmm[127]

    ln2 = {}
    ystore = {}
    for tt in range(8):
        par = tt % 2
        w("y", "Y0", tokstore[15])
        w("y", "V", ln2.get(tt - 2, 0))
        ssem = "S%d" % par
        r("y", lambda e, tt=tt, par=par: e.dma_start(
            out=B["ffin"][par][:], in_=H["ffs"][128 * tt:128 * tt + 128, :]),
          inc=ssem, amt=16)
        ffld = r("y", lambda e, tt=tt, par=par: e.dma_start(
            out=B["x2in"][par][:], in_=H["x2s"][128 * tt:128 * tt + 128, :]),
            inc=ssem, amt=16)
        w("v", ssem, ffld)
        if tt == 0:
            w("v", "P", ffb_done_p)
            w("v", "Y0", tokstore[15])
        w("v", "Y%d" % par, ystore.get(tt - 2, 0))
        r("v", lambda e, par=par: e.tensor_tensor(
            B["outt"][par][:], B["ffin"][par][:], B["x2in"][par][:], ALU.add),
          inc="V", selfwait=True)
        r("v", lambda e, par=par: e.reduce_sum(B["ln"][:, 8:9], B["outt"][par][:], axis=AX.X),
          inc="V", selfwait=True)
        r("v", lambda e: e.tensor_scalar(B["ln"][:, 9:10], B["ln"][:, 8:9], 1.0 / D, None, ALU.mult),
          inc="V", selfwait=True)
        cen = r("v", lambda e, par=par: e.tensor_scalar(
            B["sq2"][:], B["outt"][par][:], B["ln"][:, 9:10], None, ALU.subtract),
            inc="V", selfwait=True)
        w("s", "V", cen)
        sqid = r("s", lambda e, par=par: e.activation(
            B["outt"][par][:], B["sq2"][:], AF.Square), inc="A")
        w("v", "A", sqid)
        r("v", lambda e, par=par: e.reduce_sum(B["ln"][:, 10:11], B["outt"][par][:], axis=AX.X),
          inc="V", selfwait=True)
        vv_ = r("v", lambda e: e.tensor_scalar(
            B["ln"][:, 11:12], B["ln"][:, 10:11], 1.0 / D, EPS, ALU.mult, ALU.add),
            inc="V", selfwait=True)
        w("s", "V", vv_)
        sqr = r("s", lambda e: e.activation(B["ln"][:, 12:13], B["ln"][:, 11:12], AF.Sqrt), inc="A")
        w("v", "A", sqr)
        r("v", lambda e: e.reciprocal(B["ln"][:, 13:14], B["ln"][:, 12:13]),
          inc="V", selfwait=True)
        ln2[tt] = r("v", lambda e, par=par: e.tensor_scalar(
            B["outt"][par][:], B["sq2"][:], B["ln"][:, 13:14], None, ALU.mult), inc="V")
        w("y", "V", ln2[tt])
        ystore[tt] = r("y", lambda e, tt=tt, par=par: e.dma_start(
            out=H["y"][128 * tt:128 * tt + 128, :], in_=B["outt"][par][:]),
            inc="Y%d" % par, amt=16)

    w("y", "Y0", S_.cnt["Y0"])
    w("y", "Y1", S_.cnt["Y1"])


def _pack(inputs):
    f32 = np.float32
    Wq = np.asarray(inputs["L0_attn_Wq"], f32)
    bq = np.asarray(inputs["L0_attn_bq"], f32)
    Wk = np.asarray(inputs["L0_attn_Wk"], f32)
    bk = np.asarray(inputs["L0_attn_bk"], f32)
    Wv = np.asarray(inputs["L0_attn_Wv"], f32)
    bv = np.asarray(inputs["L0_attn_bv"], f32)
    Wo = np.asarray(inputs["L0_attn_Wo"], f32)
    bo = np.asarray(inputs["L0_attn_bo"], f32)
    gw = np.ascontiguousarray(np.asarray(inputs["L0_ffn_gate_w"], np.int32))
    uw = np.ascontiguousarray(np.asarray(inputs["L0_ffn_up_w"], np.int32))
    dw = np.ascontiguousarray(np.asarray(inputs["L0_ffn_down_w"], np.int32))
    gsc = np.asarray(inputs["L0_ffn_gate_scale"], f32)
    usc = np.asarray(inputs["L0_ffn_up_scale"], f32)
    dsc = np.asarray(inputs["L0_ffn_down_scale"], f32)

    shared = {}
    wq_t = np.empty((16, 128, 2048), f32)
    bq_t = np.empty((128, 16), f32)
    for j in range(16):
        cols = np.r_[64 * j:64 * j + 64, 64 * (j + 16):64 * (j + 16) + 64]
        tmp = Wq[:, cols].reshape(16, 128, 128)
        wq_t[j] = tmp.transpose(1, 0, 2).reshape(128, 2048)
        bq_t[:, j] = bq[cols]
    shared["wq_t"] = np.ascontiguousarray(wq_t)
    shared["bq_t"] = np.ascontiguousarray(bq_t)
    col_order = np.empty(512, np.int64)
    for cc in range(4):
        for dd in range(128):
            kv = cc + (4 if dd >= 64 else 0)
            col_order[128 * cc + dd] = 64 * kv + dd % 64
    wk_re = Wk[:, col_order]
    shared["wk_t"] = np.ascontiguousarray(
        wk_re.reshape(16, 128, 512).transpose(1, 0, 2).reshape(128, 8192))
    shared["bk_t"] = np.ascontiguousarray(bk[col_order].reshape(4, 128).T)
    shared["wv_t"] = np.ascontiguousarray(
        Wv.reshape(16, 128, 512).transpose(1, 0, 2).reshape(128, 8192))
    shared["bv_bc"] = np.ascontiguousarray(np.broadcast_to(bv, (128, 512)))
    wo_t = np.empty((16, 128, 2048), f32)
    for c in range(16):
        rows = np.r_[64 * c:64 * c + 64, 64 * (c + 16):64 * (c + 16) + 64]
        wo_t[c] = Wo[rows, :]
    shared["wo_t"] = np.ascontiguousarray(wo_t)
    shared["bo_bc"] = np.ascontiguousarray(np.broadcast_to(bo, (128, 2048)))
    shared["wg_t"] = np.ascontiguousarray(
        gw.reshape(64, 128, 16, 128).transpose(0, 3, 2, 1).reshape(64, 128, 2048))
    shared["wu_t"] = np.ascontiguousarray(
        uw.reshape(64, 128, 16, 128).transpose(0, 3, 2, 1).reshape(64, 128, 2048))
    shared["wd_t"] = np.ascontiguousarray(
        dw.reshape(16, 128, 64, 128).transpose(0, 2, 3, 1))
    shared["gs_t"] = np.ascontiguousarray(gsc.reshape(64, 128).T)
    shared["us_t"] = np.ascontiguousarray(usc.reshape(64, 128).T)
    shared["ds_t"] = np.ascontiguousarray(dsc.reshape(16, 128).T)
    shared["identb"] = np.eye(128, dtype=np.float32).astype(ml_dtypes.bfloat16)
    shared["identf"] = np.eye(128, dtype=np.float32)
    shared["ones64b"] = np.ones((128, 64), np.float32).astype(ml_dtypes.bfloat16)
    shared["onesf"] = np.ones((128, 64), np.float32)
    return shared


def kernel(**inputs):
    x = np.ascontiguousarray(np.asarray(inputs["L0_attn_x"], np.float32).reshape(SEQ, D))
    shared = _pack(inputs)
    nc = build_nc()
    in_maps = []
    for c in range(NCORES):
        m = dict(shared)
        m["x_full"] = x
        m["x_own"] = np.ascontiguousarray(x[T * c:T * (c + 1)])
        in_maps.append(m)
    res = run_bass_kernel_spmd(nc, in_maps, core_ids=list(range(NCORES)))
    y = np.concatenate([np.asarray(res.results[c]["y"]) for c in range(NCORES)], axis=0)
    return y.reshape(1, SEQ, D).astype(np.float32)


if __name__ == "__main__":
    import reference as _ref
    inp = _ref.setup_inputs()
    out = kernel(**{k: np.asarray(v) for k, v in inp.items()})
    want = np.asarray(_ref.reference(**inp))
    err = np.abs(out - want)
    denom = np.mean(want.astype(np.float64) ** 2)
    print("rvar:", float(np.mean((out.astype(np.float64) - want.astype(np.float64)) ** 2) / denom))
    print("maxabs:", float(err.max()))


# version 07
# speedup vs baseline: 1.1982x; 1.0081x over previous
import numpy as np
import ml_dtypes
import concourse.bass as bass
import concourse.mybir as mybir
from concourse.bass_utils import run_bass_kernel_spmd

F32 = mybir.dt.float32
BF16 = mybir.dt.bfloat16
I32 = mybir.dt.int32
AF = mybir.ActivationFunctionType
ALU = mybir.AluOpType
AX = mybir.AxisListType

NCORES = 8
SEQ = 8192
D = 2048
T = SEQ // NCORES
NKT = SEQ // 128
EPS = 1e-5


def build_nc():
    nc = bass.Bass("TRN2", target_bir_lowering=False)

    H = {}
    H["xf"] = nc.dram_tensor("x_full", [SEQ, D], F32, kind="ExternalInput").ap()
    H["xo"] = nc.dram_tensor("x_own", [T, D], F32, kind="ExternalInput").ap()
    H["wq"] = nc.dram_tensor("wq_t", [16, 128, 2048], F32, kind="ExternalInput").ap()
    H["bq"] = nc.dram_tensor("bq_t", [128, 16], F32, kind="ExternalInput").ap()
    H["wk"] = nc.dram_tensor("wk_t", [128, 8192], F32, kind="ExternalInput").ap()
    H["bk"] = nc.dram_tensor("bk_t", [128, 4], F32, kind="ExternalInput").ap()
    H["wv"] = nc.dram_tensor("wv_t", [128, 8192], F32, kind="ExternalInput").ap()
    H["bv"] = nc.dram_tensor("bv_bc", [128, 512], F32, kind="ExternalInput").ap()
    H["wo"] = nc.dram_tensor("wo_t", [16, 128, 2048], F32, kind="ExternalInput").ap()
    H["bo"] = nc.dram_tensor("bo_bc", [128, 2048], F32, kind="ExternalInput").ap()
    H["wg"] = nc.dram_tensor("wg_t", [64, 128, 2048], I32, kind="ExternalInput").ap()
    H["wu"] = nc.dram_tensor("wu_t", [64, 128, 2048], I32, kind="ExternalInput").ap()
    H["wd"] = nc.dram_tensor("wd_t", [16, 64, 128, 128], I32, kind="ExternalInput").ap()
    H["gs"] = nc.dram_tensor("gs_t", [128, 64], F32, kind="ExternalInput").ap()
    H["us"] = nc.dram_tensor("us_t", [128, 64], F32, kind="ExternalInput").ap()
    H["ds"] = nc.dram_tensor("ds_t", [128, 16], F32, kind="ExternalInput").ap()
    H["idb"] = nc.dram_tensor("identb", [128, 128], BF16, kind="ExternalInput").ap()
    H["idf"] = nc.dram_tensor("identf", [128, 128], F32, kind="ExternalInput").ap()
    H["o64"] = nc.dram_tensor("ones64b", [128, 128], BF16, kind="ExternalInput").ap()
    H["of"] = nc.dram_tensor("onesf", [128, 64], F32, kind="ExternalInput").ap()
    H["y"] = nc.dram_tensor("y", [T, D], F32, kind="ExternalOutput").ap()
    import os
    H["_dbg"] = bool(os.environ.get("KDEBUG"))
    if H["_dbg"]:
        H["dkt"] = nc.dram_tensor("dbg_kt", [128, 32768], BF16, kind="ExternalOutput").ap()
        H["dv"] = nc.dram_tensor("dbg_v", [128, 32768], BF16, kind="ExternalOutput").ap()
        H["dqt"] = nc.dram_tensor("dbg_qt", [128, 16384], BF16, kind="ExternalOutput").ap()
        H["dao"] = nc.dram_tensor("dbg_ao", [128, 16384], BF16, kind="ExternalOutput").ap()
        H["dwo"] = nc.dram_tensor("dbg_wo", [128, 32768], BF16, kind="ExternalOutput").ap()
        H["ds_"] = nc.dram_tensor("dbg_s", [T, D], F32, kind="ExternalOutput").ap()
    import os as _os
    _scrkind = "ExternalOutput" if _os.environ.get("KDEBUG") else "Internal"
    H["x2s"] = nc.dram_tensor("x2_scr", [T, D], F32, kind=_scrkind).ap()
    H["ffs"] = nc.dram_tensor("ffn_scr", [T, D], F32, kind=_scrkind).ap()

    SB_BASE = 16512

    def at(name, shape, dtype, off):
        return nc.alloc_sbuf_tensor_at(name, shape, dtype, offset=SB_BASE + off)

    B = {}
    B["kt"] = at("kt_b", [128, 32768], BF16, 0)
    B["v"] = at("v_b", [128, 32768], BF16, 65536)
    B["wo"] = at("wo_b", [128, 32768], BF16, 0)
    B["hf"] = at("hfull", [128, 65536], BF16, 0)
    B["bo"] = at("bo_s", [128, 2048], F32, 65536)
    B["xr"] = [at(f"xr{i}", [128, 2048], F32, 73728 + 8192 * i) for i in range(2)]
    B["x2t"] = [at(f"x2t{i}", [128, 2048], F32, 90112 + 8192 * i) for i in range(2)]
    B["x2bf"] = at("x2bf", [128, 2048], BF16, 106496)
    B["sqs"] = at("sqs", [128, 2048], F32, 110592)
    B["xbo"] = at("xbo", [128, 2048], F32, 118784)
    B["qt"] = at("qt_b", [128, 16384], BF16, 131072)
    B["x2f"] = at("x2f", [128, 16384], BF16, 131072)
    B["xfblk"] = at("xfblk", [128, 16384], BF16, 131072)
    B["wkb"] = at("wkb", [128, 8192], BF16, 163840)
    B["ffin"] = [at(f"ffin{i}", [128, 2048], F32, 131072 + 8192 * i) for i in range(2)]
    B["x2in"] = [at(f"x2in{i}", [128, 2048], F32, 147456 + 8192 * i) for i in range(2)]
    B["ao"] = at("ao_b", [128, 16384], BF16, 163840)
    B["xfo"] = at("xfown", [128, 16384], BF16, 163840)
    B["wvb"] = at("wvb", [128, 8192], BF16, 180224)
    B["bv"] = at("bv_s", [128, 512], F32, 206752)
    B["wgst"] = at("wgstage", [128, 2048], I32, 163840)
    B["wust"] = at("wustage", [128, 2048], I32, 172032)
    B["wgb"] = [at(f"wgb{i}", [128, 2048], BF16, 180224 + 4096 * i) for i in range(2)]
    B["wub"] = [at(f"wub{i}", [128, 2048], BF16, 188416 + 4096 * i) for i in range(2)]
    B["wdst"] = [at(f"wdst{i}", [128, 2048], I32, 163840 + 8192 * i) for i in range(2)]
    B["wdb"] = [at(f"wdb{i}", [128, 2048], BF16, 180224 + 4096 * i) for i in range(2)]
    B["fm"] = at("fm_s", [128, 1024], F32, 188416)
    B["tok"] = [at(f"tokst{i}", [128, 512], F32, 192512 + 2048 * i) for i in range(2)]
    B["outt"] = [at(f"outt{i}", [128, 2048], F32, 163840 + 8192 * i) for i in range(2)]
    B["sq2"] = at("sq2", [128, 2048], F32, 180224)
    B["idb"] = at("idb_s", [128, 128], BF16, 196608)
    B["idf"] = at("idf_s", [128, 128], F32, 196864)
    B["o64"] = at("o64_s", [128, 128], BF16, 197376)
    B["of"] = at("of_s", [128, 64], F32, 197632)
    B["bq"] = at("bq_s", [128, 16], F32, 197888)
    B["bk"] = at("bk_s", [128, 4], F32, 197952)
    B["gs"] = at("gs_s", [128, 64], F32, 197984)
    B["us"] = at("us_s", [128, 64], F32, 198240)
    B["ds"] = at("ds_s", [128, 16], F32, 198496)
    B["ln"] = at("lnst", [128, 16], F32, 210848)
    B["xst"] = [at(f"xstag{i}", [128, 2048], BF16, 198560 + 4096 * i) for i in range(2)]
    B["pt"] = [at(f"pt{i}", [128, 1024], BF16, 198560 + 2048 * i) for i in range(4)]
    B["rd"] = at("rd_t", [128, 512], F32, 206752)
    B["bcs"] = at("bcs", [128, 512], F32, 208800)
    B["sg"] = at("sg_s", [128, 512], F32, 206752)
    B["rdb"] = at("rdb", [128, 512], BF16, 210912)
    B["usv"] = at("usv_s", [128, 512], F32, 208800)
    B["borow"] = at("borow", [1, 2048], BF16, 118784)

    B["nc"] = nc
    B["ps"] = nc.alloc_psum_tensor("ps32", [128, 3072], F32)
    B["psbf"] = nc.alloc_psum_tensor("psbf", [128, 2048], BF16)

    from contextlib import ExitStack
    with ExitStack() as es:
        ec = es.enter_context
        sems = {k: ec(nc.semaphore(f"sem{k}")) for k in
                ("P", "A", "V", "G0", "G1", "W", "S0", "S1", "Y0", "Y1")}
        block = ec(nc.Block())

        class Sched:
            def __init__(self, only, eng):
                self.only = only
                self.eng = eng
                self.cnt = {k: 0 for k in
                            ("P", "A", "V", "G0", "G1", "W", "S0", "S1", "Y0", "Y1")}

            def wait(self, engname, sem, val):
                if val is None or val <= 0:
                    return
                if self.only == engname:
                    self.eng.wait_ge(sems[sem], val)

            def run(self, engname, fn, inc=None, amt=1, selfwait=False):
                if inc is not None:
                    self.cnt[inc] += amt
                if self.only == engname:
                    inst = fn(self.eng)
                    if inc is not None:
                        inst.then_inc(sems[inc], amt)
                        if selfwait:
                            self.eng.wait_ge(sems[inc], self.cnt[inc])
                return self.cnt[inc] if inc is not None else None

        @block.gpsimd
        def _(g):
            emit_program(Sched("g", g), H, B)

        @block.tensor
        def _(t):
            emit_program(Sched("t", t), H, B)

        @block.scalar
        def _(s):
            emit_program(Sched("s", s), H, B)

        @block.vector
        def _(v):
            emit_program(Sched("v", v), H, B)

        @block.sync
        def _(sy):
            emit_program(Sched("y", sy), H, B)

    return nc

def emit_program(S_, H, B):
    w = S_.wait
    r = S_.run
    ps = B["ps"]
    psbf = B["psbf"]

    def bk32(i, n=512):
        return ps[:, 512 * i:512 * i + n]

    for k in ("idb", "idf", "o64", "of", "bq", "bk", "gs", "us", "ds", "bv"):
        r("g", lambda e, k=k: e.dma_start(out=B[k][:], in_=H[k]), inc="W", amt=16)
    r("g", lambda e: e.dma_start(out=B["wkb"][:], in_=H["wk"]), inc="W", amt=16)
    wconsts = r("g", lambda e: e.dma_start(out=B["wvb"][:], in_=H["wv"]), inc="W", amt=16)

    ev = {}

    def transpose_tile(src_buf, dst3):
        g0 = transpose_tile.seq
        for grp in range(4):
            half = grp % 2
            w("t", "A", ev.get(g0 + grp - 2, 0))
            for cc in range(4):
                c = 4 * grp + cc
                last = cc == 3
                tid = r("t", lambda e, c=c, half=half, cc=cc, src_buf=src_buf: e.transpose(
                    psbf[:, 1024 * half + 128 * cc:1024 * half + 128 * cc + 128],
                    src_buf[:, 128 * c:128 * c + 128], B["idb"][:]),
                    inc=("P" if last else None))
            w("s", "P", tid)
            ev[g0 + grp] = r("s", lambda e, half=half, grp=grp, dst3=dst3: e.copy(
                dst3(grp),
                psbf[:, 1024 * half:1024 * half + 512].rearrange("p (g q) -> p g q", g=4)),
                inc="A")
        transpose_tile.seq += 4
        return tid
    transpose_tile.seq = 0

    kev = {}
    vev = {}
    last_tr = {}
    NBLK = SEQ // 512
    for blk in range(NBLK):
        slot_x = 8192 * (blk % 2)
        for s4 in range(4):
            ti = 4 * blk + s4
            buf = ti % 2
            w("g", "P", last_tr.get(ti - 2, 0))
            gsem = "G%d" % buf
            r("g", lambda e, ti=ti, buf=buf: e.dma_start(
                out=B["xst"][buf][:], in_=H["xf"][128 * ti:128 * ti + 128, :]),
              inc=gsem, amt=16)
            gneed = S_.cnt[gsem]
            if ti < 2:
                w("t", "W", wconsts)
                w("v", "W", wconsts)
            w("t", gsem, gneed)

            def dview(grp, slot_x=slot_x, s4=s4):
                vv = B["xfblk"][:].rearrange("p (c q) -> p c q", q=512)
                c0 = slot_x // 512 + 4 * grp
                return vv[:, c0:c0 + 4, 128 * s4:128 * s4 + 128]
            last_tr[ti] = transpose_tile(B["xst"][buf], dview)
        w("t", "A", ev[transpose_tile.seq - 1])
        for cc in range(4):
            kseq = 4 * blk + cc
            pb = cc % 2
            w("t", "V", kev.get(kseq - 2, 0))
            for c in range(16):
                last = c == 15
                kmm = r("t", lambda e, cc=cc, c=c, pb=pb, slot_x=slot_x: e.matmul(
                    bk32(pb),
                    B["wkb"][:, 512 * c + 128 * cc:512 * c + 128 * cc + 128],
                    B["xfblk"][:, slot_x + 512 * c:slot_x + 512 * c + 512],
                    start=(c == 0), stop=(c == 15)), inc=("P" if last else None))
            w("v", "P", kmm)
            kev[kseq] = r("v", lambda e, cc=cc, blk=blk, pb=pb: e.tensor_scalar(
                B["kt"][:, 8192 * cc + 512 * blk:8192 * cc + 512 * blk + 512],
                bk32(pb), B["bk"][:, cc:cc + 1], None, ALU.add),
                inc="V")
        for s4 in range(4):
            ti = 4 * blk + s4
            vb = 2 + ti % 2
            w("t", "V", vev.get(ti - 2, 0))
            for c in range(16):
                last = c == 15
                vmm = r("t", lambda e, c=c, vb=vb, slot_x=slot_x, s4=s4: e.matmul(
                    bk32(vb), B["xfblk"][:, slot_x + 512 * c + 128 * s4:slot_x + 512 * c + 128 * s4 + 128],
                    B["wvb"][:, 512 * c:512 * c + 512],
                    start=(c == 0), stop=(c == 15)), inc=("P" if last else None))
            w("v", "P", vmm)
            vev[ti] = r("v", lambda e, ti=ti, vb=vb: e.tensor_tensor(
                B["v"][:, 512 * ti:512 * ti + 512], bk32(vb), B["bv"][:], ALU.add),
                inc="V")

    w("s", "V", vev[NKT - 1])
    for s8 in range(8):
        buf = s8 % 2
        w("g", "P", last_tr.get(NKT + s8 - 2, 0))
        gsem = "G%d" % buf
        r("g", lambda e, s8=s8, buf=buf: e.dma_start(
            out=B["xst"][buf][:], in_=H["xo"][128 * s8:128 * s8 + 128, :]),
          inc=gsem, amt=16)
        gneed = S_.cnt[gsem]
        w("t", gsem, gneed)

        def oview(grp, s8=s8):
            vv = B["xfo"][:].rearrange("p (c q) -> p c q", c=16)
            return vv[:, 4 * grp:4 * grp + 4, 128 * s8:128 * s8 + 128]
        last_tr[NKT + s8] = transpose_tile(B["xst"][buf], oview)

    qmm = {}
    qev = {}
    for j in range(16):
        buf = j % 2
        w("g", "P", qmm.get(j - 2, last_tr[NKT + 7]))
        gsem = "G%d" % buf
        r("g", lambda e, j=j, buf=buf: e.dma_start(
            out=B["xst"][buf][:], in_=H["wq"][j]), inc=gsem, amt=16)
        gneed = S_.cnt[gsem]
        w("t", gsem, gneed)
        qb_ = j % 2
        w("t", "V", qev.get(j - 2, max(kev[4 * (SEQ // 512 - 1) + 3], vev[NKT - 1])))
        for c in range(16):
            for hh in range(2):
                last = (c == 15 and hh == 1)
                idx = r("t", lambda e, c=c, hh=hh, qb_=qb_, buf=buf: e.matmul(
                    ps[:, 1024 * qb_ + 512 * hh:1024 * qb_ + 512 * hh + 512],
                    B["xst"][buf][:, 128 * c:128 * c + 128],
                    B["xfo"][:, 1024 * c + 512 * hh:1024 * c + 512 * hh + 512],
                    start=(c == 0), stop=(c == 15)), inc=("P" if last else None))
                if last:
                    qmm[j] = idx
        w("v", "P", qmm[j])
        qev[j] = r("v", lambda e, j=j, qb_=qb_: e.tensor_scalar(
            B["qt"][:, 1024 * j:1024 * j + 1024], ps[:, 1024 * qb_:1024 * qb_ + 1024],
            B["bq"][:, j:j + 1], None, ALU.add), inc="V")

    if H["_dbg"]:
        w("y", "V", max(qev[15], vev[NKT - 1], kev[4 * (SEQ // 512 - 1) + 3]))
        r("y", lambda e: e.dma_start(out=H["dkt"], in_=B["kt"][:]), inc="Y0", amt=16)
        r("y", lambda e: e.dma_start(out=H["dv"], in_=B["v"][:]), inc="Y0", amt=16)
        dbg0 = r("y", lambda e: e.dma_start(out=H["dqt"], in_=B["qt"][:]), inc="Y0", amt=16)
    psA = ps[:, 2048:2560]
    psB = ps[:, 2560:3072]
    pair_exp_base = qev[15] * 0 + S_.cnt["A"]
    prev_tb = vev[NKT - 1]
    prev_av = {}
    sc_last = {}
    wo_dma = {}
    qt_ready = qev[15]
    attn_first = True
    for j in range(16):
        for qb in range(2):
            cc = j // 4
            kvA = j // 4
            kvB = 4 + j // 4
            exp_id = {}
            av_id = {}
            sc_id = {}

            def emit_av(kt, kvA=kvA, kvB=kvB, exp_id=exp_id, av_id=av_id,
                        first_gate=None):
                pti = kt % 4
                w("t", "A", exp_id[kt])
                if kt == 0:
                    w("t", "V", first_gate)
                st = kt == 0
                sp = kt == 63
                r("t", lambda e, kt=kt, pti=pti, kvA=kvA, st=st, sp=sp: e.matmul(
                    psA[0:64, :], B["v"][:, 512 * kt + 64 * kvA:512 * kt + 64 * kvA + 64],
                    B["pt"][pti][:, 0:512], start=st, stop=sp, tile_position=(0, 0),
                    skip_group_check=True))
                r("t", lambda e, pti=pti, st=st, sp=sp: e.matmul(
                    psA[64:128, :], B["o64"][:, 0:64], B["pt"][pti][:, 512:1024],
                    start=st, stop=sp, tile_position=(0, 64), skip_group_check=True))
                r("t", lambda e, kt=kt, pti=pti, kvB=kvB, st=st, sp=sp: e.matmul(
                    psB[64:128, :], B["v"][:, 512 * kt + 64 * kvB:512 * kt + 64 * kvB + 64],
                    B["pt"][pti][:, 512:1024], start=st, stop=sp, tile_position=(0, 64),
                    skip_group_check=True))
                av_id[kt] = r("t", lambda e, pti=pti, st=st, sp=sp: e.matmul(
                    psB[0:64, :], B["o64"][:, 0:64], B["pt"][pti][:, 0:512],
                    start=st, stop=sp, tile_position=(0, 0), skip_group_check=True), inc="P")

            gate0 = prev_tb
            for kt in range(64):
                scb = kt % 2
                w("t", "A", exp_id.get(kt - 2, pair_exp_base))
                if attn_first and kt == 0:
                    w("t", "V", qt_ready)
                r("t", lambda e, kt=kt, scb=scb, cc=cc, j=j, qb=qb: e.matmul(
                    bk32(2 * scb),
                    B["kt"][0:64, 8192 * cc + 128 * kt:8192 * cc + 128 * kt + 128],
                    B["qt"][0:64, 1024 * j + 512 * qb:1024 * j + 512 * qb + 512],
                    start=True, stop=True, tile_position=(0, 0)))
                sc_id[kt] = r("t", lambda e, kt=kt, scb=scb, cc=cc, j=j, qb=qb: e.matmul(
                    bk32(2 * scb + 1),
                    B["kt"][64:128, 8192 * cc + 128 * kt:8192 * cc + 128 * kt + 128],
                    B["qt"][64:128, 1024 * j + 512 * qb:1024 * j + 512 * qb + 512],
                    start=True, stop=True, tile_position=(64, 0)), inc="P")
                if kt >= 1:
                    emit_av(kt - 1, first_gate=gate0)
                ptw = av_id.get(kt - 4, prev_av.get(60 + kt, 0))
                w("s", "P", max(sc_id[kt], ptw))
                exp_id[kt] = r("s", lambda e, kt=kt, scb=scb: e.activation(
                    B["pt"][kt % 4][:], ps[:, 1024 * scb:1024 * scb + 1024],
                    AF.Exp, bias=0.0, scale=0.125), inc="A")
            emit_av(63, first_gate=gate0)
            attn_first = False

            acols = slice(1024 * j + 512 * qb, 1024 * j + 512 * qb + 512)
            w("v", "P", av_id[63])
            def _recip(e, dst, src_):
                with B["nc"].allow_low_precision(reason="softmax 1/D broadcast via bf16 matmul"):
                    return e.reciprocal(dst, src_)
            r("v", lambda e: _recip(e, B["rdb"][0:1, :], psB[0:1, :]),
              inc="V", selfwait=True)
            rid = r("v", lambda e: _recip(e, B["rdb"][64:65, :], psA[64:65, :]), inc="V")
            w("t", "V", rid)
            bA = r("t", lambda e: e.matmul(
                psB[0:64, :], B["o64"][0:1, 0:64], B["rdb"][0:1, :],
                start=True, stop=True), inc="P")
            w("v", "P", bA)
            r("v", lambda e: e.tensor_copy(B["bcs"][0:64, :], psB[0:64, :]),
              inc="V", selfwait=True)
            tA = r("v", lambda e, acols=acols: e.tensor_tensor(
                B["ao"][0:64, acols], psA[0:64, :], B["bcs"][0:64, :], ALU.mult),
                inc="V")
            w("t", "V", tA)
            bB = r("t", lambda e: e.matmul(
                psA[64:128, :], B["o64"][64:65, 0:64], B["rdb"][64:65, :],
                start=True, stop=True), inc="P")
            w("v", "P", bB)
            r("v", lambda e: e.tensor_copy(B["bcs"][64:128, :], psA[64:128, :]),
              inc="V", selfwait=True)
            tB = r("v", lambda e, acols=acols: e.tensor_tensor(
                B["ao"][64:128, acols], psB[64:128, :], B["bcs"][64:128, :], ALU.mult),
                inc="V")
            prev_tb = tB
            prev_av = {k: v for k, v in av_id.items() if k >= 60}
            pair_exp_base = exp_id[63]
            sc_last[(j, qb)] = sc_id[63]
        if j % 4 == 3:
            grp4 = j // 4
            for c4 in range(4):
                c = 4 * grp4 + c4
                w("g", "P", sc_last[(j, 1)])
                if H["_dbg"]:
                    w("g", "Y0", dbg0)
                wo_dma[c] = r("g", lambda e, c=c: e.dma_start(
                    out=B["wo"][:, 2048 * c:2048 * c + 2048], in_=H["wo"][c]),
                    inc="W", amt=16)

    attn_done_v = prev_tb
    attn_done_p = S_.cnt["P"]
    if H["_dbg"]:
        w("y", "V", attn_done_v)
        dbg1 = r("y", lambda e: e.dma_start(out=H["dao"], in_=B["ao"][:]), inc="Y0", amt=16)
    attn_last_exp = pair_exp_base

    w("g", "P", attn_done_p)
    bo_id = r("g", lambda e: e.dma_start(out=B["borow"][:], in_=H["bo"][0:1, :]), inc="W", amt=16)

    if H["_dbg"]:
        w("y", "W", wo_dma[15])
        dbg2 = r("y", lambda e: e.dma_start(out=H["dwo"], in_=B["wo"][:]), inc="Y0", amt=16)
    fbid = {}
    fbev = {}
    xboid = {}
    ln1 = {}
    castid = {}
    trxid = {}
    x2fev = {}
    x2store = {}

    def op_part1(tt):
        par = tt % 2
        w("y", "V", fbev.get(4 * (tt - 2) + 3, 0))
        if tt < 2:
            w("y", "P", attn_done_p)
        ssem = "S%d" % par
        xrid = r("y", lambda e, tt=tt, par=par: e.dma_start(
            out=B["xr"][par][:], in_=H["xo"][128 * tt:128 * tt + 128, :]),
            inc=ssem, amt=16)
        w("v", ssem, xrid)
        w("v", "A", castid.get(tt - 2, 0))
        w("v", "Y%d" % par, x2store.get(tt - 2, 0))
        for fb in range(4):
            n = 4 * tt + fb
            slot = n % 4
            w("t", "V", fbev.get(n - 4, 0))
            if n == 0:
                w("t", "V", attn_done_v)
                w("t", "A", attn_last_exp)
                w("t", "W", wo_dma[15])
            if n == 0:
                w("t", "W", bo_id)
            for c in range(16):
                r("t", lambda e, c=c, fb=fb, tt=tt, slot=slot: e.matmul(
                    bk32(slot), B["ao"][:, 1024 * c + 128 * tt:1024 * c + 128 * tt + 128],
                    B["wo"][:, 2048 * c + 512 * fb:2048 * c + 512 * fb + 512],
                    start=(c == 0), stop=False))
            idx = r("t", lambda e, fb=fb, slot=slot: e.matmul(
                bk32(slot), B["o64"][0:1, :], B["borow"][0:1, 512 * fb:512 * fb + 512],
                start=False, stop=True), inc="P")
            fbid[(tt, fb)] = idx
            w("v", "P", idx)
            fbev[n] = r("v", lambda e, fb=fb, par=par, slot=slot: e.tensor_tensor(
                B["x2t"][par][:, 512 * fb:512 * fb + 512], bk32(slot),
                B["xr"][par][:, 512 * fb:512 * fb + 512], ALU.add), inc="V",
                selfwait=(fb == 3))

    def op_part2(tt):
        par = tt % 2
        w("s", "V", fbev[4 * tt + 3])
        cpid = r("s", lambda e, par=par: e.activation(
            B["sqs"][:], B["x2t"][par][:], AF.Copy, accum_out=B["ln"][:, 0:1]), inc="A")
        w("v", "A", cpid)
        r("v", lambda e: e.tensor_scalar(B["ln"][:, 1:2], B["ln"][:, 0:1], 1.0 / D, None, ALU.mult),
          inc="V", selfwait=True)
        cen = r("v", lambda e, par=par: e.tensor_scalar(
            B["sqs"][:], B["x2t"][par][:], B["ln"][:, 1:2], None, ALU.subtract),
            inc="V", selfwait=True)
        w("s", "V", cen)
        sqid = r("s", lambda e, par=par: e.activation(
            B["x2t"][par][:], B["sqs"][:], AF.Square, accum_out=B["ln"][:, 2:3]), inc="A")
        w("v", "A", sqid)
        vv_ = r("v", lambda e: e.tensor_scalar(
            B["ln"][:, 3:4], B["ln"][:, 2:3], 1.0 / D, EPS, ALU.mult, ALU.add),
            inc="V", selfwait=True)
        w("s", "V", vv_)
        sqr = r("s", lambda e: e.activation(B["ln"][:, 4:5], B["ln"][:, 3:4], AF.Sqrt), inc="A")
        w("v", "A", sqr)
        r("v", lambda e: e.reciprocal(B["ln"][:, 5:6], B["ln"][:, 4:5]),
          inc="V", selfwait=True)
        ln1[tt] = r("v", lambda e, par=par: e.tensor_scalar(
            B["x2t"][par][:], B["sqs"][:], B["ln"][:, 5:6], None, ALU.mult), inc="V")
        w("s", "V", ln1[tt])
        w("s", "P", trxid.get(tt - 1, 0))
        castid[tt] = r("s", lambda e, par=par: e.copy(B["x2bf"][:], B["x2t"][par][:]), inc="A")
        w("t", "A", castid[tt])

        def xview(grp, tt=tt):
            vv2 = B["x2f"][:].rearrange("p (c q) -> p c q", c=16)
            return vv2[:, 4 * grp:4 * grp + 4, 128 * tt:128 * tt + 128]
        trxid[tt] = transpose_tile(B["x2bf"], xview)
        x2fev[tt] = ev[transpose_tile.seq - 1]
        w("y", "V", ln1[tt])
        x2store[tt] = r("y", lambda e, tt=tt, par=par: e.dma_start(
            out=H["x2s"][128 * tt:128 * tt + 128, :], in_=B["x2t"][par][:]),
            inc="Y%d" % par, amt=16)

    for tt in range(8):
        op_part1(tt)
        if tt >= 1:
            op_part2(tt - 1)
    op_part2(7)

    oproj_done_p = S_.cnt["P"]
    oproj_last_fbev = fbev[31]

    conv_id = {}
    gmm = {}
    umm = {}
    hTT = {}
    silu_id = {}
    for ib in range(64):
        par = ib % 2
        w("y", "V", conv_id.get(ib - 1, 0))
        if ib == 0:
            w("y", "P", oproj_done_p)
            if H["_dbg"]:
                w("y", "Y0", dbg1)
        r("y", lambda e, ib=ib: e.dma_start(out=B["wgst"][:], in_=H["wg"][ib]),
          inc="S0", amt=16)
        wld = r("y", lambda e, ib=ib: e.dma_start(out=B["wust"][:], in_=H["wu"][ib]),
                inc="S0", amt=16)
        w("v", "S0", wld)
        w("v", "P", umm.get(ib - 2, 0))
        r("v", lambda e, par=par: e.tensor_scalar(
            B["wgb"][par][:], B["wgst"][:], -128.0, None, ALU.add))
        conv_id[ib] = r("v", lambda e, par=par: e.tensor_scalar(
            B["wub"][par][:], B["wust"][:], -128.0, None, ALU.add), inc="V")
        for th in range(2):
            n = 2 * ib + th
            s3 = n % 3
            w("t", "V", conv_id[ib])
            w("t", "V", hTT.get(n - 3, 0))
            if n == 0:
                w("t", "A", x2fev[7])
                w("t", "V", oproj_last_fbev)
            for c in range(16):
                last = c == 15
                idx = r("t", lambda e, c=c, th=th, par=par, s3=s3: e.matmul(
                    ps[:, 1024 * s3:1024 * s3 + 512],
                    B["wgb"][par][:, 128 * c:128 * c + 128],
                    B["x2f"][:, 1024 * c + 512 * th:1024 * c + 512 * th + 512],
                    start=(c == 0), stop=(c == 15)), inc=("P" if last else None))
            gmm[n] = idx
            for c in range(16):
                last = c == 15
                idx = r("t", lambda e, c=c, th=th, par=par, s3=s3: e.matmul(
                    ps[:, 1024 * s3 + 512:1024 * s3 + 1024],
                    B["wub"][par][:, 128 * c:128 * c + 128],
                    B["x2f"][:, 1024 * c + 512 * th:1024 * c + 512 * th + 512],
                    start=(c == 0), stop=(c == 15)), inc=("P" if last else None))
            umm[n] = idx
            w("s", "P", gmm[n])
            w("s", "V", hTT.get(n - 1, 0))
            silu_id[n] = r("s", lambda e, ib=ib, s3=s3: e.activation(
                B["sg"][:], ps[:, 1024 * s3:1024 * s3 + 512], AF.Silu,
                bias=0.0, scale=B["gs"][:, ib:ib + 1]), inc="A")
            w("v", "P", umm[n])
            r("v", lambda e, ib=ib, s3=s3: e.tensor_scalar(
                B["usv"][:], ps[:, 1024 * s3 + 512:1024 * s3 + 1024],
                B["us"][:, ib:ib + 1], None, ALU.mult), inc="V", selfwait=True)
            w("v", "A", silu_id[n])
            if n == 0 and H["_dbg"]:
                w("v", "Y0", dbg2)
            if ib == 32 and th == 0:
                w("v", "Y1", x2store[7])
                w("v", "A", castid[7])
            hTT[n] = r("v", lambda e, ib=ib, th=th: e.tensor_tensor(
                B["hf"][:, 1024 * ib + 512 * th:1024 * ib + 512 * th + 512],
                B["sg"][:], B["usv"][:], ALU.mult), inc="V")
    hlast = hTT[127]
    ffa_last_silu = silu_id[127]

    dmm = {}
    fmid = {}
    trf = {}
    tev = {}
    tokstore = {}
    wdc = {}

    def ffb_evac(hc):
        par = hc % 2
        w("v", "P", dmm[4 * hc + 3])
        w("v", "P", trf.get(hc - 1, 0))
        fmid[hc] = r("v", lambda e, hc=hc, par=par: e.tensor_scalar(
            B["fm"][:], ps[:, 1024 * par:1024 * par + 1024],
            B["ds"][:, hc:hc + 1], None, ALU.mult), inc="V", selfwait=True)
        w("t", "V", fmid[hc])
        w("t", "A", tev.get(hc - 1, 0))
        for wv_ in range(8):
            off = 2048 + 128 * wv_
            last = wv_ == 7
            idx = r("t", lambda e, wv_=wv_, off=off: e.transpose(
                ps[:, off:off + 128], B["fm"][:, 128 * wv_:128 * wv_ + 128],
                B["idf"][:]), inc=("P" if last else None))
        trf[hc] = idx
        w("s", "P", trf[hc])
        w("s", "Y0", tokstore.get(hc - 1, 0))
        r("s", lambda e: e.copy(B["tok"][0][:], bk32(4)))
        tev[hc] = r("s", lambda e: e.copy(B["tok"][1][:], bk32(5)), inc="A")
        w("y", "A", tev[hc])
        for gi2 in range(2):
            tokstore[hc] = r("y", lambda e, gi2=gi2, hc=hc: e.dma_start(
                out=H["ffs"].rearrange("(w p) h -> p w h", p=128)[
                    :, 4 * gi2:4 * gi2 + 4, 128 * hc:128 * hc + 128],
                in_=B["tok"][gi2][:].rearrange("p (w h) -> p w h", w=4)),
                inc="Y0", amt=16)

    for hc in range(16):
        par = hc % 2
        for ibg in range(4):
            gi = 4 * hc + ibg
            sp_ = gi % 2
            w("y", "V", wdc.get(gi - 2, 0))
            if gi == 0:
                w("y", "V", hlast)
            ssem = "S%d" % sp_
            ldid = r("y", lambda e, hc=hc, ibg=ibg, sp_=sp_: e.dma_start(
                out=B["wdst"][sp_][:].rearrange("p (g h) -> p g h", g=16),
                in_=H["wd"][hc, 16 * ibg:16 * ibg + 16].rearrange("g p h -> p g h")),
                inc=ssem, amt=16)
            w("v", ssem, ldid)
            w("v", "P", dmm.get(gi - 2, 0))
            wdc[gi] = r("v", lambda e, sp_=sp_: e.tensor_scalar(
                B["wdb"][sp_][:], B["wdst"][sp_][:], -128.0, None, ALU.add), inc="V")
            w("t", "V", wdc[gi])
            if ibg == 0:
                w("t", "V", fmid.get(hc - 2, 0))
                if hc == 0:
                    w("t", "V", hlast)
                    w("t", "A", ffa_last_silu)
            for i16 in range(16):
                ib = 16 * ibg + i16
                for th in range(2):
                    last = (i16 == 15 and th == 1)
                    idx = r("t", lambda e, i16=i16, th=th, ib=ib, par=par, sp_=sp_: e.matmul(
                        ps[:, 1024 * par + 512 * th:1024 * par + 512 * th + 512],
                        B["wdb"][sp_][:, 128 * i16:128 * i16 + 128],
                        B["hf"][:, 1024 * ib + 512 * th:1024 * ib + 512 * th + 512],
                        start=(ib == 0), stop=(ib == 63)), inc=("P" if last else None))
            dmm[gi] = idx
            if ibg == 0 and hc >= 1:
                ffb_evac(hc - 1)
    ffb_evac(15)

    ffb_done_p = dmm[63]

    ln2 = {}
    ystore = {}
    for tt in range(8):
        par = tt % 2
        w("y", "Y0", tokstore[15])
        w("y", "V", ln2.get(tt - 2, 0))
        ssem = "S%d" % par
        r("y", lambda e, tt=tt, par=par: e.dma_start(
            out=B["ffin"][par][:], in_=H["ffs"][128 * tt:128 * tt + 128, :]),
          inc=ssem, amt=16)
        ffld = r("y", lambda e, tt=tt, par=par: e.dma_start(
            out=B["x2in"][par][:], in_=H["x2s"][128 * tt:128 * tt + 128, :]),
            inc=ssem, amt=16)
        w("v", ssem, ffld)
        if tt == 0:
            w("v", "P", ffb_done_p)
            w("v", "Y0", tokstore[15])
        w("v", "Y%d" % par, ystore.get(tt - 2, 0))
        tta = r("v", lambda e, par=par: e.tensor_tensor(
            B["outt"][par][:], B["ffin"][par][:], B["x2in"][par][:], ALU.add),
          inc="V", selfwait=True)
        w("s", "V", tta)
        cpid = r("s", lambda e, par=par: e.activation(
            B["sq2"][:], B["outt"][par][:], AF.Copy, accum_out=B["ln"][:, 8:9]), inc="A")
        w("v", "A", cpid)
        r("v", lambda e: e.tensor_scalar(B["ln"][:, 9:10], B["ln"][:, 8:9], 1.0 / D, None, ALU.mult),
          inc="V", selfwait=True)
        cen = r("v", lambda e, par=par: e.tensor_scalar(
            B["sq2"][:], B["outt"][par][:], B["ln"][:, 9:10], None, ALU.subtract),
            inc="V", selfwait=True)
        w("s", "V", cen)
        sqid = r("s", lambda e, par=par: e.activation(
            B["outt"][par][:], B["sq2"][:], AF.Square, accum_out=B["ln"][:, 10:11]), inc="A")
        w("v", "A", sqid)
        vv_ = r("v", lambda e: e.tensor_scalar(
            B["ln"][:, 11:12], B["ln"][:, 10:11], 1.0 / D, EPS, ALU.mult, ALU.add),
            inc="V", selfwait=True)
        w("s", "V", vv_)
        sqr = r("s", lambda e: e.activation(B["ln"][:, 12:13], B["ln"][:, 11:12], AF.Sqrt), inc="A")
        w("v", "A", sqr)
        r("v", lambda e: e.reciprocal(B["ln"][:, 13:14], B["ln"][:, 12:13]),
          inc="V", selfwait=True)
        ln2[tt] = r("v", lambda e, par=par: e.tensor_scalar(
            B["outt"][par][:], B["sq2"][:], B["ln"][:, 13:14], None, ALU.mult), inc="V")
        w("y", "V", ln2[tt])
        ystore[tt] = r("y", lambda e, tt=tt, par=par: e.dma_start(
            out=H["y"][128 * tt:128 * tt + 128, :], in_=B["outt"][par][:]),
            inc="Y%d" % par, amt=16)

    w("y", "Y0", S_.cnt["Y0"])
    w("y", "Y1", S_.cnt["Y1"])


def _pack(inputs):
    f32 = np.float32
    Wq = np.asarray(inputs["L0_attn_Wq"], f32)
    bq = np.asarray(inputs["L0_attn_bq"], f32)
    Wk = np.asarray(inputs["L0_attn_Wk"], f32)
    bk = np.asarray(inputs["L0_attn_bk"], f32)
    Wv = np.asarray(inputs["L0_attn_Wv"], f32)
    bv = np.asarray(inputs["L0_attn_bv"], f32)
    Wo = np.asarray(inputs["L0_attn_Wo"], f32)
    bo = np.asarray(inputs["L0_attn_bo"], f32)
    gw = np.ascontiguousarray(np.asarray(inputs["L0_ffn_gate_w"], np.int32))
    uw = np.ascontiguousarray(np.asarray(inputs["L0_ffn_up_w"], np.int32))
    dw = np.ascontiguousarray(np.asarray(inputs["L0_ffn_down_w"], np.int32))
    gsc = np.asarray(inputs["L0_ffn_gate_scale"], f32)
    usc = np.asarray(inputs["L0_ffn_up_scale"], f32)
    dsc = np.asarray(inputs["L0_ffn_down_scale"], f32)

    shared = {}
    wq_t = np.empty((16, 128, 2048), f32)
    bq_t = np.empty((128, 16), f32)
    for j in range(16):
        cols = np.r_[64 * j:64 * j + 64, 64 * (j + 16):64 * (j + 16) + 64]
        tmp = Wq[:, cols].reshape(16, 128, 128)
        wq_t[j] = tmp.transpose(1, 0, 2).reshape(128, 2048)
        bq_t[:, j] = bq[cols]
    shared["wq_t"] = np.ascontiguousarray(wq_t)
    shared["bq_t"] = np.ascontiguousarray(bq_t)
    col_order = np.empty(512, np.int64)
    for cc in range(4):
        for dd in range(128):
            kv = cc + (4 if dd >= 64 else 0)
            col_order[128 * cc + dd] = 64 * kv + dd % 64
    wk_re = Wk[:, col_order]
    shared["wk_t"] = np.ascontiguousarray(
        wk_re.reshape(16, 128, 512).transpose(1, 0, 2).reshape(128, 8192))
    shared["bk_t"] = np.ascontiguousarray(bk[col_order].reshape(4, 128).T)
    shared["wv_t"] = np.ascontiguousarray(
        Wv.reshape(16, 128, 512).transpose(1, 0, 2).reshape(128, 8192))
    shared["bv_bc"] = np.ascontiguousarray(np.broadcast_to(bv, (128, 512)))
    wo_t = np.empty((16, 128, 2048), f32)
    for c in range(16):
        rows = np.r_[64 * c:64 * c + 64, 64 * (c + 16):64 * (c + 16) + 64]
        wo_t[c] = Wo[rows, :]
    shared["wo_t"] = np.ascontiguousarray(wo_t)
    shared["bo_bc"] = np.ascontiguousarray(np.broadcast_to(bo, (128, 2048)))
    shared["wg_t"] = np.ascontiguousarray(
        gw.reshape(64, 128, 16, 128).transpose(0, 3, 2, 1).reshape(64, 128, 2048))
    shared["wu_t"] = np.ascontiguousarray(
        uw.reshape(64, 128, 16, 128).transpose(0, 3, 2, 1).reshape(64, 128, 2048))
    shared["wd_t"] = np.ascontiguousarray(
        dw.reshape(16, 128, 64, 128).transpose(0, 2, 3, 1))
    shared["gs_t"] = np.ascontiguousarray(gsc.reshape(64, 128).T)
    shared["us_t"] = np.ascontiguousarray(usc.reshape(64, 128).T)
    shared["ds_t"] = np.ascontiguousarray(dsc.reshape(16, 128).T)
    shared["identb"] = np.eye(128, dtype=np.float32).astype(ml_dtypes.bfloat16)
    shared["identf"] = np.eye(128, dtype=np.float32)
    shared["ones64b"] = np.ones((128, 128), np.float32).astype(ml_dtypes.bfloat16)
    shared["onesf"] = np.ones((128, 64), np.float32)
    return shared


def kernel(**inputs):
    x = np.ascontiguousarray(np.asarray(inputs["L0_attn_x"], np.float32).reshape(SEQ, D))
    shared = _pack(inputs)
    nc = build_nc()
    in_maps = []
    for c in range(NCORES):
        m = dict(shared)
        m["x_full"] = x
        m["x_own"] = np.ascontiguousarray(x[T * c:T * (c + 1)])
        in_maps.append(m)
    res = run_bass_kernel_spmd(nc, in_maps, core_ids=list(range(NCORES)))
    y = np.concatenate([np.asarray(res.results[c]["y"]) for c in range(NCORES)], axis=0)
    return y.reshape(1, SEQ, D).astype(np.float32)


if __name__ == "__main__":
    import reference as _ref
    inp = _ref.setup_inputs()
    out = kernel(**{k: np.asarray(v) for k, v in inp.items()})
    want = np.asarray(_ref.reference(**inp))
    err = np.abs(out - want)
    denom = np.mean(want.astype(np.float64) ** 2)
    print("rvar:", float(np.mean((out.astype(np.float64) - want.astype(np.float64)) ** 2) / denom))
    print("maxabs:", float(err.max()))


# version 08
# speedup vs baseline: 1.2019x; 1.0031x over previous
import numpy as np
import ml_dtypes
import concourse.bass as bass
import concourse.mybir as mybir
from concourse.bass_utils import run_bass_kernel_spmd

F32 = mybir.dt.float32
BF16 = mybir.dt.bfloat16
I32 = mybir.dt.int32
AF = mybir.ActivationFunctionType
ALU = mybir.AluOpType
AX = mybir.AxisListType

NCORES = 8
SEQ = 8192
D = 2048
T = SEQ // NCORES
NKT = SEQ // 128
EPS = 1e-5


def build_nc():
    nc = bass.Bass("TRN2", target_bir_lowering=False)

    H = {}
    H["xf"] = nc.dram_tensor("x_full", [SEQ, D], F32, kind="ExternalInput").ap()
    H["xo"] = nc.dram_tensor("x_own", [T, D], F32, kind="ExternalInput").ap()
    H["wq"] = nc.dram_tensor("wq_t", [16, 128, 2048], F32, kind="ExternalInput").ap()
    H["bq"] = nc.dram_tensor("bq_t", [128, 16], F32, kind="ExternalInput").ap()
    H["wk"] = nc.dram_tensor("wk_t", [128, 8192], F32, kind="ExternalInput").ap()
    H["bk"] = nc.dram_tensor("bk_t", [128, 4], F32, kind="ExternalInput").ap()
    H["wv"] = nc.dram_tensor("wv_t", [128, 8192], F32, kind="ExternalInput").ap()
    H["bv"] = nc.dram_tensor("bv_bc", [128, 512], F32, kind="ExternalInput").ap()
    H["wo"] = nc.dram_tensor("wo_t", [16, 128, 2048], F32, kind="ExternalInput").ap()
    H["bo"] = nc.dram_tensor("bo_bc", [128, 2048], F32, kind="ExternalInput").ap()
    H["wg"] = nc.dram_tensor("wg_t", [64, 128, 2048], I32, kind="ExternalInput").ap()
    H["wu"] = nc.dram_tensor("wu_t", [64, 128, 2048], I32, kind="ExternalInput").ap()
    H["wd"] = nc.dram_tensor("wd_t", [16, 64, 128, 128], I32, kind="ExternalInput").ap()
    H["gs"] = nc.dram_tensor("gs_t", [128, 64], F32, kind="ExternalInput").ap()
    H["us"] = nc.dram_tensor("us_t", [128, 64], F32, kind="ExternalInput").ap()
    H["ds"] = nc.dram_tensor("ds_t", [128, 16], F32, kind="ExternalInput").ap()
    H["idb"] = nc.dram_tensor("identb", [128, 128], BF16, kind="ExternalInput").ap()
    H["idf"] = nc.dram_tensor("identf", [128, 128], F32, kind="ExternalInput").ap()
    H["o64"] = nc.dram_tensor("ones64b", [128, 128], BF16, kind="ExternalInput").ap()
    H["of"] = nc.dram_tensor("onesf", [128, 64], F32, kind="ExternalInput").ap()
    H["y"] = nc.dram_tensor("y", [T, D], F32, kind="ExternalOutput").ap()
    import os
    H["_dbg"] = bool(os.environ.get("KDEBUG"))
    if H["_dbg"]:
        H["dkt"] = nc.dram_tensor("dbg_kt", [128, 32768], BF16, kind="ExternalOutput").ap()
        H["dv"] = nc.dram_tensor("dbg_v", [128, 32768], BF16, kind="ExternalOutput").ap()
        H["dqt"] = nc.dram_tensor("dbg_qt", [128, 16384], BF16, kind="ExternalOutput").ap()
        H["dao"] = nc.dram_tensor("dbg_ao", [128, 16384], BF16, kind="ExternalOutput").ap()
        H["dwo"] = nc.dram_tensor("dbg_wo", [128, 32768], BF16, kind="ExternalOutput").ap()
        H["ds_"] = nc.dram_tensor("dbg_s", [T, D], F32, kind="ExternalOutput").ap()
    import os as _os
    _scrkind = "ExternalOutput" if _os.environ.get("KDEBUG") else "Internal"
    H["x2s"] = nc.dram_tensor("x2_scr", [T, D], F32, kind=_scrkind).ap()
    H["ffs"] = nc.dram_tensor("ffn_scr", [T, D], F32, kind=_scrkind).ap()

    SB_BASE = 16512

    def at(name, shape, dtype, off):
        return nc.alloc_sbuf_tensor_at(name, shape, dtype, offset=SB_BASE + off)

    B = {}
    B["kt"] = at("kt_b", [128, 32768], BF16, 0)
    B["v"] = at("v_b", [128, 32768], BF16, 65536)
    B["wo"] = at("wo_b", [128, 32768], BF16, 0)
    B["hf"] = at("hfull", [128, 65536], BF16, 0)
    B["bo"] = at("bo_s", [128, 2048], F32, 65536)
    B["xr"] = [at(f"xr{i}", [128, 2048], F32, 73728 + 8192 * i) for i in range(2)]
    B["x2t"] = [at(f"x2t{i}", [128, 2048], F32, 90112 + 8192 * i) for i in range(2)]
    B["x2bf"] = at("x2bf", [128, 2048], BF16, 106496)
    B["sqs"] = at("sqs", [128, 2048], F32, 110592)
    B["xbo"] = at("xbo", [128, 2048], F32, 118784)
    B["qt"] = at("qt_b", [128, 16384], BF16, 131072)
    B["x2f"] = at("x2f", [128, 16384], BF16, 131072)
    B["xfblk"] = at("xfblk", [128, 16384], BF16, 131072)
    B["wkb"] = at("wkb", [128, 8192], BF16, 163840)
    B["ffin"] = [at(f"ffin{i}", [128, 2048], F32, 131072 + 8192 * i) for i in range(2)]
    B["x2in"] = [at(f"x2in{i}", [128, 2048], F32, 147456 + 8192 * i) for i in range(2)]
    B["ao"] = at("ao_b", [128, 16384], BF16, 163840)
    B["xfo"] = at("xfown", [128, 16384], BF16, 163840)
    B["wvb"] = at("wvb", [128, 8192], BF16, 180224)
    B["bv"] = at("bv_s", [128, 512], F32, 206752)
    B["wgst"] = at("wgstage", [128, 2048], I32, 163840)
    B["wust"] = at("wustage", [128, 2048], I32, 172032)
    B["wgb"] = [at(f"wgb{i}", [128, 2048], BF16, 180224 + 4096 * i) for i in range(2)]
    B["wub"] = [at(f"wub{i}", [128, 2048], BF16, 188416 + 4096 * i) for i in range(2)]
    B["wdst"] = [at(f"wdst{i}", [128, 2048], I32, 163840 + 8192 * i) for i in range(2)]
    B["wdb"] = [at(f"wdb{i}", [128, 2048], BF16, 180224 + 4096 * i) for i in range(2)]
    B["fm"] = at("fm_s", [128, 1024], F32, 188416)
    B["tok"] = [at(f"tokst{i}", [128, 512], F32, 192512 + 2048 * i) for i in range(2)]
    B["outt"] = [at(f"outt{i}", [128, 2048], F32, 163840 + 8192 * i) for i in range(2)]
    B["sq2"] = at("sq2", [128, 2048], F32, 180224)
    B["idb"] = at("idb_s", [128, 128], BF16, 196608)
    B["idf"] = at("idf_s", [128, 128], F32, 196864)
    B["o64"] = at("o64_s", [128, 128], BF16, 197376)
    B["of"] = at("of_s", [128, 64], F32, 197632)
    B["bq"] = at("bq_s", [128, 16], F32, 197888)
    B["bk"] = at("bk_s", [128, 4], F32, 197952)
    B["gs"] = at("gs_s", [128, 64], F32, 197984)
    B["us"] = at("us_s", [128, 64], F32, 198240)
    B["ds"] = at("ds_s", [128, 16], F32, 198496)
    B["ln"] = at("lnst", [128, 16], F32, 210848)
    B["xst"] = [at(f"xstag{i}", [128, 2048], BF16, 198560 + 4096 * i) for i in range(2)]
    B["pt"] = [at(f"pt{i}", [128, 1024], BF16, 198560 + 2048 * i) for i in range(4)]
    B["rd"] = at("rd_t", [128, 512], F32, 206752)
    B["bcs"] = at("bcs", [128, 512], F32, 208800)
    B["sg"] = at("sg_s", [128, 512], F32, 206752)
    B["rdb"] = at("rdb", [128, 512], BF16, 210912)
    B["usv"] = at("usv_s", [128, 512], F32, 208800)
    B["borow"] = at("borow", [1, 2048], BF16, 118784)

    B["nc"] = nc
    B["ps"] = nc.alloc_psum_tensor("ps32", [128, 3072], F32)
    B["psbf"] = nc.alloc_psum_tensor("psbf", [128, 2048], BF16)

    from contextlib import ExitStack
    with ExitStack() as es:
        ec = es.enter_context
        sems = {k: ec(nc.semaphore(f"sem{k}")) for k in
                ("P", "A", "V", "G0", "G1", "W", "S0", "S1", "Y0", "Y1")}
        block = ec(nc.Block())

        class Sched:
            def __init__(self, only, eng):
                self.only = only
                self.eng = eng
                self.cnt = {k: 0 for k in
                            ("P", "A", "V", "G0", "G1", "W", "S0", "S1", "Y0", "Y1")}

            def wait(self, engname, sem, val):
                if val is None or val <= 0:
                    return
                if self.only == engname:
                    self.eng.wait_ge(sems[sem], val)

            def run(self, engname, fn, inc=None, amt=1, selfwait=False):
                if inc is not None:
                    self.cnt[inc] += amt
                if self.only == engname:
                    inst = fn(self.eng)
                    if inc is not None:
                        inst.then_inc(sems[inc], amt)
                        if selfwait:
                            self.eng.wait_ge(sems[inc], self.cnt[inc])
                return self.cnt[inc] if inc is not None else None

        @block.gpsimd
        def _(g):
            emit_program(Sched("g", g), H, B)

        @block.tensor
        def _(t):
            emit_program(Sched("t", t), H, B)

        @block.scalar
        def _(s):
            emit_program(Sched("s", s), H, B)

        @block.vector
        def _(v):
            emit_program(Sched("v", v), H, B)

        @block.sync
        def _(sy):
            emit_program(Sched("y", sy), H, B)

    return nc

def emit_program(S_, H, B):
    w = S_.wait
    r = S_.run
    ps = B["ps"]
    psbf = B["psbf"]

    def bk32(i, n=512):
        return ps[:, 512 * i:512 * i + n]

    for k in ("idb", "idf", "o64", "of", "bq", "bk", "gs", "us", "ds", "bv"):
        r("g", lambda e, k=k: e.dma_start(out=B[k][:], in_=H[k]), inc="W", amt=16)
    r("g", lambda e: e.dma_start(out=B["wkb"][:], in_=H["wk"]), inc="W", amt=16)
    wconsts = r("g", lambda e: e.dma_start(out=B["wvb"][:], in_=H["wv"]), inc="W", amt=16)

    ev = {}

    def transpose_tile(src_buf, dst3):
        g0 = transpose_tile.seq
        for grp in range(4):
            half = grp % 2
            w("t", "A", ev.get(g0 + grp - 2, 0))
            for cc in range(4):
                c = 4 * grp + cc
                last = cc == 3
                tid = r("t", lambda e, c=c, half=half, cc=cc, src_buf=src_buf: e.transpose(
                    psbf[:, 1024 * half + 128 * cc:1024 * half + 128 * cc + 128],
                    src_buf[:, 128 * c:128 * c + 128], B["idb"][:]),
                    inc=("P" if last else None))
            w("s", "P", tid)
            ev[g0 + grp] = r("s", lambda e, half=half, grp=grp, dst3=dst3: e.copy(
                dst3(grp),
                psbf[:, 1024 * half:1024 * half + 512].rearrange("p (g q) -> p g q", g=4)),
                inc="A")
        transpose_tile.seq += 4
        return tid
    transpose_tile.seq = 0

    kev = {}
    vev = {}
    last_tr = {}
    NBLK = SEQ // 512
    for blk in range(NBLK):
        slot_x = 8192 * (blk % 2)
        for s4 in range(4):
            ti = 4 * blk + s4
            buf = ti % 2
            w("g", "P", last_tr.get(ti - 2, 0))
            gsem = "G%d" % buf
            r("g", lambda e, ti=ti, buf=buf: e.dma_start(
                out=B["xst"][buf][:], in_=H["xf"][128 * ti:128 * ti + 128, :]),
              inc=gsem, amt=16)
            gneed = S_.cnt[gsem]
            if ti < 2:
                w("t", "W", wconsts)
                w("v", "W", wconsts)
            w("t", gsem, gneed)

            def dview(grp, slot_x=slot_x, s4=s4):
                vv = B["xfblk"][:].rearrange("p (c q) -> p c q", q=512)
                c0 = slot_x // 512 + 4 * grp
                return vv[:, c0:c0 + 4, 128 * s4:128 * s4 + 128]
            last_tr[ti] = transpose_tile(B["xst"][buf], dview)
        w("t", "A", ev[transpose_tile.seq - 1])
        for cc in range(4):
            kseq = 4 * blk + cc
            pb = cc % 2
            w("t", "V", kev.get(kseq - 2, 0))
            for c in range(16):
                last = c == 15
                kmm = r("t", lambda e, cc=cc, c=c, pb=pb, slot_x=slot_x: e.matmul(
                    bk32(pb),
                    B["wkb"][:, 512 * c + 128 * cc:512 * c + 128 * cc + 128],
                    B["xfblk"][:, slot_x + 512 * c:slot_x + 512 * c + 512],
                    start=(c == 0), stop=(c == 15)), inc=("P" if last else None))
            w("v", "P", kmm)
            kev[kseq] = r("v", lambda e, cc=cc, blk=blk, pb=pb: e.tensor_scalar(
                B["kt"][:, 8192 * cc + 512 * blk:8192 * cc + 512 * blk + 512],
                bk32(pb), B["bk"][:, cc:cc + 1], None, ALU.add),
                inc="V")
        for s4 in range(4):
            ti = 4 * blk + s4
            vb = 2 + ti % 2
            w("t", "V", vev.get(ti - 2, 0))
            for c in range(16):
                last = c == 15
                vmm = r("t", lambda e, c=c, vb=vb, slot_x=slot_x, s4=s4: e.matmul(
                    bk32(vb), B["xfblk"][:, slot_x + 512 * c + 128 * s4:slot_x + 512 * c + 128 * s4 + 128],
                    B["wvb"][:, 512 * c:512 * c + 512],
                    start=(c == 0), stop=(c == 15)), inc=("P" if last else None))
            w("v", "P", vmm)
            vev[ti] = r("v", lambda e, ti=ti, vb=vb: e.tensor_tensor(
                B["v"][:, 512 * ti:512 * ti + 512], bk32(vb), B["bv"][:], ALU.add),
                inc="V")

    w("s", "V", vev[NKT - 1])
    for s8 in range(8):
        buf = s8 % 2
        w("g", "P", last_tr.get(NKT + s8 - 2, 0))
        gsem = "G%d" % buf
        r("g", lambda e, s8=s8, buf=buf: e.dma_start(
            out=B["xst"][buf][:], in_=H["xo"][128 * s8:128 * s8 + 128, :]),
          inc=gsem, amt=16)
        gneed = S_.cnt[gsem]
        w("t", gsem, gneed)

        def oview(grp, s8=s8):
            vv = B["xfo"][:].rearrange("p (c q) -> p c q", c=16)
            return vv[:, 4 * grp:4 * grp + 4, 128 * s8:128 * s8 + 128]
        last_tr[NKT + s8] = transpose_tile(B["xst"][buf], oview)

    qmm = {}
    qev = {}
    for j in range(16):
        buf = j % 2
        w("g", "P", qmm.get(j - 2, last_tr[NKT + 7]))
        gsem = "G%d" % buf
        r("g", lambda e, j=j, buf=buf: e.dma_start(
            out=B["xst"][buf][:], in_=H["wq"][j]), inc=gsem, amt=16)
        gneed = S_.cnt[gsem]
        w("t", gsem, gneed)
        qb_ = j % 2
        w("t", "V", qev.get(j - 2, max(kev[4 * (SEQ // 512 - 1) + 3], vev[NKT - 1])))
        for c in range(16):
            for hh in range(2):
                last = (c == 15 and hh == 1)
                idx = r("t", lambda e, c=c, hh=hh, qb_=qb_, buf=buf: e.matmul(
                    ps[:, 1024 * qb_ + 512 * hh:1024 * qb_ + 512 * hh + 512],
                    B["xst"][buf][:, 128 * c:128 * c + 128],
                    B["xfo"][:, 1024 * c + 512 * hh:1024 * c + 512 * hh + 512],
                    start=(c == 0), stop=(c == 15)), inc=("P" if last else None))
                if last:
                    qmm[j] = idx
        w("v", "P", qmm[j])
        qev[j] = r("v", lambda e, j=j, qb_=qb_: e.tensor_scalar(
            B["qt"][:, 1024 * j:1024 * j + 1024], ps[:, 1024 * qb_:1024 * qb_ + 1024],
            B["bq"][:, j:j + 1], None, ALU.add), inc="V")

    if H["_dbg"]:
        w("y", "V", max(qev[15], vev[NKT - 1], kev[4 * (SEQ // 512 - 1) + 3]))
        r("y", lambda e: e.dma_start(out=H["dkt"], in_=B["kt"][:]), inc="Y0", amt=16)
        r("y", lambda e: e.dma_start(out=H["dv"], in_=B["v"][:]), inc="Y0", amt=16)
        dbg0 = r("y", lambda e: e.dma_start(out=H["dqt"], in_=B["qt"][:]), inc="Y0", amt=16)
    psA = ps[:, 2048:2560]
    psB = ps[:, 2560:3072]
    pair_exp_base = qev[15] * 0 + S_.cnt["A"]
    prev_tb = vev[NKT - 1]
    prev_av = {}
    sc_last = {}
    wo_dma = {}
    qt_ready = qev[15]
    attn_first = True
    for j in range(16):
        for qb in range(2):
            cc = j // 4
            kvA = j // 4
            kvB = 4 + j // 4
            exp_id = {}
            av_id = {}
            sc_id = {}

            def emit_av(kt, kvA=kvA, kvB=kvB, exp_id=exp_id, av_id=av_id,
                        first_gate=None):
                pti = kt % 4
                w("t", "A", exp_id[kt])
                if kt == 0:
                    w("t", "V", first_gate)
                st = kt == 0
                sp = kt == 63
                r("t", lambda e, kt=kt, pti=pti, kvA=kvA, st=st, sp=sp: e.matmul(
                    psA[0:64, :], B["v"][:, 512 * kt + 64 * kvA:512 * kt + 64 * kvA + 64],
                    B["pt"][pti][:, 0:512], start=st, stop=sp, tile_position=(0, 0),
                    skip_group_check=True))
                r("t", lambda e, pti=pti, st=st, sp=sp: e.matmul(
                    psA[64:128, :], B["o64"][:, 0:64], B["pt"][pti][:, 512:1024],
                    start=st, stop=sp, tile_position=(0, 64), skip_group_check=True))
                r("t", lambda e, kt=kt, pti=pti, kvB=kvB, st=st, sp=sp: e.matmul(
                    psB[64:128, :], B["v"][:, 512 * kt + 64 * kvB:512 * kt + 64 * kvB + 64],
                    B["pt"][pti][:, 512:1024], start=st, stop=sp, tile_position=(0, 64),
                    skip_group_check=True))
                av_id[kt] = r("t", lambda e, pti=pti, st=st, sp=sp: e.matmul(
                    psB[0:64, :], B["o64"][:, 0:64], B["pt"][pti][:, 0:512],
                    start=st, stop=sp, tile_position=(0, 0), skip_group_check=True), inc="P")

            gate0 = prev_tb
            for kt in range(64):
                scb = kt % 2
                w("t", "A", exp_id.get(kt - 2, pair_exp_base))
                if attn_first and kt == 0:
                    w("t", "V", qt_ready)
                r("t", lambda e, kt=kt, scb=scb, cc=cc, j=j, qb=qb: e.matmul(
                    bk32(2 * scb),
                    B["kt"][0:64, 8192 * cc + 128 * kt:8192 * cc + 128 * kt + 128],
                    B["qt"][0:64, 1024 * j + 512 * qb:1024 * j + 512 * qb + 512],
                    start=True, stop=True, tile_position=(0, 0)))
                sc_id[kt] = r("t", lambda e, kt=kt, scb=scb, cc=cc, j=j, qb=qb: e.matmul(
                    bk32(2 * scb + 1),
                    B["kt"][64:128, 8192 * cc + 128 * kt:8192 * cc + 128 * kt + 128],
                    B["qt"][64:128, 1024 * j + 512 * qb:1024 * j + 512 * qb + 512],
                    start=True, stop=True, tile_position=(64, 0)), inc="P")
                if kt >= 1:
                    emit_av(kt - 1, first_gate=gate0)
                ptw = av_id.get(kt - 4, prev_av.get(60 + kt, 0))
                w("s", "P", max(sc_id[kt], ptw))
                exp_id[kt] = r("s", lambda e, kt=kt, scb=scb: e.activation(
                    B["pt"][kt % 4][:], ps[:, 1024 * scb:1024 * scb + 1024],
                    AF.Exp, bias=0.0, scale=0.125), inc="A")
            emit_av(63, first_gate=gate0)
            attn_first = False

            acols = slice(1024 * j + 512 * qb, 1024 * j + 512 * qb + 512)
            w("v", "P", av_id[63])
            def _recip(e, dst, src_):
                with B["nc"].allow_low_precision(reason="softmax 1/D broadcast via bf16 matmul"):
                    return e.reciprocal(dst, src_)
            r("v", lambda e: _recip(e, B["rdb"][0:1, :], psB[0:1, :]),
              inc="V", selfwait=True)
            rid = r("v", lambda e: _recip(e, B["rdb"][64:65, :], psA[64:65, :]), inc="V")
            w("t", "V", rid)
            bA = r("t", lambda e: e.matmul(
                psB[0:64, :], B["o64"][0:1, 0:64], B["rdb"][0:1, :],
                start=True, stop=True), inc="P")
            w("v", "P", bA)
            r("v", lambda e: e.tensor_copy(B["bcs"][0:64, :], psB[0:64, :]),
              inc="V", selfwait=True)
            tA = r("v", lambda e, acols=acols: e.tensor_tensor(
                B["ao"][0:64, acols], psA[0:64, :], B["bcs"][0:64, :], ALU.mult),
                inc="V")
            w("t", "V", tA)
            bB = r("t", lambda e: e.matmul(
                psA[64:128, :], B["o64"][64:65, 0:64], B["rdb"][64:65, :],
                start=True, stop=True), inc="P")
            w("v", "P", bB)
            r("v", lambda e: e.tensor_copy(B["bcs"][64:128, :], psA[64:128, :]),
              inc="V", selfwait=True)
            tB = r("v", lambda e, acols=acols: e.tensor_tensor(
                B["ao"][64:128, acols], psB[64:128, :], B["bcs"][64:128, :], ALU.mult),
                inc="V")
            prev_tb = tB
            prev_av = {k: v for k, v in av_id.items() if k >= 60}
            pair_exp_base = exp_id[63]
            sc_last[(j, qb)] = sc_id[63]
        if j % 4 == 3:
            grp4 = j // 4
            for c4 in range(4):
                c = 4 * grp4 + c4
                w("g", "P", sc_last[(j, 1)])
                if H["_dbg"]:
                    w("g", "Y0", dbg0)
                wo_dma[c] = r("g", lambda e, c=c: e.dma_start(
                    out=B["wo"][:, 2048 * c:2048 * c + 2048], in_=H["wo"][c]),
                    inc="W", amt=16)

    attn_done_v = prev_tb
    attn_done_p = S_.cnt["P"]
    if H["_dbg"]:
        w("y", "V", attn_done_v)
        dbg1 = r("y", lambda e: e.dma_start(out=H["dao"], in_=B["ao"][:]), inc="Y0", amt=16)
    attn_last_exp = pair_exp_base

    w("g", "P", attn_done_p)
    bo_id = r("g", lambda e: e.dma_start(out=B["borow"][:], in_=H["bo"][0:1, :]), inc="W", amt=16)

    if H["_dbg"]:
        w("y", "W", wo_dma[15])
        dbg2 = r("y", lambda e: e.dma_start(out=H["dwo"], in_=B["wo"][:]), inc="Y0", amt=16)
    fbid = {}
    fbev = {}
    xboid = {}
    ln1 = {}
    castid = {}
    trxid = {}
    x2fev = {}
    x2store = {}

    def op_part1(tt):
        par = tt % 2
        w("y", "V", fbev.get(4 * (tt - 2) + 3, 0))
        if tt < 2:
            w("y", "P", attn_done_p)
        ssem = "S%d" % par
        xrid = r("y", lambda e, tt=tt, par=par: e.dma_start(
            out=B["xr"][par][:], in_=H["xo"][128 * tt:128 * tt + 128, :]),
            inc=ssem, amt=16)
        w("v", ssem, xrid)
        w("v", "A", castid.get(tt - 2, 0))
        w("v", "Y%d" % par, x2store.get(tt - 2, 0))
        for fb in range(4):
            n = 4 * tt + fb
            slot = n % 4
            w("t", "V", fbev.get(n - 4, 0))
            if n == 0:
                w("t", "V", attn_done_v)
                w("t", "A", attn_last_exp)
                w("t", "W", wo_dma[15])
            if n == 0:
                w("t", "W", bo_id)
            for c in range(16):
                r("t", lambda e, c=c, fb=fb, tt=tt, slot=slot: e.matmul(
                    bk32(slot), B["ao"][:, 1024 * c + 128 * tt:1024 * c + 128 * tt + 128],
                    B["wo"][:, 2048 * c + 512 * fb:2048 * c + 512 * fb + 512],
                    start=(c == 0), stop=False))
            idx = r("t", lambda e, fb=fb, slot=slot: e.matmul(
                bk32(slot), B["o64"][0:1, :], B["borow"][0:1, 512 * fb:512 * fb + 512],
                start=False, stop=True), inc="P")
            fbid[(tt, fb)] = idx
            w("v", "P", idx)
            fbev[n] = r("v", lambda e, fb=fb, par=par, slot=slot: e.tensor_tensor(
                B["x2t"][par][:, 512 * fb:512 * fb + 512], bk32(slot),
                B["xr"][par][:, 512 * fb:512 * fb + 512], ALU.add), inc="V",
                selfwait=(fb == 3))

    def op_part2(tt):
        par = tt % 2
        w("s", "V", fbev[4 * tt + 3])
        r("s", lambda e, par=par: e.activation(
            B["sqs"][:], B["x2t"][par][:], AF.Copy, accum_out=B["ln"][:, 0:1]))
        cpid = r("s", lambda e: e.activation(
            B["ln"][:, 6:7], B["ln"][:, 6:7], AF.Copy), inc="A")
        w("v", "A", cpid)
        r("v", lambda e: e.tensor_scalar(B["ln"][:, 1:2], B["ln"][:, 0:1], 1.0 / D, None, ALU.mult),
          inc="V", selfwait=True)
        cen = r("v", lambda e, par=par: e.tensor_scalar(
            B["sqs"][:], B["x2t"][par][:], B["ln"][:, 1:2], None, ALU.subtract),
            inc="V", selfwait=True)
        w("s", "V", cen)
        r("s", lambda e, par=par: e.activation(
            B["x2t"][par][:], B["sqs"][:], AF.Square, accum_out=B["ln"][:, 2:3]))
        sqid = r("s", lambda e: e.activation(
            B["ln"][:, 6:7], B["ln"][:, 6:7], AF.Copy), inc="A")
        w("v", "A", sqid)
        vv_ = r("v", lambda e: e.tensor_scalar(
            B["ln"][:, 3:4], B["ln"][:, 2:3], 1.0 / D, EPS, ALU.mult, ALU.add),
            inc="V", selfwait=True)
        w("s", "V", vv_)
        sqr = r("s", lambda e: e.activation(B["ln"][:, 4:5], B["ln"][:, 3:4], AF.Sqrt), inc="A")
        w("v", "A", sqr)
        r("v", lambda e: e.reciprocal(B["ln"][:, 5:6], B["ln"][:, 4:5]),
          inc="V", selfwait=True)
        ln1[tt] = r("v", lambda e, par=par: e.tensor_scalar(
            B["x2t"][par][:], B["sqs"][:], B["ln"][:, 5:6], None, ALU.mult), inc="V")
        w("s", "V", ln1[tt])
        w("s", "P", trxid.get(tt - 1, 0))
        castid[tt] = r("s", lambda e, par=par: e.copy(B["x2bf"][:], B["x2t"][par][:]), inc="A")
        w("t", "A", castid[tt])

        def xview(grp, tt=tt):
            vv2 = B["x2f"][:].rearrange("p (c q) -> p c q", c=16)
            return vv2[:, 4 * grp:4 * grp + 4, 128 * tt:128 * tt + 128]
        trxid[tt] = transpose_tile(B["x2bf"], xview)
        x2fev[tt] = ev[transpose_tile.seq - 1]
        w("y", "V", ln1[tt])
        x2store[tt] = r("y", lambda e, tt=tt, par=par: e.dma_start(
            out=H["x2s"][128 * tt:128 * tt + 128, :], in_=B["x2t"][par][:]),
            inc="Y%d" % par, amt=16)

    for tt in range(8):
        op_part1(tt)
        if tt >= 1:
            op_part2(tt - 1)
    op_part2(7)

    oproj_done_p = S_.cnt["P"]
    oproj_last_fbev = fbev[31]

    conv_id = {}
    gmm = {}
    umm = {}
    hTT = {}
    silu_id = {}
    for ib in range(64):
        par = ib % 2
        w("y", "V", conv_id.get(ib - 1, 0))
        if ib == 0:
            w("y", "P", oproj_done_p)
            if H["_dbg"]:
                w("y", "Y0", dbg1)
        r("y", lambda e, ib=ib: e.dma_start(out=B["wgst"][:], in_=H["wg"][ib]),
          inc="S0", amt=16)
        wld = r("y", lambda e, ib=ib: e.dma_start(out=B["wust"][:], in_=H["wu"][ib]),
                inc="S0", amt=16)
        w("v", "S0", wld)
        w("v", "P", umm.get(ib - 2, 0))
        r("v", lambda e, par=par: e.tensor_scalar(
            B["wgb"][par][:], B["wgst"][:], -128.0, None, ALU.add))
        conv_id[ib] = r("v", lambda e, par=par: e.tensor_scalar(
            B["wub"][par][:], B["wust"][:], -128.0, None, ALU.add), inc="V")
        for th in range(2):
            n = 2 * ib + th
            s3 = n % 3
            w("t", "V", conv_id[ib])
            w("t", "V", hTT.get(n - 3, 0))
            if n == 0:
                w("t", "A", x2fev[7])
                w("t", "V", oproj_last_fbev)
            for c in range(16):
                last = c == 15
                idx = r("t", lambda e, c=c, th=th, par=par, s3=s3: e.matmul(
                    ps[:, 1024 * s3:1024 * s3 + 512],
                    B["wgb"][par][:, 128 * c:128 * c + 128],
                    B["x2f"][:, 1024 * c + 512 * th:1024 * c + 512 * th + 512],
                    start=(c == 0), stop=(c == 15)), inc=("P" if last else None))
            gmm[n] = idx
            for c in range(16):
                last = c == 15
                idx = r("t", lambda e, c=c, th=th, par=par, s3=s3: e.matmul(
                    ps[:, 1024 * s3 + 512:1024 * s3 + 1024],
                    B["wub"][par][:, 128 * c:128 * c + 128],
                    B["x2f"][:, 1024 * c + 512 * th:1024 * c + 512 * th + 512],
                    start=(c == 0), stop=(c == 15)), inc=("P" if last else None))
            umm[n] = idx
            w("s", "P", gmm[n])
            w("s", "V", hTT.get(n - 1, 0))
            silu_id[n] = r("s", lambda e, ib=ib, s3=s3: e.activation(
                B["sg"][:], ps[:, 1024 * s3:1024 * s3 + 512], AF.Silu,
                bias=0.0, scale=B["gs"][:, ib:ib + 1]), inc="A")
            w("v", "P", umm[n])
            r("v", lambda e, ib=ib, s3=s3: e.tensor_scalar(
                B["usv"][:], ps[:, 1024 * s3 + 512:1024 * s3 + 1024],
                B["us"][:, ib:ib + 1], None, ALU.mult), inc="V", selfwait=True)
            w("v", "A", silu_id[n])
            if n == 0 and H["_dbg"]:
                w("v", "Y0", dbg2)
            if ib == 32 and th == 0:
                w("v", "Y1", x2store[7])
                w("v", "A", castid[7])
            hTT[n] = r("v", lambda e, ib=ib, th=th: e.tensor_tensor(
                B["hf"][:, 1024 * ib + 512 * th:1024 * ib + 512 * th + 512],
                B["sg"][:], B["usv"][:], ALU.mult), inc="V")
    hlast = hTT[127]
    ffa_last_silu = silu_id[127]

    dmm = {}
    fmid = {}
    trf = {}
    tev = {}
    tokstore = {}
    wdc = {}

    def ffb_evac(hc):
        par = hc % 2
        w("v", "P", dmm[4 * hc + 3])
        w("v", "P", trf.get(hc - 1, 0))
        fmid[hc] = r("v", lambda e, hc=hc, par=par: e.tensor_scalar(
            B["fm"][:], ps[:, 1024 * par:1024 * par + 1024],
            B["ds"][:, hc:hc + 1], None, ALU.mult), inc="V", selfwait=True)
        w("t", "V", fmid[hc])
        w("t", "A", tev.get(hc - 1, 0))
        for wv_ in range(8):
            off = 2048 + 128 * wv_
            last = wv_ == 7
            idx = r("t", lambda e, wv_=wv_, off=off: e.transpose(
                ps[:, off:off + 128], B["fm"][:, 128 * wv_:128 * wv_ + 128],
                B["idf"][:]), inc=("P" if last else None))
        trf[hc] = idx
        w("s", "P", trf[hc])
        w("s", "Y0", tokstore.get(hc - 1, 0))
        r("s", lambda e: e.copy(B["tok"][0][:], bk32(4)))
        tev[hc] = r("s", lambda e: e.copy(B["tok"][1][:], bk32(5)), inc="A")
        w("y", "A", tev[hc])
        for gi2 in range(2):
            tokstore[hc] = r("y", lambda e, gi2=gi2, hc=hc: e.dma_start(
                out=H["ffs"].rearrange("(w p) h -> p w h", p=128)[
                    :, 4 * gi2:4 * gi2 + 4, 128 * hc:128 * hc + 128],
                in_=B["tok"][gi2][:].rearrange("p (w h) -> p w h", w=4)),
                inc="Y0", amt=16)

    for hc in range(16):
        par = hc % 2
        for ibg in range(4):
            gi = 4 * hc + ibg
            sp_ = gi % 2
            w("y", "V", wdc.get(gi - 2, 0))
            if gi == 0:
                w("y", "V", hlast)
            ssem = "S%d" % sp_
            ldid = r("y", lambda e, hc=hc, ibg=ibg, sp_=sp_: e.dma_start(
                out=B["wdst"][sp_][:].rearrange("p (g h) -> p g h", g=16),
                in_=H["wd"][hc, 16 * ibg:16 * ibg + 16].rearrange("g p h -> p g h")),
                inc=ssem, amt=16)
            w("v", ssem, ldid)
            w("v", "P", dmm.get(gi - 2, 0))
            wdc[gi] = r("v", lambda e, sp_=sp_: e.tensor_scalar(
                B["wdb"][sp_][:], B["wdst"][sp_][:], -128.0, None, ALU.add), inc="V")
            w("t", "V", wdc[gi])
            if ibg == 0:
                w("t", "V", fmid.get(hc - 2, 0))
                if hc == 0:
                    w("t", "V", hlast)
                    w("t", "A", ffa_last_silu)
            for i16 in range(16):
                ib = 16 * ibg + i16
                for th in range(2):
                    last = (i16 == 15 and th == 1)
                    idx = r("t", lambda e, i16=i16, th=th, ib=ib, par=par, sp_=sp_: e.matmul(
                        ps[:, 1024 * par + 512 * th:1024 * par + 512 * th + 512],
                        B["wdb"][sp_][:, 128 * i16:128 * i16 + 128],
                        B["hf"][:, 1024 * ib + 512 * th:1024 * ib + 512 * th + 512],
                        start=(ib == 0), stop=(ib == 63)), inc=("P" if last else None))
            dmm[gi] = idx
            if ibg == 0 and hc >= 1:
                ffb_evac(hc - 1)
    ffb_evac(15)

    ffb_done_p = dmm[63]

    ln2 = {}
    ystore = {}
    for tt in range(8):
        par = tt % 2
        w("y", "Y0", tokstore[15])
        w("y", "V", ln2.get(tt - 2, 0))
        ssem = "S%d" % par
        r("y", lambda e, tt=tt, par=par: e.dma_start(
            out=B["ffin"][par][:], in_=H["ffs"][128 * tt:128 * tt + 128, :]),
          inc=ssem, amt=16)
        ffld = r("y", lambda e, tt=tt, par=par: e.dma_start(
            out=B["x2in"][par][:], in_=H["x2s"][128 * tt:128 * tt + 128, :]),
            inc=ssem, amt=16)
        w("v", ssem, ffld)
        if tt == 0:
            w("v", "P", ffb_done_p)
            w("v", "Y0", tokstore[15])
        w("v", "Y%d" % par, ystore.get(tt - 2, 0))
        tta = r("v", lambda e, par=par: e.tensor_tensor(
            B["outt"][par][:], B["ffin"][par][:], B["x2in"][par][:], ALU.add),
          inc="V", selfwait=True)
        w("s", "V", tta)
        r("s", lambda e, par=par: e.activation(
            B["sq2"][:], B["outt"][par][:], AF.Copy, accum_out=B["ln"][:, 8:9]))
        cpid = r("s", lambda e: e.activation(
            B["ln"][:, 14:15], B["ln"][:, 14:15], AF.Copy), inc="A")
        w("v", "A", cpid)
        r("v", lambda e: e.tensor_scalar(B["ln"][:, 9:10], B["ln"][:, 8:9], 1.0 / D, None, ALU.mult),
          inc="V", selfwait=True)
        cen = r("v", lambda e, par=par: e.tensor_scalar(
            B["sq2"][:], B["outt"][par][:], B["ln"][:, 9:10], None, ALU.subtract),
            inc="V", selfwait=True)
        w("s", "V", cen)
        r("s", lambda e, par=par: e.activation(
            B["outt"][par][:], B["sq2"][:], AF.Square, accum_out=B["ln"][:, 10:11]))
        sqid = r("s", lambda e: e.activation(
            B["ln"][:, 14:15], B["ln"][:, 14:15], AF.Copy), inc="A")
        w("v", "A", sqid)
        vv_ = r("v", lambda e: e.tensor_scalar(
            B["ln"][:, 11:12], B["ln"][:, 10:11], 1.0 / D, EPS, ALU.mult, ALU.add),
            inc="V", selfwait=True)
        w("s", "V", vv_)
        sqr = r("s", lambda e: e.activation(B["ln"][:, 12:13], B["ln"][:, 11:12], AF.Sqrt), inc="A")
        w("v", "A", sqr)
        r("v", lambda e: e.reciprocal(B["ln"][:, 13:14], B["ln"][:, 12:13]),
          inc="V", selfwait=True)
        ln2[tt] = r("v", lambda e, par=par: e.tensor_scalar(
            B["outt"][par][:], B["sq2"][:], B["ln"][:, 13:14], None, ALU.mult), inc="V")
        w("y", "V", ln2[tt])
        ystore[tt] = r("y", lambda e, tt=tt, par=par: e.dma_start(
            out=H["y"][128 * tt:128 * tt + 128, :], in_=B["outt"][par][:]),
            inc="Y%d" % par, amt=16)

    w("y", "Y0", S_.cnt["Y0"])
    w("y", "Y1", S_.cnt["Y1"])


def _pack(inputs):
    f32 = np.float32
    Wq = np.asarray(inputs["L0_attn_Wq"], f32)
    bq = np.asarray(inputs["L0_attn_bq"], f32)
    Wk = np.asarray(inputs["L0_attn_Wk"], f32)
    bk = np.asarray(inputs["L0_attn_bk"], f32)
    Wv = np.asarray(inputs["L0_attn_Wv"], f32)
    bv = np.asarray(inputs["L0_attn_bv"], f32)
    Wo = np.asarray(inputs["L0_attn_Wo"], f32)
    bo = np.asarray(inputs["L0_attn_bo"], f32)
    gw = np.ascontiguousarray(np.asarray(inputs["L0_ffn_gate_w"], np.int32))
    uw = np.ascontiguousarray(np.asarray(inputs["L0_ffn_up_w"], np.int32))
    dw = np.ascontiguousarray(np.asarray(inputs["L0_ffn_down_w"], np.int32))
    gsc = np.asarray(inputs["L0_ffn_gate_scale"], f32)
    usc = np.asarray(inputs["L0_ffn_up_scale"], f32)
    dsc = np.asarray(inputs["L0_ffn_down_scale"], f32)

    shared = {}
    wq_t = np.empty((16, 128, 2048), f32)
    bq_t = np.empty((128, 16), f32)
    for j in range(16):
        cols = np.r_[64 * j:64 * j + 64, 64 * (j + 16):64 * (j + 16) + 64]
        tmp = Wq[:, cols].reshape(16, 128, 128)
        wq_t[j] = tmp.transpose(1, 0, 2).reshape(128, 2048)
        bq_t[:, j] = bq[cols]
    shared["wq_t"] = np.ascontiguousarray(wq_t)
    shared["bq_t"] = np.ascontiguousarray(bq_t)
    col_order = np.empty(512, np.int64)
    for cc in range(4):
        for dd in range(128):
            kv = cc + (4 if dd >= 64 else 0)
            col_order[128 * cc + dd] = 64 * kv + dd % 64
    wk_re = Wk[:, col_order]
    shared["wk_t"] = np.ascontiguousarray(
        wk_re.reshape(16, 128, 512).transpose(1, 0, 2).reshape(128, 8192))
    shared["bk_t"] = np.ascontiguousarray(bk[col_order].reshape(4, 128).T)
    shared["wv_t"] = np.ascontiguousarray(
        Wv.reshape(16, 128, 512).transpose(1, 0, 2).reshape(128, 8192))
    shared["bv_bc"] = np.ascontiguousarray(np.broadcast_to(bv, (128, 512)))
    wo_t = np.empty((16, 128, 2048), f32)
    for c in range(16):
        rows = np.r_[64 * c:64 * c + 64, 64 * (c + 16):64 * (c + 16) + 64]
        wo_t[c] = Wo[rows, :]
    shared["wo_t"] = np.ascontiguousarray(wo_t)
    shared["bo_bc"] = np.ascontiguousarray(np.broadcast_to(bo, (128, 2048)))
    shared["wg_t"] = np.ascontiguousarray(
        gw.reshape(64, 128, 16, 128).transpose(0, 3, 2, 1).reshape(64, 128, 2048))
    shared["wu_t"] = np.ascontiguousarray(
        uw.reshape(64, 128, 16, 128).transpose(0, 3, 2, 1).reshape(64, 128, 2048))
    shared["wd_t"] = np.ascontiguousarray(
        dw.reshape(16, 128, 64, 128).transpose(0, 2, 3, 1))
    shared["gs_t"] = np.ascontiguousarray(gsc.reshape(64, 128).T)
    shared["us_t"] = np.ascontiguousarray(usc.reshape(64, 128).T)
    shared["ds_t"] = np.ascontiguousarray(dsc.reshape(16, 128).T)
    shared["identb"] = np.eye(128, dtype=np.float32).astype(ml_dtypes.bfloat16)
    shared["identf"] = np.eye(128, dtype=np.float32)
    shared["ones64b"] = np.ones((128, 128), np.float32).astype(ml_dtypes.bfloat16)
    shared["onesf"] = np.ones((128, 64), np.float32)
    return shared


def kernel(**inputs):
    x = np.ascontiguousarray(np.asarray(inputs["L0_attn_x"], np.float32).reshape(SEQ, D))
    shared = _pack(inputs)
    nc = build_nc()
    in_maps = []
    for c in range(NCORES):
        m = dict(shared)
        m["x_full"] = x
        m["x_own"] = np.ascontiguousarray(x[T * c:T * (c + 1)])
        in_maps.append(m)
    res = run_bass_kernel_spmd(nc, in_maps, core_ids=list(range(NCORES)))
    y = np.concatenate([np.asarray(res.results[c]["y"]) for c in range(NCORES)], axis=0)
    return y.reshape(1, SEQ, D).astype(np.float32)


if __name__ == "__main__":
    import reference as _ref
    inp = _ref.setup_inputs()
    out = kernel(**{k: np.asarray(v) for k, v in inp.items()})
    want = np.asarray(_ref.reference(**inp))
    err = np.abs(out - want)
    denom = np.mean(want.astype(np.float64) ** 2)
    print("rvar:", float(np.mean((out.astype(np.float64) - want.astype(np.float64)) ** 2) / denom))
    print("maxabs:", float(err.max()))


# version 09
# speedup vs baseline: 1.2614x; 1.0495x over previous
import numpy as np
import ml_dtypes
import concourse.bass as bass
import concourse.mybir as mybir
from concourse.bass_utils import run_bass_kernel_spmd

F32 = mybir.dt.float32
BF16 = mybir.dt.bfloat16
I32 = mybir.dt.int32
AF = mybir.ActivationFunctionType
ALU = mybir.AluOpType
AX = mybir.AxisListType

NCORES = 8
SEQ = 8192
D = 2048
T = SEQ // NCORES
NKT = SEQ // 128
EPS = 1e-5


def build_nc():
    nc = bass.Bass("TRN2", target_bir_lowering=False)

    H = {}
    H["xf"] = nc.dram_tensor("x_full", [SEQ, D], F32, kind="ExternalInput").ap()
    H["xo"] = nc.dram_tensor("x_own", [T, D], F32, kind="ExternalInput").ap()
    H["wq"] = nc.dram_tensor("wq_t", [16, 128, 2048], F32, kind="ExternalInput").ap()
    H["bq"] = nc.dram_tensor("bq_t", [128, 16], F32, kind="ExternalInput").ap()
    H["wk"] = nc.dram_tensor("wk_t", [128, 8192], F32, kind="ExternalInput").ap()
    H["bk"] = nc.dram_tensor("bk_t", [128, 4], F32, kind="ExternalInput").ap()
    H["wv"] = nc.dram_tensor("wv_t", [128, 8192], F32, kind="ExternalInput").ap()
    H["bv"] = nc.dram_tensor("bv_bc", [128, 512], F32, kind="ExternalInput").ap()
    H["wo"] = nc.dram_tensor("wo_t", [16, 128, 2048], F32, kind="ExternalInput").ap()
    H["bo"] = nc.dram_tensor("bo_bc", [128, 2048], F32, kind="ExternalInput").ap()
    H["wg"] = nc.dram_tensor("wg_t", [64, 128, 2048], I32, kind="ExternalInput").ap()
    H["wu"] = nc.dram_tensor("wu_t", [64, 128, 2048], I32, kind="ExternalInput").ap()
    H["wd"] = nc.dram_tensor("wd_t", [16, 64, 128, 128], I32, kind="ExternalInput").ap()
    H["gs"] = nc.dram_tensor("gs_t", [128, 64], F32, kind="ExternalInput").ap()
    H["us"] = nc.dram_tensor("us_t", [128, 64], F32, kind="ExternalInput").ap()
    H["ds"] = nc.dram_tensor("ds_t", [128, 16], F32, kind="ExternalInput").ap()
    H["idb"] = nc.dram_tensor("identb", [128, 128], BF16, kind="ExternalInput").ap()
    H["idf"] = nc.dram_tensor("identf", [128, 128], F32, kind="ExternalInput").ap()
    H["o64"] = nc.dram_tensor("ones64b", [128, 128], BF16, kind="ExternalInput").ap()
    H["of"] = nc.dram_tensor("onesf", [128, 64], F32, kind="ExternalInput").ap()
    H["y"] = nc.dram_tensor("y", [T, D], F32, kind="ExternalOutput").ap()
    import os
    H["_dbg"] = bool(os.environ.get("KDEBUG"))
    if H["_dbg"]:
        H["dkt"] = nc.dram_tensor("dbg_kt", [128, 32768], BF16, kind="ExternalOutput").ap()
        H["dv"] = nc.dram_tensor("dbg_v", [128, 32768], BF16, kind="ExternalOutput").ap()
        H["dqt"] = nc.dram_tensor("dbg_qt", [128, 16384], BF16, kind="ExternalOutput").ap()
        H["dao"] = nc.dram_tensor("dbg_ao", [128, 16384], BF16, kind="ExternalOutput").ap()
        H["dwo"] = nc.dram_tensor("dbg_wo", [128, 32768], BF16, kind="ExternalOutput").ap()
        H["ds_"] = nc.dram_tensor("dbg_s", [T, D], F32, kind="ExternalOutput").ap()
    import os as _os
    _scrkind = "ExternalOutput" if _os.environ.get("KDEBUG") else "Internal"
    H["x2s"] = nc.dram_tensor("x2_scr", [T, D], F32, kind=_scrkind).ap()
    H["ffs"] = nc.dram_tensor("ffn_scr", [T, D], F32, kind=_scrkind).ap()

    SB_BASE = 16512

    def at(name, shape, dtype, off):
        return nc.alloc_sbuf_tensor_at(name, shape, dtype, offset=SB_BASE + off)

    B = {}
    B["kt"] = at("kt_b", [128, 32768], BF16, 0)
    B["v"] = at("v_b", [128, 32768], BF16, 65536)
    B["wo"] = at("wo_b", [128, 32768], BF16, 0)
    B["hf"] = at("hfull", [128, 65536], BF16, 0)
    B["bo"] = at("bo_s", [128, 2048], F32, 65536)
    B["xr"] = [at(f"xr{i}", [128, 2048], F32, 73728 + 8192 * i) for i in range(2)]
    B["x2t"] = [at(f"x2t{i}", [128, 2048], F32, 90112 + 8192 * i) for i in range(2)]
    B["x2bf"] = at("x2bf", [128, 2048], BF16, 106496)
    B["sqs"] = at("sqs", [128, 2048], F32, 110592)
    B["xbo"] = at("xbo", [128, 2048], F32, 118784)
    B["qt"] = at("qt_b", [128, 16384], BF16, 131072)
    B["x2f"] = at("x2f", [128, 16384], BF16, 131072)
    B["xfblk"] = at("xfblk", [128, 16384], BF16, 131072)
    B["wkb"] = at("wkb", [128, 8192], BF16, 163840)
    B["ffin"] = [at(f"ffin{i}", [128, 2048], F32, 131072 + 8192 * i) for i in range(2)]
    B["x2in"] = [at(f"x2in{i}", [128, 2048], F32, 147456 + 8192 * i) for i in range(2)]
    B["ao"] = at("ao_b", [128, 16384], BF16, 163840)
    B["xfo"] = at("xfown", [128, 16384], BF16, 163840)
    B["wvb"] = at("wvb", [128, 8192], BF16, 180224)
    B["bv"] = at("bv_s", [128, 512], F32, 206752)
    B["wgst"] = at("wgstage", [128, 2048], I32, 163840)
    B["wust"] = at("wustage", [128, 2048], I32, 172032)
    B["wgb"] = [at(f"wgb{i}", [128, 2048], BF16, 180224 + 4096 * i) for i in range(2)]
    B["wub"] = [at(f"wub{i}", [128, 2048], BF16, 188416 + 4096 * i) for i in range(2)]
    B["wdst"] = [at(f"wdst{i}", [128, 2048], I32, 163840 + 8192 * i) for i in range(2)]
    B["wdb"] = [at(f"wdb{i}", [128, 2048], BF16, 180224 + 4096 * i) for i in range(2)]
    B["fm"] = at("fm_s", [128, 1024], F32, 188416)
    B["tok"] = [at(f"tokst{i}", [128, 512], F32, 192512 + 2048 * i) for i in range(2)]
    B["outt"] = [at(f"outt{i}", [128, 2048], F32, 163840 + 8192 * i) for i in range(2)]
    B["sq2"] = at("sq2", [128, 2048], F32, 180224)
    B["idb"] = at("idb_s", [128, 128], BF16, 196608)
    B["idf"] = at("idf_s", [128, 128], F32, 196864)
    B["o64"] = at("o64_s", [128, 128], BF16, 197376)
    B["of"] = at("of_s", [128, 64], F32, 197632)
    B["bq"] = at("bq_s", [128, 16], F32, 197888)
    B["bk"] = at("bk_s", [128, 4], F32, 197952)
    B["gs"] = at("gs_s", [128, 64], F32, 197984)
    B["us"] = at("us_s", [128, 64], F32, 198240)
    B["ds"] = at("ds_s", [128, 16], F32, 198496)
    B["ln"] = at("lnst", [128, 16], F32, 210848)
    B["xst"] = [at(f"xstag{i}", [128, 2048], BF16, 198560 + 4096 * i) for i in range(2)]
    B["pt"] = [at(f"pt{i}", [128, 1024], BF16, 198560 + 2048 * i) for i in range(4)]
    B["rd"] = at("rd_t", [128, 512], F32, 206752)
    B["bcs"] = at("bcs", [128, 512], F32, 208800)
    B["sg"] = at("sg_s", [128, 512], F32, 206752)
    B["rdb"] = at("rdb", [128, 512], BF16, 210912)
    B["usv"] = at("usv_s", [128, 512], F32, 208800)
    B["borow"] = at("borow", [1, 2048], BF16, 118784)

    B["nc"] = nc
    B["ps"] = nc.alloc_psum_tensor("ps32", [128, 3072], F32)
    B["psbf"] = nc.alloc_psum_tensor("psbf", [128, 2048], BF16)

    from contextlib import ExitStack
    with ExitStack() as es:
        ec = es.enter_context
        sems = {k: ec(nc.semaphore(f"sem{k}")) for k in
                ("P", "A", "V", "G0", "G1", "W", "S0", "S1", "Y0", "Y1")}
        block = ec(nc.Block())

        class Sched:
            def __init__(self, only, eng):
                self.only = only
                self.eng = eng
                self.cnt = {k: 0 for k in
                            ("P", "A", "V", "G0", "G1", "W", "S0", "S1", "Y0", "Y1")}

            def wait(self, engname, sem, val):
                if val is None or val <= 0:
                    return
                if self.only == engname:
                    self.eng.wait_ge(sems[sem], val)

            def run(self, engname, fn, inc=None, amt=1, selfwait=False):
                if inc is not None:
                    self.cnt[inc] += amt
                if self.only == engname:
                    inst = fn(self.eng)
                    if inc is not None:
                        inst.then_inc(sems[inc], amt)
                        if selfwait:
                            self.eng.wait_ge(sems[inc], self.cnt[inc])
                return self.cnt[inc] if inc is not None else None

        @block.gpsimd
        def _(g):
            emit_program(Sched("g", g), H, B)

        @block.tensor
        def _(t):
            emit_program(Sched("t", t), H, B)

        @block.scalar
        def _(s):
            emit_program(Sched("s", s), H, B)

        @block.vector
        def _(v):
            emit_program(Sched("v", v), H, B)

        @block.sync
        def _(sy):
            emit_program(Sched("y", sy), H, B)

    return nc

def emit_program(S_, H, B):
    w = S_.wait
    r = S_.run
    ps = B["ps"]
    psbf = B["psbf"]

    def bk32(i, n=512):
        return ps[:, 512 * i:512 * i + n]

    for k in ("idb", "idf", "o64", "of", "bq", "bk", "gs", "us", "ds", "bv"):
        r("g", lambda e, k=k: e.dma_start(out=B[k][:], in_=H[k]), inc="W", amt=16)
    r("g", lambda e: e.dma_start(out=B["wkb"][:], in_=H["wk"]), inc="W", amt=16)
    wconsts = r("g", lambda e: e.dma_start(out=B["wvb"][:], in_=H["wv"]), inc="W", amt=16)

    ev = {}

    def transpose_tile(src_buf, dst3):
        g0 = transpose_tile.seq
        for grp in range(4):
            half = grp % 2
            w("t", "A", ev.get(g0 + grp - 2, 0))
            for cc in range(4):
                c = 4 * grp + cc
                last = cc == 3
                tid = r("t", lambda e, c=c, half=half, cc=cc, src_buf=src_buf: e.transpose(
                    psbf[:, 1024 * half + 128 * cc:1024 * half + 128 * cc + 128],
                    src_buf[:, 128 * c:128 * c + 128], B["idb"][:]),
                    inc=("P" if last else None))
            w("s", "P", tid)
            ev[g0 + grp] = r("s", lambda e, half=half, grp=grp, dst3=dst3: e.copy(
                dst3(grp),
                psbf[:, 1024 * half:1024 * half + 512].rearrange("p (g q) -> p g q", g=4)),
                inc="A")
        transpose_tile.seq += 4
        return tid
    transpose_tile.seq = 0

    kev = {}
    vev = {}
    last_tr = {}
    NBLK = SEQ // 512
    for blk in range(NBLK):
        slot_x = 8192 * (blk % 2)
        for s4 in range(4):
            ti = 4 * blk + s4
            buf = ti % 2
            w("g", "P", last_tr.get(ti - 2, 0))
            gsem = "G%d" % buf
            r("g", lambda e, ti=ti, buf=buf: e.dma_start(
                out=B["xst"][buf][:], in_=H["xf"][128 * ti:128 * ti + 128, :]),
              inc=gsem, amt=16)
            gneed = S_.cnt[gsem]
            if ti < 2:
                w("t", "W", wconsts)
                w("v", "W", wconsts)
            w("t", gsem, gneed)

            def dview(grp, slot_x=slot_x, s4=s4):
                vv = B["xfblk"][:].rearrange("p (c q) -> p c q", q=512)
                c0 = slot_x // 512 + 4 * grp
                return vv[:, c0:c0 + 4, 128 * s4:128 * s4 + 128]
            last_tr[ti] = transpose_tile(B["xst"][buf], dview)
        w("t", "A", ev[transpose_tile.seq - 1])
        for cc in range(4):
            kseq = 4 * blk + cc
            pb = cc % 2
            w("t", "V", kev.get(kseq - 2, 0))
            for c in range(16):
                last = c == 15
                kmm = r("t", lambda e, cc=cc, c=c, pb=pb, slot_x=slot_x: e.matmul(
                    bk32(pb),
                    B["wkb"][:, 512 * c + 128 * cc:512 * c + 128 * cc + 128],
                    B["xfblk"][:, slot_x + 512 * c:slot_x + 512 * c + 512],
                    start=(c == 0), stop=(c == 15)), inc=("P" if last else None))
            w("v", "P", kmm)
            kev[kseq] = r("v", lambda e, cc=cc, blk=blk, pb=pb: e.tensor_scalar(
                B["kt"][:, 8192 * cc + 512 * blk:8192 * cc + 512 * blk + 512],
                bk32(pb), B["bk"][:, cc:cc + 1], None, ALU.add),
                inc="V")
        for s4 in range(4):
            ti = 4 * blk + s4
            vb = 2 + ti % 2
            w("t", "V", vev.get(ti - 2, 0))
            for c in range(16):
                last = c == 15
                vmm = r("t", lambda e, c=c, vb=vb, slot_x=slot_x, s4=s4: e.matmul(
                    bk32(vb), B["xfblk"][:, slot_x + 512 * c + 128 * s4:slot_x + 512 * c + 128 * s4 + 128],
                    B["wvb"][:, 512 * c:512 * c + 512],
                    start=(c == 0), stop=(c == 15)), inc=("P" if last else None))
            w("v", "P", vmm)
            vev[ti] = r("v", lambda e, ti=ti, vb=vb: e.tensor_tensor(
                B["v"][:, 512 * ti:512 * ti + 512], bk32(vb), B["bv"][:], ALU.add),
                inc="V")

    w("s", "V", vev[NKT - 1])
    for s8 in range(8):
        buf = s8 % 2
        w("g", "P", last_tr.get(NKT + s8 - 2, 0))
        gsem = "G%d" % buf
        r("g", lambda e, s8=s8, buf=buf: e.dma_start(
            out=B["xst"][buf][:], in_=H["xo"][128 * s8:128 * s8 + 128, :]),
          inc=gsem, amt=16)
        gneed = S_.cnt[gsem]
        w("t", gsem, gneed)

        def oview(grp, s8=s8):
            vv = B["xfo"][:].rearrange("p (c q) -> p c q", c=16)
            return vv[:, 4 * grp:4 * grp + 4, 128 * s8:128 * s8 + 128]
        last_tr[NKT + s8] = transpose_tile(B["xst"][buf], oview)

    qmm = {}
    qev = {}
    for j in range(16):
        buf = j % 2
        w("g", "P", qmm.get(j - 2, last_tr[NKT + 7]))
        gsem = "G%d" % buf
        r("g", lambda e, j=j, buf=buf: e.dma_start(
            out=B["xst"][buf][:], in_=H["wq"][j]), inc=gsem, amt=16)
        gneed = S_.cnt[gsem]
        w("t", gsem, gneed)
        qb_ = j % 2
        w("t", "V", qev.get(j - 2, max(kev[4 * (SEQ // 512 - 1) + 3], vev[NKT - 1])))
        for c in range(16):
            for hh in range(2):
                last = (c == 15 and hh == 1)
                idx = r("t", lambda e, c=c, hh=hh, qb_=qb_, buf=buf: e.matmul(
                    ps[:, 1024 * qb_ + 512 * hh:1024 * qb_ + 512 * hh + 512],
                    B["xst"][buf][:, 128 * c:128 * c + 128],
                    B["xfo"][:, 1024 * c + 512 * hh:1024 * c + 512 * hh + 512],
                    start=(c == 0), stop=(c == 15)), inc=("P" if last else None))
                if last:
                    qmm[j] = idx
        w("v", "P", qmm[j])
        qev[j] = r("v", lambda e, j=j, qb_=qb_: e.tensor_scalar(
            B["qt"][:, 1024 * j:1024 * j + 1024], ps[:, 1024 * qb_:1024 * qb_ + 1024],
            B["bq"][:, j:j + 1], None, ALU.add), inc="V")

    if H["_dbg"]:
        w("y", "V", max(qev[15], vev[NKT - 1], kev[4 * (SEQ // 512 - 1) + 3]))
        r("y", lambda e: e.dma_start(out=H["dkt"], in_=B["kt"][:]), inc="Y0", amt=16)
        r("y", lambda e: e.dma_start(out=H["dv"], in_=B["v"][:]), inc="Y0", amt=16)
        dbg0 = r("y", lambda e: e.dma_start(out=H["dqt"], in_=B["qt"][:]), inc="Y0", amt=16)
    psA = ps[:, 2048:2560]
    psB = ps[:, 2560:3072]
    pair_exp_base = qev[15] * 0 + S_.cnt["A"]
    prev_tb = vev[NKT - 1]
    prev_av = {}
    sc_last = {}
    wo_dma = {}
    qt_ready = qev[15]
    attn_first = True
    for j in range(16):
        for qb in range(2):
            cc = j // 4
            kvA = j // 4
            kvB = 4 + j // 4
            exp_id = {}
            av_id = {}
            sc_id = {}

            def emit_av(kt, kvA=kvA, kvB=kvB, exp_id=exp_id, av_id=av_id,
                        first_gate=None):
                pti = kt % 4
                w("t", "A", exp_id[kt])
                if kt == 0:
                    w("t", "V", first_gate)
                st = kt == 0
                sp = kt == 63
                r("t", lambda e, kt=kt, pti=pti, kvA=kvA, st=st, sp=sp: e.matmul(
                    psA[0:64, :], B["v"][:, 512 * kt + 64 * kvA:512 * kt + 64 * kvA + 64],
                    B["pt"][pti][:, 0:512], start=st, stop=sp, tile_position=(0, 0),
                    skip_group_check=True))
                r("t", lambda e, pti=pti, st=st, sp=sp: e.matmul(
                    psA[64:128, :], B["o64"][:, 0:64], B["pt"][pti][:, 512:1024],
                    start=st, stop=sp, tile_position=(0, 64), skip_group_check=True))
                r("t", lambda e, kt=kt, pti=pti, kvB=kvB, st=st, sp=sp: e.matmul(
                    psB[64:128, :], B["v"][:, 512 * kt + 64 * kvB:512 * kt + 64 * kvB + 64],
                    B["pt"][pti][:, 512:1024], start=st, stop=sp, tile_position=(0, 64),
                    skip_group_check=True))
                av_id[kt] = r("t", lambda e, pti=pti, st=st, sp=sp: e.matmul(
                    psB[0:64, :], B["o64"][:, 0:64], B["pt"][pti][:, 0:512],
                    start=st, stop=sp, tile_position=(0, 0), skip_group_check=True), inc="P")

            gate0 = prev_tb

            def emit_sc(kt, cc=cc, j=j, qb=qb, exp_id=exp_id, sc_id=sc_id):
                scb = kt % 2
                w("t", "A", exp_id.get(kt - 2, pair_exp_base))
                if attn_first and kt == 0:
                    w("t", "V", qt_ready)
                r("t", lambda e, kt=kt, scb=scb: e.matmul(
                    bk32(2 * scb),
                    B["kt"][0:64, 8192 * cc + 128 * kt:8192 * cc + 128 * kt + 128],
                    B["qt"][0:64, 1024 * j + 512 * qb:1024 * j + 512 * qb + 512],
                    start=True, stop=True, tile_position=(0, 0)))
                sc_id[kt] = r("t", lambda e, kt=kt, scb=scb: e.matmul(
                    bk32(2 * scb + 1),
                    B["kt"][64:128, 8192 * cc + 128 * kt:8192 * cc + 128 * kt + 128],
                    B["qt"][64:128, 1024 * j + 512 * qb:1024 * j + 512 * qb + 512],
                    start=True, stop=True, tile_position=(64, 0)), inc="P")

            def emit_exp(kt, exp_id=exp_id, sc_id=sc_id, av_id=av_id):
                ptw = av_id.get(kt - 4, prev_av.get(60 + kt, 0))
                w("s", "P", max(sc_id[kt], ptw))
                exp_id[kt] = r("s", lambda e, kt=kt: e.activation(
                    B["pt"][kt % 4][:], ps[:, 1024 * (kt % 2):1024 * (kt % 2) + 1024],
                    AF.Exp, bias=0.0, scale=0.125), inc="A")

            for k2 in range(0, 64, 2):
                emit_sc(k2)
                emit_exp(k2)
                emit_sc(k2 + 1)
                emit_exp(k2 + 1)
                if k2 >= 2:
                    emit_av(k2 - 2, first_gate=gate0)
                    emit_av(k2 - 1, first_gate=gate0)
            emit_av(62, first_gate=gate0)
            emit_av(63, first_gate=gate0)
            attn_first = False

            acols = slice(1024 * j + 512 * qb, 1024 * j + 512 * qb + 512)
            w("v", "P", av_id[63])
            def _recip(e, dst, src_):
                with B["nc"].allow_low_precision(reason="softmax 1/D broadcast via bf16 matmul"):
                    return e.reciprocal(dst, src_)
            r("v", lambda e: _recip(e, B["rdb"][0:1, :], psB[0:1, :]),
              inc="V", selfwait=True)
            rid = r("v", lambda e: _recip(e, B["rdb"][64:65, :], psA[64:65, :]), inc="V")
            w("t", "V", rid)
            bA = r("t", lambda e: e.matmul(
                psB[0:64, :], B["o64"][0:1, 0:64], B["rdb"][0:1, :],
                start=True, stop=True), inc="P")
            w("v", "P", bA)
            r("v", lambda e: e.tensor_copy(B["bcs"][0:64, :], psB[0:64, :]),
              inc="V", selfwait=True)
            tA = r("v", lambda e, acols=acols: e.tensor_tensor(
                B["ao"][0:64, acols], psA[0:64, :], B["bcs"][0:64, :], ALU.mult),
                inc="V")
            w("t", "V", tA)
            bB = r("t", lambda e: e.matmul(
                psA[64:128, :], B["o64"][64:65, 0:64], B["rdb"][64:65, :],
                start=True, stop=True), inc="P")
            w("v", "P", bB)
            r("v", lambda e: e.tensor_copy(B["bcs"][64:128, :], psA[64:128, :]),
              inc="V", selfwait=True)
            tB = r("v", lambda e, acols=acols: e.tensor_tensor(
                B["ao"][64:128, acols], psB[64:128, :], B["bcs"][64:128, :], ALU.mult),
                inc="V")
            prev_tb = tB
            prev_av = {k: v for k, v in av_id.items() if k >= 60}
            pair_exp_base = exp_id[63]
            sc_last[(j, qb)] = sc_id[63]
        if j % 4 == 3:
            grp4 = j // 4
            for c4 in range(4):
                c = 4 * grp4 + c4
                w("g", "P", sc_last[(j, 1)])
                if H["_dbg"]:
                    w("g", "Y0", dbg0)
                wo_dma[c] = r("g", lambda e, c=c: e.dma_start(
                    out=B["wo"][:, 2048 * c:2048 * c + 2048], in_=H["wo"][c]),
                    inc="W", amt=16)

    attn_done_v = prev_tb
    attn_done_p = S_.cnt["P"]
    if H["_dbg"]:
        w("y", "V", attn_done_v)
        dbg1 = r("y", lambda e: e.dma_start(out=H["dao"], in_=B["ao"][:]), inc="Y0", amt=16)
    attn_last_exp = pair_exp_base

    w("g", "P", attn_done_p)
    bo_id = r("g", lambda e: e.dma_start(out=B["borow"][:], in_=H["bo"][0:1, :]), inc="W", amt=16)

    if H["_dbg"]:
        w("y", "W", wo_dma[15])
        dbg2 = r("y", lambda e: e.dma_start(out=H["dwo"], in_=B["wo"][:]), inc="Y0", amt=16)
    fbid = {}
    fbev = {}
    xboid = {}
    ln1 = {}
    castid = {}
    trxid = {}
    x2fev = {}
    x2store = {}

    def op_part1(tt):
        par = tt % 2
        w("y", "V", fbev.get(4 * (tt - 2) + 3, 0))
        if tt < 2:
            w("y", "P", attn_done_p)
        ssem = "S%d" % par
        xrid = r("y", lambda e, tt=tt, par=par: e.dma_start(
            out=B["xr"][par][:], in_=H["xo"][128 * tt:128 * tt + 128, :]),
            inc=ssem, amt=16)
        w("v", ssem, xrid)
        w("v", "A", castid.get(tt - 2, 0))
        w("v", "Y%d" % par, x2store.get(tt - 2, 0))
        for fb in range(4):
            n = 4 * tt + fb
            slot = n % 4
            w("t", "V", fbev.get(n - 4, 0))
            if n == 0:
                w("t", "V", attn_done_v)
                w("t", "A", attn_last_exp)
                w("t", "W", wo_dma[15])
            if n == 0:
                w("t", "W", bo_id)
            for c in range(16):
                r("t", lambda e, c=c, fb=fb, tt=tt, slot=slot: e.matmul(
                    bk32(slot), B["ao"][:, 1024 * c + 128 * tt:1024 * c + 128 * tt + 128],
                    B["wo"][:, 2048 * c + 512 * fb:2048 * c + 512 * fb + 512],
                    start=(c == 0), stop=False))
            idx = r("t", lambda e, fb=fb, slot=slot: e.matmul(
                bk32(slot), B["o64"][0:1, :], B["borow"][0:1, 512 * fb:512 * fb + 512],
                start=False, stop=True), inc="P")
            fbid[(tt, fb)] = idx
            w("v", "P", idx)
            fbev[n] = r("v", lambda e, fb=fb, par=par, slot=slot: e.tensor_tensor(
                B["x2t"][par][:, 512 * fb:512 * fb + 512], bk32(slot),
                B["xr"][par][:, 512 * fb:512 * fb + 512], ALU.add), inc="V",
                selfwait=(fb == 3))

    def op_part2(tt):
        par = tt % 2
        w("s", "V", fbev[4 * tt + 3])
        r("s", lambda e, par=par: e.activation(
            B["sqs"][:], B["x2t"][par][:], AF.Copy, accum_out=B["ln"][:, 0:1]))
        cpid = r("s", lambda e: e.activation(
            B["ln"][:, 6:7], B["ln"][:, 6:7], AF.Copy), inc="A")
        w("v", "A", cpid)
        r("v", lambda e: e.tensor_scalar(B["ln"][:, 1:2], B["ln"][:, 0:1], 1.0 / D, None, ALU.mult),
          inc="V", selfwait=True)
        cen = r("v", lambda e, par=par: e.tensor_scalar(
            B["sqs"][:], B["x2t"][par][:], B["ln"][:, 1:2], None, ALU.subtract),
            inc="V", selfwait=True)
        w("s", "V", cen)
        r("s", lambda e, par=par: e.activation(
            B["x2t"][par][:], B["sqs"][:], AF.Square, accum_out=B["ln"][:, 2:3]))
        sqid = r("s", lambda e: e.activation(
            B["ln"][:, 6:7], B["ln"][:, 6:7], AF.Copy), inc="A")
        w("v", "A", sqid)
        vv_ = r("v", lambda e: e.tensor_scalar(
            B["ln"][:, 3:4], B["ln"][:, 2:3], 1.0 / D, EPS, ALU.mult, ALU.add),
            inc="V", selfwait=True)
        w("s", "V", vv_)
        sqr = r("s", lambda e: e.activation(B["ln"][:, 4:5], B["ln"][:, 3:4], AF.Sqrt), inc="A")
        w("v", "A", sqr)
        r("v", lambda e: e.reciprocal(B["ln"][:, 5:6], B["ln"][:, 4:5]),
          inc="V", selfwait=True)
        ln1[tt] = r("v", lambda e, par=par: e.tensor_scalar(
            B["x2t"][par][:], B["sqs"][:], B["ln"][:, 5:6], None, ALU.mult), inc="V")
        w("s", "V", ln1[tt])
        w("s", "P", trxid.get(tt - 1, 0))
        castid[tt] = r("s", lambda e, par=par: e.copy(B["x2bf"][:], B["x2t"][par][:]), inc="A")
        w("t", "A", castid[tt])

        def xview(grp, tt=tt):
            vv2 = B["x2f"][:].rearrange("p (c q) -> p c q", c=16)
            return vv2[:, 4 * grp:4 * grp + 4, 128 * tt:128 * tt + 128]
        trxid[tt] = transpose_tile(B["x2bf"], xview)
        x2fev[tt] = ev[transpose_tile.seq - 1]
        w("y", "V", ln1[tt])
        x2store[tt] = r("y", lambda e, tt=tt, par=par: e.dma_start(
            out=H["x2s"][128 * tt:128 * tt + 128, :], in_=B["x2t"][par][:]),
            inc="Y%d" % par, amt=16)

    for tt in range(8):
        op_part1(tt)
        if tt >= 1:
            op_part2(tt - 1)
    op_part2(7)

    oproj_done_p = S_.cnt["P"]
    oproj_last_fbev = fbev[31]

    conv_id = {}
    gmm = {}
    umm = {}
    hTT = {}
    silu_id = {}
    for ib in range(64):
        par = ib % 2
        w("y", "V", conv_id.get(ib - 1, 0))
        if ib == 0:
            w("y", "P", oproj_done_p)
            if H["_dbg"]:
                w("y", "Y0", dbg1)
        r("y", lambda e, ib=ib: e.dma_start(out=B["wgst"][:], in_=H["wg"][ib]),
          inc="S0", amt=16)
        wld = r("y", lambda e, ib=ib: e.dma_start(out=B["wust"][:], in_=H["wu"][ib]),
                inc="S0", amt=16)
        w("v", "S0", wld)
        w("v", "P", umm.get(ib - 2, 0))
        r("v", lambda e, par=par: e.tensor_scalar(
            B["wgb"][par][:], B["wgst"][:], -128.0, None, ALU.add))
        conv_id[ib] = r("v", lambda e, par=par: e.tensor_scalar(
            B["wub"][par][:], B["wust"][:], -128.0, None, ALU.add), inc="V")
        for th in range(2):
            n = 2 * ib + th
            s3 = n % 3
            w("t", "V", conv_id[ib])
            w("t", "V", hTT.get(n - 3, 0))
            if n == 0:
                w("t", "A", x2fev[7])
                w("t", "V", oproj_last_fbev)
            for c in range(16):
                last = c == 15
                idx = r("t", lambda e, c=c, th=th, par=par, s3=s3: e.matmul(
                    ps[:, 1024 * s3:1024 * s3 + 512],
                    B["wgb"][par][:, 128 * c:128 * c + 128],
                    B["x2f"][:, 1024 * c + 512 * th:1024 * c + 512 * th + 512],
                    start=(c == 0), stop=(c == 15)), inc=("P" if last else None))
            gmm[n] = idx
            for c in range(16):
                last = c == 15
                idx = r("t", lambda e, c=c, th=th, par=par, s3=s3: e.matmul(
                    ps[:, 1024 * s3 + 512:1024 * s3 + 1024],
                    B["wub"][par][:, 128 * c:128 * c + 128],
                    B["x2f"][:, 1024 * c + 512 * th:1024 * c + 512 * th + 512],
                    start=(c == 0), stop=(c == 15)), inc=("P" if last else None))
            umm[n] = idx
            w("s", "P", gmm[n])
            w("s", "V", hTT.get(n - 1, 0))
            silu_id[n] = r("s", lambda e, ib=ib, s3=s3: e.activation(
                B["sg"][:], ps[:, 1024 * s3:1024 * s3 + 512], AF.Silu,
                bias=0.0, scale=B["gs"][:, ib:ib + 1]), inc="A")
            w("v", "P", umm[n])
            r("v", lambda e, ib=ib, s3=s3: e.tensor_scalar(
                B["usv"][:], ps[:, 1024 * s3 + 512:1024 * s3 + 1024],
                B["us"][:, ib:ib + 1], None, ALU.mult), inc="V", selfwait=True)
            w("v", "A", silu_id[n])
            if n == 0 and H["_dbg"]:
                w("v", "Y0", dbg2)
            if ib == 32 and th == 0:
                w("v", "Y1", x2store[7])
                w("v", "A", castid[7])
            hTT[n] = r("v", lambda e, ib=ib, th=th: e.tensor_tensor(
                B["hf"][:, 1024 * ib + 512 * th:1024 * ib + 512 * th + 512],
                B["sg"][:], B["usv"][:], ALU.mult), inc="V")
    hlast = hTT[127]
    ffa_last_silu = silu_id[127]

    dmm = {}
    fmid = {}
    trf = {}
    tev = {}
    tokstore = {}
    wdc = {}

    def ffb_evac(hc):
        par = hc % 2
        w("v", "P", dmm[4 * hc + 3])
        w("v", "P", trf.get(hc - 1, 0))
        fmid[hc] = r("v", lambda e, hc=hc, par=par: e.tensor_scalar(
            B["fm"][:], ps[:, 1024 * par:1024 * par + 1024],
            B["ds"][:, hc:hc + 1], None, ALU.mult), inc="V", selfwait=True)
        w("t", "V", fmid[hc])
        w("t", "A", tev.get(hc - 1, 0))
        for wv_ in range(8):
            off = 2048 + 128 * wv_
            last = wv_ == 7
            idx = r("t", lambda e, wv_=wv_, off=off: e.transpose(
                ps[:, off:off + 128], B["fm"][:, 128 * wv_:128 * wv_ + 128],
                B["idf"][:]), inc=("P" if last else None))
        trf[hc] = idx
        w("s", "P", trf[hc])
        w("s", "Y0", tokstore.get(hc - 1, 0))
        r("s", lambda e: e.copy(B["tok"][0][:], bk32(4)))
        tev[hc] = r("s", lambda e: e.copy(B["tok"][1][:], bk32(5)), inc="A")
        w("y", "A", tev[hc])
        for gi2 in range(2):
            tokstore[hc] = r("y", lambda e, gi2=gi2, hc=hc: e.dma_start(
                out=H["ffs"].rearrange("(w p) h -> p w h", p=128)[
                    :, 4 * gi2:4 * gi2 + 4, 128 * hc:128 * hc + 128],
                in_=B["tok"][gi2][:].rearrange("p (w h) -> p w h", w=4)),
                inc="Y0", amt=16)

    for hc in range(16):
        par = hc % 2
        for ibg in range(4):
            gi = 4 * hc + ibg
            sp_ = gi % 2
            w("y", "V", wdc.get(gi - 2, 0))
            if gi == 0:
                w("y", "V", hlast)
            ssem = "S%d" % sp_
            ldid = r("y", lambda e, hc=hc, ibg=ibg, sp_=sp_: e.dma_start(
                out=B["wdst"][sp_][:].rearrange("p (g h) -> p g h", g=16),
                in_=H["wd"][hc, 16 * ibg:16 * ibg + 16].rearrange("g p h -> p g h")),
                inc=ssem, amt=16)
            w("v", ssem, ldid)
            w("v", "P", dmm.get(gi - 2, 0))
            wdc[gi] = r("v", lambda e, sp_=sp_: e.tensor_scalar(
                B["wdb"][sp_][:], B["wdst"][sp_][:], -128.0, None, ALU.add), inc="V")
            w("t", "V", wdc[gi])
            if ibg == 0:
                w("t", "V", fmid.get(hc - 2, 0))
                if hc == 0:
                    w("t", "V", hlast)
                    w("t", "A", ffa_last_silu)
            for i16 in range(16):
                ib = 16 * ibg + i16
                for th in range(2):
                    last = (i16 == 15 and th == 1)
                    idx = r("t", lambda e, i16=i16, th=th, ib=ib, par=par, sp_=sp_: e.matmul(
                        ps[:, 1024 * par + 512 * th:1024 * par + 512 * th + 512],
                        B["wdb"][sp_][:, 128 * i16:128 * i16 + 128],
                        B["hf"][:, 1024 * ib + 512 * th:1024 * ib + 512 * th + 512],
                        start=(ib == 0), stop=(ib == 63)), inc=("P" if last else None))
            dmm[gi] = idx
            if ibg == 0 and hc >= 1:
                ffb_evac(hc - 1)
    ffb_evac(15)

    ffb_done_p = dmm[63]

    ln2 = {}
    ystore = {}
    for tt in range(8):
        par = tt % 2
        w("y", "Y0", tokstore[15])
        w("y", "V", ln2.get(tt - 2, 0))
        ssem = "S%d" % par
        r("y", lambda e, tt=tt, par=par: e.dma_start(
            out=B["ffin"][par][:], in_=H["ffs"][128 * tt:128 * tt + 128, :]),
          inc=ssem, amt=16)
        ffld = r("y", lambda e, tt=tt, par=par: e.dma_start(
            out=B["x2in"][par][:], in_=H["x2s"][128 * tt:128 * tt + 128, :]),
            inc=ssem, amt=16)
        w("v", ssem, ffld)
        if tt == 0:
            w("v", "P", ffb_done_p)
            w("v", "Y0", tokstore[15])
        w("v", "Y%d" % par, ystore.get(tt - 2, 0))
        tta = r("v", lambda e, par=par: e.tensor_tensor(
            B["outt"][par][:], B["ffin"][par][:], B["x2in"][par][:], ALU.add),
          inc="V", selfwait=True)
        w("s", "V", tta)
        r("s", lambda e, par=par: e.activation(
            B["sq2"][:], B["outt"][par][:], AF.Copy, accum_out=B["ln"][:, 8:9]))
        cpid = r("s", lambda e: e.activation(
            B["ln"][:, 14:15], B["ln"][:, 14:15], AF.Copy), inc="A")
        w("v", "A", cpid)
        r("v", lambda e: e.tensor_scalar(B["ln"][:, 9:10], B["ln"][:, 8:9], 1.0 / D, None, ALU.mult),
          inc="V", selfwait=True)
        cen = r("v", lambda e, par=par: e.tensor_scalar(
            B["sq2"][:], B["outt"][par][:], B["ln"][:, 9:10], None, ALU.subtract),
            inc="V", selfwait=True)
        w("s", "V", cen)
        r("s", lambda e, par=par: e.activation(
            B["outt"][par][:], B["sq2"][:], AF.Square, accum_out=B["ln"][:, 10:11]))
        sqid = r("s", lambda e: e.activation(
            B["ln"][:, 14:15], B["ln"][:, 14:15], AF.Copy), inc="A")
        w("v", "A", sqid)
        vv_ = r("v", lambda e: e.tensor_scalar(
            B["ln"][:, 11:12], B["ln"][:, 10:11], 1.0 / D, EPS, ALU.mult, ALU.add),
            inc="V", selfwait=True)
        w("s", "V", vv_)
        sqr = r("s", lambda e: e.activation(B["ln"][:, 12:13], B["ln"][:, 11:12], AF.Sqrt), inc="A")
        w("v", "A", sqr)
        r("v", lambda e: e.reciprocal(B["ln"][:, 13:14], B["ln"][:, 12:13]),
          inc="V", selfwait=True)
        ln2[tt] = r("v", lambda e, par=par: e.tensor_scalar(
            B["outt"][par][:], B["sq2"][:], B["ln"][:, 13:14], None, ALU.mult), inc="V")
        w("y", "V", ln2[tt])
        ystore[tt] = r("y", lambda e, tt=tt, par=par: e.dma_start(
            out=H["y"][128 * tt:128 * tt + 128, :], in_=B["outt"][par][:]),
            inc="Y%d" % par, amt=16)

    w("y", "Y0", S_.cnt["Y0"])
    w("y", "Y1", S_.cnt["Y1"])


def _pack(inputs):
    f32 = np.float32
    Wq = np.asarray(inputs["L0_attn_Wq"], f32)
    bq = np.asarray(inputs["L0_attn_bq"], f32)
    Wk = np.asarray(inputs["L0_attn_Wk"], f32)
    bk = np.asarray(inputs["L0_attn_bk"], f32)
    Wv = np.asarray(inputs["L0_attn_Wv"], f32)
    bv = np.asarray(inputs["L0_attn_bv"], f32)
    Wo = np.asarray(inputs["L0_attn_Wo"], f32)
    bo = np.asarray(inputs["L0_attn_bo"], f32)
    gw = np.ascontiguousarray(np.asarray(inputs["L0_ffn_gate_w"], np.int32))
    uw = np.ascontiguousarray(np.asarray(inputs["L0_ffn_up_w"], np.int32))
    dw = np.ascontiguousarray(np.asarray(inputs["L0_ffn_down_w"], np.int32))
    gsc = np.asarray(inputs["L0_ffn_gate_scale"], f32)
    usc = np.asarray(inputs["L0_ffn_up_scale"], f32)
    dsc = np.asarray(inputs["L0_ffn_down_scale"], f32)

    shared = {}
    wq_t = np.empty((16, 128, 2048), f32)
    bq_t = np.empty((128, 16), f32)
    for j in range(16):
        cols = np.r_[64 * j:64 * j + 64, 64 * (j + 16):64 * (j + 16) + 64]
        tmp = Wq[:, cols].reshape(16, 128, 128)
        wq_t[j] = tmp.transpose(1, 0, 2).reshape(128, 2048)
        bq_t[:, j] = bq[cols]
    shared["wq_t"] = np.ascontiguousarray(wq_t)
    shared["bq_t"] = np.ascontiguousarray(bq_t)
    col_order = np.empty(512, np.int64)
    for cc in range(4):
        for dd in range(128):
            kv = cc + (4 if dd >= 64 else 0)
            col_order[128 * cc + dd] = 64 * kv + dd % 64
    wk_re = Wk[:, col_order]
    shared["wk_t"] = np.ascontiguousarray(
        wk_re.reshape(16, 128, 512).transpose(1, 0, 2).reshape(128, 8192))
    shared["bk_t"] = np.ascontiguousarray(bk[col_order].reshape(4, 128).T)
    shared["wv_t"] = np.ascontiguousarray(
        Wv.reshape(16, 128, 512).transpose(1, 0, 2).reshape(128, 8192))
    shared["bv_bc"] = np.ascontiguousarray(np.broadcast_to(bv, (128, 512)))
    wo_t = np.empty((16, 128, 2048), f32)
    for c in range(16):
        rows = np.r_[64 * c:64 * c + 64, 64 * (c + 16):64 * (c + 16) + 64]
        wo_t[c] = Wo[rows, :]
    shared["wo_t"] = np.ascontiguousarray(wo_t)
    shared["bo_bc"] = np.ascontiguousarray(np.broadcast_to(bo, (128, 2048)))
    shared["wg_t"] = np.ascontiguousarray(
        gw.reshape(64, 128, 16, 128).transpose(0, 3, 2, 1).reshape(64, 128, 2048))
    shared["wu_t"] = np.ascontiguousarray(
        uw.reshape(64, 128, 16, 128).transpose(0, 3, 2, 1).reshape(64, 128, 2048))
    shared["wd_t"] = np.ascontiguousarray(
        dw.reshape(16, 128, 64, 128).transpose(0, 2, 3, 1))
    shared["gs_t"] = np.ascontiguousarray(gsc.reshape(64, 128).T)
    shared["us_t"] = np.ascontiguousarray(usc.reshape(64, 128).T)
    shared["ds_t"] = np.ascontiguousarray(dsc.reshape(16, 128).T)
    shared["identb"] = np.eye(128, dtype=np.float32).astype(ml_dtypes.bfloat16)
    shared["identf"] = np.eye(128, dtype=np.float32)
    shared["ones64b"] = np.ones((128, 128), np.float32).astype(ml_dtypes.bfloat16)
    shared["onesf"] = np.ones((128, 64), np.float32)
    return shared


def kernel(**inputs):
    x = np.ascontiguousarray(np.asarray(inputs["L0_attn_x"], np.float32).reshape(SEQ, D))
    shared = _pack(inputs)
    nc = build_nc()
    in_maps = []
    for c in range(NCORES):
        m = dict(shared)
        m["x_full"] = x
        m["x_own"] = np.ascontiguousarray(x[T * c:T * (c + 1)])
        in_maps.append(m)
    res = run_bass_kernel_spmd(nc, in_maps, core_ids=list(range(NCORES)))
    y = np.concatenate([np.asarray(res.results[c]["y"]) for c in range(NCORES)], axis=0)
    return y.reshape(1, SEQ, D).astype(np.float32)


if __name__ == "__main__":
    import reference as _ref
    inp = _ref.setup_inputs()
    out = kernel(**{k: np.asarray(v) for k, v in inp.items()})
    want = np.asarray(_ref.reference(**inp))
    err = np.abs(out - want)
    denom = np.mean(want.astype(np.float64) ** 2)
    print("rvar:", float(np.mean((out.astype(np.float64) - want.astype(np.float64)) ** 2) / denom))
    print("maxabs:", float(err.max()))


# version 11
# speedup vs baseline: 1.3047x; 1.0343x over previous
import numpy as np
import ml_dtypes
import concourse.bass as bass
import concourse.mybir as mybir
from concourse.bass_utils import run_bass_kernel_spmd

F32 = mybir.dt.float32
BF16 = mybir.dt.bfloat16
I32 = mybir.dt.int32
AF = mybir.ActivationFunctionType
ALU = mybir.AluOpType
AX = mybir.AxisListType

NCORES = 8
SEQ = 8192
D = 2048
T = SEQ // NCORES
NKT = SEQ // 128
EPS = 1e-5


def build_nc():
    nc = bass.Bass("TRN2", target_bir_lowering=False)

    H = {}
    H["xf"] = nc.dram_tensor("x_full", [SEQ, D], F32, kind="ExternalInput").ap()
    H["xo"] = nc.dram_tensor("x_own", [T, D], F32, kind="ExternalInput").ap()
    H["wq"] = nc.dram_tensor("wq_t", [16, 128, 2048], F32, kind="ExternalInput").ap()
    H["bq"] = nc.dram_tensor("bq_t", [128, 16], F32, kind="ExternalInput").ap()
    H["wk"] = nc.dram_tensor("wk_t", [128, 8192], F32, kind="ExternalInput").ap()
    H["bk"] = nc.dram_tensor("bk_t", [128, 4], F32, kind="ExternalInput").ap()
    H["wv"] = nc.dram_tensor("wv_t", [128, 8192], F32, kind="ExternalInput").ap()
    H["bv"] = nc.dram_tensor("bv_bc", [128, 512], F32, kind="ExternalInput").ap()
    H["wo"] = nc.dram_tensor("wo_t", [16, 128, 2048], F32, kind="ExternalInput").ap()
    H["bo"] = nc.dram_tensor("bo_bc", [128, 2048], F32, kind="ExternalInput").ap()
    H["wg"] = nc.dram_tensor("wg_t", [64, 128, 2048], I32, kind="ExternalInput").ap()
    H["wu"] = nc.dram_tensor("wu_t", [64, 128, 2048], I32, kind="ExternalInput").ap()
    H["wd"] = nc.dram_tensor("wd_t", [16, 64, 128, 128], I32, kind="ExternalInput").ap()
    H["gs"] = nc.dram_tensor("gs_t", [128, 64], F32, kind="ExternalInput").ap()
    H["us"] = nc.dram_tensor("us_t", [128, 64], F32, kind="ExternalInput").ap()
    H["ds"] = nc.dram_tensor("ds_t", [128, 16], F32, kind="ExternalInput").ap()
    H["idb"] = nc.dram_tensor("identb", [128, 128], BF16, kind="ExternalInput").ap()
    H["idf"] = nc.dram_tensor("identf", [128, 128], F32, kind="ExternalInput").ap()
    H["o64"] = nc.dram_tensor("ones64b", [128, 128], BF16, kind="ExternalInput").ap()
    H["of"] = nc.dram_tensor("onesf", [128, 64], F32, kind="ExternalInput").ap()
    H["y"] = nc.dram_tensor("y", [T, D], F32, kind="ExternalOutput").ap()
    import os
    H["_dbg"] = bool(os.environ.get("KDEBUG"))
    if H["_dbg"]:
        H["dkt"] = nc.dram_tensor("dbg_kt", [128, 32768], BF16, kind="ExternalOutput").ap()
        H["dv"] = nc.dram_tensor("dbg_v", [128, 32768], BF16, kind="ExternalOutput").ap()
        H["dqt"] = nc.dram_tensor("dbg_qt", [128, 16384], BF16, kind="ExternalOutput").ap()
        H["dao"] = nc.dram_tensor("dbg_ao", [128, 16384], BF16, kind="ExternalOutput").ap()
        H["dwo"] = nc.dram_tensor("dbg_wo", [128, 32768], BF16, kind="ExternalOutput").ap()
        H["ds_"] = nc.dram_tensor("dbg_s", [T, D], F32, kind="ExternalOutput").ap()
    import os as _os
    _scrkind = "ExternalOutput" if _os.environ.get("KDEBUG") else "Internal"
    H["x2s"] = nc.dram_tensor("x2_scr", [T, D], F32, kind=_scrkind).ap()
    H["ffs"] = nc.dram_tensor("ffn_scr", [T, D], F32, kind=_scrkind).ap()

    SB_BASE = 16512

    def at(name, shape, dtype, off):
        return nc.alloc_sbuf_tensor_at(name, shape, dtype, offset=SB_BASE + off)

    B = {}
    B["kt"] = at("kt_b", [128, 32768], BF16, 0)
    B["v"] = at("v_b", [128, 32768], BF16, 65536)
    B["wo"] = at("wo_b", [128, 32768], BF16, 0)
    B["hf"] = at("hfull", [128, 65536], BF16, 0)
    B["bo"] = at("bo_s", [128, 2048], F32, 65536)
    B["xr"] = [at(f"xr{i}", [128, 2048], F32, 73728 + 8192 * i) for i in range(2)]
    B["x2t"] = [at(f"x2t{i}", [128, 2048], F32, 90112 + 8192 * i) for i in range(2)]
    B["x2bf"] = at("x2bf", [128, 2048], BF16, 106496)
    B["sqs"] = at("sqs", [128, 2048], F32, 110592)
    B["xbo"] = at("xbo", [128, 2048], F32, 118784)
    B["qt"] = at("qt_b", [128, 16384], BF16, 131072)
    B["x2f"] = at("x2f", [128, 16384], BF16, 131072)
    B["xfblk"] = at("xfblk", [128, 16384], BF16, 131072)
    B["wkb"] = at("wkb", [128, 8192], BF16, 163840)
    B["ffin"] = [at(f"ffin{i}", [128, 2048], F32, 131072 + 8192 * i) for i in range(2)]
    B["x2in"] = [at(f"x2in{i}", [128, 2048], F32, 147456 + 8192 * i) for i in range(2)]
    B["ao"] = at("ao_b", [128, 16384], BF16, 163840)
    B["xfo"] = at("xfown", [128, 16384], BF16, 163840)
    B["wvb"] = at("wvb", [128, 8192], BF16, 180224)
    B["bv"] = at("bv_s", [128, 512], F32, 206752)
    B["wgst"] = at("wgstage", [128, 2048], I32, 163840)
    B["wust"] = at("wustage", [128, 2048], I32, 172032)
    B["wgb"] = [at(f"wgb{i}", [128, 2048], BF16, 180224 + 4096 * i) for i in range(2)]
    B["wub"] = [at(f"wub{i}", [128, 2048], BF16, 188416 + 4096 * i) for i in range(2)]
    B["wdst"] = [at(f"wdst{i}", [128, 2048], I32, 163840 + 8192 * i) for i in range(2)]
    B["wdb"] = [at(f"wdb{i}", [128, 2048], BF16, 180224 + 4096 * i) for i in range(2)]
    B["fm"] = at("fm_s", [128, 1024], F32, 188416)
    B["tok"] = [at(f"tokst{i}", [128, 512], F32, 192512 + 2048 * i) for i in range(2)]
    B["outt"] = [at(f"outt{i}", [128, 2048], F32, 163840 + 8192 * i) for i in range(2)]
    B["sq2"] = at("sq2", [128, 2048], F32, 180224)
    B["idb"] = at("idb_s", [128, 128], BF16, 196608)
    B["idf"] = at("idf_s", [128, 128], F32, 196864)
    B["o64"] = at("o64_s", [128, 128], BF16, 197376)
    B["of"] = at("of_s", [128, 64], F32, 197632)
    B["bq"] = at("bq_s", [128, 16], F32, 197888)
    B["bk"] = at("bk_s", [128, 4], F32, 197952)
    B["gs"] = at("gs_s", [128, 64], F32, 197984)
    B["us"] = at("us_s", [128, 64], F32, 198240)
    B["ds"] = at("ds_s", [128, 16], F32, 198496)
    B["ln"] = at("lnst", [128, 16], F32, 210848)
    B["xst"] = [at(f"xstag{i}", [128, 2048], BF16, 198560 + 4096 * i) for i in range(2)]
    B["pt"] = [at(f"pt{i}", [128, 1024], BF16, 198560 + 2048 * i) for i in range(4)]
    B["rd"] = at("rd_t", [128, 512], F32, 206752)
    B["bcs"] = at("bcs", [128, 512], F32, 208800)
    B["sg"] = at("sg_s", [128, 512], F32, 206752)
    B["rdb"] = at("rdb", [128, 512], BF16, 210912)
    B["usv"] = at("usv_s", [128, 512], F32, 208800)
    B["borow"] = at("borow", [1, 2048], BF16, 118784)

    B["nc"] = nc
    B["ps"] = nc.alloc_psum_tensor("ps32", [128, 3072], F32)
    B["psbf"] = nc.alloc_psum_tensor("psbf", [128, 2048], BF16)

    from contextlib import ExitStack
    with ExitStack() as es:
        ec = es.enter_context
        sems = {k: ec(nc.semaphore(f"sem{k}")) for k in
                ("P", "A", "V", "G0", "G1", "W", "S0", "S1", "Y0", "Y1")}
        block = ec(nc.Block())

        class Sched:
            def __init__(self, only, eng):
                self.only = only
                self.eng = eng
                self.cnt = {k: 0 for k in
                            ("P", "A", "V", "G0", "G1", "W", "S0", "S1", "Y0", "Y1")}

            def wait(self, engname, sem, val):
                if val is None or val <= 0:
                    return
                if self.only == engname:
                    self.eng.wait_ge(sems[sem], val)

            def run(self, engname, fn, inc=None, amt=1, selfwait=False):
                if inc is not None:
                    self.cnt[inc] += amt
                if self.only == engname:
                    inst = fn(self.eng)
                    if inc is not None:
                        inst.then_inc(sems[inc], amt)
                        if selfwait:
                            self.eng.wait_ge(sems[inc], self.cnt[inc])
                return self.cnt[inc] if inc is not None else None

        @block.gpsimd
        def _(g):
            emit_program(Sched("g", g), H, B)

        @block.tensor
        def _(t):
            emit_program(Sched("t", t), H, B)

        @block.scalar
        def _(s):
            emit_program(Sched("s", s), H, B)

        @block.vector
        def _(v):
            emit_program(Sched("v", v), H, B)

        @block.sync
        def _(sy):
            emit_program(Sched("y", sy), H, B)

    return nc

def emit_program(S_, H, B):
    w = S_.wait
    r = S_.run
    ps = B["ps"]
    psbf = B["psbf"]

    def bk32(i, n=512):
        return ps[:, 512 * i:512 * i + n]

    for k in ("idb", "idf", "o64", "of", "bq", "bk", "gs", "us", "ds", "bv"):
        r("g", lambda e, k=k: e.dma_start(out=B[k][:], in_=H[k]), inc="W", amt=16)
    r("g", lambda e: e.dma_start(out=B["wkb"][:], in_=H["wk"]), inc="W", amt=16)
    wconsts = r("g", lambda e: e.dma_start(out=B["wvb"][:], in_=H["wv"]), inc="W", amt=16)

    ev = {}

    def transpose_tile(src_buf, dst3):
        g0 = transpose_tile.seq
        for grp in range(4):
            half = grp % 2
            w("t", "A", ev.get(g0 + grp - 2, 0))
            for cc in range(4):
                c = 4 * grp + cc
                last = cc == 3
                tid = r("t", lambda e, c=c, half=half, cc=cc, src_buf=src_buf: e.transpose(
                    psbf[:, 1024 * half + 128 * cc:1024 * half + 128 * cc + 128],
                    src_buf[:, 128 * c:128 * c + 128], B["idb"][:]),
                    inc=("P" if last else None))
            w("s", "P", tid)
            ev[g0 + grp] = r("s", lambda e, half=half, grp=grp, dst3=dst3: e.copy(
                dst3(grp),
                psbf[:, 1024 * half:1024 * half + 512].rearrange("p (g q) -> p g q", g=4)),
                inc="A")
        transpose_tile.seq += 4
        return tid
    transpose_tile.seq = 0

    kev = {}
    vev = {}
    last_tr = {}
    NBLK = SEQ // 512
    for blk in range(NBLK):
        slot_x = 8192 * (blk % 2)
        for s4 in range(4):
            ti = 4 * blk + s4
            buf = ti % 2
            w("g", "P", last_tr.get(ti - 2, 0))
            gsem = "G%d" % buf
            r("g", lambda e, ti=ti, buf=buf: e.dma_start(
                out=B["xst"][buf][:], in_=H["xf"][128 * ti:128 * ti + 128, :]),
              inc=gsem, amt=16)
            gneed = S_.cnt[gsem]
            if ti < 2:
                w("t", "W", wconsts)
                w("v", "W", wconsts)
            w("t", gsem, gneed)

            def dview(grp, slot_x=slot_x, s4=s4):
                vv = B["xfblk"][:].rearrange("p (c q) -> p c q", q=512)
                c0 = slot_x // 512 + 4 * grp
                return vv[:, c0:c0 + 4, 128 * s4:128 * s4 + 128]
            last_tr[ti] = transpose_tile(B["xst"][buf], dview)
        w("t", "A", ev[transpose_tile.seq - 1])
        for cc in range(4):
            kseq = 4 * blk + cc
            pb = cc % 2
            w("t", "V", kev.get(kseq - 2, 0))
            for c in range(16):
                last = c == 15
                kmm = r("t", lambda e, cc=cc, c=c, pb=pb, slot_x=slot_x: e.matmul(
                    bk32(pb),
                    B["wkb"][:, 512 * c + 128 * cc:512 * c + 128 * cc + 128],
                    B["xfblk"][:, slot_x + 512 * c:slot_x + 512 * c + 512],
                    start=(c == 0), stop=(c == 15)), inc=("P" if last else None))
            w("v", "P", kmm)
            kev[kseq] = r("v", lambda e, cc=cc, blk=blk, pb=pb: e.tensor_scalar(
                B["kt"][:, 8192 * cc + 512 * blk:8192 * cc + 512 * blk + 512],
                bk32(pb), B["bk"][:, cc:cc + 1], None, ALU.add),
                inc="V")
        for s4 in range(4):
            ti = 4 * blk + s4
            vb = 2 + ti % 2
            w("t", "V", vev.get(ti - 2, 0))
            for c in range(16):
                last = c == 15
                vmm = r("t", lambda e, c=c, vb=vb, slot_x=slot_x, s4=s4: e.matmul(
                    bk32(vb), B["xfblk"][:, slot_x + 512 * c + 128 * s4:slot_x + 512 * c + 128 * s4 + 128],
                    B["wvb"][:, 512 * c:512 * c + 512],
                    start=(c == 0), stop=(c == 15)), inc=("P" if last else None))
            w("v", "P", vmm)
            vev[ti] = r("v", lambda e, ti=ti, vb=vb: e.tensor_tensor(
                B["v"][:, 512 * ti:512 * ti + 512], bk32(vb), B["bv"][:], ALU.add),
                inc="V")

    w("s", "V", vev[NKT - 1])
    for s8 in range(8):
        buf = s8 % 2
        w("g", "P", last_tr.get(NKT + s8 - 2, 0))
        gsem = "G%d" % buf
        r("g", lambda e, s8=s8, buf=buf: e.dma_start(
            out=B["xst"][buf][:], in_=H["xo"][128 * s8:128 * s8 + 128, :]),
          inc=gsem, amt=16)
        gneed = S_.cnt[gsem]
        w("t", gsem, gneed)

        def oview(grp, s8=s8):
            vv = B["xfo"][:].rearrange("p (c q) -> p c q", c=16)
            return vv[:, 4 * grp:4 * grp + 4, 128 * s8:128 * s8 + 128]
        last_tr[NKT + s8] = transpose_tile(B["xst"][buf], oview)

    qmm = {}
    qev = {}
    for j in range(16):
        buf = j % 2
        w("g", "P", qmm.get(j - 2, last_tr[NKT + 7]))
        gsem = "G%d" % buf
        r("g", lambda e, j=j, buf=buf: e.dma_start(
            out=B["xst"][buf][:], in_=H["wq"][j]), inc=gsem, amt=16)
        gneed = S_.cnt[gsem]
        w("t", gsem, gneed)
        qb_ = j % 2
        w("t", "V", qev.get(j - 2, max(kev[4 * (SEQ // 512 - 1) + 3], vev[NKT - 1])))
        for c in range(16):
            for hh in range(2):
                last = (c == 15 and hh == 1)
                idx = r("t", lambda e, c=c, hh=hh, qb_=qb_, buf=buf: e.matmul(
                    ps[:, 1024 * qb_ + 512 * hh:1024 * qb_ + 512 * hh + 512],
                    B["xst"][buf][:, 128 * c:128 * c + 128],
                    B["xfo"][:, 1024 * c + 512 * hh:1024 * c + 512 * hh + 512],
                    start=(c == 0), stop=(c == 15)), inc=("P" if last else None))
                if last:
                    qmm[j] = idx
        w("v", "P", qmm[j])
        qev[j] = r("v", lambda e, j=j, qb_=qb_: e.tensor_scalar(
            B["qt"][:, 1024 * j:1024 * j + 1024], ps[:, 1024 * qb_:1024 * qb_ + 1024],
            B["bq"][:, j:j + 1], None, ALU.add), inc="V")

    if H["_dbg"]:
        w("y", "V", max(qev[15], vev[NKT - 1], kev[4 * (SEQ // 512 - 1) + 3]))
        r("y", lambda e: e.dma_start(out=H["dkt"], in_=B["kt"][:]), inc="Y0", amt=16)
        r("y", lambda e: e.dma_start(out=H["dv"], in_=B["v"][:]), inc="Y0", amt=16)
        dbg0 = r("y", lambda e: e.dma_start(out=H["dqt"], in_=B["qt"][:]), inc="Y0", amt=16)
    psA = ps[:, 2048:2560]
    psB = ps[:, 2560:3072]
    pair_exp_base = qev[15] * 0 + S_.cnt["A"]
    prev_tb = vev[NKT - 1]
    prev_av = {}
    sc_last = {}
    wo_dma = {}
    qt_ready = qev[15]
    attn_first = True
    for j in range(16):
        for qb in range(2):
            cc = j // 4
            kvA = j // 4
            kvB = 4 + j // 4
            exp_id = {}
            av_id = {}
            sc_id = {}

            def emit_av(kt, kvA=kvA, kvB=kvB, exp_id=exp_id, av_id=av_id,
                        first_gate=None):
                pti = kt % 4
                w("t", "A", exp_id[kt])
                if kt == 0:
                    w("t", "V", first_gate)
                st = kt == 0
                sp = kt == 63
                r("t", lambda e, kt=kt, pti=pti, kvA=kvA, st=st, sp=sp: e.matmul(
                    psA[0:64, :], B["v"][:, 512 * kt + 64 * kvA:512 * kt + 64 * kvA + 64],
                    B["pt"][pti][:, 0:512], start=st, stop=sp, tile_position=(0, 0),
                    skip_group_check=True))
                r("t", lambda e, pti=pti, st=st, sp=sp: e.matmul(
                    psA[64:128, :], B["o64"][:, 0:64], B["pt"][pti][:, 512:1024],
                    start=st, stop=sp, tile_position=(0, 64), skip_group_check=True))
                r("t", lambda e, kt=kt, pti=pti, kvB=kvB, st=st, sp=sp: e.matmul(
                    psB[64:128, :], B["v"][:, 512 * kt + 64 * kvB:512 * kt + 64 * kvB + 64],
                    B["pt"][pti][:, 512:1024], start=st, stop=sp, tile_position=(0, 64),
                    skip_group_check=True))
                av_id[kt] = r("t", lambda e, pti=pti, st=st, sp=sp: e.matmul(
                    psB[0:64, :], B["o64"][:, 0:64], B["pt"][pti][:, 0:512],
                    start=st, stop=sp, tile_position=(0, 0), skip_group_check=True), inc="P")

            gate0 = prev_tb

            def emit_sc(kt, cc=cc, j=j, qb=qb, exp_id=exp_id, sc_id=sc_id):
                scb = kt % 2
                w("t", "A", exp_id.get(kt - 2, pair_exp_base))
                if attn_first and kt == 0:
                    w("t", "V", qt_ready)
                r("t", lambda e, kt=kt, scb=scb: e.matmul(
                    bk32(2 * scb),
                    B["kt"][0:64, 8192 * cc + 128 * kt:8192 * cc + 128 * kt + 128],
                    B["qt"][0:64, 1024 * j + 512 * qb:1024 * j + 512 * qb + 512],
                    start=True, stop=True, tile_position=(0, 0)))
                sc_id[kt] = r("t", lambda e, kt=kt, scb=scb: e.matmul(
                    bk32(2 * scb + 1),
                    B["kt"][64:128, 8192 * cc + 128 * kt:8192 * cc + 128 * kt + 128],
                    B["qt"][64:128, 1024 * j + 512 * qb:1024 * j + 512 * qb + 512],
                    start=True, stop=True, tile_position=(64, 0)), inc="P")

            def emit_exp(kt, exp_id=exp_id, sc_id=sc_id, av_id=av_id):
                ptw = av_id.get(kt - 4, prev_av.get(60 + kt, 0))
                w("s", "P", max(sc_id[kt], ptw))
                exp_id[kt] = r("s", lambda e, kt=kt: e.activation(
                    B["pt"][kt % 4][:], ps[:, 1024 * (kt % 2):1024 * (kt % 2) + 1024],
                    AF.Exp, bias=0.0, scale=0.125), inc="A")

            for k2 in range(0, 64, 2):
                emit_sc(k2)
                emit_exp(k2)
                emit_sc(k2 + 1)
                emit_exp(k2 + 1)
                if k2 >= 2:
                    emit_av(k2 - 2, first_gate=gate0)
                    emit_av(k2 - 1, first_gate=gate0)
            emit_av(62, first_gate=gate0)
            emit_av(63, first_gate=gate0)
            attn_first = False

            acols = slice(1024 * j + 512 * qb, 1024 * j + 512 * qb + 512)
            w("v", "P", av_id[63])
            def _recip(e, dst, src_):
                with B["nc"].allow_low_precision(reason="softmax 1/D broadcast via bf16 matmul"):
                    return e.reciprocal(dst, src_)
            r("v", lambda e: _recip(e, B["rdb"][0:1, :], psB[0:1, :]),
              inc="V", selfwait=True)
            rid = r("v", lambda e: _recip(e, B["rdb"][64:65, :], psA[64:65, :]), inc="V")
            w("t", "V", rid)
            bA = r("t", lambda e: e.matmul(
                psB[0:64, :], B["o64"][0:1, 0:64], B["rdb"][0:1, :],
                start=True, stop=True), inc="P")
            w("v", "P", bA)
            r("v", lambda e: e.tensor_copy(B["bcs"][0:64, :], psB[0:64, :]),
              inc="V", selfwait=True)
            tA = r("v", lambda e, acols=acols: e.tensor_tensor(
                B["ao"][0:64, acols], psA[0:64, :], B["bcs"][0:64, :], ALU.mult),
                inc="V")
            w("t", "V", tA)
            bB = r("t", lambda e: e.matmul(
                psA[64:128, :], B["o64"][64:65, 0:64], B["rdb"][64:65, :],
                start=True, stop=True), inc="P")
            w("v", "P", bB)
            r("v", lambda e: e.tensor_copy(B["bcs"][64:128, :], psA[64:128, :]),
              inc="V", selfwait=True)
            tB = r("v", lambda e, acols=acols: e.tensor_tensor(
                B["ao"][64:128, acols], psB[64:128, :], B["bcs"][64:128, :], ALU.mult),
                inc="V")
            prev_tb = tB
            prev_av = {k: v for k, v in av_id.items() if k >= 60}
            pair_exp_base = exp_id[63]
            sc_last[(j, qb)] = sc_id[63]
        if j % 4 == 3:
            grp4 = j // 4
            for c4 in range(4):
                c = 4 * grp4 + c4
                w("g", "P", sc_last[(j, 1)])
                if H["_dbg"]:
                    w("g", "Y0", dbg0)
                wo_dma[c] = r("g", lambda e, c=c: e.dma_start(
                    out=B["wo"][:, 2048 * c:2048 * c + 2048], in_=H["wo"][c]),
                    inc="W", amt=16)

    attn_done_v = prev_tb
    attn_done_p = S_.cnt["P"]
    if H["_dbg"]:
        w("y", "V", attn_done_v)
        dbg1 = r("y", lambda e: e.dma_start(out=H["dao"], in_=B["ao"][:]), inc="Y0", amt=16)
    attn_last_exp = pair_exp_base

    w("g", "P", attn_done_p)
    bo_id = r("g", lambda e: e.dma_start(out=B["borow"][:], in_=H["bo"][0:1, :]), inc="W", amt=16)

    if H["_dbg"]:
        w("y", "W", wo_dma[15])
        dbg2 = r("y", lambda e: e.dma_start(out=H["dwo"], in_=B["wo"][:]), inc="Y0", amt=16)
    fbid = {}
    fbev = {}
    xboid = {}
    ln1 = {}
    castid = {}
    trxid = {}
    x2fev = {}
    x2store = {}

    def op_part1(tt):
        par = tt % 2
        w("y", "V", fbev.get(4 * (tt - 2) + 3, 0))
        if tt < 2:
            w("y", "P", attn_done_p)
        ssem = "S%d" % par
        xrid = r("y", lambda e, tt=tt, par=par: e.dma_start(
            out=B["xr"][par][:], in_=H["xo"][128 * tt:128 * tt + 128, :]),
            inc=ssem, amt=16)
        w("v", ssem, xrid)
        w("v", "A", castid.get(tt - 2, 0))
        w("v", "Y%d" % par, x2store.get(tt - 2, 0))
        for fb in range(4):
            n = 4 * tt + fb
            slot = n % 4
            w("t", "V", fbev.get(n - 4, 0))
            if n == 0:
                w("t", "V", attn_done_v)
                w("t", "A", attn_last_exp)
                w("t", "W", wo_dma[15])
            if n == 0:
                w("t", "W", bo_id)
            for c in range(16):
                r("t", lambda e, c=c, fb=fb, tt=tt, slot=slot: e.matmul(
                    bk32(slot), B["ao"][:, 1024 * c + 128 * tt:1024 * c + 128 * tt + 128],
                    B["wo"][:, 2048 * c + 512 * fb:2048 * c + 512 * fb + 512],
                    start=(c == 0), stop=False))
            idx = r("t", lambda e, fb=fb, slot=slot: e.matmul(
                bk32(slot), B["o64"][0:1, :], B["borow"][0:1, 512 * fb:512 * fb + 512],
                start=False, stop=True), inc="P")
            fbid[(tt, fb)] = idx
            w("v", "P", idx)
            fbev[n] = r("v", lambda e, fb=fb, par=par, slot=slot: e.tensor_tensor(
                B["x2t"][par][:, 512 * fb:512 * fb + 512], bk32(slot),
                B["xr"][par][:, 512 * fb:512 * fb + 512], ALU.add), inc="V",
                selfwait=(fb == 3))

    def op_part2(tt):
        par = tt % 2
        w("s", "V", fbev[4 * tt + 3])
        r("s", lambda e, par=par: e.activation(
            B["sqs"][:], B["x2t"][par][:], AF.Copy, accum_out=B["ln"][:, 0:1]))
        cpid = r("s", lambda e: e.activation(
            B["ln"][:, 6:7], B["bk"][:, 0:1], AF.Copy), inc="A")
        w("v", "A", cpid)
        r("v", lambda e: e.tensor_scalar(B["ln"][:, 1:2], B["ln"][:, 0:1], 1.0 / D, None, ALU.mult),
          inc="V", selfwait=True)
        cen = r("v", lambda e, par=par: e.tensor_scalar(
            B["sqs"][:], B["x2t"][par][:], B["ln"][:, 1:2], None, ALU.subtract),
            inc="V", selfwait=True)
        w("s", "V", cen)
        r("s", lambda e, par=par: e.activation(
            B["x2t"][par][:], B["sqs"][:], AF.Square, accum_out=B["ln"][:, 2:3]))
        sqid = r("s", lambda e: e.activation(
            B["ln"][:, 6:7], B["bk"][:, 0:1], AF.Copy), inc="A")
        w("v", "A", sqid)
        vv_ = r("v", lambda e: e.tensor_scalar(
            B["ln"][:, 3:4], B["ln"][:, 2:3], 1.0 / D, EPS, ALU.mult, ALU.add),
            inc="V", selfwait=True)
        w("s", "V", vv_)
        sqr = r("s", lambda e: e.activation(B["ln"][:, 4:5], B["ln"][:, 3:4], AF.Sqrt), inc="A")
        w("v", "A", sqr)
        r("v", lambda e: e.reciprocal(B["ln"][:, 5:6], B["ln"][:, 4:5]),
          inc="V", selfwait=True)
        ln1[tt] = r("v", lambda e, par=par: e.tensor_scalar(
            B["x2t"][par][:], B["sqs"][:], B["ln"][:, 5:6], None, ALU.mult), inc="V")
        w("s", "V", ln1[tt])
        w("s", "P", trxid.get(tt - 1, 0))
        castid[tt] = r("s", lambda e, par=par: e.copy(B["x2bf"][:], B["x2t"][par][:]), inc="A")
        w("t", "A", castid[tt])

        def xview(grp, tt=tt):
            vv2 = B["x2f"][:].rearrange("p (c q) -> p c q", c=16)
            return vv2[:, 4 * grp:4 * grp + 4, 128 * tt:128 * tt + 128]
        trxid[tt] = transpose_tile(B["x2bf"], xview)
        x2fev[tt] = ev[transpose_tile.seq - 1]
        w("y", "V", ln1[tt])
        x2store[tt] = r("y", lambda e, tt=tt, par=par: e.dma_start(
            out=H["x2s"][128 * tt:128 * tt + 128, :], in_=B["x2t"][par][:]),
            inc="Y%d" % par, amt=16)

    for tt in range(8):
        op_part1(tt)
        if tt >= 1:
            op_part2(tt - 1)
    op_part2(7)

    oproj_done_p = S_.cnt["P"]
    oproj_last_fbev = fbev[31]

    conv_id = {}
    gmm = {}
    umm = {}
    hTT = {}
    silu_id = {}
    for ib in range(64):
        par = ib % 2
        w("y", "V", conv_id.get(ib - 1, 0))
        if ib == 0:
            w("y", "P", oproj_done_p)
            if H["_dbg"]:
                w("y", "Y0", dbg1)
        r("y", lambda e, ib=ib: e.dma_start(out=B["wgst"][:], in_=H["wg"][ib]),
          inc="S0", amt=16)
        wld = r("y", lambda e, ib=ib: e.dma_start(out=B["wust"][:], in_=H["wu"][ib]),
                inc="S0", amt=16)
        w("v", "S0", wld)
        w("v", "P", umm.get(ib - 2, 0))
        r("v", lambda e, par=par: e.tensor_scalar(
            B["wgb"][par][:], B["wgst"][:], -128.0, None, ALU.add))
        conv_id[ib] = r("v", lambda e, par=par: e.tensor_scalar(
            B["wub"][par][:], B["wust"][:], -128.0, None, ALU.add), inc="V")
        for th in range(2):
            n = 2 * ib + th
            s3 = n % 3
            w("t", "V", conv_id[ib])
            w("t", "V", hTT.get(n - 3, 0))
            if n == 0:
                w("t", "A", x2fev[7])
                w("t", "V", oproj_last_fbev)
            for c in range(16):
                last = c == 15
                idx = r("t", lambda e, c=c, th=th, par=par, s3=s3: e.matmul(
                    ps[:, 1024 * s3:1024 * s3 + 512],
                    B["wgb"][par][:, 128 * c:128 * c + 128],
                    B["x2f"][:, 1024 * c + 512 * th:1024 * c + 512 * th + 512],
                    start=(c == 0), stop=(c == 15)), inc=("P" if last else None))
            gmm[n] = idx
            for c in range(16):
                last = c == 15
                idx = r("t", lambda e, c=c, th=th, par=par, s3=s3: e.matmul(
                    ps[:, 1024 * s3 + 512:1024 * s3 + 1024],
                    B["wub"][par][:, 128 * c:128 * c + 128],
                    B["x2f"][:, 1024 * c + 512 * th:1024 * c + 512 * th + 512],
                    start=(c == 0), stop=(c == 15)), inc=("P" if last else None))
            umm[n] = idx
            w("s", "P", gmm[n])
            w("s", "V", hTT.get(n - 1, 0))
            silu_id[n] = r("s", lambda e, ib=ib, s3=s3: e.activation(
                B["sg"][:], ps[:, 1024 * s3:1024 * s3 + 512], AF.Silu,
                bias=0.0, scale=B["gs"][:, ib:ib + 1]), inc="A")
            w("v", "P", umm[n])
            r("v", lambda e, ib=ib, s3=s3: e.tensor_scalar(
                B["usv"][:], ps[:, 1024 * s3 + 512:1024 * s3 + 1024],
                B["us"][:, ib:ib + 1], None, ALU.mult), inc="V", selfwait=True)
            w("v", "A", silu_id[n])
            if n == 0 and H["_dbg"]:
                w("v", "Y0", dbg2)
            if ib == 32 and th == 0:
                w("v", "Y1", x2store[7])
                w("v", "A", castid[7])
            hTT[n] = r("v", lambda e, ib=ib, th=th: e.tensor_tensor(
                B["hf"][:, 1024 * ib + 512 * th:1024 * ib + 512 * th + 512],
                B["sg"][:], B["usv"][:], ALU.mult), inc="V")
    hlast = hTT[127]
    ffa_last_silu = silu_id[127]

    dmm = {}
    fmid = {}
    trf = {}
    tev = {}
    tokstore = {}
    wdc = {}

    def ffb_evac(hc):
        par = hc % 2
        w("v", "P", dmm[4 * hc + 3])
        w("v", "P", trf.get(hc - 1, 0))
        fmid[hc] = r("v", lambda e, hc=hc, par=par: e.tensor_scalar(
            B["fm"][:], ps[:, 1024 * par:1024 * par + 1024],
            B["ds"][:, hc:hc + 1], None, ALU.mult), inc="V", selfwait=True)
        w("t", "V", fmid[hc])
        w("t", "A", tev.get(hc - 1, 0))
        for wv_ in range(8):
            off = 2048 + 128 * wv_
            last = wv_ == 7
            idx = r("t", lambda e, wv_=wv_, off=off: e.transpose(
                ps[:, off:off + 128], B["fm"][:, 128 * wv_:128 * wv_ + 128],
                B["idf"][:]), inc=("P" if last else None))
        trf[hc] = idx
        w("s", "P", trf[hc])
        w("s", "Y0", tokstore.get(hc - 1, 0))
        r("s", lambda e: e.copy(B["tok"][0][:], bk32(4)))
        tev[hc] = r("s", lambda e: e.copy(B["tok"][1][:], bk32(5)), inc="A")
        w("y", "A", tev[hc])
        for gi2 in range(2):
            tokstore[hc] = r("y", lambda e, gi2=gi2, hc=hc: e.dma_start(
                out=H["ffs"].rearrange("(w p) h -> p w h", p=128)[
                    :, 4 * gi2:4 * gi2 + 4, 128 * hc:128 * hc + 128],
                in_=B["tok"][gi2][:].rearrange("p (w h) -> p w h", w=4)),
                inc="Y0", amt=16)

    for hc in range(16):
        par = hc % 2
        for ibg in range(4):
            gi = 4 * hc + ibg
            sp_ = gi % 2
            w("g", "V", wdc.get(gi - 2, 0))
            if gi == 0:
                w("g", "V", hlast)
            gsem = "G%d" % sp_
            ldid = r("g", lambda e, hc=hc, ibg=ibg, sp_=sp_: e.dma_start(
                out=B["wdst"][sp_][:].rearrange("p (g h) -> p g h", g=16),
                in_=H["wd"][hc, 16 * ibg:16 * ibg + 16].rearrange("g p h -> p g h")),
                inc=gsem, amt=16)
            w("v", gsem, ldid)
            w("v", "P", dmm.get(gi - 2, 0))
            wdc[gi] = r("v", lambda e, sp_=sp_: e.tensor_scalar(
                B["wdb"][sp_][:], B["wdst"][sp_][:], -128.0, None, ALU.add), inc="V")
            w("t", "V", wdc[gi])
            if ibg == 0:
                w("t", "V", fmid.get(hc - 2, 0))
                if hc == 0:
                    w("t", "V", hlast)
                    w("t", "A", ffa_last_silu)
            for i16 in range(16):
                ib = 16 * ibg + i16
                for th in range(2):
                    last = (i16 == 15 and th == 1)
                    idx = r("t", lambda e, i16=i16, th=th, ib=ib, par=par, sp_=sp_: e.matmul(
                        ps[:, 1024 * par + 512 * th:1024 * par + 512 * th + 512],
                        B["wdb"][sp_][:, 128 * i16:128 * i16 + 128],
                        B["hf"][:, 1024 * ib + 512 * th:1024 * ib + 512 * th + 512],
                        start=(ib == 0), stop=(ib == 63)), inc=("P" if last else None))
            dmm[gi] = idx
            if ibg == 0 and hc >= 1:
                ffb_evac(hc - 1)
    ffb_evac(15)

    ffb_done_p = dmm[63]

    ln2 = {}
    ystore = {}
    for tt in range(8):
        par = tt % 2
        w("y", "Y0", tokstore[15])
        w("y", "V", ln2.get(tt - 2, 0))
        ssem = "S%d" % par
        r("y", lambda e, tt=tt, par=par: e.dma_start(
            out=B["ffin"][par][:], in_=H["ffs"][128 * tt:128 * tt + 128, :]),
          inc=ssem, amt=16)
        ffld = r("y", lambda e, tt=tt, par=par: e.dma_start(
            out=B["x2in"][par][:], in_=H["x2s"][128 * tt:128 * tt + 128, :]),
            inc=ssem, amt=16)
        w("v", ssem, ffld)
        if tt == 0:
            w("v", "P", ffb_done_p)
            w("v", "Y0", tokstore[15])
        w("v", "Y%d" % par, ystore.get(tt - 2, 0))
        tta = r("v", lambda e, par=par: e.tensor_tensor(
            B["outt"][par][:], B["ffin"][par][:], B["x2in"][par][:], ALU.add),
          inc="V", selfwait=True)
        w("s", "V", tta)
        r("s", lambda e, par=par: e.activation(
            B["sq2"][:], B["outt"][par][:], AF.Copy, accum_out=B["ln"][:, 8:9]))
        cpid = r("s", lambda e: e.activation(
            B["ln"][:, 14:15], B["bk"][:, 0:1], AF.Copy), inc="A")
        w("v", "A", cpid)
        r("v", lambda e: e.tensor_scalar(B["ln"][:, 9:10], B["ln"][:, 8:9], 1.0 / D, None, ALU.mult),
          inc="V", selfwait=True)
        cen = r("v", lambda e, par=par: e.tensor_scalar(
            B["sq2"][:], B["outt"][par][:], B["ln"][:, 9:10], None, ALU.subtract),
            inc="V", selfwait=True)
        w("s", "V", cen)
        r("s", lambda e, par=par: e.activation(
            B["outt"][par][:], B["sq2"][:], AF.Square, accum_out=B["ln"][:, 10:11]))
        sqid = r("s", lambda e: e.activation(
            B["ln"][:, 14:15], B["bk"][:, 0:1], AF.Copy), inc="A")
        w("v", "A", sqid)
        vv_ = r("v", lambda e: e.tensor_scalar(
            B["ln"][:, 11:12], B["ln"][:, 10:11], 1.0 / D, EPS, ALU.mult, ALU.add),
            inc="V", selfwait=True)
        w("s", "V", vv_)
        sqr = r("s", lambda e: e.activation(B["ln"][:, 12:13], B["ln"][:, 11:12], AF.Sqrt), inc="A")
        w("v", "A", sqr)
        r("v", lambda e: e.reciprocal(B["ln"][:, 13:14], B["ln"][:, 12:13]),
          inc="V", selfwait=True)
        ln2[tt] = r("v", lambda e, par=par: e.tensor_scalar(
            B["outt"][par][:], B["sq2"][:], B["ln"][:, 13:14], None, ALU.mult), inc="V")
        w("y", "V", ln2[tt])
        ystore[tt] = r("y", lambda e, tt=tt, par=par: e.dma_start(
            out=H["y"][128 * tt:128 * tt + 128, :], in_=B["outt"][par][:]),
            inc="Y%d" % par, amt=16)

    w("y", "Y0", S_.cnt["Y0"])
    w("y", "Y1", S_.cnt["Y1"])


def _pack(inputs):
    f32 = np.float32
    Wq = np.asarray(inputs["L0_attn_Wq"], f32)
    bq = np.asarray(inputs["L0_attn_bq"], f32)
    Wk = np.asarray(inputs["L0_attn_Wk"], f32)
    bk = np.asarray(inputs["L0_attn_bk"], f32)
    Wv = np.asarray(inputs["L0_attn_Wv"], f32)
    bv = np.asarray(inputs["L0_attn_bv"], f32)
    Wo = np.asarray(inputs["L0_attn_Wo"], f32)
    bo = np.asarray(inputs["L0_attn_bo"], f32)
    gw = np.ascontiguousarray(np.asarray(inputs["L0_ffn_gate_w"], np.int32))
    uw = np.ascontiguousarray(np.asarray(inputs["L0_ffn_up_w"], np.int32))
    dw = np.ascontiguousarray(np.asarray(inputs["L0_ffn_down_w"], np.int32))
    gsc = np.asarray(inputs["L0_ffn_gate_scale"], f32)
    usc = np.asarray(inputs["L0_ffn_up_scale"], f32)
    dsc = np.asarray(inputs["L0_ffn_down_scale"], f32)

    shared = {}
    wq_t = np.empty((16, 128, 2048), f32)
    bq_t = np.empty((128, 16), f32)
    for j in range(16):
        cols = np.r_[64 * j:64 * j + 64, 64 * (j + 16):64 * (j + 16) + 64]
        tmp = Wq[:, cols].reshape(16, 128, 128)
        wq_t[j] = tmp.transpose(1, 0, 2).reshape(128, 2048)
        bq_t[:, j] = bq[cols]
    shared["wq_t"] = np.ascontiguousarray(wq_t)
    shared["bq_t"] = np.ascontiguousarray(bq_t)
    col_order = np.empty(512, np.int64)
    for cc in range(4):
        for dd in range(128):
            kv = cc + (4 if dd >= 64 else 0)
            col_order[128 * cc + dd] = 64 * kv + dd % 64
    wk_re = Wk[:, col_order]
    shared["wk_t"] = np.ascontiguousarray(
        wk_re.reshape(16, 128, 512).transpose(1, 0, 2).reshape(128, 8192))
    shared["bk_t"] = np.ascontiguousarray(bk[col_order].reshape(4, 128).T)
    shared["wv_t"] = np.ascontiguousarray(
        Wv.reshape(16, 128, 512).transpose(1, 0, 2).reshape(128, 8192))
    shared["bv_bc"] = np.ascontiguousarray(np.broadcast_to(bv, (128, 512)))
    wo_t = np.empty((16, 128, 2048), f32)
    for c in range(16):
        rows = np.r_[64 * c:64 * c + 64, 64 * (c + 16):64 * (c + 16) + 64]
        wo_t[c] = Wo[rows, :]
    shared["wo_t"] = np.ascontiguousarray(wo_t)
    shared["bo_bc"] = np.ascontiguousarray(np.broadcast_to(bo, (128, 2048)))
    shared["wg_t"] = np.ascontiguousarray(
        gw.reshape(64, 128, 16, 128).transpose(0, 3, 2, 1).reshape(64, 128, 2048))
    shared["wu_t"] = np.ascontiguousarray(
        uw.reshape(64, 128, 16, 128).transpose(0, 3, 2, 1).reshape(64, 128, 2048))
    shared["wd_t"] = np.ascontiguousarray(
        dw.reshape(16, 128, 64, 128).transpose(0, 2, 3, 1))
    shared["gs_t"] = np.ascontiguousarray(gsc.reshape(64, 128).T)
    shared["us_t"] = np.ascontiguousarray(usc.reshape(64, 128).T)
    shared["ds_t"] = np.ascontiguousarray(dsc.reshape(16, 128).T)
    shared["identb"] = np.eye(128, dtype=np.float32).astype(ml_dtypes.bfloat16)
    shared["identf"] = np.eye(128, dtype=np.float32)
    shared["ones64b"] = np.ones((128, 128), np.float32).astype(ml_dtypes.bfloat16)
    shared["onesf"] = np.ones((128, 64), np.float32)
    return shared


def kernel(**inputs):
    x = np.ascontiguousarray(np.asarray(inputs["L0_attn_x"], np.float32).reshape(SEQ, D))
    shared = _pack(inputs)
    nc = build_nc()
    in_maps = []
    for c in range(NCORES):
        m = dict(shared)
        m["x_full"] = x
        m["x_own"] = np.ascontiguousarray(x[T * c:T * (c + 1)])
        in_maps.append(m)
    res = run_bass_kernel_spmd(nc, in_maps, core_ids=list(range(NCORES)))
    y = np.concatenate([np.asarray(res.results[c]["y"]) for c in range(NCORES)], axis=0)
    return y.reshape(1, SEQ, D).astype(np.float32)


if __name__ == "__main__":
    import reference as _ref
    inp = _ref.setup_inputs()
    out = kernel(**{k: np.asarray(v) for k, v in inp.items()})
    want = np.asarray(_ref.reference(**inp))
    err = np.abs(out - want)
    denom = np.mean(want.astype(np.float64) ** 2)
    print("rvar:", float(np.mean((out.astype(np.float64) - want.astype(np.float64)) ** 2) / denom))
    print("maxabs:", float(err.max()))


# version 12
# speedup vs baseline: 1.3143x; 1.0074x over previous
import numpy as np
import ml_dtypes
import concourse.bass as bass
import concourse.mybir as mybir
from concourse.bass_utils import run_bass_kernel_spmd

F32 = mybir.dt.float32
BF16 = mybir.dt.bfloat16
I32 = mybir.dt.int32
AF = mybir.ActivationFunctionType
ALU = mybir.AluOpType
AX = mybir.AxisListType

NCORES = 8
SEQ = 8192
D = 2048
T = SEQ // NCORES
NKT = SEQ // 128
EPS = 1e-5


def build_nc():
    nc = bass.Bass("TRN2", target_bir_lowering=False)

    H = {}
    H["xf"] = nc.dram_tensor("x_full", [SEQ, D], F32, kind="ExternalInput").ap()
    H["xo"] = nc.dram_tensor("x_own", [T, D], F32, kind="ExternalInput").ap()
    H["wq"] = nc.dram_tensor("wq_t", [16, 128, 2048], F32, kind="ExternalInput").ap()
    H["bq"] = nc.dram_tensor("bq_t", [128, 16], F32, kind="ExternalInput").ap()
    H["wk"] = nc.dram_tensor("wk_t", [128, 8192], F32, kind="ExternalInput").ap()
    H["bk"] = nc.dram_tensor("bk_t", [128, 4], F32, kind="ExternalInput").ap()
    H["wv"] = nc.dram_tensor("wv_t", [128, 8192], F32, kind="ExternalInput").ap()
    H["bv"] = nc.dram_tensor("bv_bc", [128, 512], F32, kind="ExternalInput").ap()
    H["wo"] = nc.dram_tensor("wo_t", [16, 128, 2048], F32, kind="ExternalInput").ap()
    H["bo"] = nc.dram_tensor("bo_bc", [128, 2048], F32, kind="ExternalInput").ap()
    H["wg"] = nc.dram_tensor("wg_t", [64, 128, 2048], I32, kind="ExternalInput").ap()
    H["wu"] = nc.dram_tensor("wu_t", [64, 128, 2048], I32, kind="ExternalInput").ap()
    H["wd"] = nc.dram_tensor("wd_t", [16, 64, 128, 128], I32, kind="ExternalInput").ap()
    H["gs"] = nc.dram_tensor("gs_t", [128, 64], F32, kind="ExternalInput").ap()
    H["us"] = nc.dram_tensor("us_t", [128, 64], F32, kind="ExternalInput").ap()
    H["ds"] = nc.dram_tensor("ds_t", [128, 16], F32, kind="ExternalInput").ap()
    H["idb"] = nc.dram_tensor("identb", [128, 128], BF16, kind="ExternalInput").ap()
    H["idf"] = nc.dram_tensor("identf", [128, 128], F32, kind="ExternalInput").ap()
    H["o64"] = nc.dram_tensor("ones64b", [128, 128], BF16, kind="ExternalInput").ap()
    H["of"] = nc.dram_tensor("onesf", [128, 64], F32, kind="ExternalInput").ap()
    H["y"] = nc.dram_tensor("y", [T, D], F32, kind="ExternalOutput").ap()
    import os
    H["_dbg"] = bool(os.environ.get("KDEBUG"))
    if H["_dbg"]:
        H["dkt"] = nc.dram_tensor("dbg_kt", [128, 32768], BF16, kind="ExternalOutput").ap()
        H["dv"] = nc.dram_tensor("dbg_v", [128, 32768], BF16, kind="ExternalOutput").ap()
        H["dqt"] = nc.dram_tensor("dbg_qt", [128, 16384], BF16, kind="ExternalOutput").ap()
        H["dao"] = nc.dram_tensor("dbg_ao", [128, 16384], BF16, kind="ExternalOutput").ap()
        H["dwo"] = nc.dram_tensor("dbg_wo", [128, 32768], BF16, kind="ExternalOutput").ap()
        H["ds_"] = nc.dram_tensor("dbg_s", [T, D], F32, kind="ExternalOutput").ap()
    import os as _os
    _scrkind = "ExternalOutput" if _os.environ.get("KDEBUG") else "Internal"
    H["x2s"] = nc.dram_tensor("x2_scr", [T, D], F32, kind=_scrkind).ap()
    H["ffs"] = nc.dram_tensor("ffn_scr", [T, D], F32, kind=_scrkind).ap()

    SB_BASE = 16512

    def at(name, shape, dtype, off):
        return nc.alloc_sbuf_tensor_at(name, shape, dtype, offset=SB_BASE + off)

    B = {}
    B["kt"] = at("kt_b", [128, 32768], BF16, 0)
    B["v"] = at("v_b", [128, 32768], BF16, 65536)
    B["wo"] = at("wo_b", [128, 32768], BF16, 0)
    B["hf"] = at("hfull", [128, 65536], BF16, 0)
    B["bo"] = at("bo_s", [128, 2048], F32, 65536)
    B["xr"] = [at(f"xr{i}", [128, 2048], F32, 73728 + 8192 * i) for i in range(2)]
    B["x2t"] = [at(f"x2t{i}", [128, 2048], F32, 90112 + 8192 * i) for i in range(2)]
    B["x2bf"] = at("x2bf", [128, 2048], BF16, 106496)
    B["sqs"] = at("sqs", [128, 2048], F32, 110592)
    B["xbo"] = at("xbo", [128, 2048], F32, 118784)
    B["qt"] = at("qt_b", [128, 16384], BF16, 131072)
    B["x2f"] = at("x2f", [128, 16384], BF16, 131072)
    B["xfblk"] = at("xfblk", [128, 16384], BF16, 131072)
    B["wkb"] = at("wkb", [128, 8192], BF16, 163840)
    B["ffin"] = [at(f"ffin{i}", [128, 2048], F32, 131072 + 8192 * i) for i in range(2)]
    B["x2in"] = [at(f"x2in{i}", [128, 2048], F32, 147456 + 8192 * i) for i in range(2)]
    B["ao"] = at("ao_b", [128, 16384], BF16, 163840)
    B["xfo"] = at("xfown", [128, 16384], BF16, 163840)
    B["wvb"] = at("wvb", [128, 8192], BF16, 180224)
    B["bv"] = at("bv_s", [128, 512], F32, 206752)
    B["wgst"] = at("wgstage", [128, 2048], I32, 163840)
    B["wust"] = at("wustage", [128, 2048], I32, 172032)
    B["wgb"] = [at(f"wgb{i}", [128, 2048], BF16, 180224 + 4096 * i) for i in range(2)]
    B["wub"] = [at(f"wub{i}", [128, 2048], BF16, 188416 + 4096 * i) for i in range(2)]
    B["wdst"] = [at(f"wdst{i}", [128, 2048], I32, 163840 + 8192 * i) for i in range(2)]
    B["wdb"] = [at(f"wdb{i}", [128, 2048], BF16, 180224 + 4096 * i) for i in range(2)]
    B["fm"] = at("fm_s", [128, 1024], F32, 188416)
    B["tok"] = [at(f"tokst{i}", [128, 512], F32, 192512 + 2048 * i) for i in range(2)]
    B["outt"] = [at(f"outt{i}", [128, 2048], F32, 163840 + 8192 * i) for i in range(2)]
    B["sq2"] = at("sq2", [128, 2048], F32, 180224)
    B["idb"] = at("idb_s", [128, 128], BF16, 196608)
    B["idf"] = at("idf_s", [128, 128], F32, 196864)
    B["o64"] = at("o64_s", [128, 128], BF16, 197376)
    B["of"] = at("of_s", [128, 64], F32, 197632)
    B["bq"] = at("bq_s", [128, 16], F32, 197888)
    B["bk"] = at("bk_s", [128, 4], F32, 197952)
    B["gs"] = at("gs_s", [128, 64], F32, 197984)
    B["us"] = at("us_s", [128, 64], F32, 198240)
    B["ds"] = at("ds_s", [128, 16], F32, 198496)
    B["ln"] = at("lnst", [128, 16], F32, 210848)
    B["xst"] = [at(f"xstag{i}", [128, 2048], BF16, 198560 + 4096 * i) for i in range(2)]
    B["pt"] = [at(f"pt{i}", [128, 1024], BF16, 198560 + 2048 * i) for i in range(4)]
    B["rd"] = at("rd_t", [128, 512], F32, 206752)
    B["bcs"] = at("bcs", [128, 512], F32, 208800)
    B["sg"] = at("sg_s", [128, 512], F32, 206752)
    B["rdb"] = at("rdb", [128, 512], BF16, 210912)
    B["usv"] = at("usv_s", [128, 512], F32, 208800)
    B["borow"] = at("borow", [1, 2048], BF16, 118784)

    B["nc"] = nc
    B["ps"] = nc.alloc_psum_tensor("ps32", [128, 3072], F32)
    B["psbf"] = nc.alloc_psum_tensor("psbf", [128, 2048], BF16)

    from contextlib import ExitStack
    with ExitStack() as es:
        ec = es.enter_context
        sems = {k: ec(nc.semaphore(f"sem{k}")) for k in
                ("P", "A", "V", "G0", "G1", "W", "S0", "S1", "Y0", "Y1")}
        block = ec(nc.Block())

        class Sched:
            def __init__(self, only, eng):
                self.only = only
                self.eng = eng
                self.cnt = {k: 0 for k in
                            ("P", "A", "V", "G0", "G1", "W", "S0", "S1", "Y0", "Y1")}

            def wait(self, engname, sem, val):
                if val is None or val <= 0:
                    return
                if self.only == engname:
                    self.eng.wait_ge(sems[sem], val)

            def run(self, engname, fn, inc=None, amt=1, selfwait=False):
                if inc is not None:
                    self.cnt[inc] += amt
                if self.only == engname:
                    inst = fn(self.eng)
                    if inc is not None:
                        inst.then_inc(sems[inc], amt)
                        if selfwait:
                            self.eng.wait_ge(sems[inc], self.cnt[inc])
                return self.cnt[inc] if inc is not None else None

        @block.gpsimd
        def _(g):
            emit_program(Sched("g", g), H, B)

        @block.tensor
        def _(t):
            emit_program(Sched("t", t), H, B)

        @block.scalar
        def _(s):
            emit_program(Sched("s", s), H, B)

        @block.vector
        def _(v):
            emit_program(Sched("v", v), H, B)

        @block.sync
        def _(sy):
            emit_program(Sched("y", sy), H, B)

    return nc

def emit_program(S_, H, B):
    w = S_.wait
    r = S_.run
    ps = B["ps"]
    psbf = B["psbf"]

    def bk32(i, n=512):
        return ps[:, 512 * i:512 * i + n]

    for k in ("idb", "idf", "o64", "of", "bq", "bk", "gs", "us", "ds", "bv"):
        r("g", lambda e, k=k: e.dma_start(out=B[k][:], in_=H[k]), inc="W", amt=16)
    r("g", lambda e: e.dma_start(out=B["wkb"][:], in_=H["wk"]), inc="W", amt=16)
    wconsts = r("g", lambda e: e.dma_start(out=B["wvb"][:], in_=H["wv"]), inc="W", amt=16)

    ev = {}

    def transpose_tile(src_buf, dst3):
        g0 = transpose_tile.seq
        for grp in range(4):
            half = grp % 2
            w("t", "A", ev.get(g0 + grp - 2, 0))
            for cc in range(4):
                c = 4 * grp + cc
                last = cc == 3
                tid = r("t", lambda e, c=c, half=half, cc=cc, src_buf=src_buf: e.transpose(
                    psbf[:, 1024 * half + 128 * cc:1024 * half + 128 * cc + 128],
                    src_buf[:, 128 * c:128 * c + 128], B["idb"][:]),
                    inc=("P" if last else None))
            w("s", "P", tid)
            ev[g0 + grp] = r("s", lambda e, half=half, grp=grp, dst3=dst3: e.copy(
                dst3(grp),
                psbf[:, 1024 * half:1024 * half + 512].rearrange("p (g q) -> p g q", g=4)),
                inc="A")
        transpose_tile.seq += 4
        return tid
    transpose_tile.seq = 0

    kev = {}
    vev = {}
    last_tr = {}
    NBLK = SEQ // 512
    for blk in range(NBLK):
        slot_x = 8192 * (blk % 2)
        for s4 in range(4):
            ti = 4 * blk + s4
            buf = ti % 2
            w("g", "P", last_tr.get(ti - 2, 0))
            gsem = "G%d" % buf
            r("g", lambda e, ti=ti, buf=buf: e.dma_start(
                out=B["xst"][buf][:], in_=H["xf"][128 * ti:128 * ti + 128, :]),
              inc=gsem, amt=16)
            gneed = S_.cnt[gsem]
            if ti < 2:
                w("t", "W", wconsts)
                w("v", "W", wconsts)
            w("t", gsem, gneed)

            def dview(grp, slot_x=slot_x, s4=s4):
                vv = B["xfblk"][:].rearrange("p (c q) -> p c q", q=512)
                c0 = slot_x // 512 + 4 * grp
                return vv[:, c0:c0 + 4, 128 * s4:128 * s4 + 128]
            last_tr[ti] = transpose_tile(B["xst"][buf], dview)
        w("t", "A", ev[transpose_tile.seq - 1])
        for cc in range(4):
            kseq = 4 * blk + cc
            pb = cc % 2
            w("t", "V", kev.get(kseq - 2, 0))
            for c in range(16):
                last = c == 15
                kmm = r("t", lambda e, cc=cc, c=c, pb=pb, slot_x=slot_x: e.matmul(
                    bk32(pb),
                    B["wkb"][:, 512 * c + 128 * cc:512 * c + 128 * cc + 128],
                    B["xfblk"][:, slot_x + 512 * c:slot_x + 512 * c + 512],
                    start=(c == 0), stop=(c == 15)), inc=("P" if last else None))
            w("v", "P", kmm)
            kev[kseq] = r("v", lambda e, cc=cc, blk=blk, pb=pb: e.tensor_scalar(
                B["kt"][:, 8192 * cc + 512 * blk:8192 * cc + 512 * blk + 512],
                bk32(pb), B["bk"][:, cc:cc + 1], None, ALU.add),
                inc="V")
        for s4 in range(4):
            ti = 4 * blk + s4
            vb = 2 + ti % 2
            w("t", "V", vev.get(ti - 2, 0))
            for c in range(16):
                last = c == 15
                vmm = r("t", lambda e, c=c, vb=vb, slot_x=slot_x, s4=s4: e.matmul(
                    bk32(vb), B["xfblk"][:, slot_x + 512 * c + 128 * s4:slot_x + 512 * c + 128 * s4 + 128],
                    B["wvb"][:, 512 * c:512 * c + 512],
                    start=(c == 0), stop=(c == 15)), inc=("P" if last else None))
            w("v", "P", vmm)
            vev[ti] = r("v", lambda e, ti=ti, vb=vb: e.tensor_tensor(
                B["v"][:, 512 * ti:512 * ti + 512], bk32(vb), B["bv"][:], ALU.add),
                inc="V")

    w("s", "V", vev[NKT - 1])
    for s8 in range(8):
        buf = s8 % 2
        w("g", "P", last_tr.get(NKT + s8 - 2, 0))
        gsem = "G%d" % buf
        r("g", lambda e, s8=s8, buf=buf: e.dma_start(
            out=B["xst"][buf][:], in_=H["xo"][128 * s8:128 * s8 + 128, :]),
          inc=gsem, amt=16)
        gneed = S_.cnt[gsem]
        w("t", gsem, gneed)

        def oview(grp, s8=s8):
            vv = B["xfo"][:].rearrange("p (c q) -> p c q", c=16)
            return vv[:, 4 * grp:4 * grp + 4, 128 * s8:128 * s8 + 128]
        last_tr[NKT + s8] = transpose_tile(B["xst"][buf], oview)

    qmm = {}
    qev = {}
    for j in range(16):
        buf = j % 2
        w("g", "P", qmm.get(j - 2, last_tr[NKT + 7]))
        gsem = "G%d" % buf
        r("g", lambda e, j=j, buf=buf: e.dma_start(
            out=B["xst"][buf][:], in_=H["wq"][j]), inc=gsem, amt=16)
        gneed = S_.cnt[gsem]
        w("t", gsem, gneed)
        qb_ = j % 2
        w("t", "V", qev.get(j - 2, max(kev[4 * (SEQ // 512 - 1) + 3], vev[NKT - 1])))
        for c in range(16):
            for hh in range(2):
                last = (c == 15 and hh == 1)
                idx = r("t", lambda e, c=c, hh=hh, qb_=qb_, buf=buf: e.matmul(
                    ps[:, 1024 * qb_ + 512 * hh:1024 * qb_ + 512 * hh + 512],
                    B["xst"][buf][:, 128 * c:128 * c + 128],
                    B["xfo"][:, 1024 * c + 512 * hh:1024 * c + 512 * hh + 512],
                    start=(c == 0), stop=(c == 15)), inc=("P" if last else None))
                if last:
                    qmm[j] = idx
        w("v", "P", qmm[j])
        qev[j] = r("v", lambda e, j=j, qb_=qb_: e.tensor_scalar(
            B["qt"][:, 1024 * j:1024 * j + 1024], ps[:, 1024 * qb_:1024 * qb_ + 1024],
            B["bq"][:, j:j + 1], None, ALU.add), inc="V")

    if H["_dbg"]:
        w("y", "V", max(qev[15], vev[NKT - 1], kev[4 * (SEQ // 512 - 1) + 3]))
        r("y", lambda e: e.dma_start(out=H["dkt"], in_=B["kt"][:]), inc="Y0", amt=16)
        r("y", lambda e: e.dma_start(out=H["dv"], in_=B["v"][:]), inc="Y0", amt=16)
        dbg0 = r("y", lambda e: e.dma_start(out=H["dqt"], in_=B["qt"][:]), inc="Y0", amt=16)
    psA = ps[:, 2048:2560]
    psB = ps[:, 2560:3072]
    emit_program.pending_norm = None
    emit_program.last_tb = 0
    pair_exp_base = qev[15] * 0 + S_.cnt["A"]
    prev_tb = vev[NKT - 1]
    prev_av = {}
    sc_last = {}
    wo_dma = {}
    qt_ready = qev[15]
    attn_first = True
    for j in range(16):
        for qb in range(2):
            cc = j // 4
            kvA = j // 4
            kvB = 4 + j // 4
            exp_id = {}
            av_id = {}
            sc_id = {}

            def emit_av(kt, kvA=kvA, kvB=kvB, exp_id=exp_id, av_id=av_id,
                        first_gate=None):
                pti = kt % 4
                w("t", "A", exp_id[kt])
                if kt == 0:
                    w("t", "V", first_gate)
                st = kt == 0
                sp = kt == 63
                r("t", lambda e, kt=kt, pti=pti, kvA=kvA, st=st, sp=sp: e.matmul(
                    psA[0:64, :], B["v"][:, 512 * kt + 64 * kvA:512 * kt + 64 * kvA + 64],
                    B["pt"][pti][:, 0:512], start=st, stop=sp, tile_position=(0, 0),
                    skip_group_check=True))
                r("t", lambda e, pti=pti, st=st, sp=sp: e.matmul(
                    psA[64:128, :], B["o64"][:, 0:64], B["pt"][pti][:, 512:1024],
                    start=st, stop=sp, tile_position=(0, 64), skip_group_check=True))
                r("t", lambda e, kt=kt, pti=pti, kvB=kvB, st=st, sp=sp: e.matmul(
                    psB[64:128, :], B["v"][:, 512 * kt + 64 * kvB:512 * kt + 64 * kvB + 64],
                    B["pt"][pti][:, 512:1024], start=st, stop=sp, tile_position=(0, 64),
                    skip_group_check=True))
                av_id[kt] = r("t", lambda e, pti=pti, st=st, sp=sp: e.matmul(
                    psB[0:64, :], B["o64"][:, 0:64], B["pt"][pti][:, 0:512],
                    start=st, stop=sp, tile_position=(0, 0), skip_group_check=True), inc="P")

            gate0 = prev_tb

            def emit_sc(kt, cc=cc, j=j, qb=qb, exp_id=exp_id, sc_id=sc_id):
                scb = kt % 2
                w("t", "A", exp_id.get(kt - 2, pair_exp_base))
                if attn_first and kt == 0:
                    w("t", "V", qt_ready)
                r("t", lambda e, kt=kt, scb=scb: e.matmul(
                    bk32(2 * scb),
                    B["kt"][0:64, 8192 * cc + 128 * kt:8192 * cc + 128 * kt + 128],
                    B["qt"][0:64, 1024 * j + 512 * qb:1024 * j + 512 * qb + 512],
                    start=True, stop=True, tile_position=(0, 0)))
                sc_id[kt] = r("t", lambda e, kt=kt, scb=scb: e.matmul(
                    bk32(2 * scb + 1),
                    B["kt"][64:128, 8192 * cc + 128 * kt:8192 * cc + 128 * kt + 128],
                    B["qt"][64:128, 1024 * j + 512 * qb:1024 * j + 512 * qb + 512],
                    start=True, stop=True, tile_position=(64, 0)), inc="P")

            def emit_exp(kt, exp_id=exp_id, sc_id=sc_id, av_id=av_id):
                ptw = av_id.get(kt - 4, prev_av.get(60 + kt, 0))
                w("s", "P", max(sc_id[kt], ptw))
                exp_id[kt] = r("s", lambda e, kt=kt: e.activation(
                    B["pt"][kt % 4][:], ps[:, 1024 * (kt % 2):1024 * (kt % 2) + 1024],
                    AF.Exp, bias=0.0, scale=0.125), inc="A")

            for k2 in range(0, 64, 2):
                emit_sc(k2)
                emit_exp(k2)
                emit_sc(k2 + 1)
                emit_exp(k2 + 1)
                if k2 == 0 and emit_program.pending_norm is not None:
                    emit_program.pending_norm()
                    emit_program.pending_norm = None
                if k2 >= 2:
                    emit_av(k2 - 2, first_gate=gate0)
                    emit_av(k2 - 1, first_gate=gate0)
            emit_av(62, first_gate=gate0)
            emit_av(63, first_gate=gate0)
            attn_first = False

            acols = slice(1024 * j + 512 * qb, 1024 * j + 512 * qb + 512)
            av63 = av_id[63]

            def do_norm(acols=acols, av63=av63):
                w("v", "P", av63)

                def _recip(e, dst, src_):
                    with B["nc"].allow_low_precision(reason="softmax 1/D broadcast via bf16 matmul"):
                        return e.reciprocal(dst, src_)
                r("v", lambda e: _recip(e, B["rdb"][0:1, :], psB[0:1, :]),
                  inc="V", selfwait=True)
                rid = r("v", lambda e: _recip(e, B["rdb"][64:65, :], psA[64:65, :]), inc="V")
                w("t", "V", rid)
                bA = r("t", lambda e: e.matmul(
                    psB[0:64, :], B["o64"][0:1, 0:64], B["rdb"][0:1, :],
                    start=True, stop=True), inc="P")
                w("v", "P", bA)
                r("v", lambda e: e.tensor_copy(B["bcs"][0:64, :], psB[0:64, :]),
                  inc="V", selfwait=True)
                tA = r("v", lambda e: e.tensor_tensor(
                    B["ao"][0:64, acols], psA[0:64, :], B["bcs"][0:64, :], ALU.mult),
                    inc="V")
                w("t", "V", tA)
                bB = r("t", lambda e: e.matmul(
                    psA[64:128, :], B["o64"][64:65, 0:64], B["rdb"][64:65, :],
                    start=True, stop=True), inc="P")
                w("v", "P", bB)
                r("v", lambda e: e.tensor_copy(B["bcs"][64:128, :], psA[64:128, :]),
                  inc="V", selfwait=True)
                tB = r("v", lambda e: e.tensor_tensor(
                    B["ao"][64:128, acols], psB[64:128, :], B["bcs"][64:128, :], ALU.mult),
                    inc="V")
                return tB
            emit_program.pending_norm_ret = None

            def pend(do_norm=do_norm):
                emit_program.last_tb = do_norm()
            emit_program.pending_norm = pend
            prev_tb_predicted = S_.cnt["V"] + 6
            prev_tb = prev_tb_predicted
            prev_av = {k: v for k, v in av_id.items() if k >= 60}
            pair_exp_base = exp_id[63]
            sc_last[(j, qb)] = sc_id[63]
        if j % 4 == 3:
            grp4 = j // 4
            for c4 in range(4):
                c = 4 * grp4 + c4
                w("g", "P", sc_last[(j, 1)])
                if H["_dbg"]:
                    w("g", "Y0", dbg0)
                wo_dma[c] = r("g", lambda e, c=c: e.dma_start(
                    out=B["wo"][:, 2048 * c:2048 * c + 2048], in_=H["wo"][c]),
                    inc="W", amt=16)

    if emit_program.pending_norm is not None:
        emit_program.pending_norm()
        emit_program.pending_norm = None
    attn_done_v = prev_tb
    attn_done_p = S_.cnt["P"]
    if H["_dbg"]:
        w("y", "V", attn_done_v)
        dbg1 = r("y", lambda e: e.dma_start(out=H["dao"], in_=B["ao"][:]), inc="Y0", amt=16)
    attn_last_exp = pair_exp_base

    w("g", "P", attn_done_p)
    bo_id = r("g", lambda e: e.dma_start(out=B["borow"][:], in_=H["bo"][0:1, :]), inc="W", amt=16)

    if H["_dbg"]:
        w("y", "W", wo_dma[15])
        dbg2 = r("y", lambda e: e.dma_start(out=H["dwo"], in_=B["wo"][:]), inc="Y0", amt=16)
    fbid = {}
    fbev = {}
    xboid = {}
    ln1 = {}
    castid = {}
    trxid = {}
    x2fev = {}
    x2store = {}

    def op_part1(tt):
        par = tt % 2
        w("y", "V", fbev.get(4 * (tt - 2) + 3, 0))
        if tt < 2:
            w("y", "P", attn_done_p)
        ssem = "S%d" % par
        xrid = r("y", lambda e, tt=tt, par=par: e.dma_start(
            out=B["xr"][par][:], in_=H["xo"][128 * tt:128 * tt + 128, :]),
            inc=ssem, amt=16)
        w("v", ssem, xrid)
        w("v", "A", castid.get(tt - 2, 0))
        w("v", "Y%d" % par, x2store.get(tt - 2, 0))
        for fb in range(4):
            n = 4 * tt + fb
            slot = n % 4
            w("t", "V", fbev.get(n - 4, 0))
            if n == 0:
                w("t", "V", attn_done_v)
                w("t", "A", attn_last_exp)
                w("t", "W", wo_dma[15])
            if n == 0:
                w("t", "W", bo_id)
            for c in range(16):
                r("t", lambda e, c=c, fb=fb, tt=tt, slot=slot: e.matmul(
                    bk32(slot), B["ao"][:, 1024 * c + 128 * tt:1024 * c + 128 * tt + 128],
                    B["wo"][:, 2048 * c + 512 * fb:2048 * c + 512 * fb + 512],
                    start=(c == 0), stop=False))
            idx = r("t", lambda e, fb=fb, slot=slot: e.matmul(
                bk32(slot), B["o64"][0:1, :], B["borow"][0:1, 512 * fb:512 * fb + 512],
                start=False, stop=True), inc="P")
            fbid[(tt, fb)] = idx
            w("v", "P", idx)
            fbev[n] = r("v", lambda e, fb=fb, par=par, slot=slot: e.tensor_tensor(
                B["x2t"][par][:, 512 * fb:512 * fb + 512], bk32(slot),
                B["xr"][par][:, 512 * fb:512 * fb + 512], ALU.add), inc="V",
                selfwait=(fb == 3))

    def op_part2(tt):
        par = tt % 2
        w("s", "V", fbev[4 * tt + 3])
        r("s", lambda e, par=par: e.activation(
            B["sqs"][:], B["x2t"][par][:], AF.Copy, accum_out=B["ln"][:, 0:1]))
        cpid = r("s", lambda e: e.activation(
            B["ln"][:, 6:7], B["bk"][:, 0:1], AF.Copy), inc="A")
        w("v", "A", cpid)
        r("v", lambda e: e.tensor_scalar(B["ln"][:, 1:2], B["ln"][:, 0:1], 1.0 / D, None, ALU.mult),
          inc="V", selfwait=True)
        cen = r("v", lambda e, par=par: e.tensor_scalar(
            B["sqs"][:], B["x2t"][par][:], B["ln"][:, 1:2], None, ALU.subtract),
            inc="V", selfwait=True)
        w("s", "V", cen)
        r("s", lambda e, par=par: e.activation(
            B["x2t"][par][:], B["sqs"][:], AF.Square, accum_out=B["ln"][:, 2:3]))
        sqid = r("s", lambda e: e.activation(
            B["ln"][:, 6:7], B["bk"][:, 0:1], AF.Copy), inc="A")
        w("v", "A", sqid)
        vv_ = r("v", lambda e: e.tensor_scalar(
            B["ln"][:, 3:4], B["ln"][:, 2:3], 1.0 / D, EPS, ALU.mult, ALU.add),
            inc="V", selfwait=True)
        w("s", "V", vv_)
        sqr = r("s", lambda e: e.activation(B["ln"][:, 4:5], B["ln"][:, 3:4], AF.Sqrt), inc="A")
        w("v", "A", sqr)
        r("v", lambda e: e.reciprocal(B["ln"][:, 5:6], B["ln"][:, 4:5]),
          inc="V", selfwait=True)
        ln1[tt] = r("v", lambda e, par=par: e.tensor_scalar(
            B["x2t"][par][:], B["sqs"][:], B["ln"][:, 5:6], None, ALU.mult), inc="V")
        w("s", "V", ln1[tt])
        w("s", "P", trxid.get(tt - 1, 0))
        castid[tt] = r("s", lambda e, par=par: e.copy(B["x2bf"][:], B["x2t"][par][:]), inc="A")
        w("t", "A", castid[tt])

        def xview(grp, tt=tt):
            vv2 = B["x2f"][:].rearrange("p (c q) -> p c q", c=16)
            return vv2[:, 4 * grp:4 * grp + 4, 128 * tt:128 * tt + 128]
        trxid[tt] = transpose_tile(B["x2bf"], xview)
        x2fev[tt] = ev[transpose_tile.seq - 1]
        w("y", "V", ln1[tt])
        x2store[tt] = r("y", lambda e, tt=tt, par=par: e.dma_start(
            out=H["x2s"][128 * tt:128 * tt + 128, :], in_=B["x2t"][par][:]),
            inc="Y%d" % par, amt=16)

    for tt in range(8):
        op_part1(tt)
        if tt >= 1:
            op_part2(tt - 1)
    op_part2(7)

    oproj_done_p = S_.cnt["P"]
    oproj_last_fbev = fbev[31]

    conv_id = {}
    gmm = {}
    umm = {}
    hTT = {}
    silu_id = {}
    for ib in range(64):
        par = ib % 2
        w("y", "V", conv_id.get(ib - 1, 0))
        if ib == 0:
            w("y", "P", oproj_done_p)
            if H["_dbg"]:
                w("y", "Y0", dbg1)
        r("y", lambda e, ib=ib: e.dma_start(out=B["wgst"][:], in_=H["wg"][ib]),
          inc="S0", amt=16)
        wld = r("y", lambda e, ib=ib: e.dma_start(out=B["wust"][:], in_=H["wu"][ib]),
                inc="S0", amt=16)
        w("v", "S0", wld)
        w("v", "P", umm.get(ib - 2, 0))
        r("v", lambda e, par=par: e.tensor_scalar(
            B["wgb"][par][:], B["wgst"][:], -128.0, None, ALU.add))
        conv_id[ib] = r("v", lambda e, par=par: e.tensor_scalar(
            B["wub"][par][:], B["wust"][:], -128.0, None, ALU.add), inc="V")
        for th in range(2):
            n = 2 * ib + th
            s3 = n % 3
            w("t", "V", conv_id[ib])
            w("t", "V", hTT.get(n - 3, 0))
            if n == 0:
                w("t", "A", x2fev[7])
                w("t", "V", oproj_last_fbev)
            for c in range(16):
                last = c == 15
                idx = r("t", lambda e, c=c, th=th, par=par, s3=s3: e.matmul(
                    ps[:, 1024 * s3:1024 * s3 + 512],
                    B["wgb"][par][:, 128 * c:128 * c + 128],
                    B["x2f"][:, 1024 * c + 512 * th:1024 * c + 512 * th + 512],
                    start=(c == 0), stop=(c == 15)), inc=("P" if last else None))
            gmm[n] = idx
            for c in range(16):
                last = c == 15
                idx = r("t", lambda e, c=c, th=th, par=par, s3=s3: e.matmul(
                    ps[:, 1024 * s3 + 512:1024 * s3 + 1024],
                    B["wub"][par][:, 128 * c:128 * c + 128],
                    B["x2f"][:, 1024 * c + 512 * th:1024 * c + 512 * th + 512],
                    start=(c == 0), stop=(c == 15)), inc=("P" if last else None))
            umm[n] = idx
            w("s", "P", gmm[n])
            w("s", "V", hTT.get(n - 1, 0))
            silu_id[n] = r("s", lambda e, ib=ib, s3=s3: e.activation(
                B["sg"][:], ps[:, 1024 * s3:1024 * s3 + 512], AF.Silu,
                bias=0.0, scale=B["gs"][:, ib:ib + 1]), inc="A")
            w("v", "P", umm[n])
            r("v", lambda e, ib=ib, s3=s3: e.tensor_scalar(
                B["usv"][:], ps[:, 1024 * s3 + 512:1024 * s3 + 1024],
                B["us"][:, ib:ib + 1], None, ALU.mult), inc="V", selfwait=True)
            w("v", "A", silu_id[n])
            if n == 0 and H["_dbg"]:
                w("v", "Y0", dbg2)
            if ib == 32 and th == 0:
                w("v", "Y1", x2store[7])
                w("v", "A", castid[7])
            hTT[n] = r("v", lambda e, ib=ib, th=th: e.tensor_tensor(
                B["hf"][:, 1024 * ib + 512 * th:1024 * ib + 512 * th + 512],
                B["sg"][:], B["usv"][:], ALU.mult), inc="V")
    hlast = hTT[127]
    ffa_last_silu = silu_id[127]

    dmm = {}
    fmid = {}
    trf = {}
    tev = {}
    tokstore = {}
    wdc = {}

    def ffb_evac(hc):
        par = hc % 2
        w("v", "P", dmm[4 * hc + 3])
        w("v", "P", trf.get(hc - 1, 0))
        fmid[hc] = r("v", lambda e, hc=hc, par=par: e.tensor_scalar(
            B["fm"][:], ps[:, 1024 * par:1024 * par + 1024],
            B["ds"][:, hc:hc + 1], None, ALU.mult), inc="V", selfwait=True)
        w("t", "V", fmid[hc])
        w("t", "A", tev.get(hc - 1, 0))
        for wv_ in range(8):
            off = 2048 + 128 * wv_
            last = wv_ == 7
            idx = r("t", lambda e, wv_=wv_, off=off: e.transpose(
                ps[:, off:off + 128], B["fm"][:, 128 * wv_:128 * wv_ + 128],
                B["idf"][:]), inc=("P" if last else None))
        trf[hc] = idx
        w("s", "P", trf[hc])
        w("s", "Y0", tokstore.get(hc - 1, 0))
        r("s", lambda e: e.copy(B["tok"][0][:], bk32(4)))
        tev[hc] = r("s", lambda e: e.copy(B["tok"][1][:], bk32(5)), inc="A")
        w("y", "A", tev[hc])
        for gi2 in range(2):
            tokstore[hc] = r("y", lambda e, gi2=gi2, hc=hc: e.dma_start(
                out=H["ffs"].rearrange("(w p) h -> p w h", p=128)[
                    :, 4 * gi2:4 * gi2 + 4, 128 * hc:128 * hc + 128],
                in_=B["tok"][gi2][:].rearrange("p (w h) -> p w h", w=4)),
                inc="Y0", amt=16)

    for hc in range(16):
        par = hc % 2
        for ibg in range(4):
            gi = 4 * hc + ibg
            sp_ = gi % 2
            w("g", "V", wdc.get(gi - 2, 0))
            if gi == 0:
                w("g", "V", hlast)
            gsem = "G%d" % sp_
            ldid = r("g", lambda e, hc=hc, ibg=ibg, sp_=sp_: e.dma_start(
                out=B["wdst"][sp_][:].rearrange("p (g h) -> p g h", g=16),
                in_=H["wd"][hc, 16 * ibg:16 * ibg + 16].rearrange("g p h -> p g h")),
                inc=gsem, amt=16)
            w("v", gsem, ldid)
            w("v", "P", dmm.get(gi - 2, 0))
            wdc[gi] = r("v", lambda e, sp_=sp_: e.tensor_scalar(
                B["wdb"][sp_][:], B["wdst"][sp_][:], -128.0, None, ALU.add), inc="V")
            w("t", "V", wdc[gi])
            if ibg == 0:
                w("t", "V", fmid.get(hc - 2, 0))
                if hc == 0:
                    w("t", "V", hlast)
                    w("t", "A", ffa_last_silu)
            for i16 in range(16):
                ib = 16 * ibg + i16
                for th in range(2):
                    last = (i16 == 15 and th == 1)
                    idx = r("t", lambda e, i16=i16, th=th, ib=ib, par=par, sp_=sp_: e.matmul(
                        ps[:, 1024 * par + 512 * th:1024 * par + 512 * th + 512],
                        B["wdb"][sp_][:, 128 * i16:128 * i16 + 128],
                        B["hf"][:, 1024 * ib + 512 * th:1024 * ib + 512 * th + 512],
                        start=(ib == 0), stop=(ib == 63)), inc=("P" if last else None))
            dmm[gi] = idx
            if ibg == 0 and hc >= 1:
                ffb_evac(hc - 1)
    ffb_evac(15)

    ffb_done_p = dmm[63]

    ln2 = {}
    ystore = {}
    for tt in range(8):
        par = tt % 2
        w("y", "Y0", tokstore[15])
        w("y", "V", ln2.get(tt - 2, 0))
        ssem = "S%d" % par
        r("y", lambda e, tt=tt, par=par: e.dma_start(
            out=B["ffin"][par][:], in_=H["ffs"][128 * tt:128 * tt + 128, :]),
          inc=ssem, amt=16)
        ffld = r("y", lambda e, tt=tt, par=par: e.dma_start(
            out=B["x2in"][par][:], in_=H["x2s"][128 * tt:128 * tt + 128, :]),
            inc=ssem, amt=16)
        w("v", ssem, ffld)
        if tt == 0:
            w("v", "P", ffb_done_p)
            w("v", "Y0", tokstore[15])
        w("v", "Y%d" % par, ystore.get(tt - 2, 0))
        tta = r("v", lambda e, par=par: e.tensor_tensor(
            B["outt"][par][:], B["ffin"][par][:], B["x2in"][par][:], ALU.add),
          inc="V", selfwait=True)
        w("s", "V", tta)
        r("s", lambda e, par=par: e.activation(
            B["sq2"][:], B["outt"][par][:], AF.Copy, accum_out=B["ln"][:, 8:9]))
        cpid = r("s", lambda e: e.activation(
            B["ln"][:, 14:15], B["bk"][:, 0:1], AF.Copy), inc="A")
        w("v", "A", cpid)
        r("v", lambda e: e.tensor_scalar(B["ln"][:, 9:10], B["ln"][:, 8:9], 1.0 / D, None, ALU.mult),
          inc="V", selfwait=True)
        cen = r("v", lambda e, par=par: e.tensor_scalar(
            B["sq2"][:], B["outt"][par][:], B["ln"][:, 9:10], None, ALU.subtract),
            inc="V", selfwait=True)
        w("s", "V", cen)
        r("s", lambda e, par=par: e.activation(
            B["outt"][par][:], B["sq2"][:], AF.Square, accum_out=B["ln"][:, 10:11]))
        sqid = r("s", lambda e: e.activation(
            B["ln"][:, 14:15], B["bk"][:, 0:1], AF.Copy), inc="A")
        w("v", "A", sqid)
        vv_ = r("v", lambda e: e.tensor_scalar(
            B["ln"][:, 11:12], B["ln"][:, 10:11], 1.0 / D, EPS, ALU.mult, ALU.add),
            inc="V", selfwait=True)
        w("s", "V", vv_)
        sqr = r("s", lambda e: e.activation(B["ln"][:, 12:13], B["ln"][:, 11:12], AF.Sqrt), inc="A")
        w("v", "A", sqr)
        r("v", lambda e: e.reciprocal(B["ln"][:, 13:14], B["ln"][:, 12:13]),
          inc="V", selfwait=True)
        ln2[tt] = r("v", lambda e, par=par: e.tensor_scalar(
            B["outt"][par][:], B["sq2"][:], B["ln"][:, 13:14], None, ALU.mult), inc="V")
        w("y", "V", ln2[tt])
        ystore[tt] = r("y", lambda e, tt=tt, par=par: e.dma_start(
            out=H["y"][128 * tt:128 * tt + 128, :], in_=B["outt"][par][:]),
            inc="Y%d" % par, amt=16)

    w("y", "Y0", S_.cnt["Y0"])
    w("y", "Y1", S_.cnt["Y1"])


def _pack(inputs):
    f32 = np.float32
    Wq = np.asarray(inputs["L0_attn_Wq"], f32)
    bq = np.asarray(inputs["L0_attn_bq"], f32)
    Wk = np.asarray(inputs["L0_attn_Wk"], f32)
    bk = np.asarray(inputs["L0_attn_bk"], f32)
    Wv = np.asarray(inputs["L0_attn_Wv"], f32)
    bv = np.asarray(inputs["L0_attn_bv"], f32)
    Wo = np.asarray(inputs["L0_attn_Wo"], f32)
    bo = np.asarray(inputs["L0_attn_bo"], f32)
    gw = np.ascontiguousarray(np.asarray(inputs["L0_ffn_gate_w"], np.int32))
    uw = np.ascontiguousarray(np.asarray(inputs["L0_ffn_up_w"], np.int32))
    dw = np.ascontiguousarray(np.asarray(inputs["L0_ffn_down_w"], np.int32))
    gsc = np.asarray(inputs["L0_ffn_gate_scale"], f32)
    usc = np.asarray(inputs["L0_ffn_up_scale"], f32)
    dsc = np.asarray(inputs["L0_ffn_down_scale"], f32)

    shared = {}
    wq_t = np.empty((16, 128, 2048), f32)
    bq_t = np.empty((128, 16), f32)
    for j in range(16):
        cols = np.r_[64 * j:64 * j + 64, 64 * (j + 16):64 * (j + 16) + 64]
        tmp = Wq[:, cols].reshape(16, 128, 128)
        wq_t[j] = tmp.transpose(1, 0, 2).reshape(128, 2048)
        bq_t[:, j] = bq[cols]
    shared["wq_t"] = np.ascontiguousarray(wq_t)
    shared["bq_t"] = np.ascontiguousarray(bq_t)
    col_order = np.empty(512, np.int64)
    for cc in range(4):
        for dd in range(128):
            kv = cc + (4 if dd >= 64 else 0)
            col_order[128 * cc + dd] = 64 * kv + dd % 64
    wk_re = Wk[:, col_order]
    shared["wk_t"] = np.ascontiguousarray(
        wk_re.reshape(16, 128, 512).transpose(1, 0, 2).reshape(128, 8192))
    shared["bk_t"] = np.ascontiguousarray(bk[col_order].reshape(4, 128).T)
    shared["wv_t"] = np.ascontiguousarray(
        Wv.reshape(16, 128, 512).transpose(1, 0, 2).reshape(128, 8192))
    shared["bv_bc"] = np.ascontiguousarray(np.broadcast_to(bv, (128, 512)))
    wo_t = np.empty((16, 128, 2048), f32)
    for c in range(16):
        rows = np.r_[64 * c:64 * c + 64, 64 * (c + 16):64 * (c + 16) + 64]
        wo_t[c] = Wo[rows, :]
    shared["wo_t"] = np.ascontiguousarray(wo_t)
    shared["bo_bc"] = np.ascontiguousarray(np.broadcast_to(bo, (128, 2048)))
    shared["wg_t"] = np.ascontiguousarray(
        gw.reshape(64, 128, 16, 128).transpose(0, 3, 2, 1).reshape(64, 128, 2048))
    shared["wu_t"] = np.ascontiguousarray(
        uw.reshape(64, 128, 16, 128).transpose(0, 3, 2, 1).reshape(64, 128, 2048))
    shared["wd_t"] = np.ascontiguousarray(
        dw.reshape(16, 128, 64, 128).transpose(0, 2, 3, 1))
    shared["gs_t"] = np.ascontiguousarray(gsc.reshape(64, 128).T)
    shared["us_t"] = np.ascontiguousarray(usc.reshape(64, 128).T)
    shared["ds_t"] = np.ascontiguousarray(dsc.reshape(16, 128).T)
    shared["identb"] = np.eye(128, dtype=np.float32).astype(ml_dtypes.bfloat16)
    shared["identf"] = np.eye(128, dtype=np.float32)
    shared["ones64b"] = np.ones((128, 128), np.float32).astype(ml_dtypes.bfloat16)
    shared["onesf"] = np.ones((128, 64), np.float32)
    return shared


def kernel(**inputs):
    x = np.ascontiguousarray(np.asarray(inputs["L0_attn_x"], np.float32).reshape(SEQ, D))
    shared = _pack(inputs)
    nc = build_nc()
    in_maps = []
    for c in range(NCORES):
        m = dict(shared)
        m["x_full"] = x
        m["x_own"] = np.ascontiguousarray(x[T * c:T * (c + 1)])
        in_maps.append(m)
    res = run_bass_kernel_spmd(nc, in_maps, core_ids=list(range(NCORES)))
    y = np.concatenate([np.asarray(res.results[c]["y"]) for c in range(NCORES)], axis=0)
    return y.reshape(1, SEQ, D).astype(np.float32)


if __name__ == "__main__":
    import reference as _ref
    inp = _ref.setup_inputs()
    out = kernel(**{k: np.asarray(v) for k, v in inp.items()})
    want = np.asarray(_ref.reference(**inp))
    err = np.abs(out - want)
    denom = np.mean(want.astype(np.float64) ** 2)
    print("rvar:", float(np.mean((out.astype(np.float64) - want.astype(np.float64)) ** 2) / denom))
    print("maxabs:", float(err.max()))


# version 13
# speedup vs baseline: 1.3168x; 1.0019x over previous
import numpy as np
import ml_dtypes
import concourse.bass as bass
import concourse.mybir as mybir
from concourse.bass_utils import run_bass_kernel_spmd

F32 = mybir.dt.float32
BF16 = mybir.dt.bfloat16
I32 = mybir.dt.int32
AF = mybir.ActivationFunctionType
ALU = mybir.AluOpType
AX = mybir.AxisListType

NCORES = 8
SEQ = 8192
D = 2048
T = SEQ // NCORES
NKT = SEQ // 128
EPS = 1e-5


def build_nc():
    nc = bass.Bass("TRN2", target_bir_lowering=False)

    H = {}
    H["xf"] = nc.dram_tensor("x_full", [SEQ, D], F32, kind="ExternalInput").ap()
    H["xo"] = nc.dram_tensor("x_own", [T, D], F32, kind="ExternalInput").ap()
    H["wq"] = nc.dram_tensor("wq_t", [16, 128, 2048], F32, kind="ExternalInput").ap()
    H["bq"] = nc.dram_tensor("bq_t", [128, 16], F32, kind="ExternalInput").ap()
    H["wk"] = nc.dram_tensor("wk_t", [128, 8192], F32, kind="ExternalInput").ap()
    H["bk"] = nc.dram_tensor("bk_t", [128, 4], F32, kind="ExternalInput").ap()
    H["wv"] = nc.dram_tensor("wv_t", [128, 8192], F32, kind="ExternalInput").ap()
    H["bv"] = nc.dram_tensor("bv_bc", [128, 512], F32, kind="ExternalInput").ap()
    H["wo"] = nc.dram_tensor("wo_t", [16, 128, 2048], F32, kind="ExternalInput").ap()
    H["bo"] = nc.dram_tensor("bo_bc", [128, 2048], F32, kind="ExternalInput").ap()
    H["wg"] = nc.dram_tensor("wg_t", [64, 128, 2048], I32, kind="ExternalInput").ap()
    H["wu"] = nc.dram_tensor("wu_t", [64, 128, 2048], I32, kind="ExternalInput").ap()
    H["wd"] = nc.dram_tensor("wd_t", [16, 64, 128, 128], I32, kind="ExternalInput").ap()
    H["gs"] = nc.dram_tensor("gs_t", [128, 64], F32, kind="ExternalInput").ap()
    H["us"] = nc.dram_tensor("us_t", [128, 64], F32, kind="ExternalInput").ap()
    H["ds"] = nc.dram_tensor("ds_t", [128, 16], F32, kind="ExternalInput").ap()
    H["idb"] = nc.dram_tensor("identb", [128, 128], BF16, kind="ExternalInput").ap()
    H["idf"] = nc.dram_tensor("identf", [128, 128], F32, kind="ExternalInput").ap()
    H["o64"] = nc.dram_tensor("ones64b", [128, 128], BF16, kind="ExternalInput").ap()
    H["of"] = nc.dram_tensor("onesf", [128, 64], F32, kind="ExternalInput").ap()
    H["y"] = nc.dram_tensor("y", [T, D], F32, kind="ExternalOutput").ap()
    import os
    H["_dbg"] = bool(os.environ.get("KDEBUG"))
    if H["_dbg"]:
        H["dkt"] = nc.dram_tensor("dbg_kt", [128, 32768], BF16, kind="ExternalOutput").ap()
        H["dv"] = nc.dram_tensor("dbg_v", [128, 32768], BF16, kind="ExternalOutput").ap()
        H["dqt"] = nc.dram_tensor("dbg_qt", [128, 16384], BF16, kind="ExternalOutput").ap()
        H["dao"] = nc.dram_tensor("dbg_ao", [128, 16384], BF16, kind="ExternalOutput").ap()
        H["dwo"] = nc.dram_tensor("dbg_wo", [128, 32768], BF16, kind="ExternalOutput").ap()
        H["ds_"] = nc.dram_tensor("dbg_s", [T, D], F32, kind="ExternalOutput").ap()
    import os as _os
    _scrkind = "ExternalOutput" if _os.environ.get("KDEBUG") else "Internal"
    H["x2s"] = nc.dram_tensor("x2_scr", [T, D], F32, kind=_scrkind).ap()
    H["ffs"] = nc.dram_tensor("ffn_scr", [T, D], F32, kind=_scrkind).ap()

    SB_BASE = 16512

    def at(name, shape, dtype, off):
        return nc.alloc_sbuf_tensor_at(name, shape, dtype, offset=SB_BASE + off)

    B = {}
    B["kt"] = at("kt_b", [128, 32768], BF16, 0)
    B["v"] = at("v_b", [128, 32768], BF16, 65536)
    B["wo"] = at("wo_b", [128, 32768], BF16, 0)
    B["hf"] = at("hfull", [128, 65536], BF16, 0)
    B["bo"] = at("bo_s", [128, 2048], F32, 65536)
    B["xr"] = [at(f"xr{i}", [128, 2048], F32, 73728 + 8192 * i) for i in range(2)]
    B["x2t"] = [at(f"x2t{i}", [128, 2048], F32, 90112 + 8192 * i) for i in range(2)]
    B["x2bf"] = [at("x2bf0", [128, 2048], BF16, 106496), at("x2bf1", [128, 2048], BF16, 122880)]
    B["sqs"] = at("sqs", [128, 2048], F32, 110592)
    B["xbo"] = at("xbo", [128, 2048], F32, 118784)
    B["qt"] = at("qt_b", [128, 16384], BF16, 131072)
    B["x2f"] = at("x2f", [128, 16384], BF16, 131072)
    B["xfblk"] = at("xfblk", [128, 16384], BF16, 131072)
    B["wkb"] = at("wkb", [128, 8192], BF16, 163840)
    B["ffin"] = [at(f"ffin{i}", [128, 2048], F32, 131072 + 8192 * i) for i in range(2)]
    B["x2in"] = [at(f"x2in{i}", [128, 2048], F32, 147456 + 8192 * i) for i in range(2)]
    B["ao"] = at("ao_b", [128, 16384], BF16, 163840)
    B["xfo"] = at("xfown", [128, 16384], BF16, 163840)
    B["wvb"] = at("wvb", [128, 8192], BF16, 180224)
    B["bv"] = at("bv_s", [128, 512], F32, 206752)
    B["wgst"] = at("wgstage", [128, 2048], I32, 163840)
    B["wust"] = at("wustage", [128, 2048], I32, 172032)
    B["wgb"] = [at(f"wgb{i}", [128, 2048], BF16, 180224 + 4096 * i) for i in range(2)]
    B["wub"] = [at(f"wub{i}", [128, 2048], BF16, 188416 + 4096 * i) for i in range(2)]
    B["wdst"] = [at(f"wdst{i}", [128, 2048], I32, 163840 + 8192 * i) for i in range(2)]
    B["wdb"] = [at(f"wdb{i}", [128, 2048], BF16, 180224 + 4096 * i) for i in range(2)]
    B["fm"] = at("fm_s", [128, 1024], F32, 188416)
    B["tok"] = [at(f"tokst{i}", [128, 512], F32, 192512 + 2048 * i) for i in range(2)]
    B["outt"] = [at(f"outt{i}", [128, 2048], F32, 163840 + 8192 * i) for i in range(2)]
    B["sq2"] = at("sq2", [128, 2048], F32, 180224)
    B["idb"] = at("idb_s", [128, 128], BF16, 196608)
    B["idf"] = at("idf_s", [128, 128], F32, 196864)
    B["o64"] = at("o64_s", [128, 128], BF16, 197376)
    B["of"] = at("of_s", [128, 64], F32, 197632)
    B["bq"] = at("bq_s", [128, 16], F32, 197888)
    B["bk"] = at("bk_s", [128, 4], F32, 197952)
    B["gs"] = at("gs_s", [128, 64], F32, 197984)
    B["us"] = at("us_s", [128, 64], F32, 198240)
    B["ds"] = at("ds_s", [128, 16], F32, 198496)
    B["ln"] = at("lnst", [128, 16], F32, 210848)
    B["xst"] = [at(f"xstag{i}", [128, 2048], BF16, 198560 + 4096 * i) for i in range(2)]
    B["pt"] = [at(f"pt{i}", [128, 1024], BF16, 198560 + 2048 * i) for i in range(4)]
    B["rd"] = at("rd_t", [128, 512], F32, 206752)
    B["bcs"] = at("bcs", [128, 512], F32, 208800)
    B["sg"] = at("sg_s", [128, 512], F32, 206752)
    B["rdb"] = at("rdb", [128, 512], BF16, 210912)
    B["usv"] = at("usv_s", [128, 512], F32, 208800)
    B["borow"] = at("borow", [1, 2048], BF16, 118784)

    B["nc"] = nc
    B["ps"] = nc.alloc_psum_tensor("ps32", [128, 3072], F32)
    B["psbf"] = nc.alloc_psum_tensor("psbf", [128, 2048], BF16)

    from contextlib import ExitStack
    with ExitStack() as es:
        ec = es.enter_context
        sems = {k: ec(nc.semaphore(f"sem{k}")) for k in
                ("P", "A", "V", "G0", "G1", "W", "S0", "S1", "Y0", "Y1")}
        block = ec(nc.Block())

        class Sched:
            def __init__(self, only, eng):
                self.only = only
                self.eng = eng
                self.cnt = {k: 0 for k in
                            ("P", "A", "V", "G0", "G1", "W", "S0", "S1", "Y0", "Y1")}

            def wait(self, engname, sem, val):
                if val is None or val <= 0:
                    return
                if self.only == engname:
                    self.eng.wait_ge(sems[sem], val)

            def run(self, engname, fn, inc=None, amt=1, selfwait=False):
                if inc is not None:
                    self.cnt[inc] += amt
                if self.only == engname:
                    inst = fn(self.eng)
                    if inc is not None:
                        inst.then_inc(sems[inc], amt)
                        if selfwait:
                            self.eng.wait_ge(sems[inc], self.cnt[inc])
                return self.cnt[inc] if inc is not None else None

        @block.gpsimd
        def _(g):
            emit_program(Sched("g", g), H, B)

        @block.tensor
        def _(t):
            emit_program(Sched("t", t), H, B)

        @block.scalar
        def _(s):
            emit_program(Sched("s", s), H, B)

        @block.vector
        def _(v):
            emit_program(Sched("v", v), H, B)

        @block.sync
        def _(sy):
            emit_program(Sched("y", sy), H, B)

    return nc

def emit_program(S_, H, B):
    w = S_.wait
    r = S_.run
    ps = B["ps"]
    psbf = B["psbf"]

    def bk32(i, n=512):
        return ps[:, 512 * i:512 * i + n]

    for k in ("idb", "idf", "o64", "of", "bq", "bk", "gs", "us", "ds", "bv"):
        r("g", lambda e, k=k: e.dma_start(out=B[k][:], in_=H[k]), inc="W", amt=16)
    r("g", lambda e: e.dma_start(out=B["wkb"][:], in_=H["wk"]), inc="W", amt=16)
    wconsts = r("g", lambda e: e.dma_start(out=B["wvb"][:], in_=H["wv"]), inc="W", amt=16)

    ev = {}

    def transpose_tile(src_buf, dst3):
        g0 = transpose_tile.seq
        for grp in range(4):
            half = grp % 2
            w("t", "A", ev.get(g0 + grp - 2, 0))
            for cc in range(4):
                c = 4 * grp + cc
                last = cc == 3
                tid = r("t", lambda e, c=c, half=half, cc=cc, src_buf=src_buf: e.transpose(
                    psbf[:, 1024 * half + 128 * cc:1024 * half + 128 * cc + 128],
                    src_buf[:, 128 * c:128 * c + 128], B["idb"][:]),
                    inc=("P" if last else None))
            w("s", "P", tid)
            ev[g0 + grp] = r("s", lambda e, half=half, grp=grp, dst3=dst3: e.copy(
                dst3(grp),
                psbf[:, 1024 * half:1024 * half + 512].rearrange("p (g q) -> p g q", g=4)),
                inc="A")
        transpose_tile.seq += 4
        return tid
    transpose_tile.seq = 0

    kev = {}
    vev = {}
    last_tr = {}
    NBLK = SEQ // 512
    for blk in range(NBLK):
        slot_x = 8192 * (blk % 2)
        for s4 in range(4):
            ti = 4 * blk + s4
            buf = ti % 2
            w("g", "P", last_tr.get(ti - 2, 0))
            gsem = "G%d" % buf
            r("g", lambda e, ti=ti, buf=buf: e.dma_start(
                out=B["xst"][buf][:], in_=H["xf"][128 * ti:128 * ti + 128, :]),
              inc=gsem, amt=16)
            gneed = S_.cnt[gsem]
            if ti < 2:
                w("t", "W", wconsts)
                w("v", "W", wconsts)
            w("t", gsem, gneed)

            def dview(grp, slot_x=slot_x, s4=s4):
                vv = B["xfblk"][:].rearrange("p (c q) -> p c q", q=512)
                c0 = slot_x // 512 + 4 * grp
                return vv[:, c0:c0 + 4, 128 * s4:128 * s4 + 128]
            last_tr[ti] = transpose_tile(B["xst"][buf], dview)
        w("t", "A", ev[transpose_tile.seq - 1])
        for cc in range(4):
            kseq = 4 * blk + cc
            pb = cc % 2
            w("t", "V", kev.get(kseq - 2, 0))
            for c in range(16):
                last = c == 15
                kmm = r("t", lambda e, cc=cc, c=c, pb=pb, slot_x=slot_x: e.matmul(
                    bk32(pb),
                    B["wkb"][:, 512 * c + 128 * cc:512 * c + 128 * cc + 128],
                    B["xfblk"][:, slot_x + 512 * c:slot_x + 512 * c + 512],
                    start=(c == 0), stop=(c == 15)), inc=("P" if last else None))
            w("v", "P", kmm)
            kev[kseq] = r("v", lambda e, cc=cc, blk=blk, pb=pb: e.tensor_scalar(
                B["kt"][:, 8192 * cc + 512 * blk:8192 * cc + 512 * blk + 512],
                bk32(pb), B["bk"][:, cc:cc + 1], None, ALU.add),
                inc="V")
        for s4 in range(4):
            ti = 4 * blk + s4
            vb = 2 + ti % 2
            w("t", "V", vev.get(ti - 2, 0))
            for c in range(16):
                last = c == 15
                vmm = r("t", lambda e, c=c, vb=vb, slot_x=slot_x, s4=s4: e.matmul(
                    bk32(vb), B["xfblk"][:, slot_x + 512 * c + 128 * s4:slot_x + 512 * c + 128 * s4 + 128],
                    B["wvb"][:, 512 * c:512 * c + 512],
                    start=(c == 0), stop=(c == 15)), inc=("P" if last else None))
            w("v", "P", vmm)
            vev[ti] = r("v", lambda e, ti=ti, vb=vb: e.tensor_tensor(
                B["v"][:, 512 * ti:512 * ti + 512], bk32(vb), B["bv"][:], ALU.add),
                inc="V")

    w("s", "V", vev[NKT - 1])
    for s8 in range(8):
        buf = s8 % 2
        w("g", "P", last_tr.get(NKT + s8 - 2, 0))
        gsem = "G%d" % buf
        r("g", lambda e, s8=s8, buf=buf: e.dma_start(
            out=B["xst"][buf][:], in_=H["xo"][128 * s8:128 * s8 + 128, :]),
          inc=gsem, amt=16)
        gneed = S_.cnt[gsem]
        w("t", gsem, gneed)

        def oview(grp, s8=s8):
            vv = B["xfo"][:].rearrange("p (c q) -> p c q", c=16)
            return vv[:, 4 * grp:4 * grp + 4, 128 * s8:128 * s8 + 128]
        last_tr[NKT + s8] = transpose_tile(B["xst"][buf], oview)

    qmm = {}
    qev = {}
    for j in range(16):
        buf = j % 2
        w("g", "P", qmm.get(j - 2, last_tr[NKT + 7]))
        gsem = "G%d" % buf
        r("g", lambda e, j=j, buf=buf: e.dma_start(
            out=B["xst"][buf][:], in_=H["wq"][j]), inc=gsem, amt=16)
        gneed = S_.cnt[gsem]
        w("t", gsem, gneed)
        qb_ = j % 2
        w("t", "V", qev.get(j - 2, max(kev[4 * (SEQ // 512 - 1) + 3], vev[NKT - 1])))
        for c in range(16):
            for hh in range(2):
                last = (c == 15 and hh == 1)
                idx = r("t", lambda e, c=c, hh=hh, qb_=qb_, buf=buf: e.matmul(
                    ps[:, 1024 * qb_ + 512 * hh:1024 * qb_ + 512 * hh + 512],
                    B["xst"][buf][:, 128 * c:128 * c + 128],
                    B["xfo"][:, 1024 * c + 512 * hh:1024 * c + 512 * hh + 512],
                    start=(c == 0), stop=(c == 15)), inc=("P" if last else None))
                if last:
                    qmm[j] = idx
        w("v", "P", qmm[j])
        qev[j] = r("v", lambda e, j=j, qb_=qb_: e.tensor_scalar(
            B["qt"][:, 1024 * j:1024 * j + 1024], ps[:, 1024 * qb_:1024 * qb_ + 1024],
            B["bq"][:, j:j + 1], None, ALU.add), inc="V")

    if H["_dbg"]:
        w("y", "V", max(qev[15], vev[NKT - 1], kev[4 * (SEQ // 512 - 1) + 3]))
        r("y", lambda e: e.dma_start(out=H["dkt"], in_=B["kt"][:]), inc="Y0", amt=16)
        r("y", lambda e: e.dma_start(out=H["dv"], in_=B["v"][:]), inc="Y0", amt=16)
        dbg0 = r("y", lambda e: e.dma_start(out=H["dqt"], in_=B["qt"][:]), inc="Y0", amt=16)
    psA = ps[:, 2048:2560]
    psB = ps[:, 2560:3072]
    emit_program.pending_norm = None
    emit_program.last_tb = 0
    prev_exp = (0, 0)
    pair_exp_base = qev[15] * 0 + S_.cnt["A"]
    prev_tb = vev[NKT - 1]
    prev_av = {}
    sc_last = {}
    wo_dma = {}
    qt_ready = qev[15]
    attn_first = True
    for j in range(16):
        for qb in range(2):
            cc = j // 4
            kvA = j // 4
            kvB = 4 + j // 4
            exp_id = {}
            av_id = {}
            sc_id = {}

            def emit_av(kt, kvA=kvA, kvB=kvB, exp_id=exp_id, av_id=av_id,
                        first_gate=None):
                pti = kt % 4
                w("t", "A", exp_id[kt])
                if kt == 0:
                    w("t", "V", first_gate)
                st = kt == 0
                sp = kt == 63
                r("t", lambda e, kt=kt, pti=pti, kvA=kvA, st=st, sp=sp: e.matmul(
                    psA[0:64, :], B["v"][:, 512 * kt + 64 * kvA:512 * kt + 64 * kvA + 64],
                    B["pt"][pti][:, 0:512], start=st, stop=sp, tile_position=(0, 0),
                    skip_group_check=True))
                r("t", lambda e, pti=pti, st=st, sp=sp: e.matmul(
                    psA[64:128, :], B["o64"][:, 0:64], B["pt"][pti][:, 512:1024],
                    start=st, stop=sp, tile_position=(0, 64), skip_group_check=True))
                r("t", lambda e, kt=kt, pti=pti, kvB=kvB, st=st, sp=sp: e.matmul(
                    psB[64:128, :], B["v"][:, 512 * kt + 64 * kvB:512 * kt + 64 * kvB + 64],
                    B["pt"][pti][:, 512:1024], start=st, stop=sp, tile_position=(0, 64),
                    skip_group_check=True))
                av_id[kt] = r("t", lambda e, pti=pti, st=st, sp=sp: e.matmul(
                    psB[0:64, :], B["o64"][:, 0:64], B["pt"][pti][:, 0:512],
                    start=st, stop=sp, tile_position=(0, 0), skip_group_check=True), inc="P")

            gate0 = prev_tb

            def emit_sc(kt, cc=cc, j=j, qb=qb, exp_id=exp_id, sc_id=sc_id):
                scb = kt % 2
                w("t", "A", exp_id.get(kt - 2, prev_exp[scb]))
                if attn_first and kt == 0:
                    w("t", "V", qt_ready)
                r("t", lambda e, kt=kt, scb=scb: e.matmul(
                    bk32(2 * scb),
                    B["kt"][0:64, 8192 * cc + 128 * kt:8192 * cc + 128 * kt + 128],
                    B["qt"][0:64, 1024 * j + 512 * qb:1024 * j + 512 * qb + 512],
                    start=True, stop=True, tile_position=(0, 0)))
                sc_id[kt] = r("t", lambda e, kt=kt, scb=scb: e.matmul(
                    bk32(2 * scb + 1),
                    B["kt"][64:128, 8192 * cc + 128 * kt:8192 * cc + 128 * kt + 128],
                    B["qt"][64:128, 1024 * j + 512 * qb:1024 * j + 512 * qb + 512],
                    start=True, stop=True, tile_position=(64, 0)), inc="P")

            def emit_exp(kt, exp_id=exp_id, sc_id=sc_id, av_id=av_id):
                ptw = av_id.get(kt - 4, prev_av.get(60 + kt, 0))
                w("s", "P", max(sc_id[kt], ptw))
                exp_id[kt] = r("s", lambda e, kt=kt: e.activation(
                    B["pt"][kt % 4][:], ps[:, 1024 * (kt % 2):1024 * (kt % 2) + 1024],
                    AF.Exp, bias=0.0, scale=0.125), inc="A")

            for k2 in range(0, 64, 2):
                emit_sc(k2)
                emit_exp(k2)
                emit_sc(k2 + 1)
                emit_exp(k2 + 1)
                if k2 == 0 and emit_program.pending_norm is not None:
                    emit_program.pending_norm()
                    emit_program.pending_norm = None
                if k2 >= 2:
                    emit_av(k2 - 2, first_gate=gate0)
                    emit_av(k2 - 1, first_gate=gate0)
            emit_av(62, first_gate=gate0)
            emit_av(63, first_gate=gate0)
            attn_first = False

            acols = slice(1024 * j + 512 * qb, 1024 * j + 512 * qb + 512)
            av63 = av_id[63]

            def do_norm(acols=acols, av63=av63):
                w("v", "P", av63)

                def _recip(e, dst, src_):
                    with B["nc"].allow_low_precision(reason="softmax 1/D broadcast via bf16 matmul"):
                        return e.reciprocal(dst, src_)
                r("v", lambda e: _recip(e, B["rdb"][0:1, :], psB[0:1, :]),
                  inc="V", selfwait=True)
                rid = r("v", lambda e: _recip(e, B["rdb"][64:65, :], psA[64:65, :]), inc="V")
                w("t", "V", rid)
                bA = r("t", lambda e: e.matmul(
                    psB[0:64, :], B["o64"][0:1, 0:64], B["rdb"][0:1, :],
                    start=True, stop=True), inc="P")
                w("v", "P", bA)
                r("v", lambda e: e.tensor_copy(B["bcs"][0:64, :], psB[0:64, :]),
                  inc="V", selfwait=True)
                tA = r("v", lambda e: e.tensor_tensor(
                    B["ao"][0:64, acols], psA[0:64, :], B["bcs"][0:64, :], ALU.mult),
                    inc="V")
                w("t", "V", tA)
                bB = r("t", lambda e: e.matmul(
                    psA[64:128, :], B["o64"][64:65, 0:64], B["rdb"][64:65, :],
                    start=True, stop=True), inc="P")
                w("v", "P", bB)
                r("v", lambda e: e.tensor_copy(B["bcs"][64:128, :], psA[64:128, :]),
                  inc="V", selfwait=True)
                tB = r("v", lambda e: e.tensor_tensor(
                    B["ao"][64:128, acols], psB[64:128, :], B["bcs"][64:128, :], ALU.mult),
                    inc="V")
                return tB
            emit_program.pending_norm_ret = None

            def pend(do_norm=do_norm):
                emit_program.last_tb = do_norm()
            emit_program.pending_norm = pend
            prev_tb_predicted = S_.cnt["V"] + 6
            prev_tb = prev_tb_predicted
            prev_av = {k: v for k, v in av_id.items() if k >= 60}
            pair_exp_base = exp_id[63]
            prev_exp = (exp_id[62], exp_id[63])
            sc_last[(j, qb)] = sc_id[63]
        if j % 4 == 3:
            grp4 = j // 4
            for c4 in range(4):
                c = 4 * grp4 + c4
                w("g", "P", sc_last[(j, 1)])
                if H["_dbg"]:
                    w("g", "Y0", dbg0)
                wo_dma[c] = r("g", lambda e, c=c: e.dma_start(
                    out=B["wo"][:, 2048 * c:2048 * c + 2048], in_=H["wo"][c]),
                    inc="W", amt=16)

    if emit_program.pending_norm is not None:
        emit_program.pending_norm()
        emit_program.pending_norm = None
    attn_done_v = prev_tb
    attn_done_p = S_.cnt["P"]
    if H["_dbg"]:
        w("y", "V", attn_done_v)
        dbg1 = r("y", lambda e: e.dma_start(out=H["dao"], in_=B["ao"][:]), inc="Y0", amt=16)
    attn_last_exp = pair_exp_base

    w("g", "P", attn_done_p)
    bo_id = r("g", lambda e: e.dma_start(out=B["borow"][:], in_=H["bo"][0:1, :]), inc="W", amt=16)

    if H["_dbg"]:
        w("y", "W", wo_dma[15])
        dbg2 = r("y", lambda e: e.dma_start(out=H["dwo"], in_=B["wo"][:]), inc="Y0", amt=16)
    fbid = {}
    fbev = {}
    xboid = {}
    ln1 = {}
    castid = {}
    trxid = {}
    x2fev = {}
    x2store = {}

    def op_part1(tt):
        par = tt % 2
        w("y", "V", fbev.get(4 * (tt - 2) + 3, 0))
        if tt < 2:
            w("y", "P", attn_done_p)
        ssem = "S%d" % par
        xrid = r("y", lambda e, tt=tt, par=par: e.dma_start(
            out=B["xr"][par][:], in_=H["xo"][128 * tt:128 * tt + 128, :]),
            inc=ssem, amt=16)
        w("v", ssem, xrid)
        w("v", "A", castid.get(tt - 2, 0))
        w("v", "Y%d" % par, x2store.get(tt - 2, 0))
        for fb in range(4):
            n = 4 * tt + fb
            slot = n % 4
            w("t", "V", fbev.get(n - 4, 0))
            if n == 0:
                w("t", "V", attn_done_v)
                w("t", "A", attn_last_exp)
                w("t", "W", wo_dma[15])
            if n == 0:
                w("t", "W", bo_id)
            for c in range(16):
                r("t", lambda e, c=c, fb=fb, tt=tt, slot=slot: e.matmul(
                    bk32(slot), B["ao"][:, 1024 * c + 128 * tt:1024 * c + 128 * tt + 128],
                    B["wo"][:, 2048 * c + 512 * fb:2048 * c + 512 * fb + 512],
                    start=(c == 0), stop=False))
            idx = r("t", lambda e, fb=fb, slot=slot: e.matmul(
                bk32(slot), B["o64"][0:1, :], B["borow"][0:1, 512 * fb:512 * fb + 512],
                start=False, stop=True), inc="P")
            fbid[(tt, fb)] = idx
            w("v", "P", idx)
            fbev[n] = r("v", lambda e, fb=fb, par=par, slot=slot: e.tensor_tensor(
                B["x2t"][par][:, 512 * fb:512 * fb + 512], bk32(slot),
                B["xr"][par][:, 512 * fb:512 * fb + 512], ALU.add), inc="V",
                selfwait=(fb == 3))

    def op_part2(tt):
        par = tt % 2
        w("s", "V", fbev[4 * tt + 3])
        r("s", lambda e, par=par: e.activation(
            B["sqs"][:], B["x2t"][par][:], AF.Copy, accum_out=B["ln"][:, 0:1]))
        cpid = r("s", lambda e: e.activation(
            B["ln"][:, 6:7], B["bk"][:, 0:1], AF.Copy), inc="A")
        w("v", "A", cpid)
        r("v", lambda e: e.tensor_scalar(B["ln"][:, 1:2], B["ln"][:, 0:1], 1.0 / D, None, ALU.mult),
          inc="V", selfwait=True)
        cen = r("v", lambda e, par=par: e.tensor_scalar(
            B["sqs"][:], B["x2t"][par][:], B["ln"][:, 1:2], None, ALU.subtract),
            inc="V", selfwait=True)
        w("s", "V", cen)
        r("s", lambda e, par=par: e.activation(
            B["x2t"][par][:], B["sqs"][:], AF.Square, accum_out=B["ln"][:, 2:3]))
        sqid = r("s", lambda e: e.activation(
            B["ln"][:, 6:7], B["bk"][:, 0:1], AF.Copy), inc="A")
        w("v", "A", sqid)
        vv_ = r("v", lambda e: e.tensor_scalar(
            B["ln"][:, 3:4], B["ln"][:, 2:3], 1.0 / D, EPS, ALU.mult, ALU.add),
            inc="V", selfwait=True)
        w("s", "V", vv_)
        sqr = r("s", lambda e: e.activation(B["ln"][:, 4:5], B["ln"][:, 3:4], AF.Sqrt), inc="A")
        w("v", "A", sqr)
        r("v", lambda e: e.reciprocal(B["ln"][:, 5:6], B["ln"][:, 4:5]),
          inc="V", selfwait=True)
        ln1[tt] = r("v", lambda e, par=par: e.tensor_scalar(
            B["x2t"][par][:], B["sqs"][:], B["ln"][:, 5:6], None, ALU.mult), inc="V")
        w("s", "V", ln1[tt])
        w("s", "P", trxid.get(tt - 2, 0))
        castid[tt] = r("s", lambda e, par=par: e.copy(B["x2bf"][par][:], B["x2t"][par][:]), inc="A")

    def op_part2b(tt):
        par = tt % 2
        w("t", "A", castid[tt])

        def xview(grp, tt=tt):
            vv2 = B["x2f"][:].rearrange("p (c q) -> p c q", c=16)
            return vv2[:, 4 * grp:4 * grp + 4, 128 * tt:128 * tt + 128]
        trxid[tt] = transpose_tile(B["x2bf"][par], xview)
        x2fev[tt] = ev[transpose_tile.seq - 1]
        w("y", "V", ln1[tt])
        x2store[tt] = r("y", lambda e, tt=tt, par=par: e.dma_start(
            out=H["x2s"][128 * tt:128 * tt + 128, :], in_=B["x2t"][par][:]),
            inc="Y%d" % par, amt=16)

    for tt in range(8):
        if tt >= 2:
            op_part2b(tt - 2)
        op_part1(tt)
        if tt >= 1:
            op_part2(tt - 1)
    op_part2(7)
    op_part2b(6)
    op_part2b(7)

    oproj_done_p = S_.cnt["P"]
    oproj_last_fbev = fbev[31]

    conv_id = {}
    gmm = {}
    umm = {}
    hTT = {}
    silu_id = {}
    for ib in range(64):
        par = ib % 2
        w("y", "V", conv_id.get(ib - 1, 0))
        if ib == 0:
            w("y", "P", oproj_done_p)
            if H["_dbg"]:
                w("y", "Y0", dbg1)
        r("y", lambda e, ib=ib: e.dma_start(out=B["wgst"][:], in_=H["wg"][ib]),
          inc="S0", amt=16)
        wld = r("y", lambda e, ib=ib: e.dma_start(out=B["wust"][:], in_=H["wu"][ib]),
                inc="S0", amt=16)
        w("v", "S0", wld)
        w("v", "P", umm.get(ib - 2, 0))
        r("v", lambda e, par=par: e.tensor_scalar(
            B["wgb"][par][:], B["wgst"][:], -128.0, None, ALU.add))
        conv_id[ib] = r("v", lambda e, par=par: e.tensor_scalar(
            B["wub"][par][:], B["wust"][:], -128.0, None, ALU.add), inc="V")
        for th in range(2):
            n = 2 * ib + th
            s3 = n % 3
            w("t", "V", conv_id[ib])
            w("t", "V", hTT.get(n - 3, 0))
            if n == 0:
                w("t", "A", x2fev[7])
                w("t", "V", oproj_last_fbev)
            for c in range(16):
                last = c == 15
                idx = r("t", lambda e, c=c, th=th, par=par, s3=s3: e.matmul(
                    ps[:, 1024 * s3:1024 * s3 + 512],
                    B["wgb"][par][:, 128 * c:128 * c + 128],
                    B["x2f"][:, 1024 * c + 512 * th:1024 * c + 512 * th + 512],
                    start=(c == 0), stop=(c == 15)), inc=("P" if last else None))
            gmm[n] = idx
            for c in range(16):
                last = c == 15
                idx = r("t", lambda e, c=c, th=th, par=par, s3=s3: e.matmul(
                    ps[:, 1024 * s3 + 512:1024 * s3 + 1024],
                    B["wub"][par][:, 128 * c:128 * c + 128],
                    B["x2f"][:, 1024 * c + 512 * th:1024 * c + 512 * th + 512],
                    start=(c == 0), stop=(c == 15)), inc=("P" if last else None))
            umm[n] = idx
            w("s", "P", gmm[n])
            w("s", "V", hTT.get(n - 1, 0))
            silu_id[n] = r("s", lambda e, ib=ib, s3=s3: e.activation(
                B["sg"][:], ps[:, 1024 * s3:1024 * s3 + 512], AF.Silu,
                bias=0.0, scale=B["gs"][:, ib:ib + 1]), inc="A")
            w("v", "P", umm[n])
            r("v", lambda e, ib=ib, s3=s3: e.tensor_scalar(
                B["usv"][:], ps[:, 1024 * s3 + 512:1024 * s3 + 1024],
                B["us"][:, ib:ib + 1], None, ALU.mult), inc="V", selfwait=True)
            w("v", "A", silu_id[n])
            if n == 0 and H["_dbg"]:
                w("v", "Y0", dbg2)
            if ib == 32 and th == 0:
                w("v", "Y1", x2store[7])
                w("v", "A", castid[7])
            hTT[n] = r("v", lambda e, ib=ib, th=th: e.tensor_tensor(
                B["hf"][:, 1024 * ib + 512 * th:1024 * ib + 512 * th + 512],
                B["sg"][:], B["usv"][:], ALU.mult), inc="V")
    hlast = hTT[127]
    ffa_last_silu = silu_id[127]

    dmm = {}
    fmid = {}
    trf = {}
    tev = {}
    tokstore = {}
    wdc = {}

    def ffb_evac(hc):
        par = hc % 2
        w("v", "P", dmm[4 * hc + 3])
        w("v", "P", trf.get(hc - 1, 0))
        fmid[hc] = r("v", lambda e, hc=hc, par=par: e.tensor_scalar(
            B["fm"][:], ps[:, 1024 * par:1024 * par + 1024],
            B["ds"][:, hc:hc + 1], None, ALU.mult), inc="V", selfwait=True)
        w("t", "V", fmid[hc])
        w("t", "A", tev.get(hc - 1, 0))
        for wv_ in range(8):
            off = 2048 + 128 * wv_
            last = wv_ == 7
            idx = r("t", lambda e, wv_=wv_, off=off: e.transpose(
                ps[:, off:off + 128], B["fm"][:, 128 * wv_:128 * wv_ + 128],
                B["idf"][:]), inc=("P" if last else None))
        trf[hc] = idx
        w("s", "P", trf[hc])
        w("s", "Y0", tokstore.get(hc - 1, 0))
        r("s", lambda e: e.copy(B["tok"][0][:], bk32(4)))
        tev[hc] = r("s", lambda e: e.copy(B["tok"][1][:], bk32(5)), inc="A")
        w("y", "A", tev[hc])
        for gi2 in range(2):
            tokstore[hc] = r("y", lambda e, gi2=gi2, hc=hc: e.dma_start(
                out=H["ffs"].rearrange("(w p) h -> p w h", p=128)[
                    :, 4 * gi2:4 * gi2 + 4, 128 * hc:128 * hc + 128],
                in_=B["tok"][gi2][:].rearrange("p (w h) -> p w h", w=4)),
                inc="Y0", amt=16)

    for hc in range(16):
        par = hc % 2
        for ibg in range(4):
            gi = 4 * hc + ibg
            sp_ = gi % 2
            w("g", "V", wdc.get(gi - 2, 0))
            if gi == 0:
                w("g", "V", hlast)
            gsem = "G%d" % sp_
            ldid = r("g", lambda e, hc=hc, ibg=ibg, sp_=sp_: e.dma_start(
                out=B["wdst"][sp_][:].rearrange("p (g h) -> p g h", g=16),
                in_=H["wd"][hc, 16 * ibg:16 * ibg + 16].rearrange("g p h -> p g h")),
                inc=gsem, amt=16)
            w("v", gsem, ldid)
            w("v", "P", dmm.get(gi - 2, 0))
            wdc[gi] = r("v", lambda e, sp_=sp_: e.tensor_scalar(
                B["wdb"][sp_][:], B["wdst"][sp_][:], -128.0, None, ALU.add), inc="V")
            w("t", "V", wdc[gi])
            if ibg == 0:
                w("t", "V", fmid.get(hc - 2, 0))
                if hc == 0:
                    w("t", "V", hlast)
                    w("t", "A", ffa_last_silu)
            for i16 in range(16):
                ib = 16 * ibg + i16
                for th in range(2):
                    last = (i16 == 15 and th == 1)
                    idx = r("t", lambda e, i16=i16, th=th, ib=ib, par=par, sp_=sp_: e.matmul(
                        ps[:, 1024 * par + 512 * th:1024 * par + 512 * th + 512],
                        B["wdb"][sp_][:, 128 * i16:128 * i16 + 128],
                        B["hf"][:, 1024 * ib + 512 * th:1024 * ib + 512 * th + 512],
                        start=(ib == 0), stop=(ib == 63)), inc=("P" if last else None))
            dmm[gi] = idx
            if ibg == 0 and hc >= 1:
                ffb_evac(hc - 1)
    ffb_evac(15)

    ffb_done_p = dmm[63]

    ln2 = {}
    ystore = {}
    for tt in range(8):
        par = tt % 2
        w("y", "Y0", tokstore[15])
        w("y", "V", ln2.get(tt - 2, 0))
        ssem = "S%d" % par
        r("y", lambda e, tt=tt, par=par: e.dma_start(
            out=B["ffin"][par][:], in_=H["ffs"][128 * tt:128 * tt + 128, :]),
          inc=ssem, amt=16)
        ffld = r("y", lambda e, tt=tt, par=par: e.dma_start(
            out=B["x2in"][par][:], in_=H["x2s"][128 * tt:128 * tt + 128, :]),
            inc=ssem, amt=16)
        w("v", ssem, ffld)
        if tt == 0:
            w("v", "P", ffb_done_p)
            w("v", "Y0", tokstore[15])
        w("v", "Y%d" % par, ystore.get(tt - 2, 0))
        tta = r("v", lambda e, par=par: e.tensor_tensor(
            B["outt"][par][:], B["ffin"][par][:], B["x2in"][par][:], ALU.add),
          inc="V", selfwait=True)
        w("s", "V", tta)
        r("s", lambda e, par=par: e.activation(
            B["sq2"][:], B["outt"][par][:], AF.Copy, accum_out=B["ln"][:, 8:9]))
        cpid = r("s", lambda e: e.activation(
            B["ln"][:, 14:15], B["bk"][:, 0:1], AF.Copy), inc="A")
        w("v", "A", cpid)
        r("v", lambda e: e.tensor_scalar(B["ln"][:, 9:10], B["ln"][:, 8:9], 1.0 / D, None, ALU.mult),
          inc="V", selfwait=True)
        cen = r("v", lambda e, par=par: e.tensor_scalar(
            B["sq2"][:], B["outt"][par][:], B["ln"][:, 9:10], None, ALU.subtract),
            inc="V", selfwait=True)
        w("s", "V", cen)
        r("s", lambda e, par=par: e.activation(
            B["outt"][par][:], B["sq2"][:], AF.Square, accum_out=B["ln"][:, 10:11]))
        sqid = r("s", lambda e: e.activation(
            B["ln"][:, 14:15], B["bk"][:, 0:1], AF.Copy), inc="A")
        w("v", "A", sqid)
        vv_ = r("v", lambda e: e.tensor_scalar(
            B["ln"][:, 11:12], B["ln"][:, 10:11], 1.0 / D, EPS, ALU.mult, ALU.add),
            inc="V", selfwait=True)
        w("s", "V", vv_)
        sqr = r("s", lambda e: e.activation(B["ln"][:, 12:13], B["ln"][:, 11:12], AF.Sqrt), inc="A")
        w("v", "A", sqr)
        r("v", lambda e: e.reciprocal(B["ln"][:, 13:14], B["ln"][:, 12:13]),
          inc="V", selfwait=True)
        ln2[tt] = r("v", lambda e, par=par: e.tensor_scalar(
            B["outt"][par][:], B["sq2"][:], B["ln"][:, 13:14], None, ALU.mult), inc="V")
        w("y", "V", ln2[tt])
        ystore[tt] = r("y", lambda e, tt=tt, par=par: e.dma_start(
            out=H["y"][128 * tt:128 * tt + 128, :], in_=B["outt"][par][:]),
            inc="Y%d" % par, amt=16)

    w("y", "Y0", S_.cnt["Y0"])
    w("y", "Y1", S_.cnt["Y1"])


def _pack(inputs):
    f32 = np.float32
    Wq = np.asarray(inputs["L0_attn_Wq"], f32)
    bq = np.asarray(inputs["L0_attn_bq"], f32)
    Wk = np.asarray(inputs["L0_attn_Wk"], f32)
    bk = np.asarray(inputs["L0_attn_bk"], f32)
    Wv = np.asarray(inputs["L0_attn_Wv"], f32)
    bv = np.asarray(inputs["L0_attn_bv"], f32)
    Wo = np.asarray(inputs["L0_attn_Wo"], f32)
    bo = np.asarray(inputs["L0_attn_bo"], f32)
    gw = np.ascontiguousarray(np.asarray(inputs["L0_ffn_gate_w"], np.int32))
    uw = np.ascontiguousarray(np.asarray(inputs["L0_ffn_up_w"], np.int32))
    dw = np.ascontiguousarray(np.asarray(inputs["L0_ffn_down_w"], np.int32))
    gsc = np.asarray(inputs["L0_ffn_gate_scale"], f32)
    usc = np.asarray(inputs["L0_ffn_up_scale"], f32)
    dsc = np.asarray(inputs["L0_ffn_down_scale"], f32)

    shared = {}
    wq_t = np.empty((16, 128, 2048), f32)
    bq_t = np.empty((128, 16), f32)
    for j in range(16):
        cols = np.r_[64 * j:64 * j + 64, 64 * (j + 16):64 * (j + 16) + 64]
        tmp = Wq[:, cols].reshape(16, 128, 128)
        wq_t[j] = tmp.transpose(1, 0, 2).reshape(128, 2048)
        bq_t[:, j] = bq[cols]
    shared["wq_t"] = np.ascontiguousarray(wq_t)
    shared["bq_t"] = np.ascontiguousarray(bq_t)
    col_order = np.empty(512, np.int64)
    for cc in range(4):
        for dd in range(128):
            kv = cc + (4 if dd >= 64 else 0)
            col_order[128 * cc + dd] = 64 * kv + dd % 64
    wk_re = Wk[:, col_order]
    shared["wk_t"] = np.ascontiguousarray(
        wk_re.reshape(16, 128, 512).transpose(1, 0, 2).reshape(128, 8192))
    shared["bk_t"] = np.ascontiguousarray(bk[col_order].reshape(4, 128).T)
    shared["wv_t"] = np.ascontiguousarray(
        Wv.reshape(16, 128, 512).transpose(1, 0, 2).reshape(128, 8192))
    shared["bv_bc"] = np.ascontiguousarray(np.broadcast_to(bv, (128, 512)))
    wo_t = np.empty((16, 128, 2048), f32)
    for c in range(16):
        rows = np.r_[64 * c:64 * c + 64, 64 * (c + 16):64 * (c + 16) + 64]
        wo_t[c] = Wo[rows, :]
    shared["wo_t"] = np.ascontiguousarray(wo_t)
    shared["bo_bc"] = np.ascontiguousarray(np.broadcast_to(bo, (128, 2048)))
    shared["wg_t"] = np.ascontiguousarray(
        gw.reshape(64, 128, 16, 128).transpose(0, 3, 2, 1).reshape(64, 128, 2048))
    shared["wu_t"] = np.ascontiguousarray(
        uw.reshape(64, 128, 16, 128).transpose(0, 3, 2, 1).reshape(64, 128, 2048))
    shared["wd_t"] = np.ascontiguousarray(
        dw.reshape(16, 128, 64, 128).transpose(0, 2, 3, 1))
    shared["gs_t"] = np.ascontiguousarray(gsc.reshape(64, 128).T)
    shared["us_t"] = np.ascontiguousarray(usc.reshape(64, 128).T)
    shared["ds_t"] = np.ascontiguousarray(dsc.reshape(16, 128).T)
    shared["identb"] = np.eye(128, dtype=np.float32).astype(ml_dtypes.bfloat16)
    shared["identf"] = np.eye(128, dtype=np.float32)
    shared["ones64b"] = np.ones((128, 128), np.float32).astype(ml_dtypes.bfloat16)
    shared["onesf"] = np.ones((128, 64), np.float32)
    return shared


def kernel(**inputs):
    x = np.ascontiguousarray(np.asarray(inputs["L0_attn_x"], np.float32).reshape(SEQ, D))
    shared = _pack(inputs)
    nc = build_nc()
    in_maps = []
    for c in range(NCORES):
        m = dict(shared)
        m["x_full"] = x
        m["x_own"] = np.ascontiguousarray(x[T * c:T * (c + 1)])
        in_maps.append(m)
    res = run_bass_kernel_spmd(nc, in_maps, core_ids=list(range(NCORES)))
    y = np.concatenate([np.asarray(res.results[c]["y"]) for c in range(NCORES)], axis=0)
    return y.reshape(1, SEQ, D).astype(np.float32)


if __name__ == "__main__":
    import reference as _ref
    inp = _ref.setup_inputs()
    out = kernel(**{k: np.asarray(v) for k, v in inp.items()})
    want = np.asarray(_ref.reference(**inp))
    err = np.abs(out - want)
    denom = np.mean(want.astype(np.float64) ** 2)
    print("rvar:", float(np.mean((out.astype(np.float64) - want.astype(np.float64)) ** 2) / denom))
    print("maxabs:", float(err.max()))


# version 16
# speedup vs baseline: 1.3237x; 1.0052x over previous
import numpy as np
import ml_dtypes
import concourse.bass as bass
import concourse.mybir as mybir
from concourse.bass_utils import run_bass_kernel_spmd

F32 = mybir.dt.float32
BF16 = mybir.dt.bfloat16
I32 = mybir.dt.int32
AF = mybir.ActivationFunctionType
ALU = mybir.AluOpType
AX = mybir.AxisListType

NCORES = 8
SEQ = 8192
D = 2048
T = SEQ // NCORES
NKT = SEQ // 128
EPS = 1e-5


def build_nc():
    nc = bass.Bass("TRN2", target_bir_lowering=False)

    H = {}
    H["xf"] = nc.dram_tensor("x_full", [SEQ, D], F32, kind="ExternalInput").ap()
    H["xo"] = nc.dram_tensor("x_own", [T, D], F32, kind="ExternalInput").ap()
    H["wq"] = nc.dram_tensor("wq_t", [16, 128, 2048], F32, kind="ExternalInput").ap()
    H["bq"] = nc.dram_tensor("bq_t", [128, 16], F32, kind="ExternalInput").ap()
    H["wk"] = nc.dram_tensor("wk_t", [128, 8192], F32, kind="ExternalInput").ap()
    H["bk"] = nc.dram_tensor("bk_t", [128, 4], F32, kind="ExternalInput").ap()
    H["wv"] = nc.dram_tensor("wv_t", [128, 8192], F32, kind="ExternalInput").ap()
    H["bv"] = nc.dram_tensor("bv_bc", [128, 512], F32, kind="ExternalInput").ap()
    H["wo"] = nc.dram_tensor("wo_t", [16, 128, 2048], F32, kind="ExternalInput").ap()
    H["bo"] = nc.dram_tensor("bo_bc", [128, 2048], F32, kind="ExternalInput").ap()
    H["wg"] = nc.dram_tensor("wg_t", [64, 128, 2048], I32, kind="ExternalInput").ap()
    H["wu"] = nc.dram_tensor("wu_t", [64, 128, 2048], I32, kind="ExternalInput").ap()
    H["wd"] = nc.dram_tensor("wd_t", [16, 64, 128, 128], I32, kind="ExternalInput").ap()
    H["gs"] = nc.dram_tensor("gs_t", [128, 64], F32, kind="ExternalInput").ap()
    H["us"] = nc.dram_tensor("us_t", [128, 64], F32, kind="ExternalInput").ap()
    H["ds"] = nc.dram_tensor("ds_t", [128, 16], F32, kind="ExternalInput").ap()
    H["idb"] = nc.dram_tensor("identb", [128, 128], BF16, kind="ExternalInput").ap()
    H["idf"] = nc.dram_tensor("identf", [128, 128], F32, kind="ExternalInput").ap()
    H["o64"] = nc.dram_tensor("ones64b", [128, 128], BF16, kind="ExternalInput").ap()
    H["of"] = nc.dram_tensor("onesf", [128, 64], F32, kind="ExternalInput").ap()
    H["y"] = nc.dram_tensor("y", [T, D], F32, kind="ExternalOutput").ap()
    import os
    H["_dbg"] = bool(os.environ.get("KDEBUG"))
    if H["_dbg"]:
        H["dkt"] = nc.dram_tensor("dbg_kt", [128, 32768], BF16, kind="ExternalOutput").ap()
        H["dv"] = nc.dram_tensor("dbg_v", [128, 32768], BF16, kind="ExternalOutput").ap()
        H["dqt"] = nc.dram_tensor("dbg_qt", [128, 16384], BF16, kind="ExternalOutput").ap()
        H["dao"] = nc.dram_tensor("dbg_ao", [128, 16384], BF16, kind="ExternalOutput").ap()
        H["dwo"] = nc.dram_tensor("dbg_wo", [128, 32768], BF16, kind="ExternalOutput").ap()
        H["ds_"] = nc.dram_tensor("dbg_s", [T, D], F32, kind="ExternalOutput").ap()
    import os as _os
    _scrkind = "ExternalOutput" if _os.environ.get("KDEBUG") else "Internal"
    H["x2s"] = nc.dram_tensor("x2_scr", [T, D], F32, kind=_scrkind).ap()
    H["ffs"] = nc.dram_tensor("ffn_scr", [T, D], F32, kind=_scrkind).ap()

    SB_BASE = 16512

    def at(name, shape, dtype, off):
        return nc.alloc_sbuf_tensor_at(name, shape, dtype, offset=SB_BASE + off)

    B = {}
    B["kt"] = at("kt_b", [128, 32768], BF16, 0)
    B["v"] = at("v_b", [128, 32768], BF16, 65536)
    B["wo"] = at("wo_b", [128, 32768], BF16, 0)
    B["hf"] = at("hfull", [128, 65536], BF16, 0)
    B["bo"] = at("bo_s", [128, 2048], F32, 65536)
    B["xr"] = [at(f"xr{i}", [128, 2048], F32, 73728 + 8192 * i) for i in range(2)]
    B["x2t"] = [at(f"x2t{i}", [128, 2048], F32, 90112 + 8192 * i) for i in range(2)]
    B["x2bf"] = [at("x2bf0", [128, 2048], BF16, 106496), at("x2bf1", [128, 2048], BF16, 122880)]
    B["sqs"] = at("sqs", [128, 2048], F32, 110592)
    B["xbo"] = at("xbo", [128, 2048], F32, 118784)
    B["qt"] = at("qt_b", [128, 16384], BF16, 131072)
    B["x2f"] = at("x2f", [128, 16384], BF16, 131072)
    B["xfblk"] = at("xfblk", [128, 16384], BF16, 131072)
    B["wkb"] = at("wkb", [128, 8192], BF16, 163840)
    B["ffin"] = [at(f"ffin{i}", [128, 2048], F32, 131072 + 8192 * i) for i in range(2)]
    B["x2in"] = [at(f"x2in{i}", [128, 2048], F32, 147456 + 8192 * i) for i in range(2)]
    B["ao"] = at("ao_b", [128, 16384], BF16, 163840)
    B["xfo"] = at("xfown", [128, 16384], BF16, 163840)
    B["wvb"] = at("wvb", [128, 8192], BF16, 180224)
    B["bv"] = at("bv_s", [128, 512], F32, 206752)
    B["wgst"] = at("wgstage", [128, 2048], I32, 163840)
    B["wust"] = at("wustage", [128, 2048], I32, 172032)
    B["wgb"] = [at(f"wgb{i}", [128, 2048], BF16, 180224 + 4096 * i) for i in range(2)]
    B["wub"] = [at(f"wub{i}", [128, 2048], BF16, 188416 + 4096 * i) for i in range(2)]
    B["wdst"] = [at(f"wdst{i}", [128, 2048], I32, 163840 + 8192 * i) for i in range(2)]
    B["wdb"] = [at(f"wdb{i}", [128, 2048], BF16, 180224 + 4096 * i) for i in range(2)]
    B["fm"] = at("fm_s", [128, 1024], F32, 188416)
    B["tok"] = [at(f"tokst{i}", [128, 512], F32, 192512 + 2048 * i) for i in range(2)]
    B["outt"] = [at(f"outt{i}", [128, 2048], F32, 163840 + 8192 * i) for i in range(2)]
    B["sq2"] = at("sq2", [128, 2048], F32, 180224)
    B["idb"] = at("idb_s", [128, 128], BF16, 196608)
    B["idf"] = at("idf_s", [128, 128], F32, 196864)
    B["o64"] = at("o64_s", [128, 128], BF16, 197376)
    B["of"] = at("of_s", [128, 64], F32, 197632)
    B["bq"] = at("bq_s", [128, 16], F32, 197888)
    B["bk"] = at("bk_s", [128, 4], F32, 197952)
    B["gs"] = at("gs_s", [128, 64], F32, 197984)
    B["us"] = at("us_s", [128, 64], F32, 198240)
    B["ds"] = at("ds_s", [128, 16], F32, 198496)
    B["ln"] = at("lnst", [128, 16], F32, 210848)
    B["xst"] = [at(f"xstag{i}", [128, 2048], BF16, 198560 + 4096 * i) for i in range(2)]
    B["pt"] = [at(f"pt{i}", [128, 1024], BF16, 198560 + 2048 * i) for i in range(4)]
    B["rd"] = at("rd_t", [128, 512], F32, 206752)
    B["bcs"] = at("bcs", [128, 512], F32, 208800)
    B["sg"] = at("sg_s", [128, 512], F32, 206752)
    B["rdb"] = at("rdb", [128, 512], BF16, 210912)
    B["usv"] = at("usv_s", [128, 512], F32, 208800)
    B["borow"] = at("borow", [1, 2048], BF16, 118784)

    B["nc"] = nc
    B["ps"] = nc.alloc_psum_tensor("ps32", [128, 3072], F32)
    B["psbf"] = nc.alloc_psum_tensor("psbf", [128, 2048], BF16)

    from contextlib import ExitStack
    with ExitStack() as es:
        ec = es.enter_context
        sems = {k: ec(nc.semaphore(f"sem{k}")) for k in
                ("P", "A", "V", "G0", "G1", "W", "S0", "S1", "Y0", "Y1")}
        block = ec(nc.Block())

        class Sched:
            def __init__(self, only, eng):
                self.only = only
                self.eng = eng
                self.cnt = {k: 0 for k in
                            ("P", "A", "V", "G0", "G1", "W", "S0", "S1", "Y0", "Y1")}

            def wait(self, engname, sem, val):
                if val is None or val <= 0:
                    return
                if self.only == engname:
                    self.eng.wait_ge(sems[sem], val)

            def run(self, engname, fn, inc=None, amt=1, selfwait=False):
                if inc is not None:
                    self.cnt[inc] += amt
                if self.only == engname:
                    inst = fn(self.eng)
                    if inc is not None:
                        inst.then_inc(sems[inc], amt)
                        if selfwait:
                            self.eng.wait_ge(sems[inc], self.cnt[inc])
                return self.cnt[inc] if inc is not None else None

        @block.gpsimd
        def _(g):
            emit_program(Sched("g", g), H, B)

        @block.tensor
        def _(t):
            emit_program(Sched("t", t), H, B)

        @block.scalar
        def _(s):
            emit_program(Sched("s", s), H, B)

        @block.vector
        def _(v):
            emit_program(Sched("v", v), H, B)

        @block.sync
        def _(sy):
            emit_program(Sched("y", sy), H, B)

    return nc

def emit_program(S_, H, B):
    w = S_.wait
    r = S_.run
    ps = B["ps"]
    psbf = B["psbf"]

    def bk32(i, n=512):
        return ps[:, 512 * i:512 * i + n]

    for k in ("idb", "idf", "o64", "of", "bq", "bk", "gs", "us", "ds", "bv"):
        r("g", lambda e, k=k: e.dma_start(out=B[k][:], in_=H[k]), inc="W", amt=16)
    r("g", lambda e: e.dma_start(out=B["wkb"][:], in_=H["wk"]), inc="W", amt=16)
    wconsts = r("g", lambda e: e.dma_start(out=B["wvb"][:], in_=H["wv"]), inc="W", amt=16)

    ev = {}

    def transpose_tile(src_buf, dst3):
        g0 = transpose_tile.seq
        for grp in range(4):
            half = grp % 2
            w("t", "A", ev.get(g0 + grp - 2, 0))
            for cc in range(4):
                c = 4 * grp + cc
                last = cc == 3
                tid = r("t", lambda e, c=c, half=half, cc=cc, src_buf=src_buf: e.transpose(
                    psbf[:, 1024 * half + 128 * cc:1024 * half + 128 * cc + 128],
                    src_buf[:, 128 * c:128 * c + 128], B["idb"][:]),
                    inc=("P" if last else None))
            w("s", "P", tid)
            ev[g0 + grp] = r("s", lambda e, half=half, grp=grp, dst3=dst3: e.copy(
                dst3(grp),
                psbf[:, 1024 * half:1024 * half + 512].rearrange("p (g q) -> p g q", g=4)),
                inc="A")
        transpose_tile.seq += 4
        return tid
    transpose_tile.seq = 0

    kev = {}
    vev = {}
    last_tr = {}
    NBLK = SEQ // 512
    for blk in range(NBLK):
        slot_x = 8192 * (blk % 2)
        for s4 in range(4):
            ti = 4 * blk + s4
            buf = ti % 2
            w("g", "P", last_tr.get(ti - 2, 0))
            gsem = "G%d" % buf
            r("g", lambda e, ti=ti, buf=buf: e.dma_start(
                out=B["xst"][buf][:], in_=H["xf"][128 * ti:128 * ti + 128, :]),
              inc=gsem, amt=16)
            gneed = S_.cnt[gsem]
            if ti < 2:
                w("t", "W", wconsts)
                w("v", "W", wconsts)
            w("t", gsem, gneed)

            def dview(grp, slot_x=slot_x, s4=s4):
                vv = B["xfblk"][:].rearrange("p (c q) -> p c q", q=512)
                c0 = slot_x // 512 + 4 * grp
                return vv[:, c0:c0 + 4, 128 * s4:128 * s4 + 128]
            last_tr[ti] = transpose_tile(B["xst"][buf], dview)
        w("t", "A", ev[transpose_tile.seq - 1])
        for cc in range(4):
            kseq = 4 * blk + cc
            pb = cc % 2
            w("t", "V", kev.get(kseq - 2, 0))
            for c in range(16):
                last = c == 15
                kmm = r("t", lambda e, cc=cc, c=c, pb=pb, slot_x=slot_x: e.matmul(
                    bk32(pb),
                    B["wkb"][:, 512 * c + 128 * cc:512 * c + 128 * cc + 128],
                    B["xfblk"][:, slot_x + 512 * c:slot_x + 512 * c + 512],
                    start=(c == 0), stop=(c == 15)), inc=("P" if last else None))
            w("v", "P", kmm)
            kev[kseq] = r("v", lambda e, cc=cc, blk=blk, pb=pb: e.tensor_scalar(
                B["kt"][:, 8192 * cc + 512 * blk:8192 * cc + 512 * blk + 512],
                bk32(pb), B["bk"][:, cc:cc + 1], None, ALU.add),
                inc="V")
        for s4 in range(4):
            ti = 4 * blk + s4
            vb = 2 + ti % 2
            w("t", "V", vev.get(ti - 2, 0))
            for c in range(16):
                last = c == 15
                vmm = r("t", lambda e, c=c, vb=vb, slot_x=slot_x, s4=s4: e.matmul(
                    bk32(vb), B["xfblk"][:, slot_x + 512 * c + 128 * s4:slot_x + 512 * c + 128 * s4 + 128],
                    B["wvb"][:, 512 * c:512 * c + 512],
                    start=(c == 0), stop=(c == 15)), inc=("P" if last else None))
            w("v", "P", vmm)
            vev[ti] = r("v", lambda e, ti=ti, vb=vb: e.tensor_tensor(
                B["v"][:, 512 * ti:512 * ti + 512], bk32(vb), B["bv"][:], ALU.add),
                inc="V")

    w("s", "V", vev[NKT - 1])
    for s8 in range(8):
        buf = s8 % 2
        w("g", "P", last_tr.get(NKT + s8 - 2, 0))
        gsem = "G%d" % buf
        r("g", lambda e, s8=s8, buf=buf: e.dma_start(
            out=B["xst"][buf][:], in_=H["xo"][128 * s8:128 * s8 + 128, :]),
          inc=gsem, amt=16)
        gneed = S_.cnt[gsem]
        w("t", gsem, gneed)

        def oview(grp, s8=s8):
            vv = B["xfo"][:].rearrange("p (c q) -> p c q", c=16)
            return vv[:, 4 * grp:4 * grp + 4, 128 * s8:128 * s8 + 128]
        last_tr[NKT + s8] = transpose_tile(B["xst"][buf], oview)

    qmm = {}
    qev = {}
    for j in range(16):
        buf = j % 2
        w("g", "P", qmm.get(j - 2, last_tr[NKT + 7]))
        gsem = "G%d" % buf
        r("g", lambda e, j=j, buf=buf: e.dma_start(
            out=B["xst"][buf][:], in_=H["wq"][j]), inc=gsem, amt=16)
        gneed = S_.cnt[gsem]
        w("t", gsem, gneed)
        qb_ = j % 2
        w("t", "V", qev.get(j - 2, max(kev[4 * (SEQ // 512 - 1) + 3], vev[NKT - 1])))
        for c in range(16):
            for hh in range(2):
                last = (c == 15 and hh == 1)
                idx = r("t", lambda e, c=c, hh=hh, qb_=qb_, buf=buf: e.matmul(
                    ps[:, 1024 * qb_ + 512 * hh:1024 * qb_ + 512 * hh + 512],
                    B["xst"][buf][:, 128 * c:128 * c + 128],
                    B["xfo"][:, 1024 * c + 512 * hh:1024 * c + 512 * hh + 512],
                    start=(c == 0), stop=(c == 15)), inc=("P" if last else None))
                if last:
                    qmm[j] = idx
        w("v", "P", qmm[j])
        qev[j] = r("v", lambda e, j=j, qb_=qb_: e.tensor_scalar(
            B["qt"][:, 1024 * j:1024 * j + 1024], ps[:, 1024 * qb_:1024 * qb_ + 1024],
            B["bq"][:, j:j + 1], None, ALU.add), inc="V")

    if H["_dbg"]:
        w("y", "V", max(qev[15], vev[NKT - 1], kev[4 * (SEQ // 512 - 1) + 3]))
        r("y", lambda e: e.dma_start(out=H["dkt"], in_=B["kt"][:]), inc="Y0", amt=16)
        r("y", lambda e: e.dma_start(out=H["dv"], in_=B["v"][:]), inc="Y0", amt=16)
        dbg0 = r("y", lambda e: e.dma_start(out=H["dqt"], in_=B["qt"][:]), inc="Y0", amt=16)
    psA = ps[:, 2048:2560]
    psB = ps[:, 2560:3072]
    emit_program.pending_norm = None
    emit_program.last_tb = 0
    prev_exp = (0, 0)
    pair_exp_base = qev[15] * 0 + S_.cnt["A"]
    prev_tb = vev[NKT - 1]
    prev_av = {}
    sc_last = {}
    wo_dma = {}
    qt_ready = qev[15]
    attn_first = True
    for j in range(16):
        for qb in range(2):
            cc = j // 4
            kvA = j // 4
            kvB = 4 + j // 4
            exp_id = {}
            av_id = {}
            sc_id = {}

            def emit_av(kt, kvA=kvA, kvB=kvB, exp_id=exp_id, av_id=av_id,
                        first_gate=None):
                pti = kt % 4
                w("t", "A", exp_id[kt])
                if kt == 0:
                    w("t", "V", first_gate)
                st = kt == 0
                sp = kt == 63
                r("t", lambda e, kt=kt, pti=pti, kvA=kvA, st=st, sp=sp: e.matmul(
                    psA[0:64, :], B["v"][:, 512 * kt + 64 * kvA:512 * kt + 64 * kvA + 64],
                    B["pt"][pti][:, 0:512], start=st, stop=sp, tile_position=(0, 0),
                    skip_group_check=True))
                r("t", lambda e, pti=pti, st=st, sp=sp: e.matmul(
                    psB[64:128, :], B["o64"][:, 0:64], B["pt"][pti][:, 512:1024],
                    start=st, stop=sp, tile_position=(0, 64), skip_group_check=True))
                r("t", lambda e, kt=kt, pti=pti, kvB=kvB, st=st, sp=sp: e.matmul(
                    psA[64:128, :], B["v"][:, 512 * kt + 64 * kvB:512 * kt + 64 * kvB + 64],
                    B["pt"][pti][:, 512:1024], start=st, stop=sp, tile_position=(0, 64),
                    skip_group_check=True))
                av_id[kt] = r("t", lambda e, pti=pti, st=st, sp=sp: e.matmul(
                    psB[0:64, :], B["o64"][:, 0:64], B["pt"][pti][:, 0:512],
                    start=st, stop=sp, tile_position=(0, 0), skip_group_check=True), inc="P")

            gate0 = prev_tb

            def emit_sc(kt, cc=cc, j=j, qb=qb, exp_id=exp_id, sc_id=sc_id):
                scb = kt % 2
                w("t", "A", exp_id.get(kt - 2, prev_exp[scb]))
                if attn_first and kt == 0:
                    w("t", "V", qt_ready)
                r("t", lambda e, kt=kt, scb=scb: e.matmul(
                    bk32(2 * scb),
                    B["kt"][0:64, 8192 * cc + 128 * kt:8192 * cc + 128 * kt + 128],
                    B["qt"][0:64, 1024 * j + 512 * qb:1024 * j + 512 * qb + 512],
                    start=True, stop=True, tile_position=(0, 0)))
                sc_id[kt] = r("t", lambda e, kt=kt, scb=scb: e.matmul(
                    bk32(2 * scb + 1),
                    B["kt"][64:128, 8192 * cc + 128 * kt:8192 * cc + 128 * kt + 128],
                    B["qt"][64:128, 1024 * j + 512 * qb:1024 * j + 512 * qb + 512],
                    start=True, stop=True, tile_position=(64, 0)), inc="P")

            def emit_exp(kt, exp_id=exp_id, sc_id=sc_id, av_id=av_id):
                ptw = av_id.get(kt - 4, prev_av.get(60 + kt, 0))
                w("s", "P", max(sc_id[kt], ptw))
                exp_id[kt] = r("s", lambda e, kt=kt: e.activation(
                    B["pt"][kt % 4][:], ps[:, 1024 * (kt % 2):1024 * (kt % 2) + 1024],
                    AF.Exp, bias=0.0, scale=0.125), inc="A")

            for k2 in range(0, 64, 2):
                emit_sc(k2)
                emit_exp(k2)
                emit_sc(k2 + 1)
                emit_exp(k2 + 1)
                if k2 == 2 and emit_program.pending_norm is not None:
                    emit_program.pending_norm()
                    emit_program.pending_norm = None
                if k2 >= 2:
                    emit_av(k2 - 2, first_gate=gate0)
                    emit_av(k2 - 1, first_gate=gate0)
            emit_av(62, first_gate=gate0)
            emit_av(63, first_gate=gate0)
            attn_first = False

            acols = slice(1024 * j + 512 * qb, 1024 * j + 512 * qb + 512)
            av63 = av_id[63]

            def do_norm(acols=acols, av63=av63):
                w("v", "P", av63)

                w("s", "P", av63)
                r("s", lambda e: e.activation(B["bcs"][0:1, :], psB[0:1, :], AF.Ln))
                r("s", lambda e: e.activation(B["bcs"][64:65, :], psB[64:65, :], AF.Ln))
                r("s", lambda e: e.activation(B["rd"][0:1, :], B["bcs"][0:1, :], AF.Exp,
                                              bias=0.0, scale=-1.0))
                rid = r("s", lambda e: e.activation(B["rd"][64:65, :], B["bcs"][64:65, :], AF.Exp,
                                                    bias=0.0, scale=-1.0), inc="A")
                w("t", "A", rid)
                r("t", lambda e: e.matmul(
                    psB[0:64, :], B["of"][0:1, 0:64], B["rd"][0:1, :],
                    start=True, stop=True, tile_position=(0, 0), skip_group_check=True))
                bB = r("t", lambda e: e.matmul(
                    psB[64:128, :], B["of"][64:65, 0:64], B["rd"][64:65, :],
                    start=True, stop=True, tile_position=(64, 64), skip_group_check=True), inc="P")
                w("v", "P", bB)
                r("v", lambda e: e.tensor_copy(B["bcs"][:], psB[:, :]),
                  inc="V", selfwait=True)
                tB = r("v", lambda e: e.tensor_tensor(
                    B["ao"][:, acols], psA[:, :], B["bcs"][:], ALU.mult),
                    inc="V")
                return tB

            def pend(do_norm=do_norm):
                emit_program.last_tb = do_norm()
            emit_program.pending_norm = pend
            prev_tb_predicted = S_.cnt["V"] + 2
            prev_tb = prev_tb_predicted
            prev_av = {k: v for k, v in av_id.items() if k >= 60}
            pair_exp_base = exp_id[63]
            prev_exp = (exp_id[62], exp_id[63])
            sc_last[(j, qb)] = sc_id[63]
        if j % 4 == 3:
            grp4 = j // 4
            for c4 in range(4):
                c = 4 * grp4 + c4
                w("g", "P", sc_last[(j, 1)])
                if H["_dbg"]:
                    w("g", "Y0", dbg0)
                wo_dma[c] = r("g", lambda e, c=c: e.dma_start(
                    out=B["wo"][:, 2048 * c:2048 * c + 2048], in_=H["wo"][c]),
                    inc="W", amt=16)

    if emit_program.pending_norm is not None:
        emit_program.pending_norm()
        emit_program.pending_norm = None
    attn_done_v = prev_tb
    attn_done_p = S_.cnt["P"]
    if H["_dbg"]:
        w("y", "V", attn_done_v)
        dbg1 = r("y", lambda e: e.dma_start(out=H["dao"], in_=B["ao"][:]), inc="Y0", amt=16)
    attn_last_exp = pair_exp_base

    w("g", "P", attn_done_p)
    bo_id = r("g", lambda e: e.dma_start(out=B["borow"][:], in_=H["bo"][0:1, :]), inc="W", amt=16)

    if H["_dbg"]:
        w("y", "W", wo_dma[15])
        dbg2 = r("y", lambda e: e.dma_start(out=H["dwo"], in_=B["wo"][:]), inc="Y0", amt=16)
    fbid = {}
    fbev = {}
    xboid = {}
    ln1 = {}
    castid = {}
    trxid = {}
    x2fev = {}
    x2store = {}

    def op_part1(tt):
        par = tt % 2
        w("y", "V", fbev.get(4 * (tt - 2) + 3, 0))
        if tt < 2:
            w("y", "P", attn_done_p)
        ssem = "S%d" % par
        xrid = r("y", lambda e, tt=tt, par=par: e.dma_start(
            out=B["xr"][par][:], in_=H["xo"][128 * tt:128 * tt + 128, :]),
            inc=ssem, amt=16)
        w("v", ssem, xrid)
        w("v", "A", castid.get(tt - 2, 0))
        w("v", "Y%d" % par, x2store.get(tt - 2, 0))
        for fb in range(4):
            n = 4 * tt + fb
            slot = n % 4
            w("t", "V", fbev.get(n - 4, 0))
            if n == 0:
                w("t", "V", attn_done_v)
                w("t", "A", attn_last_exp)
                w("t", "W", wo_dma[15])
            if n == 0:
                w("t", "W", bo_id)
            for c in range(16):
                r("t", lambda e, c=c, fb=fb, tt=tt, slot=slot: e.matmul(
                    bk32(slot), B["ao"][:, 1024 * c + 128 * tt:1024 * c + 128 * tt + 128],
                    B["wo"][:, 2048 * c + 512 * fb:2048 * c + 512 * fb + 512],
                    start=(c == 0), stop=False))
            idx = r("t", lambda e, fb=fb, slot=slot: e.matmul(
                bk32(slot), B["o64"][0:1, :], B["borow"][0:1, 512 * fb:512 * fb + 512],
                start=False, stop=True), inc="P")
            fbid[(tt, fb)] = idx
            w("v", "P", idx)
            fbev[n] = r("v", lambda e, fb=fb, par=par, slot=slot: e.tensor_tensor(
                B["x2t"][par][:, 512 * fb:512 * fb + 512], bk32(slot),
                B["xr"][par][:, 512 * fb:512 * fb + 512], ALU.add), inc="V",
                selfwait=(fb == 3))

    def op_part2(tt):
        par = tt % 2
        w("s", "V", fbev[4 * tt + 3])
        r("s", lambda e, par=par: e.activation(
            B["sqs"][:], B["x2t"][par][:], AF.Copy, accum_out=B["ln"][:, 0:1]))
        cpid = r("s", lambda e: e.activation(
            B["ln"][:, 6:7], B["bk"][:, 0:1], AF.Copy), inc="A")
        w("v", "A", cpid)
        r("v", lambda e: e.tensor_scalar(B["ln"][:, 1:2], B["ln"][:, 0:1], 1.0 / D, None, ALU.mult),
          inc="V", selfwait=True)
        cen = r("v", lambda e, par=par: e.tensor_scalar(
            B["sqs"][:], B["x2t"][par][:], B["ln"][:, 1:2], None, ALU.subtract),
            inc="V", selfwait=True)
        w("s", "V", cen)
        r("s", lambda e, par=par: e.activation(
            B["x2t"][par][:], B["sqs"][:], AF.Square, accum_out=B["ln"][:, 2:3]))
        sqid = r("s", lambda e: e.activation(
            B["ln"][:, 6:7], B["bk"][:, 0:1], AF.Copy), inc="A")
        w("v", "A", sqid)
        vv_ = r("v", lambda e: e.tensor_scalar(
            B["ln"][:, 3:4], B["ln"][:, 2:3], 1.0 / D, EPS, ALU.mult, ALU.add),
            inc="V", selfwait=True)
        w("s", "V", vv_)
        sqr = r("s", lambda e: e.activation(B["ln"][:, 4:5], B["ln"][:, 3:4], AF.Sqrt), inc="A")
        w("v", "A", sqr)
        r("v", lambda e: e.reciprocal(B["ln"][:, 5:6], B["ln"][:, 4:5]),
          inc="V", selfwait=True)
        ln1[tt] = r("v", lambda e, par=par: e.tensor_scalar(
            B["x2t"][par][:], B["sqs"][:], B["ln"][:, 5:6], None, ALU.mult), inc="V")
        w("s", "V", ln1[tt])
        w("s", "P", trxid.get(tt - 2, 0))
        castid[tt] = r("s", lambda e, par=par: e.copy(B["x2bf"][par][:], B["x2t"][par][:]), inc="A")

    def op_part2b(tt):
        par = tt % 2
        w("t", "A", castid[tt])

        def xview(grp, tt=tt):
            vv2 = B["x2f"][:].rearrange("p (c q) -> p c q", c=16)
            return vv2[:, 4 * grp:4 * grp + 4, 128 * tt:128 * tt + 128]
        trxid[tt] = transpose_tile(B["x2bf"][par], xview)
        x2fev[tt] = ev[transpose_tile.seq - 1]
        w("y", "V", ln1[tt])
        x2store[tt] = r("y", lambda e, tt=tt, par=par: e.dma_start(
            out=H["x2s"][128 * tt:128 * tt + 128, :], in_=B["x2t"][par][:]),
            inc="Y%d" % par, amt=16)

    for tt in range(8):
        if tt >= 2:
            op_part2b(tt - 2)
        op_part1(tt)
        if tt >= 1:
            op_part2(tt - 1)
    op_part2(7)
    op_part2b(6)
    op_part2b(7)

    oproj_done_p = S_.cnt["P"]
    oproj_last_fbev = fbev[31]

    conv_id = {}
    gmm = {}
    umm = {}
    hTT = {}
    silu_id = {}
    for ib in range(64):
        par = ib % 2
        w("y", "V", conv_id.get(ib - 1, 0))
        if ib == 0:
            w("y", "P", oproj_done_p)
            if H["_dbg"]:
                w("y", "Y0", dbg1)
        r("y", lambda e, ib=ib: e.dma_start(out=B["wgst"][:], in_=H["wg"][ib]),
          inc="S0", amt=16)
        wld = r("y", lambda e, ib=ib: e.dma_start(out=B["wust"][:], in_=H["wu"][ib]),
                inc="S0", amt=16)
        w("v", "S0", wld)
        w("v", "P", umm.get(ib - 2, 0))
        r("v", lambda e, par=par: e.tensor_scalar(
            B["wgb"][par][:], B["wgst"][:], -128.0, None, ALU.add))
        conv_id[ib] = r("v", lambda e, par=par: e.tensor_scalar(
            B["wub"][par][:], B["wust"][:], -128.0, None, ALU.add), inc="V")
        for th in range(2):
            n = 2 * ib + th
            s3 = n % 3
            w("t", "V", conv_id[ib])
            w("t", "V", hTT.get(n - 3, 0))
            if n == 0:
                w("t", "A", x2fev[7])
                w("t", "V", oproj_last_fbev)
            for c in range(16):
                last = c == 15
                idx = r("t", lambda e, c=c, th=th, par=par, s3=s3: e.matmul(
                    ps[:, 1024 * s3:1024 * s3 + 512],
                    B["wgb"][par][:, 128 * c:128 * c + 128],
                    B["x2f"][:, 1024 * c + 512 * th:1024 * c + 512 * th + 512],
                    start=(c == 0), stop=(c == 15)), inc=("P" if last else None))
            gmm[n] = idx
            for c in range(16):
                last = c == 15
                idx = r("t", lambda e, c=c, th=th, par=par, s3=s3: e.matmul(
                    ps[:, 1024 * s3 + 512:1024 * s3 + 1024],
                    B["wub"][par][:, 128 * c:128 * c + 128],
                    B["x2f"][:, 1024 * c + 512 * th:1024 * c + 512 * th + 512],
                    start=(c == 0), stop=(c == 15)), inc=("P" if last else None))
            umm[n] = idx
            w("s", "P", gmm[n])
            w("s", "V", hTT.get(n - 1, 0))
            silu_id[n] = r("s", lambda e, ib=ib, s3=s3: e.activation(
                B["sg"][:], ps[:, 1024 * s3:1024 * s3 + 512], AF.Silu,
                bias=0.0, scale=B["gs"][:, ib:ib + 1]), inc="A")
            w("v", "P", umm[n])
            r("v", lambda e, ib=ib, s3=s3: e.tensor_scalar(
                B["usv"][:], ps[:, 1024 * s3 + 512:1024 * s3 + 1024],
                B["us"][:, ib:ib + 1], None, ALU.mult), inc="V", selfwait=True)
            w("v", "A", silu_id[n])
            if n == 0 and H["_dbg"]:
                w("v", "Y0", dbg2)
            if ib == 32 and th == 0:
                w("v", "Y1", x2store[7])
                w("v", "A", castid[7])
            hTT[n] = r("v", lambda e, ib=ib, th=th: e.tensor_tensor(
                B["hf"][:, 1024 * ib + 512 * th:1024 * ib + 512 * th + 512],
                B["sg"][:], B["usv"][:], ALU.mult), inc="V")
    hlast = hTT[127]
    ffa_last_silu = silu_id[127]

    dmm = {}
    fmid = {}
    trf = {}
    tev = {}
    tokstore = {}
    wdc = {}

    def ffb_evac(hc):
        par = hc % 2
        w("v", "P", dmm[4 * hc + 3])
        w("v", "P", trf.get(hc - 1, 0))
        fmid[hc] = r("v", lambda e, hc=hc, par=par: e.tensor_scalar(
            B["fm"][:], ps[:, 1024 * par:1024 * par + 1024],
            B["ds"][:, hc:hc + 1], None, ALU.mult), inc="V", selfwait=True)
        w("t", "V", fmid[hc])
        w("t", "A", tev.get(hc - 1, 0))
        for wv_ in range(8):
            off = 2048 + 128 * wv_
            last = wv_ == 7
            idx = r("t", lambda e, wv_=wv_, off=off: e.transpose(
                ps[:, off:off + 128], B["fm"][:, 128 * wv_:128 * wv_ + 128],
                B["idf"][:]), inc=("P" if last else None))
        trf[hc] = idx
        w("s", "P", trf[hc])
        w("s", "Y0", tokstore.get(hc - 1, 0))
        r("s", lambda e: e.copy(B["tok"][0][:], bk32(4)))
        tev[hc] = r("s", lambda e: e.copy(B["tok"][1][:], bk32(5)), inc="A")
        w("y", "A", tev[hc])
        for gi2 in range(2):
            tokstore[hc] = r("y", lambda e, gi2=gi2, hc=hc: e.dma_start(
                out=H["ffs"].rearrange("(w p) h -> p w h", p=128)[
                    :, 4 * gi2:4 * gi2 + 4, 128 * hc:128 * hc + 128],
                in_=B["tok"][gi2][:].rearrange("p (w h) -> p w h", w=4)),
                inc="Y0", amt=16)

    for hc in range(16):
        par = hc % 2
        for ibg in range(4):
            gi = 4 * hc + ibg
            sp_ = gi % 2
            w("g", "V", wdc.get(gi - 2, 0))
            if gi == 0:
                w("g", "V", hlast)
            gsem = "G%d" % sp_
            ldid = r("g", lambda e, hc=hc, ibg=ibg, sp_=sp_: e.dma_start(
                out=B["wdst"][sp_][:].rearrange("p (g h) -> p g h", g=16),
                in_=H["wd"][hc, 16 * ibg:16 * ibg + 16].rearrange("g p h -> p g h")),
                inc=gsem, amt=16)
            w("v", gsem, ldid)
            w("v", "P", dmm.get(gi - 2, 0))
            wdc[gi] = r("v", lambda e, sp_=sp_: e.tensor_scalar(
                B["wdb"][sp_][:], B["wdst"][sp_][:], -128.0, None, ALU.add), inc="V")
            w("t", "V", wdc[gi])
            if ibg == 0:
                w("t", "V", fmid.get(hc - 2, 0))
                if hc == 0:
                    w("t", "V", hlast)
                    w("t", "A", ffa_last_silu)
            for i16 in range(16):
                ib = 16 * ibg + i16
                for th in range(2):
                    last = (i16 == 15 and th == 1)
                    idx = r("t", lambda e, i16=i16, th=th, ib=ib, par=par, sp_=sp_: e.matmul(
                        ps[:, 1024 * par + 512 * th:1024 * par + 512 * th + 512],
                        B["wdb"][sp_][:, 128 * i16:128 * i16 + 128],
                        B["hf"][:, 1024 * ib + 512 * th:1024 * ib + 512 * th + 512],
                        start=(ib == 0), stop=(ib == 63)), inc=("P" if last else None))
            dmm[gi] = idx
            if ibg == 0 and hc >= 1:
                ffb_evac(hc - 1)
    ffb_evac(15)

    ffb_done_p = dmm[63]

    ln2 = {}
    ystore = {}
    for tt in range(8):
        par = tt % 2
        w("y", "Y0", tokstore[15])
        w("y", "V", ln2.get(tt - 2, 0))
        ssem = "S%d" % par
        r("y", lambda e, tt=tt, par=par: e.dma_start(
            out=B["ffin"][par][:], in_=H["ffs"][128 * tt:128 * tt + 128, :]),
          inc=ssem, amt=16)
        ffld = r("y", lambda e, tt=tt, par=par: e.dma_start(
            out=B["x2in"][par][:], in_=H["x2s"][128 * tt:128 * tt + 128, :]),
            inc=ssem, amt=16)
        w("v", ssem, ffld)
        if tt == 0:
            w("v", "P", ffb_done_p)
            w("v", "Y0", tokstore[15])
        w("v", "Y%d" % par, ystore.get(tt - 2, 0))
        tta = r("v", lambda e, par=par: e.tensor_tensor(
            B["outt"][par][:], B["ffin"][par][:], B["x2in"][par][:], ALU.add),
          inc="V", selfwait=True)
        w("s", "V", tta)
        r("s", lambda e, par=par: e.activation(
            B["sq2"][:], B["outt"][par][:], AF.Copy, accum_out=B["ln"][:, 8:9]))
        cpid = r("s", lambda e: e.activation(
            B["ln"][:, 14:15], B["bk"][:, 0:1], AF.Copy), inc="A")
        w("v", "A", cpid)
        r("v", lambda e: e.tensor_scalar(B["ln"][:, 9:10], B["ln"][:, 8:9], 1.0 / D, None, ALU.mult),
          inc="V", selfwait=True)
        cen = r("v", lambda e, par=par: e.tensor_scalar(
            B["sq2"][:], B["outt"][par][:], B["ln"][:, 9:10], None, ALU.subtract),
            inc="V", selfwait=True)
        w("s", "V", cen)
        r("s", lambda e, par=par: e.activation(
            B["outt"][par][:], B["sq2"][:], AF.Square, accum_out=B["ln"][:, 10:11]))
        sqid = r("s", lambda e: e.activation(
            B["ln"][:, 14:15], B["bk"][:, 0:1], AF.Copy), inc="A")
        w("v", "A", sqid)
        vv_ = r("v", lambda e: e.tensor_scalar(
            B["ln"][:, 11:12], B["ln"][:, 10:11], 1.0 / D, EPS, ALU.mult, ALU.add),
            inc="V", selfwait=True)
        w("s", "V", vv_)
        sqr = r("s", lambda e: e.activation(B["ln"][:, 12:13], B["ln"][:, 11:12], AF.Sqrt), inc="A")
        w("v", "A", sqr)
        r("v", lambda e: e.reciprocal(B["ln"][:, 13:14], B["ln"][:, 12:13]),
          inc="V", selfwait=True)
        ln2[tt] = r("v", lambda e, par=par: e.tensor_scalar(
            B["outt"][par][:], B["sq2"][:], B["ln"][:, 13:14], None, ALU.mult), inc="V")
        w("y", "V", ln2[tt])
        ystore[tt] = r("y", lambda e, tt=tt, par=par: e.dma_start(
            out=H["y"][128 * tt:128 * tt + 128, :], in_=B["outt"][par][:]),
            inc="Y%d" % par, amt=16)

    w("y", "Y0", S_.cnt["Y0"])
    w("y", "Y1", S_.cnt["Y1"])


def _pack(inputs):
    f32 = np.float32
    Wq = np.asarray(inputs["L0_attn_Wq"], f32)
    bq = np.asarray(inputs["L0_attn_bq"], f32)
    Wk = np.asarray(inputs["L0_attn_Wk"], f32)
    bk = np.asarray(inputs["L0_attn_bk"], f32)
    Wv = np.asarray(inputs["L0_attn_Wv"], f32)
    bv = np.asarray(inputs["L0_attn_bv"], f32)
    Wo = np.asarray(inputs["L0_attn_Wo"], f32)
    bo = np.asarray(inputs["L0_attn_bo"], f32)
    gw = np.ascontiguousarray(np.asarray(inputs["L0_ffn_gate_w"], np.int32))
    uw = np.ascontiguousarray(np.asarray(inputs["L0_ffn_up_w"], np.int32))
    dw = np.ascontiguousarray(np.asarray(inputs["L0_ffn_down_w"], np.int32))
    gsc = np.asarray(inputs["L0_ffn_gate_scale"], f32)
    usc = np.asarray(inputs["L0_ffn_up_scale"], f32)
    dsc = np.asarray(inputs["L0_ffn_down_scale"], f32)

    shared = {}
    wq_t = np.empty((16, 128, 2048), f32)
    bq_t = np.empty((128, 16), f32)
    for j in range(16):
        cols = np.r_[64 * j:64 * j + 64, 64 * (j + 16):64 * (j + 16) + 64]
        tmp = Wq[:, cols].reshape(16, 128, 128)
        wq_t[j] = tmp.transpose(1, 0, 2).reshape(128, 2048)
        bq_t[:, j] = bq[cols]
    shared["wq_t"] = np.ascontiguousarray(wq_t)
    shared["bq_t"] = np.ascontiguousarray(bq_t)
    col_order = np.empty(512, np.int64)
    for cc in range(4):
        for dd in range(128):
            kv = cc + (4 if dd >= 64 else 0)
            col_order[128 * cc + dd] = 64 * kv + dd % 64
    wk_re = Wk[:, col_order]
    shared["wk_t"] = np.ascontiguousarray(
        wk_re.reshape(16, 128, 512).transpose(1, 0, 2).reshape(128, 8192))
    shared["bk_t"] = np.ascontiguousarray(bk[col_order].reshape(4, 128).T)
    shared["wv_t"] = np.ascontiguousarray(
        Wv.reshape(16, 128, 512).transpose(1, 0, 2).reshape(128, 8192))
    shared["bv_bc"] = np.ascontiguousarray(np.broadcast_to(bv, (128, 512)))
    wo_t = np.empty((16, 128, 2048), f32)
    for c in range(16):
        rows = np.r_[64 * c:64 * c + 64, 64 * (c + 16):64 * (c + 16) + 64]
        wo_t[c] = Wo[rows, :]
    shared["wo_t"] = np.ascontiguousarray(wo_t)
    shared["bo_bc"] = np.ascontiguousarray(np.broadcast_to(bo, (128, 2048)))
    shared["wg_t"] = np.ascontiguousarray(
        gw.reshape(64, 128, 16, 128).transpose(0, 3, 2, 1).reshape(64, 128, 2048))
    shared["wu_t"] = np.ascontiguousarray(
        uw.reshape(64, 128, 16, 128).transpose(0, 3, 2, 1).reshape(64, 128, 2048))
    shared["wd_t"] = np.ascontiguousarray(
        dw.reshape(16, 128, 64, 128).transpose(0, 2, 3, 1))
    shared["gs_t"] = np.ascontiguousarray(gsc.reshape(64, 128).T)
    shared["us_t"] = np.ascontiguousarray(usc.reshape(64, 128).T)
    shared["ds_t"] = np.ascontiguousarray(dsc.reshape(16, 128).T)
    shared["identb"] = np.eye(128, dtype=np.float32).astype(ml_dtypes.bfloat16)
    shared["identf"] = np.eye(128, dtype=np.float32)
    shared["ones64b"] = np.ones((128, 128), np.float32).astype(ml_dtypes.bfloat16)
    shared["onesf"] = np.ones((128, 64), np.float32)
    return shared


def kernel(**inputs):
    x = np.ascontiguousarray(np.asarray(inputs["L0_attn_x"], np.float32).reshape(SEQ, D))
    shared = _pack(inputs)
    nc = build_nc()
    in_maps = []
    for c in range(NCORES):
        m = dict(shared)
        m["x_full"] = x
        m["x_own"] = np.ascontiguousarray(x[T * c:T * (c + 1)])
        in_maps.append(m)
    res = run_bass_kernel_spmd(nc, in_maps, core_ids=list(range(NCORES)))
    y = np.concatenate([np.asarray(res.results[c]["y"]) for c in range(NCORES)], axis=0)
    return y.reshape(1, SEQ, D).astype(np.float32)


if __name__ == "__main__":
    import reference as _ref
    inp = _ref.setup_inputs()
    out = kernel(**{k: np.asarray(v) for k, v in inp.items()})
    want = np.asarray(_ref.reference(**inp))
    err = np.abs(out - want)
    denom = np.mean(want.astype(np.float64) ** 2)
    print("rvar:", float(np.mean((out.astype(np.float64) - want.astype(np.float64)) ** 2) / denom))
    print("maxabs:", float(err.max()))


# version 17
# speedup vs baseline: 1.3241x; 1.0003x over previous
import numpy as np
import ml_dtypes
import concourse.bass as bass
import concourse.mybir as mybir
from concourse.bass_utils import run_bass_kernel_spmd

F32 = mybir.dt.float32
BF16 = mybir.dt.bfloat16
I32 = mybir.dt.int32
AF = mybir.ActivationFunctionType
ALU = mybir.AluOpType
AX = mybir.AxisListType

NCORES = 8
SEQ = 8192
D = 2048
T = SEQ // NCORES
NKT = SEQ // 128
EPS = 1e-5


def build_nc():
    nc = bass.Bass("TRN2", target_bir_lowering=False)

    H = {}
    H["xf"] = nc.dram_tensor("x_full", [SEQ, D], F32, kind="ExternalInput").ap()
    H["xo"] = nc.dram_tensor("x_own", [T, D], F32, kind="ExternalInput").ap()
    H["wq"] = nc.dram_tensor("wq_t", [16, 128, 2048], F32, kind="ExternalInput").ap()
    H["bq"] = nc.dram_tensor("bq_t", [128, 16], F32, kind="ExternalInput").ap()
    H["wk"] = nc.dram_tensor("wk_t", [128, 8192], F32, kind="ExternalInput").ap()
    H["bk"] = nc.dram_tensor("bk_t", [128, 4], F32, kind="ExternalInput").ap()
    H["wv"] = nc.dram_tensor("wv_t", [128, 8192], F32, kind="ExternalInput").ap()
    H["bv"] = nc.dram_tensor("bv_bc", [128, 512], F32, kind="ExternalInput").ap()
    H["wo"] = nc.dram_tensor("wo_t", [16, 128, 2048], F32, kind="ExternalInput").ap()
    H["bo"] = nc.dram_tensor("bo_bc", [128, 2048], F32, kind="ExternalInput").ap()
    H["wg"] = nc.dram_tensor("wg_t", [64, 128, 2048], I32, kind="ExternalInput").ap()
    H["wu"] = nc.dram_tensor("wu_t", [64, 128, 2048], I32, kind="ExternalInput").ap()
    H["wd"] = nc.dram_tensor("wd_t", [16, 64, 128, 128], I32, kind="ExternalInput").ap()
    H["gs"] = nc.dram_tensor("gs_t", [128, 64], F32, kind="ExternalInput").ap()
    H["us"] = nc.dram_tensor("us_t", [128, 64], F32, kind="ExternalInput").ap()
    H["ds"] = nc.dram_tensor("ds_t", [128, 16], F32, kind="ExternalInput").ap()
    H["idb"] = nc.dram_tensor("identb", [128, 128], BF16, kind="ExternalInput").ap()
    H["idf"] = nc.dram_tensor("identf", [128, 128], F32, kind="ExternalInput").ap()
    H["o64"] = nc.dram_tensor("ones64b", [128, 128], BF16, kind="ExternalInput").ap()
    H["of"] = nc.dram_tensor("onesf", [128, 64], F32, kind="ExternalInput").ap()
    H["y"] = nc.dram_tensor("y", [T, D], F32, kind="ExternalOutput").ap()
    import os
    H["_dbg"] = bool(os.environ.get("KDEBUG"))
    if H["_dbg"]:
        H["dkt"] = nc.dram_tensor("dbg_kt", [128, 32768], BF16, kind="ExternalOutput").ap()
        H["dv"] = nc.dram_tensor("dbg_v", [128, 32768], BF16, kind="ExternalOutput").ap()
        H["dqt"] = nc.dram_tensor("dbg_qt", [128, 16384], BF16, kind="ExternalOutput").ap()
        H["dao"] = nc.dram_tensor("dbg_ao", [128, 16384], BF16, kind="ExternalOutput").ap()
        H["dwo"] = nc.dram_tensor("dbg_wo", [128, 32768], BF16, kind="ExternalOutput").ap()
        H["ds_"] = nc.dram_tensor("dbg_s", [T, D], F32, kind="ExternalOutput").ap()
    import os as _os
    _scrkind = "ExternalOutput" if _os.environ.get("KDEBUG") else "Internal"
    H["x2s"] = nc.dram_tensor("x2_scr", [T, D], F32, kind=_scrkind).ap()
    H["ffs"] = nc.dram_tensor("ffn_scr", [T, D], F32, kind=_scrkind).ap()

    SB_BASE = 16512

    def at(name, shape, dtype, off):
        return nc.alloc_sbuf_tensor_at(name, shape, dtype, offset=SB_BASE + off)

    B = {}
    B["kt"] = at("kt_b", [128, 32768], BF16, 0)
    B["v"] = at("v_b", [128, 32768], BF16, 65536)
    B["wo"] = at("wo_b", [128, 32768], BF16, 0)
    B["hf"] = at("hfull", [128, 65536], BF16, 0)
    B["bo"] = at("bo_s", [128, 2048], F32, 65536)
    B["xr"] = [at(f"xr{i}", [128, 2048], F32, 73728 + 8192 * i) for i in range(2)]
    B["x2t"] = [at(f"x2t{i}", [128, 2048], F32, 90112 + 8192 * i) for i in range(2)]
    B["x2bf"] = [at("x2bf0", [128, 2048], BF16, 106496), at("x2bf1", [128, 2048], BF16, 122880)]
    B["sqs"] = at("sqs", [128, 2048], F32, 110592)
    B["xbo"] = at("xbo", [128, 2048], F32, 118784)
    B["qt"] = at("qt_b", [128, 16384], BF16, 131072)
    B["x2f"] = at("x2f", [128, 16384], BF16, 131072)
    B["xfblk"] = at("xfblk", [128, 16384], BF16, 131072)
    B["wkb"] = at("wkb", [128, 8192], BF16, 163840)
    B["ffin"] = [at(f"ffin{i}", [128, 2048], F32, 131072 + 8192 * i) for i in range(2)]
    B["x2in"] = [at(f"x2in{i}", [128, 2048], F32, 147456 + 8192 * i) for i in range(2)]
    B["ao"] = at("ao_b", [128, 16384], BF16, 163840)
    B["xfo"] = at("xfown", [128, 16384], BF16, 163840)
    B["wvb"] = at("wvb", [128, 8192], BF16, 180224)
    B["bv"] = at("bv_s", [128, 512], F32, 206752)
    B["wgst"] = at("wgstage", [128, 2048], I32, 163840)
    B["wust"] = at("wustage", [128, 2048], I32, 172032)
    B["wgb"] = [at(f"wgb{i}", [128, 2048], BF16, 180224 + 4096 * i) for i in range(2)]
    B["wub"] = [at(f"wub{i}", [128, 2048], BF16, 188416 + 4096 * i) for i in range(2)]
    B["wdst"] = [at(f"wdst{i}", [128, 2048], I32, 163840 + 8192 * i) for i in range(2)]
    B["wdb"] = [at(f"wdb{i}", [128, 2048], BF16, 180224 + 4096 * i) for i in range(2)]
    B["fm"] = at("fm_s", [128, 1024], F32, 188416)
    B["tok"] = [at(f"tokst{i}", [128, 512], F32, 192512 + 2048 * i) for i in range(2)]
    B["outt"] = [at(f"outt{i}", [128, 2048], F32, 163840 + 8192 * i) for i in range(2)]
    B["sq2"] = at("sq2", [128, 2048], F32, 180224)
    B["idb"] = at("idb_s", [128, 128], BF16, 196608)
    B["idf"] = at("idf_s", [128, 128], F32, 196864)
    B["o64"] = at("o64_s", [128, 128], BF16, 197376)
    B["of"] = at("of_s", [128, 64], F32, 197632)
    B["bq"] = at("bq_s", [128, 16], F32, 197888)
    B["bk"] = at("bk_s", [128, 4], F32, 197952)
    B["gs"] = at("gs_s", [128, 64], F32, 197984)
    B["us"] = at("us_s", [128, 64], F32, 198240)
    B["ds"] = at("ds_s", [128, 16], F32, 198496)
    B["ln"] = at("lnst", [128, 16], F32, 210848)
    B["xst"] = [at(f"xstag{i}", [128, 2048], BF16, 198560 + 4096 * i) for i in range(2)]
    B["pt"] = [at(f"pt{i}", [128, 1024], BF16, 198560 + 2048 * i) for i in range(4)]
    B["rd"] = at("rd_t", [128, 512], F32, 206752)
    B["bcs"] = at("bcs", [128, 512], F32, 208800)
    B["sg"] = at("sg_s", [128, 512], F32, 206752)
    B["rdb"] = at("rdb", [128, 512], BF16, 210912)
    B["usv"] = at("usv_s", [128, 512], F32, 208800)
    B["borow"] = at("borow", [1, 2048], BF16, 118784)

    B["nc"] = nc
    B["ps"] = nc.alloc_psum_tensor("ps32", [128, 3072], F32)
    B["psbf"] = nc.alloc_psum_tensor("psbf", [128, 2048], BF16)

    from contextlib import ExitStack
    with ExitStack() as es:
        ec = es.enter_context
        sems = {k: ec(nc.semaphore(f"sem{k}")) for k in
                ("P", "A", "V", "G0", "G1", "W", "S0", "S1", "Y0", "Y1")}
        block = ec(nc.Block())

        class Sched:
            def __init__(self, only, eng):
                self.only = only
                self.eng = eng
                self.cnt = {k: 0 for k in
                            ("P", "A", "V", "G0", "G1", "W", "S0", "S1", "Y0", "Y1")}

            def wait(self, engname, sem, val):
                if val is None or val <= 0:
                    return
                if self.only == engname:
                    self.eng.wait_ge(sems[sem], val)

            def run(self, engname, fn, inc=None, amt=1, selfwait=False):
                if inc is not None:
                    self.cnt[inc] += amt
                if self.only == engname:
                    inst = fn(self.eng)
                    if inc is not None:
                        inst.then_inc(sems[inc], amt)
                        if selfwait:
                            self.eng.wait_ge(sems[inc], self.cnt[inc])
                return self.cnt[inc] if inc is not None else None

        @block.gpsimd
        def _(g):
            emit_program(Sched("g", g), H, B)

        @block.tensor
        def _(t):
            emit_program(Sched("t", t), H, B)

        @block.scalar
        def _(s):
            emit_program(Sched("s", s), H, B)

        @block.vector
        def _(v):
            emit_program(Sched("v", v), H, B)

        @block.sync
        def _(sy):
            emit_program(Sched("y", sy), H, B)

    return nc

def emit_program(S_, H, B):
    w = S_.wait
    r = S_.run
    ps = B["ps"]
    psbf = B["psbf"]

    def bk32(i, n=512):
        return ps[:, 512 * i:512 * i + n]

    for k in ("idb", "idf", "o64", "of", "bq", "bk", "gs", "us", "ds", "bv"):
        r("g", lambda e, k=k: e.dma_start(out=B[k][:], in_=H[k]), inc="W", amt=16)
    r("g", lambda e: e.dma_start(out=B["wkb"][:], in_=H["wk"]), inc="W", amt=16)
    wconsts = r("g", lambda e: e.dma_start(out=B["wvb"][:], in_=H["wv"]), inc="W", amt=16)

    ev = {}

    def transpose_tile(src_buf, dst3):
        g0 = transpose_tile.seq
        for grp in range(4):
            half = grp % 2
            w("t", "A", ev.get(g0 + grp - 2, 0))
            for cc in range(4):
                c = 4 * grp + cc
                last = cc == 3
                tid = r("t", lambda e, c=c, half=half, cc=cc, src_buf=src_buf: e.transpose(
                    psbf[:, 1024 * half + 128 * cc:1024 * half + 128 * cc + 128],
                    src_buf[:, 128 * c:128 * c + 128], B["idb"][:]),
                    inc=("P" if last else None))
            w("s", "P", tid)
            ev[g0 + grp] = r("s", lambda e, half=half, grp=grp, dst3=dst3: e.copy(
                dst3(grp),
                psbf[:, 1024 * half:1024 * half + 512].rearrange("p (g q) -> p g q", g=4)),
                inc="A")
        transpose_tile.seq += 4
        return tid
    transpose_tile.seq = 0

    kev = {}
    vev = {}
    last_tr = {}
    NBLK = SEQ // 512
    for blk in range(NBLK):
        slot_x = 8192 * (blk % 2)
        for s4 in range(4):
            ti = 4 * blk + s4
            buf = ti % 2
            w("g", "P", last_tr.get(ti - 2, 0))
            gsem = "G%d" % buf
            r("g", lambda e, ti=ti, buf=buf: e.dma_start(
                out=B["xst"][buf][:], in_=H["xf"][128 * ti:128 * ti + 128, :]),
              inc=gsem, amt=16)
            gneed = S_.cnt[gsem]
            if ti < 2:
                w("t", "W", wconsts)
                w("v", "W", wconsts)
            w("t", gsem, gneed)

            def dview(grp, slot_x=slot_x, s4=s4):
                vv = B["xfblk"][:].rearrange("p (c q) -> p c q", q=512)
                c0 = slot_x // 512 + 4 * grp
                return vv[:, c0:c0 + 4, 128 * s4:128 * s4 + 128]
            last_tr[ti] = transpose_tile(B["xst"][buf], dview)
        w("t", "A", ev[transpose_tile.seq - 1])
        for cc in range(4):
            kseq = 4 * blk + cc
            pb = cc % 2
            w("t", "V", kev.get(kseq - 2, 0))
            for c in range(16):
                last = c == 15
                kmm = r("t", lambda e, cc=cc, c=c, pb=pb, slot_x=slot_x: e.matmul(
                    bk32(pb),
                    B["wkb"][:, 512 * c + 128 * cc:512 * c + 128 * cc + 128],
                    B["xfblk"][:, slot_x + 512 * c:slot_x + 512 * c + 512],
                    start=(c == 0), stop=(c == 15)), inc=("P" if last else None))
            w("v", "P", kmm)
            kev[kseq] = r("v", lambda e, cc=cc, blk=blk, pb=pb: e.tensor_scalar(
                B["kt"][:, 8192 * cc + 512 * blk:8192 * cc + 512 * blk + 512],
                bk32(pb), B["bk"][:, cc:cc + 1], None, ALU.add),
                inc="V")
        for s4 in range(4):
            ti = 4 * blk + s4
            vb = 2 + ti % 2
            w("t", "V", vev.get(ti - 2, 0))
            for c in range(16):
                last = c == 15
                vmm = r("t", lambda e, c=c, vb=vb, slot_x=slot_x, s4=s4: e.matmul(
                    bk32(vb), B["xfblk"][:, slot_x + 512 * c + 128 * s4:slot_x + 512 * c + 128 * s4 + 128],
                    B["wvb"][:, 512 * c:512 * c + 512],
                    start=(c == 0), stop=(c == 15)), inc=("P" if last else None))
            w("v", "P", vmm)
            vev[ti] = r("v", lambda e, ti=ti, vb=vb: e.tensor_tensor(
                B["v"][:, 512 * ti:512 * ti + 512], bk32(vb), B["bv"][:], ALU.add),
                inc="V")

    w("s", "V", vev[NKT - 1])
    for s8 in range(8):
        buf = s8 % 2
        w("g", "P", last_tr.get(NKT + s8 - 2, 0))
        gsem = "G%d" % buf
        r("g", lambda e, s8=s8, buf=buf: e.dma_start(
            out=B["xst"][buf][:], in_=H["xo"][128 * s8:128 * s8 + 128, :]),
          inc=gsem, amt=16)
        gneed = S_.cnt[gsem]
        w("t", gsem, gneed)

        def oview(grp, s8=s8):
            vv = B["xfo"][:].rearrange("p (c q) -> p c q", c=16)
            return vv[:, 4 * grp:4 * grp + 4, 128 * s8:128 * s8 + 128]
        last_tr[NKT + s8] = transpose_tile(B["xst"][buf], oview)

    qmm = {}
    qev = {}
    for j in range(16):
        buf = j % 2
        w("g", "P", qmm.get(j - 2, last_tr[NKT + 7]))
        gsem = "G%d" % buf
        r("g", lambda e, j=j, buf=buf: e.dma_start(
            out=B["xst"][buf][:], in_=H["wq"][j]), inc=gsem, amt=16)
        gneed = S_.cnt[gsem]
        w("t", gsem, gneed)
        qb_ = j % 2
        w("t", "V", qev.get(j - 2, max(kev[4 * (SEQ // 512 - 1) + 3], vev[NKT - 1])))
        for c in range(16):
            for hh in range(2):
                last = (c == 15 and hh == 1)
                idx = r("t", lambda e, c=c, hh=hh, qb_=qb_, buf=buf: e.matmul(
                    ps[:, 1024 * qb_ + 512 * hh:1024 * qb_ + 512 * hh + 512],
                    B["xst"][buf][:, 128 * c:128 * c + 128],
                    B["xfo"][:, 1024 * c + 512 * hh:1024 * c + 512 * hh + 512],
                    start=(c == 0), stop=(c == 15)), inc=("P" if last else None))
                if last:
                    qmm[j] = idx
        w("v", "P", qmm[j])
        qev[j] = r("v", lambda e, j=j, qb_=qb_: e.tensor_scalar(
            B["qt"][:, 1024 * j:1024 * j + 1024], ps[:, 1024 * qb_:1024 * qb_ + 1024],
            B["bq"][:, j:j + 1], None, ALU.add), inc="V")

    if H["_dbg"]:
        w("y", "V", max(qev[15], vev[NKT - 1], kev[4 * (SEQ // 512 - 1) + 3]))
        r("y", lambda e: e.dma_start(out=H["dkt"], in_=B["kt"][:]), inc="Y0", amt=16)
        r("y", lambda e: e.dma_start(out=H["dv"], in_=B["v"][:]), inc="Y0", amt=16)
        dbg0 = r("y", lambda e: e.dma_start(out=H["dqt"], in_=B["qt"][:]), inc="Y0", amt=16)
    psA = ps[:, 2048:2560]
    psB = ps[:, 2560:3072]
    emit_program.pending_norm = None
    emit_program.last_tb = 0
    prev_exp = (0, 0)
    pair_exp_base = qev[15] * 0 + S_.cnt["A"]
    prev_tb = vev[NKT - 1]
    prev_av = {}
    sc_last = {}
    wo_dma = {}
    qt_ready = qev[15]
    attn_first = True
    for j in range(16):
        for qb in range(2):
            cc = j // 4
            kvA = j // 4
            kvB = 4 + j // 4
            exp_id = {}
            av_id = {}
            sc_id = {}

            def emit_av(kt, kvA=kvA, kvB=kvB, exp_id=exp_id, av_id=av_id,
                        first_gate=None):
                pti = kt % 4
                w("t", "A", exp_id[kt])
                if kt == 0:
                    w("t", "V", first_gate)
                st = kt == 0
                sp = kt == 63
                r("t", lambda e, kt=kt, pti=pti, kvA=kvA, st=st, sp=sp: e.matmul(
                    psA[0:64, :], B["v"][:, 512 * kt + 64 * kvA:512 * kt + 64 * kvA + 64],
                    B["pt"][pti][:, 0:512], start=st, stop=sp, tile_position=(0, 0),
                    skip_group_check=True))
                r("t", lambda e, pti=pti, st=st, sp=sp: e.matmul(
                    psB[64:128, :], B["o64"][:, 0:64], B["pt"][pti][:, 512:1024],
                    start=st, stop=sp, tile_position=(0, 64), skip_group_check=True))
                r("t", lambda e, kt=kt, pti=pti, kvB=kvB, st=st, sp=sp: e.matmul(
                    psA[64:128, :], B["v"][:, 512 * kt + 64 * kvB:512 * kt + 64 * kvB + 64],
                    B["pt"][pti][:, 512:1024], start=st, stop=sp, tile_position=(0, 64),
                    skip_group_check=True))
                av_id[kt] = r("t", lambda e, pti=pti, st=st, sp=sp: e.matmul(
                    psB[0:64, :], B["o64"][:, 0:64], B["pt"][pti][:, 0:512],
                    start=st, stop=sp, tile_position=(0, 0), skip_group_check=True), inc="P")

            gate0 = prev_tb

            def emit_sc(kt, cc=cc, j=j, qb=qb, exp_id=exp_id, sc_id=sc_id):
                scb = kt % 2
                w("t", "A", exp_id.get(kt - 2, prev_exp[scb]))
                if attn_first and kt == 0:
                    w("t", "V", qt_ready)
                r("t", lambda e, kt=kt, scb=scb: e.matmul(
                    bk32(2 * scb),
                    B["kt"][0:64, 8192 * cc + 128 * kt:8192 * cc + 128 * kt + 128],
                    B["qt"][0:64, 1024 * j + 512 * qb:1024 * j + 512 * qb + 512],
                    start=True, stop=True, tile_position=(0, 0)))
                sc_id[kt] = r("t", lambda e, kt=kt, scb=scb: e.matmul(
                    bk32(2 * scb + 1),
                    B["kt"][64:128, 8192 * cc + 128 * kt:8192 * cc + 128 * kt + 128],
                    B["qt"][64:128, 1024 * j + 512 * qb:1024 * j + 512 * qb + 512],
                    start=True, stop=True, tile_position=(64, 0)), inc="P")

            def emit_exp(kt, exp_id=exp_id, sc_id=sc_id, av_id=av_id):
                ptw = av_id.get(kt - 4, prev_av.get(60 + kt, 0))
                w("s", "P", max(sc_id[kt], ptw))
                exp_id[kt] = r("s", lambda e, kt=kt: e.activation(
                    B["pt"][kt % 4][:], ps[:, 1024 * (kt % 2):1024 * (kt % 2) + 1024],
                    AF.Exp, bias=0.0, scale=0.125), inc="A")

            for k2 in range(0, 64, 2):
                emit_sc(k2)
                emit_exp(k2)
                emit_sc(k2 + 1)
                emit_exp(k2 + 1)
                if k2 == 2 and emit_program.pending_norm is not None:
                    emit_program.pending_norm()
                    emit_program.pending_norm = None
                if k2 >= 2:
                    emit_av(k2 - 2, first_gate=gate0)
                    emit_av(k2 - 1, first_gate=gate0)
            emit_av(62, first_gate=gate0)
            emit_av(63, first_gate=gate0)
            attn_first = False

            acols = slice(1024 * j + 512 * qb, 1024 * j + 512 * qb + 512)
            av63 = av_id[63]

            def do_norm(acols=acols, av63=av63):
                w("v", "P", av63)

                w("s", "P", av63)
                r("s", lambda e: e.activation(B["bcs"][0:1, :], psB[0:1, :], AF.Ln))
                r("s", lambda e: e.activation(B["bcs"][64:65, :], psB[64:65, :], AF.Ln))
                r("s", lambda e: e.activation(B["rd"][0:1, :], B["bcs"][0:1, :], AF.Exp,
                                              bias=0.0, scale=-1.0))
                rid = r("s", lambda e: e.activation(B["rd"][64:65, :], B["bcs"][64:65, :], AF.Exp,
                                                    bias=0.0, scale=-1.0), inc="A")
                w("t", "A", rid)
                r("t", lambda e: e.matmul(
                    psB[0:64, :], B["of"][0:1, 0:64], B["rd"][0:1, :],
                    start=True, stop=True, tile_position=(0, 0), skip_group_check=True))
                bB = r("t", lambda e: e.matmul(
                    psB[64:128, :], B["of"][64:65, 0:64], B["rd"][64:65, :],
                    start=True, stop=True, tile_position=(64, 64), skip_group_check=True), inc="P")
                w("v", "P", bB)
                r("v", lambda e: e.tensor_copy(B["bcs"][:], psB[:, :]),
                  inc="V", selfwait=True)
                tB = r("v", lambda e: e.tensor_tensor(
                    B["ao"][:, acols], psA[:, :], B["bcs"][:], ALU.mult),
                    inc="V")
                return tB

            def pend(do_norm=do_norm):
                emit_program.last_tb = do_norm()
            emit_program.pending_norm = pend
            prev_tb_predicted = S_.cnt["V"] + 2
            prev_tb = prev_tb_predicted
            prev_av = {k: v for k, v in av_id.items() if k >= 60}
            pair_exp_base = exp_id[63]
            prev_exp = (exp_id[62], exp_id[63])
            sc_last[(j, qb)] = sc_id[63]
            if j == 15 and qb == 1:
                sc_final = dict(sc_id)
        if j % 4 == 3:
            grp4 = j // 4
            for c4 in range(4):
                c = 4 * grp4 + c4
                if grp4 == 3:
                    if c4 < 3:
                        w("g", "P", sc_final[16 * c4 + 15])
                else:
                    w("g", "P", sc_last[(j, 1)])
                if H["_dbg"]:
                    w("g", "Y0", dbg0)
                if grp4 == 3 and c4 == 3:
                    w("g", "P", sc_final[55])
                    r("g", lambda e, c=c: e.dma_start(
                        out=B["wo"][:, 2048 * c:2048 * c + 1024],
                        in_=H["wo"][c][:, 0:1024]), inc="W", amt=16)
                    w("g", "P", sc_final[63])
                    wo_dma[c] = r("g", lambda e, c=c: e.dma_start(
                        out=B["wo"][:, 2048 * c + 1024:2048 * c + 2048],
                        in_=H["wo"][c][:, 1024:2048]), inc="W", amt=16)
                else:
                    wo_dma[c] = r("g", lambda e, c=c: e.dma_start(
                        out=B["wo"][:, 2048 * c:2048 * c + 2048], in_=H["wo"][c]),
                        inc="W", amt=16)

    if emit_program.pending_norm is not None:
        emit_program.pending_norm()
        emit_program.pending_norm = None
    attn_done_v = prev_tb
    attn_done_p = S_.cnt["P"]
    if H["_dbg"]:
        w("y", "V", attn_done_v)
        dbg1 = r("y", lambda e: e.dma_start(out=H["dao"], in_=B["ao"][:]), inc="Y0", amt=16)
    attn_last_exp = pair_exp_base

    w("g", "P", attn_done_p)
    bo_id = r("g", lambda e: e.dma_start(out=B["borow"][:], in_=H["bo"][0:1, :]), inc="W", amt=16)

    if H["_dbg"]:
        w("y", "W", wo_dma[15])
        dbg2 = r("y", lambda e: e.dma_start(out=H["dwo"], in_=B["wo"][:]), inc="Y0", amt=16)
    fbid = {}
    fbev = {}
    xboid = {}
    ln1 = {}
    castid = {}
    trxid = {}
    x2fev = {}
    x2store = {}

    def op_part1(tt):
        par = tt % 2
        w("y", "V", fbev.get(4 * (tt - 2) + 3, 0))
        if tt < 2:
            w("y", "P", attn_done_p)
        ssem = "S%d" % par
        xrid = r("y", lambda e, tt=tt, par=par: e.dma_start(
            out=B["xr"][par][:], in_=H["xo"][128 * tt:128 * tt + 128, :]),
            inc=ssem, amt=16)
        w("v", ssem, xrid)
        w("v", "A", castid.get(tt - 2, 0))
        w("v", "Y%d" % par, x2store.get(tt - 2, 0))
        for fb in range(4):
            n = 4 * tt + fb
            slot = n % 4
            w("t", "V", fbev.get(n - 4, 0))
            if n == 0:
                w("t", "V", attn_done_v)
                w("t", "A", attn_last_exp)
                w("t", "W", wo_dma[15])
            if n == 0:
                w("t", "W", bo_id)
            for c in range(16):
                r("t", lambda e, c=c, fb=fb, tt=tt, slot=slot: e.matmul(
                    bk32(slot), B["ao"][:, 1024 * c + 128 * tt:1024 * c + 128 * tt + 128],
                    B["wo"][:, 2048 * c + 512 * fb:2048 * c + 512 * fb + 512],
                    start=(c == 0), stop=False))
            idx = r("t", lambda e, fb=fb, slot=slot: e.matmul(
                bk32(slot), B["o64"][0:1, :], B["borow"][0:1, 512 * fb:512 * fb + 512],
                start=False, stop=True), inc="P")
            fbid[(tt, fb)] = idx
            w("v", "P", idx)
            fbev[n] = r("v", lambda e, fb=fb, par=par, slot=slot: e.tensor_tensor(
                B["x2t"][par][:, 512 * fb:512 * fb + 512], bk32(slot),
                B["xr"][par][:, 512 * fb:512 * fb + 512], ALU.add), inc="V",
                selfwait=(fb == 3))

    def op_part2(tt):
        par = tt % 2
        w("s", "V", fbev[4 * tt + 3])
        r("s", lambda e, par=par: e.activation(
            B["sqs"][:], B["x2t"][par][:], AF.Copy, accum_out=B["ln"][:, 0:1]))
        cpid = r("s", lambda e: e.activation(
            B["ln"][:, 6:7], B["bk"][:, 0:1], AF.Copy), inc="A")
        w("v", "A", cpid)
        r("v", lambda e: e.tensor_scalar(B["ln"][:, 1:2], B["ln"][:, 0:1], 1.0 / D, None, ALU.mult),
          inc="V", selfwait=True)
        cen = r("v", lambda e, par=par: e.tensor_scalar(
            B["sqs"][:], B["x2t"][par][:], B["ln"][:, 1:2], None, ALU.subtract),
            inc="V", selfwait=True)
        w("s", "V", cen)
        r("s", lambda e, par=par: e.activation(
            B["x2t"][par][:], B["sqs"][:], AF.Square, accum_out=B["ln"][:, 2:3]))
        sqid = r("s", lambda e: e.activation(
            B["ln"][:, 6:7], B["bk"][:, 0:1], AF.Copy), inc="A")
        w("v", "A", sqid)
        vv_ = r("v", lambda e: e.tensor_scalar(
            B["ln"][:, 3:4], B["ln"][:, 2:3], 1.0 / D, EPS, ALU.mult, ALU.add),
            inc="V", selfwait=True)
        w("s", "V", vv_)
        sqr = r("s", lambda e: e.activation(B["ln"][:, 4:5], B["ln"][:, 3:4], AF.Sqrt), inc="A")
        w("v", "A", sqr)
        r("v", lambda e: e.reciprocal(B["ln"][:, 5:6], B["ln"][:, 4:5]),
          inc="V", selfwait=True)
        ln1[tt] = r("v", lambda e, par=par: e.tensor_scalar(
            B["x2t"][par][:], B["sqs"][:], B["ln"][:, 5:6], None, ALU.mult), inc="V")
        w("s", "V", ln1[tt])
        w("s", "P", trxid.get(tt - 2, 0))
        castid[tt] = r("s", lambda e, par=par: e.copy(B["x2bf"][par][:], B["x2t"][par][:]), inc="A")

    def op_part2b(tt):
        par = tt % 2
        w("t", "A", castid[tt])

        def xview(grp, tt=tt):
            vv2 = B["x2f"][:].rearrange("p (c q) -> p c q", c=16)
            return vv2[:, 4 * grp:4 * grp + 4, 128 * tt:128 * tt + 128]
        trxid[tt] = transpose_tile(B["x2bf"][par], xview)
        x2fev[tt] = ev[transpose_tile.seq - 1]
        w("y", "V", ln1[tt])
        x2store[tt] = r("y", lambda e, tt=tt, par=par: e.dma_start(
            out=H["x2s"][128 * tt:128 * tt + 128, :], in_=B["x2t"][par][:]),
            inc="Y%d" % par, amt=16)

    for tt in range(8):
        if tt >= 2:
            op_part2b(tt - 2)
        op_part1(tt)
        if tt >= 1:
            op_part2(tt - 1)
    op_part2(7)
    op_part2b(6)
    op_part2b(7)

    oproj_done_p = S_.cnt["P"]
    oproj_last_fbev = fbev[31]

    conv_id = {}
    gmm = {}
    umm = {}
    hTT = {}
    silu_id = {}
    for ib in range(64):
        par = ib % 2
        w("y", "V", conv_id.get(ib - 1, 0))
        if ib == 0:
            w("y", "P", oproj_done_p)
            if H["_dbg"]:
                w("y", "Y0", dbg1)
        r("y", lambda e, ib=ib: e.dma_start(out=B["wgst"][:], in_=H["wg"][ib]),
          inc="S0", amt=16)
        wld = r("y", lambda e, ib=ib: e.dma_start(out=B["wust"][:], in_=H["wu"][ib]),
                inc="S0", amt=16)
        w("v", "S0", wld)
        w("v", "P", umm.get(ib - 2, 0))
        r("v", lambda e, par=par: e.tensor_scalar(
            B["wgb"][par][:], B["wgst"][:], -128.0, None, ALU.add))
        conv_id[ib] = r("v", lambda e, par=par: e.tensor_scalar(
            B["wub"][par][:], B["wust"][:], -128.0, None, ALU.add), inc="V")
        for th in range(2):
            n = 2 * ib + th
            s3 = n % 3
            w("t", "V", conv_id[ib])
            w("t", "V", hTT.get(n - 3, 0))
            if n == 0:
                w("t", "A", x2fev[7])
                w("t", "V", oproj_last_fbev)
            for c in range(16):
                last = c == 15
                idx = r("t", lambda e, c=c, th=th, par=par, s3=s3: e.matmul(
                    ps[:, 1024 * s3:1024 * s3 + 512],
                    B["wgb"][par][:, 128 * c:128 * c + 128],
                    B["x2f"][:, 1024 * c + 512 * th:1024 * c + 512 * th + 512],
                    start=(c == 0), stop=(c == 15)), inc=("P" if last else None))
            gmm[n] = idx
            for c in range(16):
                last = c == 15
                idx = r("t", lambda e, c=c, th=th, par=par, s3=s3: e.matmul(
                    ps[:, 1024 * s3 + 512:1024 * s3 + 1024],
                    B["wub"][par][:, 128 * c:128 * c + 128],
                    B["x2f"][:, 1024 * c + 512 * th:1024 * c + 512 * th + 512],
                    start=(c == 0), stop=(c == 15)), inc=("P" if last else None))
            umm[n] = idx
            w("s", "P", gmm[n])
            w("s", "V", hTT.get(n - 1, 0))
            silu_id[n] = r("s", lambda e, ib=ib, s3=s3: e.activation(
                B["sg"][:], ps[:, 1024 * s3:1024 * s3 + 512], AF.Silu,
                bias=0.0, scale=B["gs"][:, ib:ib + 1]), inc="A")
            w("v", "P", umm[n])
            r("v", lambda e, ib=ib, s3=s3: e.tensor_scalar(
                B["usv"][:], ps[:, 1024 * s3 + 512:1024 * s3 + 1024],
                B["us"][:, ib:ib + 1], None, ALU.mult), inc="V", selfwait=True)
            w("v", "A", silu_id[n])
            if n == 0 and H["_dbg"]:
                w("v", "Y0", dbg2)
            if ib == 32 and th == 0:
                w("v", "Y1", x2store[7])
                w("v", "A", castid[7])
            hTT[n] = r("v", lambda e, ib=ib, th=th: e.tensor_tensor(
                B["hf"][:, 1024 * ib + 512 * th:1024 * ib + 512 * th + 512],
                B["sg"][:], B["usv"][:], ALU.mult), inc="V")
    hlast = hTT[127]
    ffa_last_silu = silu_id[127]

    dmm = {}
    fmid = {}
    trf = {}
    tev = {}
    tokstore = {}
    wdc = {}

    def ffb_evac(hc):
        par = hc % 2
        w("v", "P", dmm[4 * hc + 3])
        w("v", "P", trf.get(hc - 1, 0))
        fmid[hc] = r("v", lambda e, hc=hc, par=par: e.tensor_scalar(
            B["fm"][:], ps[:, 1024 * par:1024 * par + 1024],
            B["ds"][:, hc:hc + 1], None, ALU.mult), inc="V", selfwait=True)
        w("t", "V", fmid[hc])
        w("t", "A", tev.get(hc - 1, 0))
        for wv_ in range(8):
            off = 2048 + 128 * wv_
            last = wv_ == 7
            idx = r("t", lambda e, wv_=wv_, off=off: e.transpose(
                ps[:, off:off + 128], B["fm"][:, 128 * wv_:128 * wv_ + 128],
                B["idf"][:]), inc=("P" if last else None))
        trf[hc] = idx
        w("s", "P", trf[hc])
        w("s", "Y0", tokstore.get(hc - 1, 0))
        r("s", lambda e: e.copy(B["tok"][0][:], bk32(4)))
        tev[hc] = r("s", lambda e: e.copy(B["tok"][1][:], bk32(5)), inc="A")
        w("y", "A", tev[hc])
        for gi2 in range(2):
            tokstore[hc] = r("y", lambda e, gi2=gi2, hc=hc: e.dma_start(
                out=H["ffs"].rearrange("(w p) h -> p w h", p=128)[
                    :, 4 * gi2:4 * gi2 + 4, 128 * hc:128 * hc + 128],
                in_=B["tok"][gi2][:].rearrange("p (w h) -> p w h", w=4)),
                inc="Y0", amt=16)

    for hc in range(16):
        par = hc % 2
        for ibg in range(4):
            gi = 4 * hc + ibg
            sp_ = gi % 2
            w("g", "V", wdc.get(gi - 2, 0))
            if gi == 0:
                w("g", "V", hlast)
            gsem = "G%d" % sp_
            ldid = r("g", lambda e, hc=hc, ibg=ibg, sp_=sp_: e.dma_start(
                out=B["wdst"][sp_][:].rearrange("p (g h) -> p g h", g=16),
                in_=H["wd"][hc, 16 * ibg:16 * ibg + 16].rearrange("g p h -> p g h")),
                inc=gsem, amt=16)
            w("v", gsem, ldid)
            w("v", "P", dmm.get(gi - 2, 0))
            wdc[gi] = r("v", lambda e, sp_=sp_: e.tensor_scalar(
                B["wdb"][sp_][:], B["wdst"][sp_][:], -128.0, None, ALU.add), inc="V")
            w("t", "V", wdc[gi])
            if ibg == 0:
                w("t", "V", fmid.get(hc - 2, 0))
                if hc == 0:
                    w("t", "V", hlast)
                    w("t", "A", ffa_last_silu)
            for i16 in range(16):
                ib = 16 * ibg + i16
                for th in range(2):
                    last = (i16 == 15 and th == 1)
                    idx = r("t", lambda e, i16=i16, th=th, ib=ib, par=par, sp_=sp_: e.matmul(
                        ps[:, 1024 * par + 512 * th:1024 * par + 512 * th + 512],
                        B["wdb"][sp_][:, 128 * i16:128 * i16 + 128],
                        B["hf"][:, 1024 * ib + 512 * th:1024 * ib + 512 * th + 512],
                        start=(ib == 0), stop=(ib == 63)), inc=("P" if last else None))
            dmm[gi] = idx
            if ibg == 0 and hc >= 1:
                ffb_evac(hc - 1)
    ffb_evac(15)

    ffb_done_p = dmm[63]

    ln2 = {}
    ystore = {}
    for tt in range(8):
        par = tt % 2
        w("y", "Y0", tokstore[15])
        w("y", "V", ln2.get(tt - 2, 0))
        ssem = "S%d" % par
        r("y", lambda e, tt=tt, par=par: e.dma_start(
            out=B["ffin"][par][:], in_=H["ffs"][128 * tt:128 * tt + 128, :]),
          inc=ssem, amt=16)
        ffld = r("y", lambda e, tt=tt, par=par: e.dma_start(
            out=B["x2in"][par][:], in_=H["x2s"][128 * tt:128 * tt + 128, :]),
            inc=ssem, amt=16)
        w("v", ssem, ffld)
        if tt == 0:
            w("v", "P", ffb_done_p)
            w("v", "Y0", tokstore[15])
        w("v", "Y%d" % par, ystore.get(tt - 2, 0))
        tta = r("v", lambda e, par=par: e.tensor_tensor(
            B["outt"][par][:], B["ffin"][par][:], B["x2in"][par][:], ALU.add),
          inc="V", selfwait=True)
        w("s", "V", tta)
        r("s", lambda e, par=par: e.activation(
            B["sq2"][:], B["outt"][par][:], AF.Copy, accum_out=B["ln"][:, 8:9]))
        cpid = r("s", lambda e: e.activation(
            B["ln"][:, 14:15], B["bk"][:, 0:1], AF.Copy), inc="A")
        w("v", "A", cpid)
        r("v", lambda e: e.tensor_scalar(B["ln"][:, 9:10], B["ln"][:, 8:9], 1.0 / D, None, ALU.mult),
          inc="V", selfwait=True)
        cen = r("v", lambda e, par=par: e.tensor_scalar(
            B["sq2"][:], B["outt"][par][:], B["ln"][:, 9:10], None, ALU.subtract),
            inc="V", selfwait=True)
        w("s", "V", cen)
        r("s", lambda e, par=par: e.activation(
            B["outt"][par][:], B["sq2"][:], AF.Square, accum_out=B["ln"][:, 10:11]))
        sqid = r("s", lambda e: e.activation(
            B["ln"][:, 14:15], B["bk"][:, 0:1], AF.Copy), inc="A")
        w("v", "A", sqid)
        vv_ = r("v", lambda e: e.tensor_scalar(
            B["ln"][:, 11:12], B["ln"][:, 10:11], 1.0 / D, EPS, ALU.mult, ALU.add),
            inc="V", selfwait=True)
        w("s", "V", vv_)
        sqr = r("s", lambda e: e.activation(B["ln"][:, 12:13], B["ln"][:, 11:12], AF.Sqrt), inc="A")
        w("v", "A", sqr)
        r("v", lambda e: e.reciprocal(B["ln"][:, 13:14], B["ln"][:, 12:13]),
          inc="V", selfwait=True)
        ln2[tt] = r("v", lambda e, par=par: e.tensor_scalar(
            B["outt"][par][:], B["sq2"][:], B["ln"][:, 13:14], None, ALU.mult), inc="V")
        w("y", "V", ln2[tt])
        ystore[tt] = r("y", lambda e, tt=tt, par=par: e.dma_start(
            out=H["y"][128 * tt:128 * tt + 128, :], in_=B["outt"][par][:]),
            inc="Y%d" % par, amt=16)

    w("y", "Y0", S_.cnt["Y0"])
    w("y", "Y1", S_.cnt["Y1"])


def _pack(inputs):
    f32 = np.float32
    Wq = np.asarray(inputs["L0_attn_Wq"], f32)
    bq = np.asarray(inputs["L0_attn_bq"], f32)
    Wk = np.asarray(inputs["L0_attn_Wk"], f32)
    bk = np.asarray(inputs["L0_attn_bk"], f32)
    Wv = np.asarray(inputs["L0_attn_Wv"], f32)
    bv = np.asarray(inputs["L0_attn_bv"], f32)
    Wo = np.asarray(inputs["L0_attn_Wo"], f32)
    bo = np.asarray(inputs["L0_attn_bo"], f32)
    gw = np.ascontiguousarray(np.asarray(inputs["L0_ffn_gate_w"], np.int32))
    uw = np.ascontiguousarray(np.asarray(inputs["L0_ffn_up_w"], np.int32))
    dw = np.ascontiguousarray(np.asarray(inputs["L0_ffn_down_w"], np.int32))
    gsc = np.asarray(inputs["L0_ffn_gate_scale"], f32)
    usc = np.asarray(inputs["L0_ffn_up_scale"], f32)
    dsc = np.asarray(inputs["L0_ffn_down_scale"], f32)

    shared = {}
    wq_t = np.empty((16, 128, 2048), f32)
    bq_t = np.empty((128, 16), f32)
    for j in range(16):
        cols = np.r_[64 * j:64 * j + 64, 64 * (j + 16):64 * (j + 16) + 64]
        tmp = Wq[:, cols].reshape(16, 128, 128)
        wq_t[j] = tmp.transpose(1, 0, 2).reshape(128, 2048)
        bq_t[:, j] = bq[cols]
    shared["wq_t"] = np.ascontiguousarray(wq_t)
    shared["bq_t"] = np.ascontiguousarray(bq_t)
    col_order = np.empty(512, np.int64)
    for cc in range(4):
        for dd in range(128):
            kv = cc + (4 if dd >= 64 else 0)
            col_order[128 * cc + dd] = 64 * kv + dd % 64
    wk_re = Wk[:, col_order]
    shared["wk_t"] = np.ascontiguousarray(
        wk_re.reshape(16, 128, 512).transpose(1, 0, 2).reshape(128, 8192))
    shared["bk_t"] = np.ascontiguousarray(bk[col_order].reshape(4, 128).T)
    shared["wv_t"] = np.ascontiguousarray(
        Wv.reshape(16, 128, 512).transpose(1, 0, 2).reshape(128, 8192))
    shared["bv_bc"] = np.ascontiguousarray(np.broadcast_to(bv, (128, 512)))
    wo_t = np.empty((16, 128, 2048), f32)
    for c in range(16):
        rows = np.r_[64 * c:64 * c + 64, 64 * (c + 16):64 * (c + 16) + 64]
        wo_t[c] = Wo[rows, :]
    shared["wo_t"] = np.ascontiguousarray(wo_t)
    shared["bo_bc"] = np.ascontiguousarray(np.broadcast_to(bo, (128, 2048)))
    shared["wg_t"] = np.ascontiguousarray(
        gw.reshape(64, 128, 16, 128).transpose(0, 3, 2, 1).reshape(64, 128, 2048))
    shared["wu_t"] = np.ascontiguousarray(
        uw.reshape(64, 128, 16, 128).transpose(0, 3, 2, 1).reshape(64, 128, 2048))
    shared["wd_t"] = np.ascontiguousarray(
        dw.reshape(16, 128, 64, 128).transpose(0, 2, 3, 1))
    shared["gs_t"] = np.ascontiguousarray(gsc.reshape(64, 128).T)
    shared["us_t"] = np.ascontiguousarray(usc.reshape(64, 128).T)
    shared["ds_t"] = np.ascontiguousarray(dsc.reshape(16, 128).T)
    shared["identb"] = np.eye(128, dtype=np.float32).astype(ml_dtypes.bfloat16)
    shared["identf"] = np.eye(128, dtype=np.float32)
    shared["ones64b"] = np.ones((128, 128), np.float32).astype(ml_dtypes.bfloat16)
    shared["onesf"] = np.ones((128, 64), np.float32)
    return shared


def kernel(**inputs):
    x = np.ascontiguousarray(np.asarray(inputs["L0_attn_x"], np.float32).reshape(SEQ, D))
    shared = _pack(inputs)
    nc = build_nc()
    in_maps = []
    for c in range(NCORES):
        m = dict(shared)
        m["x_full"] = x
        m["x_own"] = np.ascontiguousarray(x[T * c:T * (c + 1)])
        in_maps.append(m)
    res = run_bass_kernel_spmd(nc, in_maps, core_ids=list(range(NCORES)))
    y = np.concatenate([np.asarray(res.results[c]["y"]) for c in range(NCORES)], axis=0)
    return y.reshape(1, SEQ, D).astype(np.float32)


if __name__ == "__main__":
    import reference as _ref
    inp = _ref.setup_inputs()
    out = kernel(**{k: np.asarray(v) for k, v in inp.items()})
    want = np.asarray(_ref.reference(**inp))
    err = np.abs(out - want)
    denom = np.mean(want.astype(np.float64) ** 2)
    print("rvar:", float(np.mean((out.astype(np.float64) - want.astype(np.float64)) ** 2) / denom))
    print("maxabs:", float(err.max()))
